# version 10
# baseline (speedup 1.0000x reference)
import math
import numpy as np
import ml_dtypes
from contextlib import ExitStack

import concourse.bass as bass
import concourse.mybir as mybir
from concourse.bass_utils import run_bass_kernel_spmd

F32 = mybir.dt.float32
BF16 = mybir.dt.bfloat16
AF = mybir.ActivationFunctionType
ALU = mybir.AluOpType

B, T, E, H, D = 2, 8192, 1024, 16, 64
NCORE = 8
TO = T // 4
HO = H // 4
NTT = TO // 128
NCH = E // 128
DH = 4096
EPS = 1e-5


class Ctx:

    def __init__(self, nc):
        self.nc = nc
        self.es = ExitStack()
        self.sem_es = ExitStack()
        self._scopes = []
        self.sems = {}
        self.counts = {}

    def push(self):
        self._scopes.append(self.es)
        self.es = ExitStack()

    def pop(self):
        self.es.close()
        self.es = self._scopes.pop()

    def sb(self, name, shape, dtype):
        return self.es.enter_context(self.nc.sbuf_tensor(name, shape, dtype))

    def ps(self, name, shape, dtype=F32):
        return self.es.enter_context(self.nc.psum_tensor(name, shape, dtype))

    def sem(self, name):
        if name not in self.sems:
            self.sems[name] = self.sem_es.enter_context(self.nc.semaphore(name))
            self.counts[name] = 0
        return self.sems[name]

    def inc(self, inst, name, n=1):
        inst.then_inc(self.sem(name), n)
        self.counts[name] += n
        return self.counts[name]

    def wait(self, eng, name, val=None):
        if val is None:
            val = self.counts[name]
        eng.wait_ge(self.sem(name), val)

    def barrier(self):
        self.nc.all_engine_barrier()

    def close(self):
        self.es.close()
        self.sem_es.close()


def bfc(x):
    return np.ascontiguousarray(x).astype(ml_dtypes.bfloat16)



def emit_ln_transpose(c, x_src, xln_dst, tag, x_sb=None):
    nc = c.nc
    c.push()
    own_x = x_sb is None
    if own_x:
        x_sb = c.sb(f"x_{tag}", [128, NTT * E], F32)
    xln_tm = c.sb(f"xlntm_{tag}", [128, NTT * E], BF16)
    stats = c.sb(f"stats_{tag}", [128, NTT * 12], F32)
    mv = c.sb(f"mv_{tag}", [128, NTT * 2], F32)
    rstd = c.sb(f"rstd_{tag}", [128, NTT], F32)
    sid = c.sb(f"id_{tag}", [128, 128], BF16)
    xfm = c.sb(f"xfm_{tag}", [128, NCH * TO], BF16)
    ptr = c.ps(f"ptr_{tag}", [128, 2 * 1024], BF16)

    mi = nc.gpsimd.memset(sid[:], 1.0)
    c.inc(mi, "g")
    c.wait(nc.gpsimd, "g")
    ai = nc.gpsimd.affine_select(out=sid[:], in_=sid[:], compare_op=ALU.is_equal,
                                 fill=0.0, base=0, pattern=[[-1, 128]],
                                 channel_multiplier=1)
    c.inc(ai, "g")

    if own_x:
        ld = nc.sync.dma_start(
            out=x_sb.ap().rearrange("p (n f) -> p n f", f=E),
            in_=x_src.rearrange("(n p) f -> p n f", p=128))
        c.inc(ld, "dh", 16)
        c.wait(nc.sync, "dh")
        c.wait(nc.vector, "dh")

    for i in range(NTT):
        xt = x_sb[:, i * E:(i + 1) * E]
        st = stats[:, i * 12:(i + 1) * 12]
        s1 = nc.vector.bn_stats(out=st[:, 0:6], in_=xt[:, 0:512])
        c.inc(s1, "v")
        s2 = nc.vector.bn_stats(out=st[:, 6:12], in_=xt[:, 512:1024])
        c.inc(s2, "v")
        c.wait(nc.vector, "v")
        ag = nc.vector.bn_aggr(out=mv[:, 2 * i:2 * i + 2], in_=st[:])
        c.inc(ag, "v")
    seps = c.sb(f"eps_{tag}", [128, 1], F32)
    me = nc.gpsimd.memset(seps[:], EPS)
    c.inc(me, "g")
    c.wait(nc.scalar, "g")
    c.wait(nc.scalar, "v")
    lg = nc.scalar.activation(rstd.ap().rearrange("p (n one) -> p n one", one=1),
                              mv.ap().rearrange("p (n two) -> p n two", two=2)[:, :, 1:2],
                              AF.Ln, bias=seps[:])
    c.inc(lg, "s")
    c.wait(nc.scalar, "s")
    ex = nc.scalar.activation(rstd[:], rstd[:], AF.Exp, scale=-0.5)
    c.inc(ex, "s")
    c.wait(nc.vector, "s")
    for i in range(NTT):
        tsn = nc.vector.tensor_scalar(
            out=xln_tm[:, i * E:(i + 1) * E], in0=x_sb[:, i * E:(i + 1) * E],
            scalar1=mv[:, 2 * i:2 * i + 1], scalar2=rstd[:, i:i + 1],
            op0=ALU.subtract, op1=ALU.mult)
        c.inc(tsn, "v")
    c.wait(nc.tensor, "v")
    c.wait(nc.tensor, "g")
    c.sem("vtr")
    for i in range(NTT):
        par = i % 2
        bank = ptr[:, par * 1024:(par + 1) * 1024]
        if i >= 2:
            c.wait(nc.tensor, "vtr", c.counts["vtr"] - 1)
        for ch in range(NCH):
            tr = nc.tensor.transpose(
                bank[:, ch * 128:(ch + 1) * 128],
                xln_tm[:, i * E + ch * 128: i * E + (ch + 1) * 128], sid[:])
            if ch == NCH - 1:
                c.inc(tr, "tr")
        c.wait(nc.vector, "tr")
        cp = nc.vector.tensor_copy(
            out=xfm.ap().rearrange("p (ch t) -> p ch t", ch=NCH)[:, :, i * 128:(i + 1) * 128],
            in_=bank.rearrange("p (ch t) -> p ch t", ch=NCH))
        c.inc(cp, "vtr")
    c.wait(nc.sync, "vtr")
    stfm = nc.sync.dma_start(out=xln_dst.rearrange("(ch p) t -> p ch t", p=128),
                             in_=xfm.ap().rearrange("p (ch t) -> p ch t", ch=NCH))
    c.inc(stfm, "dh", 16)
    c.wait(nc.sync, "dh")
    c.barrier()
    c.pop()
    return x_sb


def emit_qkv(c, xln_full, wq, wk, wv, q_dst, k_dst, v_dst, tag, rope_cs=None):
    nc = c.nc
    c.push()
    sw = c.sb(f"w_{tag}", [128, NCH * 3 * 256], BF16)
    sx = c.sb(f"xg_{tag}", [128, 2 * NCH * 512], BF16)
    sqk = c.sb(f"qk_{tag}", [128, 2 * 512], BF16)
    sv = c.sb(f"v_{tag}", [128, 1024], BF16)
    pq = c.ps(f"pq_{tag}", [128, 2 * 512], F32)
    pk = c.ps(f"pk_{tag}", [128, 2 * 512], F32)
    pv = c.ps(f"pv_{tag}", [128, 2 * 512], F32)
    if rope_cs is not None:
        scos = c.sb(f"cos_{tag}", [128, T], BF16)
        ssin = c.sb(f"sin_{tag}", [128, T], BF16)
        sqr = c.sb(f"qr_{tag}", [128, 2048], BF16)
        sp32 = c.sb(f"p32_{tag}", [128, 128], BF16)
        prope = c.ps(f"pr_{tag}", [128, 1024], F32)
        c.inc(nc.gpsimd.dma_start(out=scos[:], in_=rope_cs[0]), "dg", 16)
        c.inc(nc.gpsimd.dma_start(out=ssin[:], in_=rope_cs[1]), "dg", 16)
        c.inc(nc.gpsimd.dma_start(out=sp32[:], in_=rope_cs[2]), "dg", 16)
        c.sem("rp")
    else:
        scos = ssin = sqr = sp32 = prope = None
    for wi, w in enumerate((wq, wk, wv)):
        ld = nc.gpsimd.dma_start(
            out=sw.ap().rearrange("p (ch w s) -> p ch w s", ch=NCH, w=3)[:, :, wi, :],
            in_=w.rearrange("(ch p) s -> p ch s", p=128))
        c.inc(ld, "dg", 16)
    c.wait(nc.tensor, "dg")

    c.sem("mmq"); c.sem("v"); c.sem("dh")
    qfree = kfree = vfree = 0
    for tt in range(16):
        r, tl = tt // 4, (tt % 4) * 512
        buf = tt % 2
        xb = sx[:, buf * NCH * 512:(buf + 1) * NCH * 512]
        if tt >= 2:
            c.wait(nc.sync, "mmq", tt - 1)
        lxs = "lx%d" % buf
        ld = nc.sync.dma_start(
            out=xb.rearrange("p (ch t) -> p ch t", ch=NCH),
            in_=xln_full[r * E:(r + 1) * E, tl:tl + 512].rearrange("(ch p) t -> p ch t", p=128))
        nld = c.inc(ld, lxs, 16)
        c.wait(nc.tensor, lxs, nld)
        if tt >= 1:
            c.wait(nc.tensor, "v", qfree)
        for ch in range(NCH):
            st, sp = (ch == 0), (ch == NCH - 1)
            xt = xb[:, ch * 512:(ch + 1) * 512]
            wch = sw[:, ch * 768:(ch + 1) * 768]
            for half in range(2):
                nc.tensor.matmul(pq[:, half * 512:half * 512 + 512],
                                 wch[:, half * 128:half * 128 + 128], xt,
                                 start=st, stop=sp)
        if tt >= 1:
            c.wait(nc.tensor, "v", kfree)
        for ch in range(NCH):
            st, sp = (ch == 0), (ch == NCH - 1)
            xt = xb[:, ch * 512:(ch + 1) * 512]
            wch = sw[:, ch * 768:(ch + 1) * 768]
            for half in range(2):
                nc.tensor.matmul(pk[:, half * 512:half * 512 + 512],
                                 wch[:, 256 + half * 128:256 + half * 128 + 128], xt,
                                 start=st, stop=sp)
        if tt >= 1:
            c.wait(nc.tensor, "v", vfree)
        for sub in range(4):
            for ch in range(NCH):
                st, sp = (ch == 0), (ch == NCH - 1)
                xt = xb[:, ch * 512:(ch + 1) * 512]
                wch = sw[:, ch * 768:(ch + 1) * 768]
                mm = nc.tensor.matmul(
                    pv[:, sub * 256:(sub + 1) * 256],
                    xt[:, sub * 128:(sub + 1) * 128], wch[:, 512:768],
                    start=st, stop=sp)
        c.inc(mm, "mmq")
        gt = tt * 512
        c.wait(nc.vector, "mmq")
        if tt >= 1:
            c.wait(nc.vector, "dh")
        for half in range(2):
            cp = nc.vector.tensor_copy(out=sqk[:, half * 512:half * 512 + 512],
                                       in_=pq[:, half * 512:half * 512 + 512])
            c.inc(cp, "v")
        qfree = c.counts["v"]
        if rope_cs is not None:
            _emit_rope(c, sqk, sqr, scos, ssin, sp32, prope, gt)
        c.wait(nc.sync, "v")
        st1 = nc.sync.dma_start(
            out=q_dst.rearrange("(h p) t -> p h t", p=128)[:, :, gt:gt + 512],
            in_=sqk.ap().rearrange("p (h t) -> p h t", h=2))
        c.inc(st1, "dh", 16)
        c.wait(nc.vector, "dh")
        for half in range(2):
            cp = nc.vector.tensor_copy(out=sqk[:, half * 512:half * 512 + 512],
                                       in_=pk[:, half * 512:half * 512 + 512])
            c.inc(cp, "v")
        kfree = c.counts["v"]
        if rope_cs is not None:
            _emit_rope(c, sqk, sqr, scos, ssin, sp32, prope, gt)
        c.wait(nc.sync, "v")
        st2 = nc.sync.dma_start(
            out=k_dst.rearrange("(h p) t -> p h t", p=128)[:, :, gt:gt + 512],
            in_=sqk.ap().rearrange("p (h t) -> p h t", h=2))
        c.inc(st2, "dh", 16)
        if tt >= 1:
            c.wait(nc.vector, "dh")
        cpv = nc.vector.tensor_copy(out=sv[:], in_=pv[:])
        vfree = c.inc(cpv, "v")
        c.wait(nc.sync, "v")
        stv = nc.sync.dma_start(
            out=v_dst.rearrange("(n p) d -> p n d", p=128)[:, 4 * tt:4 * tt + 4, :],
            in_=sv.ap().rearrange("p (s d) -> p s d", s=4))
        c.inc(stv, "dh", 16)
    c.wait(nc.sync, "dh")
    c.barrier()
    c.pop()


def _emit_rope(c, sqk, sqr, scos, ssin, p32_sb, prope, gt):
    nc = c.nc
    cosw = scos[:, gt:gt + 512]
    sinw = ssin[:, gt:gt + 512]
    c.wait(nc.vector, "v")
    for half in range(2):
        q = sqk[:, half * 512:half * 512 + 512]
        t1 = sqr[:, half * 512:half * 512 + 512]
        u = sqr[:, 1024 + half * 512:1024 + half * 512 + 512]
        c.inc(nc.vector.tensor_mul(out=t1, in0=q, in1=cosw), "v")
        c.inc(nc.vector.tensor_mul(out=u, in0=q, in1=sinw), "v")
    c.wait(nc.tensor, "v")
    c.wait(nc.tensor, "v", c.counts.get("_ropefree", 0))
    for half in range(2):
        u = sqr[:, 1024 + half * 512:1024 + half * 512 + 512]
        mm = nc.tensor.matmul(prope[:, half * 512:half * 512 + 512],
                              p32_sb[:], u, start=True, stop=True)
        c.inc(mm, "rp")
    c.wait(nc.vector, "rp")
    for half in range(2):
        q = sqk[:, half * 512:half * 512 + 512]
        t1 = sqr[:, half * 512:half * 512 + 512]
        ad = nc.vector.tensor_add(out=q, in0=t1,
                                  in1=prope[:, half * 512:half * 512 + 512])
        c.inc(ad, "v")
    c.counts["_ropefree"] = c.counts["v"]


def emit_attention(c, q_src, k_src, v_src, z_dst, tag, scale):
    nc = c.nc
    c.push()
    NKT = T // 128
    sk = [c.sb(f"ak{x}_{tag}", [128, T], BF16) for x in range(2)]
    sq = [c.sb(f"aq{x}_{tag}", [128, T], BF16) for x in range(2)]
    sva = [c.sb(f"av{x}a_{tag}", [128, NKT * 65], BF16) for x in range(2)]
    svb = [c.sb(f"av{x}b_{tag}", [128, NKT * 65], BF16) for x in range(2)]
    sattn = [c.sb(f"aat{x}_{tag}", [128, 2 * 1024], BF16) for x in range(2)]
    srec = [c.sb(f"arc{x}_{tag}", [1, 1024], F32) for x in range(2)]
    srecf = [c.sb(f"arf{x}_{tag}", [1, 1024], F32) for x in range(2)]
    sbc = [c.sb(f"abc{x}_{tag}", [128, 1024], F32) for x in range(2)]
    szx = [c.sb(f"azx{x}_{tag}", [128, 2 * 512], BF16) for x in range(2)]
    sones1 = c.sb(f"aon_{tag}", [1, 64], F32)
    smask = c.sb(f"amk_{tag}", [128, 4 * 512], BF16)
    ps = [c.ps(f"aps{x}_{tag}", [128, 2 * 512], F32) for x in range(2)]
    pz = [c.ps(f"apz{x}_{tag}", [128, 2 * 512], F32) for x in range(2)]

    c.inc(nc.gpsimd.memset(sones1[:], 1.0), "g")
    for m in range(4):
        c.inc(nc.gpsimd.memset(smask[:, m * 512:(m + 1) * 512], 1.0), "g")
    c.wait(nc.gpsimd, "g")
    for m in range(4):
        am = nc.gpsimd.affine_select(
            out=smask[:, m * 512:(m + 1) * 512], in_=smask[:, m * 512:(m + 1) * 512],
            compare_op=ALU.is_ge, fill=0.0, base=-128 * m,
            pattern=[[1, 512]], channel_multiplier=-1)
        c.inc(am, "g")
    for x in range(2):
        c.inc(nc.sync.dma_start(out=sk[x][:], in_=k_src[x * 128:(x + 1) * 128, :]),
              "dh", 16)
        c.inc(nc.sync.dma_start(out=sq[x][:], in_=q_src[x * 128:(x + 1) * 128, :]),
              "dh", 16)
        for hh, sv in ((0, sva[x]), (1, svb[x])):
            ld = nc.sync.dma_start(
                out=sv.ap().rearrange("p (n c) -> p n c", c=65)[:, :, 0:64],
                in_=v_src.rearrange("(n p) c -> p n c", p=128)[
                    :, :, x * 128 + hh * 64:x * 128 + hh * 64 + 64])
            c.inc(ld, "dh", 16)
            c.inc(nc.gpsimd.memset(
                sv.ap().rearrange("p (n c) -> p n c", c=65)[:, :, 64:65], 1.0), "g")
    c.wait(nc.sync, "dh")
    c.wait(nc.gpsimd, "g")
    c.barrier()

    scs = ["scA", "scB"]; exs = ["exA", "exB"]; avs = ["avA", "avB"]
    mks = ["mkA", "mkB"]; rcs = ["rcA", "rcB"]; bcs = ["bcA", "bcB"]
    zes = ["zeA", "zeB"]
    for n_ in scs + exs + avs + mks + rcs + bcs + zes:
        c.sem(n_)
    ex0 = [c.counts[exs[0]], c.counts[exs[1]]]
    mk0 = [c.counts[mks[0]], c.counts[mks[1]]]

    def e_scores(x, i, jq, kt):
        if i >= 1:
            c.wait(nc.tensor, exs[x], c.counts[exs[x]])
        if kt == 0:
            c.wait(nc.tensor, zes[x], c.counts[zes[x]])
        qsl = sq[x][:, jq * 512:(jq + 1) * 512]
        ksl = sk[x][:, kt * 128:(kt + 1) * 128]
        nc.tensor.matmul(ps[x][:, 0:512], ksl[0:64, :], qsl[0:64, :],
                         start=True, stop=True)
        mmb = nc.tensor.matmul(ps[x][:, 512:1024], ksl[64:128, :], qsl[64:128, :],
                               start=True, stop=True)
        c.inc(mmb, scs[x])

    def e_exp(x, i, jq, kt):
        par = i % 2
        c.wait(nc.scalar, scs[x])
        e = nc.scalar.activation(sattn[x][:, par * 1024:(par + 1) * 1024],
                                 ps[x][:], AF.Exp, scale=scale)
        c.inc(e, exs[x])

    def e_mask(x, i, jq, kt):
        par = i % 2
        a0 = par * 1024
        m = kt - 4 * jq
        eng = nc.vector if x == 0 else nc.gpsimd
        c.wait(eng, exs[x])
        msl = smask[:, m * 512:(m + 1) * 512]
        eng.tensor_mul(out=sattn[x][:, a0:a0 + 512],
                       in0=sattn[x][:, a0:a0 + 512], in1=msl)
        mb = eng.tensor_mul(out=sattn[x][:, a0 + 512:a0 + 1024],
                            in0=sattn[x][:, a0 + 512:a0 + 1024], in1=msl)
        c.inc(mb, mks[x])

    def e_av(x, i, jq, kt, dcount):
        nk = 4 * (jq + 1)
        if kt == 0:
            c.wait(nc.tensor, zes[x], c.counts[zes[x]])
        if kt >= 4 * jq:
            c.wait(nc.tensor, mks[x], mk0[x] + dcount)
        else:
            c.wait(nc.tensor, exs[x], ex0[x] + i + 1)
        par = i % 2
        a0 = par * 1024
        st, sp = (kt == 0), (kt == nk - 1)
        nc.tensor.matmul(pz[x][0:65, 0:512], sva[x][:, kt * 65:(kt + 1) * 65],
                         sattn[x][:, a0:a0 + 512], start=st, stop=sp,
                         skip_group_check=True)
        mm = nc.tensor.matmul(pz[x][0:65, 512:1024], svb[x][:, kt * 65:(kt + 1) * 65],
                              sattn[x][:, a0 + 512:a0 + 1024], start=st, stop=sp,
                              skip_group_check=True)
        c.inc(mm, avs[x])
        if sp:
            c.wait(nc.tensor, rcs[x], c.counts[rcs[x]] + 1)
            c.wait(nc.tensor, zes[x], c.counts[zes[x]])
            nc.tensor.matmul(ps[x][0:64, 0:512], sones1[:], srec[x][0:1, 0:512],
                             start=True, stop=True)
            bc = nc.tensor.matmul(ps[x][0:64, 512:1024], sones1[:],
                                  srec[x][0:1, 512:1024], start=True, stop=True)
            c.inc(bc, bcs[x])

    def e_zevac(x, pr, jq):
        zb = szx[x][:, (jq % 2) * 512:(jq % 2) * 512 + 512]
        c.wait(nc.scalar, avs[x])
        if c.counts[bcs[x]] > 1:
            c.wait(nc.scalar, bcs[x], c.counts[bcs[x]] - 1)
        cpa = nc.scalar.activation(srecf[x][0:1, :], pz[x][64:65, :], AF.Identity)
        c.inc(cpa, "s")
        c.wait(nc.vector, "s")
        rc2 = nc.vector.reciprocal(out=srec[x][0:1, :], in_=srecf[x][0:1, :])
        c.inc(rc2, rcs[x])
        c.wait(nc.vector, bcs[x], c.counts[bcs[x]])
        cpc = nc.vector.tensor_copy(out=sbc[x][0:64, :], in_=ps[x][0:64, :])
        c.inc(cpc, "v")
        if jq >= 2:
            c.wait(nc.vector, "dh")
        c.wait(nc.vector, "v")
        nc.vector.tensor_mul(out=zb[0:64, :], in0=pz[x][0:64, 0:512],
                             in1=sbc[x][0:64, 0:512])
        ze = nc.vector.tensor_mul(out=zb[64:128, :], in0=pz[x][0:64, 512:1024],
                                  in1=sbc[x][0:64, 512:1024])
        c.inc(ze, zes[x])
        c.wait(nc.sync, zes[x], c.counts[zes[x]])
        r0 = (jq // 4) * 256 + pr * 128
        c0 = (jq % 4) * 512
        st1 = nc.sync.dma_start(out=z_dst[r0:r0 + 128, c0:c0 + 512], in_=zb)
        c.inc(st1, "dh", 16)

    pend = [None, None]
    ndiag = [0, 0]
    stepx = [0, 0]

    def substep(x, jq, kt):
        i = stepx[x]
        e_scores(x, i, jq, kt)
        e_exp(x, i, jq, kt)
        if kt >= 4 * jq:
            ndiag[x] += 1
            e_mask(x, i, jq, kt)
        if pend[x] is not None:
            e_av(x, *pend[x])
        pend[x] = (i, jq, kt, ndiag[x])
        stepx[x] += 1

    def flush(x, jq):
        e_av(x, *pend[x])
        pend[x] = None
        e_zevac(x, x, jq)

    LAG = 4
    for jq in range(16):
        nk = 4 * (jq + 1)
        for kt in range(nk + LAG):
            if kt < nk:
                substep(0, jq, kt)
            if kt == nk:
                flush(0, jq)
            if kt >= LAG:
                substep(1, jq, kt - LAG)
        flush(1, jq)
    c.wait(nc.sync, "dh")
    c.barrier()
    c.pop()


def emit_outproj_residual(c, z_own, w_ot, x_sb, tag, z_gathered=None):
    nc = c.nc
    c.push()
    sz = c.sb(f"oz_{tag}", [128, NCH * TO], BF16)
    sw = c.sb(f"ow_{tag}", [128, NCH * E], BF16)
    pp = c.ps(f"op_{tag}", [128, 2 * 512], F32)
    if z_gathered is None:
        ld = nc.sync.dma_start(out=sz.ap().rearrange("p (a t) -> p a t", a=NCH),
                               in_=z_own.rearrange("(a p) t -> p a t", p=128))
        c.inc(ld, "dh", 16)
    else:
        zg, qd = z_gathered
        for ch in range(NCH):
            r, pr = ch // 2, ch % 2
            blk = qd + 4 * r
            ap = zg.rearrange("(b q) t -> b q t", q=256)[bass.ts(blk, 1), :, :]
            ld = nc.sync.dma_start(
                out=sz[:, ch * TO:(ch + 1) * TO],
                in_=ap.squeeze()[pr * 128:(pr + 1) * 128, :])
            c.inc(ld, "dh", 16)
    ld = nc.gpsimd.dma_start(out=sw.ap().rearrange("p (a f) -> p a f", a=NCH),
                             in_=w_ot.rearrange("(a p) f -> p a f", p=128))
    c.inc(ld, "dg", 16)
    c.wait(nc.tensor, "dh")
    c.wait(nc.tensor, "dg")
    c.sem("oev")
    for ts in range(NTT):
        for fh in range(2):
            sl = (ts * 2 + fh) % 2
            if ts * 2 + fh >= 2:
                c.wait(nc.tensor, "oev", c.counts["oev"] - 1)
            for a in range(NCH):
                mm = nc.tensor.matmul(
                    pp[:, sl * 512:sl * 512 + 512],
                    sz[:, a * TO + ts * 128:a * TO + (ts + 1) * 128],
                    sw[:, a * E + fh * 512:a * E + fh * 512 + 512],
                    start=(a == 0), stop=(a == NCH - 1))
            c.inc(mm, "omm")
            c.wait(nc.vector, "omm")
            ad = nc.vector.tensor_add(
                out=x_sb[:, ts * E + fh * 512:ts * E + fh * 512 + 512],
                in0=x_sb[:, ts * E + fh * 512:ts * E + fh * 512 + 512],
                in1=pp[:, sl * 512:sl * 512 + 512])
            c.inc(ad, "oev")
    c.barrier()
    c.pop()


def emit_store_x(c, x_sb, x_dst):
    nc = c.nc
    st = nc.sync.dma_start(out=x_dst.rearrange("(n p) f -> p n f", p=128),
                           in_=x_sb.ap().rearrange("p (n f) -> p n f", f=E))
    c.inc(st, "dh", 16)
    c.wait(nc.sync, "dh")
    c.barrier()



def build_A():
    nc = bass.Bass("TRN2", target_bir_lowering=False)
    x_in = nc.dram_tensor("x_in", [TO, E], F32, kind="ExternalInput")
    xln = nc.dram_tensor("xln", [E, TO], BF16, kind="ExternalOutput")
    c = Ctx(nc)
    emit_ln_transpose(c, x_in.ap(), xln.ap(), "a")
    c.close()
    return nc


def build_B(rope, scale):
    nc = bass.Bass("TRN2", target_bir_lowering=False)
    xg = nc.dram_tensor("xg", [4 * E, TO], BF16, kind="ExternalInput")
    wq = nc.dram_tensor("wq", [E, 256], F32, kind="ExternalInput")
    wk = nc.dram_tensor("wk", [E, 256], F32, kind="ExternalInput")
    wv = nc.dram_tensor("wv", [E, 256], F32, kind="ExternalInput")
    zo = nc.dram_tensor("zo", [E, TO], BF16, kind="ExternalOutput")
    qd_ = nc.dram_tensor("q_i", [256, T], BF16)
    kd_ = nc.dram_tensor("k_i", [256, T], BF16)
    vd_ = nc.dram_tensor("v_i", [T, 256], BF16)
    rope_cs = None
    if rope:
        rc = nc.dram_tensor("rcos", [128, T], F32, kind="ExternalInput")
        rs = nc.dram_tensor("rsin", [128, T], F32, kind="ExternalInput")
        p32 = nc.dram_tensor("p32", [128, 128], F32, kind="ExternalInput")
        rope_cs = (rc.ap(), rs.ap(), p32.ap())
    c = Ctx(nc)
    emit_qkv(c, xg.ap(), wq.ap(), wk.ap(), wv.ap(), qd_.ap(), kd_.ap(), vd_.ap(),
             "b", rope_cs=rope_cs)
    emit_attention(c, qd_.ap(), kd_.ap(), vd_.ap(), zo.ap(), "b", scale)
    c.close()
    return nc


def build_Cpart():
    nc = bass.Bass("TRN2", target_bir_lowering=False)
    x_in = nc.dram_tensor("x_in", [TO, E], F32, kind="ExternalInput")
    z_in = nc.dram_tensor("z_in", [E, TO], BF16, kind="ExternalInput")
    w_ot = nc.dram_tensor("w_ot", [E, E], F32, kind="ExternalInput")
    x_out = nc.dram_tensor("x_out", [TO, E], F32, kind="ExternalOutput")
    c = Ctx(nc)
    x_sb = c.sb("xres", [128, NTT * E], F32)
    ld = nc.sync.dma_start(out=x_sb.ap().rearrange("p (n f) -> p n f", f=E),
                           in_=x_in.ap().rearrange("(n p) f -> p n f", p=128))
    c.inc(ld, "dh", 16)
    c.wait(nc.sync, "dh")
    c.barrier()
    emit_outproj_residual(c, z_in.ap(), w_ot.ap(), x_sb, "c")
    emit_store_x(c, x_sb, x_out.ap())
    c.close()
    return nc



def emit_bfly_matrices(c, bfw, mats, plans, tag):
    nc = c.nc
    c.push()
    Y0 = c.sb(f"y0_{tag}", [128, 4096], F32)
    Y1 = c.sb(f"y1_{tag}", [128, 4096], F32)
    tmp1 = c.sb(f"bt1_{tag}", [128, 2048], F32)
    tmp2 = c.sb(f"bt2_{tag}", [128, 2048], F32)
    wrow = c.sb(f"bw_{tag}", [1, 2 * 2048], BF16)
    wt = c.sb(f"bwt_{tag}", [128, 2 * 2048], F32)
    sone = c.sb(f"bon_{tag}", [1, 128], BF16)
    pwb = c.ps(f"bpw_{tag}", [128, 2048], F32)
    c.inc(nc.gpsimd.memset(sone[:], 1.0), "g")
    c.sem("v"); c.sem("s_pe"); c.sem("dg"); c.sem("g")

    def bcast_pair(base_k, pair):
        c.wait(nc.gpsimd, "v")
        ld = nc.gpsimd.dma_start(out=wrow.ap().rearrange("o (w n) -> o w n", w=2),
                                 in_=bfw[base_k, 2 * pair:2 * pair + 2, :])
        nld = c.inc(ld, "dg", 16)
        c.wait(nc.tensor, "dg", nld)
        c.wait(nc.tensor, "v")
        for w2i in range(2):
            if w2i == 1:
                c.wait(nc.tensor, "v")
            for col in range(4):
                mmb = nc.tensor.matmul(
                    pwb[:, col * 512:(col + 1) * 512], sone[:],
                    wrow[0:1, w2i * 2048 + col * 512:w2i * 2048 + (col + 1) * 512],
                    start=True, stop=True)
            c.inc(mmb, "s_pe")
            c.wait(nc.vector, "s_pe")
            cpw = nc.vector.tensor_copy(out=wt[:, w2i * 2048:(w2i + 1) * 2048],
                                        in_=pwb[:])
            c.inc(cpw, "v")

    base = 0
    order = ["M1", "D1", "N1", "N2"]
    for name in order:
        st = plans[name]
        c.wait(nc.gpsimd, "v")
        c.inc(nc.gpsimd.memset(Y0[:], 1.0), "g")
        c.wait(nc.gpsimd, "g")
        ai = nc.gpsimd.affine_select(out=Y0[:], in_=Y0[:], compare_op=ALU.is_equal,
                                     fill=0.0, base=0, pattern=[[0, 32], [-1, 128]],
                                     channel_multiplier=1)
        c.inc(ai, "g")
        c.wait(nc.vector, "g")
        cur, nxt = Y0, Y1
        for k, s2 in enumerate(st):
            nb = 128 // (2 * s2)

            def vw(t_, half):
                return t_.rearrange("p (blk kb r) -> p blk kb r", blk=32, kb=nb)[
                    :, :, :, half * s2:(half + 1) * s2]

            def cwv(i2):
                return wt.ap()[:, i2 * 2048:(i2 + 1) * 2048].rearrange(
                    "p (blk kb s) -> p blk kb s", blk=32, kb=nb)

            tmp1v = tmp1.ap().rearrange("p (blk kb s) -> p blk kb s", blk=32, kb=nb)
            tmp2v = tmp2.ap().rearrange("p (blk kb s) -> p blk kb s", blk=32, kb=nb)
            a_ap = vw(cur.ap(), 0)
            b_ap = vw(cur.ap(), 1)
            for pair in range(2):
                bcast_pair(base + k, pair)
                c.wait(nc.vector, "v")
                m1_ = nc.vector.tensor_mul(out=tmp1v, in0=a_ap, in1=cwv(0))
                c.inc(m1_, "v")
                m2_ = nc.vector.tensor_mul(out=tmp2v, in0=b_ap, in1=cwv(1))
                c.inc(m2_, "v")
                c.wait(nc.vector, "v")
                ad = nc.vector.tensor_add(out=vw(nxt.ap(), pair), in0=tmp1v,
                                          in1=tmp2v)
                c.inc(ad, "v")
            c.wait(nc.vector, "v")
            cur, nxt = nxt, cur
        base += len(st)
        cc = nc.vector.tensor_copy(out=mats[name][:], in_=cur[:])
        c.inc(cc, "v")
        c.wait(nc.vector, "v")
    c.barrier()
    c.pop()


def emit_fold_w1(c, wexpT, bexp, M1_sb, w1e_dram, b1e_dram, tag):
    nc = c.nc
    c.push()
    swx = c.sb(f"fw_{tag}", [128, 32 * 128], BF16)
    sbe = c.sb(f"fb_{tag}", [128, 32], BF16)
    acc = c.sb(f"fa_{tag}", [128, 4096], BF16)
    brow = c.sb(f"fbr_{tag}", [1, 4096], F32)
    pf = c.ps(f"fp_{tag}", [128, 2 * 512], F32)
    c.sem("v"); c.sem("s_pe"); c.sem("dg"); c.sem("dh")

    ld = nc.gpsimd.dma_start(out=sbe.ap(), in_=bexp)
    c.inc(ld, "dgb", 16)
    c.wait(nc.tensor, "dgb")
    for co in range(32):
        c.wait(nc.tensor, "v", c.counts["v"])
        mm = nc.tensor.matmul(pf[0:1, 0:128], sbe[:, co:co + 1],
                              M1_sb[:, co * 128:(co + 1) * 128],
                              start=True, stop=True)
        c.inc(mm, "s_pe")
        c.wait(nc.vector, "s_pe")
        cp = nc.vector.tensor_copy(out=brow[0:1, co * 128:(co + 1) * 128],
                                   in_=pf[0:1, 0:128])
        c.inc(cp, "v")
    c.wait(nc.sync, "v")
    stb = nc.sync.dma_start(out=b1e_dram, in_=brow.ap())
    c.inc(stb, "dh", 16)
    c.wait(nc.tensor, "v")

    mm_after_ci = 0
    pf_free = [c.counts["v"], c.counts["v"]]
    for ci in range(NCH):
        if ci >= 1:
            c.wait(nc.gpsimd, "s_pe", mm_after_ci)
        ldw = nc.gpsimd.dma_start(
            out=swx.ap().rearrange("p (co q) -> p co q", co=32),
            in_=wexpT[:, ci * 128:(ci + 1) * 128].rearrange("(co p) q -> p co q", p=128))
        nld = c.inc(ldw, "dg", 16)
        c.wait(nc.tensor, "dg", nld)
        c.wait(nc.vector, "dh")
        for co in range(32):
            sl = co % 2
            c.wait(nc.tensor, "v", pf_free[sl])
            mm = nc.tensor.matmul(
                pf[:, sl * 512:sl * 512 + 128],
                swx[:, co * 128:(co + 1) * 128],
                M1_sb[:, co * 128:(co + 1) * 128],
                start=True, stop=True)
            c.inc(mm, "s_pe")
            c.wait(nc.vector, "s_pe")
            cp = nc.vector.tensor_copy(
                out=acc.ap().rearrange("p (c2 hi lo) -> p hi c2 lo",
                                       c2=32, hi=4)[:, :, :, co:co + 1],
                in_=pf[:, sl * 512:sl * 512 + 128].rearrange(
                    "p (hi c2 one) -> p hi c2 one", hi=4, c2=32))
            pf_free[sl] = c.inc(cp, "v")
        mm_after_ci = c.counts["s_pe"]
        c.wait(nc.sync, "v")
        stw = nc.sync.dma_start(
            out=w1e_dram[:, ci * 128:(ci + 1) * 128, :].rearrange("c2 p q -> p c2 q"),
            in_=acc.ap().rearrange("p (c2 q) -> p c2 q", c2=32))
        c.inc(stw, "dh", 16)
    c.wait(nc.sync, "dh")
    c.barrier()
    c.pop()


def emit_fold_w2(c, wconP, N2_sb, w2e_dram, tag):
    nc = c.nc
    c.push()
    swc = c.sb(f"gw_{tag}", [128, 1024], BF16)
    sev = c.sb(f"ge_{tag}", [128, 1024], BF16)
    pg = c.ps(f"gp_{tag}", [128, 2 * 512], F32)
    c.sem("v"); c.sem("s_pe"); c.sem("dg"); c.sem("dh")
    mm_after = 0
    for c2 in range(32):
        if c2 >= 1:
            c.wait(nc.gpsimd, "s_pe", mm_after)
        ldw = nc.gpsimd.dma_start(out=swc.ap(),
                                  in_=wconP[c2 * 128:(c2 + 1) * 128, :])
        nld = c.inc(ldw, "dg", 16)
        c.wait(nc.tensor, "dg", nld)
        c.wait(nc.tensor, "dh")
        for dh_ in range(2):
            mm = nc.tensor.matmul(pg[:, dh_ * 512:(dh_ + 1) * 512],
                                  N2_sb[:, c2 * 128:(c2 + 1) * 128],
                                  swc[:, dh_ * 512:(dh_ + 1) * 512],
                                  start=True, stop=True)
            c.inc(mm, "s_pe")
        mm_after = c.counts["s_pe"]
        c.wait(nc.vector, "s_pe")
        cp = nc.vector.tensor_copy(out=sev[:], in_=pg[:])
        c.inc(cp, "v")
        c.wait(nc.sync, "v")
        for hi in range(4):
            r0 = c2 + 32 * hi
            dst = w2e_dram.rearrange("(co r) d -> r co d", r=128)[r0:r0 + 1, :, :].squeeze()
            st = nc.sync.dma_start(out=dst, in_=sev[hi * 32:(hi + 1) * 32, :])
            c.inc(st, "dh", 16)
    c.wait(nc.sync, "dh")
    c.barrier()
    c.pop()


def emit_ffn_l0(c, xln_fm, w1e_dram, b1e_dram, up_bP, w2e_dram, brow2,
                D1_sb, N1_sb, relay_dram, x_sb, tag):
    nc = c.nc
    c.push()
    sxln = c.sb(f"lx_{tag}", [128, NCH * TO], BF16)
    sw1 = c.sb(f"lw1_{tag}", [128, 2 * NCH * 128], BF16)
    sh1 = c.sb(f"lh1_{tag}", [128, 2 * 512], BF16)
    sh2 = c.sb(f"lh2_{tag}", [128, 2 * 512], BF16)
    sh2o = c.sb(f"lho_{tag}", [128, 2 * 512], BF16)
    sh3 = c.sb(f"lh3_{tag}", [128, 32 * 512], BF16)
    sw2 = c.sb(f"lw2_{tag}", [128, 2 * 512], BF16)
    sb1e = c.sb(f"lb1_{tag}", [128, 32], F32)
    subp = c.sb(f"lbu_{tag}", [128, 32], F32)
    sbr2 = c.sb(f"lbr_{tag}", [2, 1024], BF16)
    sone2 = c.sb(f"lo2_{tag}", [2, 128], BF16)
    psA = c.ps(f"lpa_{tag}", [128, 2 * 512], F32)
    psB = c.ps(f"lpb_{tag}", [128, 2 * 512], F32)
    psY = c.ps(f"lpy_{tag}", [128, 4 * 512], F32)
    c.sem("v"); c.sem("s_pe"); c.sem("s"); c.sem("dg"); c.sem("dh"); c.sem("dgw")

    ld = nc.sync.dma_start(out=sxln.ap().rearrange("p (ch t) -> p ch t", ch=NCH),
                           in_=xln_fm.rearrange("(ch p) t -> p ch t", p=128))
    c.inc(ld, "dh", 16)
    for hi in range(4):
        ldb = nc.gpsimd.dma_start(
            out=sb1e[hi * 32:(hi + 1) * 32, :],
            in_=b1e_dram.rearrange("o (lo r) -> lo (o r)", lo=32)[
                :, 32 * hi:32 * hi + 32])
        c.inc(ldb, "dg", 16)
    ld = nc.gpsimd.dma_start(out=subp.ap(), in_=up_bP)
    c.inc(ld, "dg", 16)
    ld = nc.gpsimd.dma_start(out=sbr2.ap(), in_=brow2)
    c.inc(ld, "dg", 16)
    c.inc(nc.gpsimd.memset(sone2[:], 1.0), "g")
    c.wait(nc.sync, "dh")
    c.wait(nc.gpsimd, "dg")
    c.wait(nc.gpsimd, "g")
    c.barrier()

    w1used = [0, 0]
    w1ld = [0, 0]
    w2ld = [0, 0]
    hold = [0, 0]
    s_after_ev1 = [0, 0]
    s_after_ev2 = [0, 0]
    mmd_cnt = [0, 0]
    st_cnt = [0, 0]
    housed = [0, 0]
    ev3_cnt = [0, 0]
    w2used = [0, 0]
    for tt in range(4):
        t0 = tt * 512
        def ld_w1e(c2_):
            rot_ = c2_ % 2
            wbuf_ = sw1[:, rot_ * NCH * 128:(rot_ + 1) * NCH * 128]
            c.wait(nc.gpsimd, "s_pe", w1used[rot_])
            ldw = nc.gpsimd.dma_start(
                out=wbuf_.rearrange("p (ci q) -> p ci q", ci=NCH),
                in_=w1e_dram[c2_, :, :].rearrange("(ci p) q -> p ci q", p=128))
            w1ld[rot_] = c.inc(ldw, "dgw%d" % rot_, 16)
        ld_w1e(0)
        for c2 in range(32):
            rot = c2 % 2
            wbuf = sw1[:, rot * NCH * 128:(rot + 1) * NCH * 128]
            if c2 + 1 < 32:
                ld_w1e(c2 + 1)
            c.wait(nc.tensor, "dgw%d" % rot, w1ld[rot])
            c.wait(nc.tensor, "s", s_after_ev1[rot])
            for ci in range(NCH):
                mm = nc.tensor.matmul(psA[:, rot * 512:rot * 512 + 512],
                                      wbuf[:, ci * 128:(ci + 1) * 128],
                                      sxln[:, ci * TO + t0:ci * TO + t0 + 512],
                                      start=(ci == 0), stop=(ci == NCH - 1))
            w1used[rot] = c.inc(mm, "s_pe")
            c.wait(nc.scalar, "s_pe")
            c.wait(nc.scalar, "s_pe", mmd_cnt[rot])
            ev1 = nc.scalar.activation(sh1[:, rot * 512:rot * 512 + 512],
                                       psA[:, rot * 512:rot * 512 + 512],
                                       AF.Identity, bias=sb1e[:, c2:c2 + 1])
            s_after_ev1[rot] = c.inc(ev1, "s")
            c.wait(nc.tensor, "s")
            c.wait(nc.tensor, "s", s_after_ev2[rot])
            mmd = nc.tensor.matmul(psB[:, rot * 512:rot * 512 + 512],
                                   D1_sb[:, c2 * 128:(c2 + 1) * 128],
                                   sh1[:, rot * 512:rot * 512 + 512],
                                   start=True, stop=True)
            mmd_cnt[rot] = c.inc(mmd, "s_pe")
            c.wait(nc.scalar, "s_pe")
            c.wait(nc.scalar, "str%d" % rot, st_cnt[rot])
            ev2 = nc.scalar.activation(sh2[:, rot * 512:rot * 512 + 512],
                                       psB[:, rot * 512:rot * 512 + 512],
                                       AF.Gelu, bias=subp[:, c2:c2 + 1])
            s_after_ev2[rot] = c.inc(ev2, "s")
            c.wait(nc.sync, "s")
            str_ = nc.sync.dma_start(out=relay_dram[c2, :, :],
                                     in_=sh2[:, rot * 512:rot * 512 + 512])
            st_cnt[rot] = c.inc(str_, "str%d" % rot, 16)
        c.wait(nc.sync, "str0")
        c.wait(nc.sync, "str1")
        c.barrier()
        def ld_h2o(co_):
            rot_ = co_ % 2
            c.wait(nc.sync, "s_pe", housed[rot_])
            for hi in range(4):
                ldo = nc.sync.dma_start(
                    out=sh2o[hi * 32:(hi + 1) * 32, rot_ * 512:rot_ * 512 + 512],
                    in_=relay_dram[:, co_ + 32 * hi, :])
                hold[rot_] = c.inc(ldo, "dho%d" % rot_, 16)
        ld_h2o(0)
        for co in range(32):
            rot = co % 2
            if co + 1 < 32:
                ld_h2o(co + 1)
            c.wait(nc.tensor, "dho%d" % rot, hold[rot])
            c.wait(nc.tensor, "v", ev3_cnt[rot])
            mmn = nc.tensor.matmul(psA[:, rot * 512:rot * 512 + 512],
                                   N1_sb[:, co * 128:(co + 1) * 128],
                                   sh2o[:, rot * 512:rot * 512 + 512],
                                   start=True, stop=True)
            housed[rot] = c.inc(mmn, "s_pe")
            c.wait(nc.vector, "s_pe")
            ev3 = nc.vector.tensor_copy(out=sh3[:, co * 512:(co + 1) * 512],
                                        in_=psA[:, rot * 512:rot * 512 + 512])
            ev3_cnt[rot] = c.inc(ev3, "v")
        c.barrier()
        for fh in range(2):
            yfree = c.counts["v"]

            def ld_w2e(co_, fh_):
                rot_ = co_ % 2
                c.wait(nc.gpsimd, "s_pe", w2used[rot_])
                ldw = nc.gpsimd.dma_start(
                    out=sw2[:, rot_ * 512:(rot_ + 1) * 512],
                    in_=w2e_dram[co_ * 128:(co_ + 1) * 128,
                                 fh_ * 512:fh_ * 512 + 512])
                w2ld[rot_] = c.inc(ldw, "dgw%d" % rot_, 16)
            if fh == 0:
                ld_w2e(0, 0)
            for co in range(32):
                rot = co % 2
                wb = sw2[:, rot * 512:(rot + 1) * 512]
                if co + 1 < 32:
                    ld_w2e(co + 1, fh)
                elif fh == 0:
                    ld_w2e(0, 1)
                c.wait(nc.tensor, "dgw%d" % rot, w2ld[rot])
                if co == 0:
                    c.wait(nc.tensor, "v", yfree)
                for ts in range(4):
                    mm = nc.tensor.matmul(
                        psY[:, ts * 512:(ts + 1) * 512],
                        sh3[:, co * 512 + ts * 128:co * 512 + (ts + 1) * 128],
                        wb, start=(co == 0), stop=False)
                w2used[rot] = c.inc(mm, "s_pe")
            for ts in range(4):
                mm = nc.tensor.matmul(psY[:, ts * 512:(ts + 1) * 512],
                                      sone2[:], sbr2[:, fh * 512:fh * 512 + 512],
                                      start=False, stop=True)
            c.inc(mm, "s_pe")
            c.wait(nc.vector, "s_pe")
            for ts in range(4):
                gtt = (tt * 4 + ts)
                ad = nc.vector.tensor_add(
                    out=x_sb[:, gtt * E + fh * 512:gtt * E + fh * 512 + 512],
                    in0=x_sb[:, gtt * E + fh * 512:gtt * E + fh * 512 + 512],
                    in1=psY[:, ts * 512:(ts + 1) * 512])
                c.inc(ad, "v")
        c.barrier()
    c.barrier()
    c.pop()


def emit_ffn_l1(c, xln_fm, w1t, b1cm, w2t, b2row, wq1_dram, wq2_dram, x_sb, tag):
    nc = c.nc
    c.push()
    swf = c.sb(f"qw_{tag}", [128, 2 * 4096], F32)
    sabs = c.sb(f"qa_{tag}", [128, NCH + 32], F32)
    sth = c.sb(f"qt_{tag}", [128, 4], F32)
    srow = c.sb(f"qr_{tag}", [1, 8], F32)
    sqt = c.sb(f"qq_{tag}", [128, 4096], BF16)
    stmp = c.sb(f"qtm_{tag}", [128, 4096], F32)
    sone = c.sb(f"qo_{tag}", [1, 128], F32)
    sonec = c.sb(f"qoc_{tag}", [128, 1], F32)
    pq_ = c.ps(f"qp_{tag}", [1, 512], F32)
    pbc = c.ps(f"qb_{tag}", [128, 512], F32)
    c.sem("v"); c.sem("s_pe"); c.sem("s"); c.sem("dg"); c.sem("dh"); c.sem("dgw")
    c.inc(nc.gpsimd.memset(sone[:], 1.0), "g")
    c.inc(nc.gpsimd.memset(sonec[:], 1.0), "g")
    c.wait(nc.gpsimd, "g")
    c.barrier()

    absused = [0, 0]
    gld = [0, 0]
    seqA = [(w1t, ch, 0 + ch, DH) for ch in range(NCH)] + \
           [(w2t, ch, NCH + ch, E) for ch in range(32)]

    def ld_gamma(idx):
        w_ap_, ch_, _, wc_ = seqA[idx]
        rot_ = idx % 2
        c.wait(nc.gpsimd, "v", absused[rot_])
        ldw = nc.gpsimd.dma_start(
            out=swf[:, rot_ * 4096:rot_ * 4096 + wc_],
            in_=w_ap_[ch_ * 128:(ch_ + 1) * 128, :])
        gld[rot_] = c.inc(ldw, "dg%d" % rot_, 16)
    ld_gamma(0)
    for idx in range(len(seqA)):
        rot = idx % 2
        _, _, col_ch, w_cols = seqA[idx]
        if idx + 1 < len(seqA):
            ld_gamma(idx + 1)
        c.wait(nc.vector, "dg%d" % rot, gld[rot])
        rs = nc.vector.reduce_sum(out=sabs[:, col_ch:col_ch + 1],
                                  in_=swf[:, rot * 4096:rot * 4096 + w_cols],
                                  axis=mybir.AxisListType.X,
                                  apply_absolute_value=True)
        absused[rot] = c.inc(rs, "v")
    c.wait(nc.tensor, "v")
    mm1 = nc.tensor.matmul(pq_[0:1, 0:NCH], sonec[:], sabs[:, 0:NCH],
                           start=True, stop=True)
    c.inc(mm1, "s_pe")
    mm2 = nc.tensor.matmul(pq_[0:1, 32:64], sonec[:], sabs[:, NCH:NCH + 32],
                           start=True, stop=True)
    c.inc(mm2, "s_pe")
    c.wait(nc.vector, "s_pe")
    r1 = nc.vector.reduce_sum(out=srow[0:1, 0:1], in_=pq_[0:1, 0:NCH],
                              axis=mybir.AxisListType.X)
    c.inc(r1, "v")
    c.wait(nc.vector, "v")
    r2 = nc.vector.reduce_sum(out=srow[0:1, 1:2], in_=pq_[0:1, 32:64],
                              axis=mybir.AxisListType.X)
    c.inc(r2, "v")
    c.wait(nc.vector, "v")
    ts1 = nc.vector.tensor_scalar(out=srow[0:1, 2:4], in0=srow[0:1, 0:2],
                                  scalar1=1.0 / (DH * E), scalar2=None,
                                  op0=ALU.mult, op1=ALU.bypass)
    c.inc(ts1, "v")
    c.wait(nc.vector, "v")
    ts2 = nc.vector.tensor_scalar(out=srow[0:1, 4:6], in0=srow[0:1, 2:4],
                                  scalar1=EPS, scalar2=0.5,
                                  op0=ALU.add, op1=ALU.mult)
    c.inc(ts2, "v")
    c.wait(nc.vector, "v")
    ts3 = nc.vector.tensor_scalar(out=srow[0:1, 6:8], in0=srow[0:1, 4:6],
                                  scalar1=-1.0, scalar2=None,
                                  op0=ALU.mult, op1=ALU.bypass)
    c.inc(ts3, "v")
    c.wait(nc.tensor, "v")
    mmb = nc.tensor.matmul(pbc[:, 0:4], sone[:], srow[0:1, 4:8],
                           start=True, stop=True)
    c.inc(mmb, "s_pe")
    c.wait(nc.vector, "s_pe")
    cpb = nc.vector.tensor_copy(out=sth[:], in_=pbc[:, 0:4])
    c.inc(cpb, "v")
    c.wait(nc.vector, "v")
    c.barrier()

    qused = [0, 0]
    seqB = [(w1t, ch, 0, DH, 0) for ch in range(NCH)] + \
           [(w2t, ch, 1, E, 1) for ch in range(32)]

    def ld_q(idx):
        w_ap_, ch_, _, wc_, _ = seqB[idx]
        rot_ = idx % 2
        c.wait(nc.gpsimd, "v", qused[rot_])
        ldw = nc.gpsimd.dma_start(
            out=swf[:, rot_ * 4096:rot_ * 4096 + wc_],
            in_=w_ap_[ch_ * 128:(ch_ + 1) * 128, :])
        gld[rot_] = c.inc(ldw, "dg%d" % rot_, 16)
    ld_q(0)
    for idx in range(len(seqB)):
        rot = idx % 2
        _, ch, thc, w_cols, mi = seqB[idx]
        buf = swf[:, rot * 4096:(rot + 1) * 4096]
        if idx + 1 < len(seqB):
            ld_q(idx + 1)
        c.wait(nc.vector, "dg%d" % rot, gld[rot])
        c.wait(nc.vector, "dh")
        tn = nc.vector.tensor_single_scalar(out=stmp[:, 0:w_cols],
                                            in_=buf[:, 0:w_cols],
                                            scalar=sth[:, 2 + thc:3 + thc],
                                            op=ALU.is_lt)
        c.inc(tn, "v")
        c.wait(nc.vector, "v")
        qq = nc.vector.scalar_tensor_tensor(out=sqt[:, 0:w_cols],
                                            in0=buf[:, 0:w_cols],
                                            scalar=sth[:, thc:thc + 1],
                                            in1=stmp[:, 0:w_cols],
                                            op0=ALU.is_gt, op1=ALU.subtract)
        qused[rot] = c.inc(qq, "v")
        c.wait(nc.sync, "v")
        if mi == 0:
            st_ = nc.sync.dma_start(
                out=wq1_dram[:, ch * 128:(ch + 1) * 128, :].rearrange(
                    "fo p q -> p fo q"),
                in_=sqt.ap().rearrange("p (fo q) -> p fo q", fo=32))
        else:
            st_ = nc.sync.dma_start(out=wq2_dram[ch * 128:(ch + 1) * 128, :],
                                    in_=sqt[:, 0:w_cols])
        c.inc(st_, "dh", 16)
    c.wait(nc.sync, "dh")
    c.barrier()
    c.pop()

    c.push()
    sxln = c.sb(f"qx_{tag}", [128, NCH * TO], BF16)
    sw1 = c.sb(f"qw1_{tag}", [128, 2 * NCH * 128], BF16)
    sh = c.sb(f"qh_{tag}", [128, 32 * 512], BF16)
    sw2 = c.sb(f"qw2_{tag}", [128, 2 * 512], BF16)
    sb1 = c.sb(f"qb1_{tag}", [128, 32], F32)
    sbr2 = c.sb(f"qbr_{tag}", [2, 1024], BF16)
    sone2 = c.sb(f"qo2_{tag}", [2, 128], BF16)
    psA = c.ps(f"qpa_{tag}", [128, 2 * 512], F32)
    psY = c.ps(f"qpy_{tag}", [128, 4 * 512], F32)
    ld = nc.sync.dma_start(out=sxln.ap().rearrange("p (ch t) -> p ch t", ch=NCH),
                           in_=xln_fm.rearrange("(ch p) t -> p ch t", p=128))
    c.inc(ld, "dh", 16)
    ldb = nc.gpsimd.dma_start(out=sb1.ap(), in_=b1cm)
    c.inc(ldb, "dg", 16)
    c.inc(nc.gpsimd.memset(sbr2[:], 0.0), "g")
    c.wait(nc.gpsimd, "g")
    ldb = nc.gpsimd.dma_start(out=sbr2[0:1, :], in_=b2row)
    c.inc(ldb, "dg", 16)
    c.inc(nc.gpsimd.memset(sone2[:], 1.0), "g")
    c.wait(nc.gpsimd, "g")
    c.wait(nc.gpsimd, "dg")
    c.wait(nc.sync, "dh")
    c.barrier()
    z1used = [0, 0]
    z2used = [0, 0]
    for tt in range(4):
        t0 = tt * 512
        for fo in range(32):
            rot = fo % 2
            wbuf = sw1[:, rot * NCH * 128:(rot + 1) * NCH * 128]
            c.wait(nc.gpsimd, "s_pe", z1used[rot])
            ldw = nc.gpsimd.dma_start(
                out=wbuf.rearrange("p (ci q) -> p ci q", ci=NCH),
                in_=wq1_dram[fo, :, :].rearrange("(ci p) q -> p ci q", p=128))
            nldw = c.inc(ldw, "dgw%d" % rot, 16)
            c.wait(nc.tensor, "dgw%d" % rot, nldw)
            if c.counts["s"] >= 2:
                c.wait(nc.tensor, "s", c.counts["s"] - 1)
            for ci in range(NCH):
                mm = nc.tensor.matmul(psA[:, rot * 512:rot * 512 + 512],
                                      wbuf[:, ci * 128:(ci + 1) * 128],
                                      sxln[:, ci * TO + t0:ci * TO + t0 + 512],
                                      start=(ci == 0), stop=(ci == NCH - 1))
            z1used[rot] = c.inc(mm, "s_pe")
            c.wait(nc.scalar, "s_pe")
            ev = nc.scalar.activation(sh[:, fo * 512:(fo + 1) * 512],
                                      psA[:, rot * 512:rot * 512 + 512],
                                      AF.Relu, bias=sb1[:, fo:fo + 1])
            c.inc(ev, "s")
        c.wait(nc.tensor, "s")
        c.barrier()
        for fh in range(2):
            yfree = c.counts["v"]
            for co in range(32):
                rot = co % 2
                wb = sw2[:, rot * 512:(rot + 1) * 512]
                c.wait(nc.gpsimd, "s_pe", z2used[rot])
                ldw = nc.gpsimd.dma_start(
                    out=wb, in_=wq2_dram[co * 128:(co + 1) * 128,
                                         fh * 512:fh * 512 + 512])
                nldw = c.inc(ldw, "dgw%d" % rot, 16)
                c.wait(nc.tensor, "dgw%d" % rot, nldw)
                if co == 0:
                    c.wait(nc.tensor, "v", yfree)
                for ts in range(4):
                    mm = nc.tensor.matmul(
                        psY[:, ts * 512:(ts + 1) * 512],
                        sh[:, co * 512 + ts * 128:co * 512 + (ts + 1) * 128],
                        wb, start=(co == 0), stop=False)
                z2used[rot] = c.inc(mm, "s_pe")
            for ts in range(4):
                mm = nc.tensor.matmul(psY[:, ts * 512:(ts + 1) * 512],
                                      sone2[:], sbr2[:, fh * 512:fh * 512 + 512],
                                      start=False, stop=True)
            c.inc(mm, "s_pe")
            c.wait(nc.vector, "s_pe")
            for ts in range(4):
                gtt = (tt * 4 + ts)
                ad = nc.vector.tensor_add(
                    out=x_sb[:, gtt * E + fh * 512:gtt * E + fh * 512 + 512],
                    in0=x_sb[:, gtt * E + fh * 512:gtt * E + fh * 512 + 512],
                    in1=psY[:, ts * 512:(ts + 1) * 512])
                c.inc(ad, "v")
        c.barrier()
    c.barrier()
    c.pop()



def p_of_f(f):
    return (f >> 7) | (((f >> 5) & 3) << 5)


def f_of_p(c2, p2):
    return c2 | ((p2 & 31) << 7) | (((p2 >> 5) & 3) << 5)


def _wvec4(stage_w, s, f_a):
    s2g = 2 ** s
    wr = stage_w.reshape(4096 // (2 * s2g), s2g, 2, 2)
    i = f_a >> (s + 1)
    j = f_a & (s2g - 1)
    w = wr[i, j]
    return np.stack([w[:, 0, 0], w[:, 0, 1], w[:, 1, 0], w[:, 1, 1]])


def _fa_orig(s2):
    co = np.arange(32)
    nb = 128 // (2 * s2)
    kb = np.arange(nb)
    ks = np.arange(s2)
    f = (co[:, None, None] * 128 + kb[None, :, None] * 2 * s2 + ks[None, None, :])
    return f.reshape(-1)


def _fa_P(s2p):
    c2 = np.arange(32)
    nb = 128 // (2 * s2p)
    kb = np.arange(nb)
    ks = np.arange(s2p)
    p2a = kb[:, None] * 2 * s2p + ks[None, :]
    f = f_of_p(c2[:, None, None], p2a[None])
    return f.reshape(-1)


def butterfly_builder_plan(up_w, dn_w):
    rows = []
    plans = {}
    st = []
    for s in range(7):
        s2 = 2 ** s
        rows.append(_wvec4(up_w[s], s, _fa_orig(s2)))
        st.append(s2)
    plans["M1"] = st
    st = []
    for s in range(7, 12):
        s2p = 2 ** (s - 7)
        rows.append(_wvec4(up_w[s], s, _fa_P(s2p)))
        st.append(s2p)
    plans["D1"] = st
    st = []
    for s in range(7):
        s2 = 2 ** s
        rows.append(_wvec4(dn_w[s], s, _fa_orig(s2)))
        st.append(s2)
    plans["N1"] = st
    st = []
    for s in range(7, 12):
        s2p = 2 ** (s - 7)
        rows.append(_wvec4(dn_w[s], s, _fa_P(s2p)))
        st.append(s2p)
    plans["N2"] = st
    bfw = np.stack(rows).astype(np.float32)
    return bfw, plans



def host_slices(inputs):
    per_core = []
    x = inputs["L0_attn_x"]
    W1 = inputs["L0_attn_W1"]
    Q1, K1, V1 = inputs["L1_attn_Q"], inputs["L1_attn_K"], inputs["L1_attn_V"]
    for cid in range(NCORE):
        b, qd = cid // 4, cid % 4
        hs = list(range(4 * qd, 4 * qd + 4))
        d = {}
        d["x0"] = np.ascontiguousarray(x[b, qd * TO:(qd + 1) * TO, :])
        d["wq0"] = np.ascontiguousarray(
            np.concatenate([W1[h * D:(h + 1) * D, :].T for h in hs], axis=1))
        d["wk0"] = np.ascontiguousarray(
            np.concatenate([W1[E + h * D:E + (h + 1) * D, :].T for h in hs], axis=1))
        d["wv0"] = np.ascontiguousarray(
            np.concatenate([W1[2 * E + h * D:2 * E + (h + 1) * D, :].T for h in hs], axis=1))
        d["wq1"] = np.ascontiguousarray(Q1[hs].transpose(2, 0, 1).reshape(E, 4 * D))
        d["wk1"] = np.ascontiguousarray(K1[hs].transpose(2, 0, 1).reshape(E, 4 * D))
        d["wv1"] = np.ascontiguousarray(V1[hs].transpose(2, 0, 1).reshape(E, 4 * D))
        per_core.append(d)
    shared = {}
    shared["w2t"] = np.ascontiguousarray(inputs["L0_attn_W2"].T)
    shared["ot"] = np.ascontiguousarray(inputs["L1_attn_O"].T)
    shared["wexpT"] = np.ascontiguousarray(inputs["L0_ffn_W_exp"].T)
    shared["bexp_cm"] = np.ascontiguousarray(
        inputs["L0_ffn_b_exp"].reshape(32, 128).T)
    bfw, _plans = butterfly_builder_plan(inputs["L0_ffn_up_weights"],
                                         inputs["L0_ffn_down_weights"])
    shared["bfw"] = bfw
    fidx = np.array([f_of_p(i // 128, i % 128) for i in range(4096)])
    ub = inputs["L0_ffn_up_bias"]
    up_bP = ub[fidx].reshape(32, 128).T
    shared["up_bP"] = np.ascontiguousarray(up_bP.astype(np.float32))
    shared["wconP"] = np.ascontiguousarray(inputs["L0_ffn_W_con"][fidx])
    shared["brow2"] = np.ascontiguousarray(
        np.stack([inputs["L0_ffn_b_con"], inputs["L0_ffn_down_bias"]]))
    shared["w1t"] = np.ascontiguousarray(inputs["L1_ffn_W1"].T)
    shared["b1cm"] = np.ascontiguousarray(inputs["L1_ffn_b1"].reshape(32, 128).T)
    shared["w2t_ffn"] = np.ascontiguousarray(inputs["L1_ffn_W2"].T)
    shared["b2row"] = np.ascontiguousarray(inputs["L1_ffn_b2"][None, :])
    p = np.arange(128)
    inv = (1.0 / 10000.0 ** (np.arange(0, D, 2, dtype=np.float32) / D))
    ang = np.arange(T, dtype=np.float32)[None, :] * inv[p % 32][:, None]
    cos = np.cos(ang).astype(np.float32)
    sin = np.sin(ang).astype(np.float32)
    sgn = np.where(((p ^ 32) % 64) < 32, -1.0, 1.0).astype(np.float32)[:, None]
    shared["rope_cos"] = cos
    shared["rope_sin"] = (sin * sgn)
    p32 = np.zeros((128, 128), dtype=np.float32)
    p32[p, p ^ 32] = 1.0
    shared["p32"] = p32
    return per_core, shared


def build_C0():
    nc = bass.Bass("TRN2", target_bir_lowering=False)
    x_in = nc.dram_tensor("x_in", [TO, E], F32, kind="ExternalInput")
    z_in = nc.dram_tensor("z_in", [E, TO], BF16, kind="ExternalInput")
    w_ot = nc.dram_tensor("w_ot", [E, E], F32, kind="ExternalInput")
    wexpT = nc.dram_tensor("wexpT", [DH, E], F32, kind="ExternalInput")
    bexp_cm = nc.dram_tensor("bexp_cm", [128, 32], F32, kind="ExternalInput")
    bfw = nc.dram_tensor("bfw", [24, 4, 2048], F32, kind="ExternalInput")
    up_bP = nc.dram_tensor("up_bP", [128, 32], F32, kind="ExternalInput")
    wconP = nc.dram_tensor("wconP", [DH, E], F32, kind="ExternalInput")
    brow2 = nc.dram_tensor("brow2", [2, E], F32, kind="ExternalInput")
    x_out = nc.dram_tensor("x_out", [TO, E], F32, kind="ExternalOutput")
    xln_out = nc.dram_tensor("xln_out", [E, TO], BF16, kind="ExternalOutput")
    xln2 = nc.dram_tensor("xln2_i", [E, TO], BF16)
    w1e = nc.dram_tensor("w1e_i", [32, E, 128], BF16)
    b1e = nc.dram_tensor("b1e_i", [1, DH], F32)
    w2e = nc.dram_tensor("w2e_i", [DH, E], BF16)
    relay = nc.dram_tensor("relay_i", [32, 128, 512], BF16)
    c = Ctx(nc)
    x_sb = c.sb("xres", [128, NTT * E], F32)
    mats = {nm: c.sb(f"mat_{nm}", [128, 32 * 128], BF16)
            for nm in ("M1", "D1", "N1", "N2")}
    _, plans = butterfly_builder_plan(np.zeros((12, 2048, 2, 2), np.float32),
                                     np.zeros((12, 2048, 2, 2), np.float32))
    ld = nc.sync.dma_start(out=x_sb.ap().rearrange("p (n f) -> p n f", f=E),
                           in_=x_in.ap().rearrange("(n p) f -> p n f", p=128))
    c.inc(ld, "dh", 16)
    c.wait(nc.sync, "dh")
    c.barrier()
    emit_outproj_residual(c, z_in.ap(), w_ot.ap(), x_sb, "c0")
    emit_ln_transpose(c, None, xln2.ap(), "c0l", x_sb=x_sb)
    emit_bfly_matrices(c, bfw.ap(), mats, plans, "c0")
    emit_fold_w1(c, wexpT.ap(), bexp_cm.ap(), mats["M1"], w1e.ap(), b1e.ap(), "c0")
    emit_fold_w2(c, wconP.ap(), mats["N2"], w2e.ap(), "c0")
    emit_ffn_l0(c, xln2.ap(), w1e.ap(), b1e.ap(), up_bP.ap(), w2e.ap(),
                brow2.ap(), mats["D1"], mats["N1"], relay.ap(), x_sb, "c0")
    emit_ln_transpose(c, None, xln_out.ap(), "c0m", x_sb=x_sb)
    emit_store_x(c, x_sb, x_out.ap())
    c.close()
    return nc


def build_C1():
    nc = bass.Bass("TRN2", target_bir_lowering=False)
    x_in = nc.dram_tensor("x_in", [TO, E], F32, kind="ExternalInput")
    z_in = nc.dram_tensor("z_in", [E, TO], BF16, kind="ExternalInput")
    w_ot = nc.dram_tensor("w_ot", [E, E], F32, kind="ExternalInput")
    w1t = nc.dram_tensor("w1t", [E, DH], F32, kind="ExternalInput")
    b1cm = nc.dram_tensor("b1cm", [128, 32], F32, kind="ExternalInput")
    w2t = nc.dram_tensor("w2t", [DH, E], F32, kind="ExternalInput")
    b2row = nc.dram_tensor("b2row", [1, E], F32, kind="ExternalInput")
    x_out = nc.dram_tensor("x_out", [TO, E], F32, kind="ExternalOutput")
    xln2 = nc.dram_tensor("xln2_i", [E, TO], BF16)
    wq1 = nc.dram_tensor("wq1_i", [32, E, 128], BF16)
    wq2 = nc.dram_tensor("wq2_i", [DH, E], BF16)
    c = Ctx(nc)
    x_sb = c.sb("xres", [128, NTT * E], F32)
    ld = nc.sync.dma_start(out=x_sb.ap().rearrange("p (n f) -> p n f", f=E),
                           in_=x_in.ap().rearrange("(n p) f -> p n f", p=128))
    c.inc(ld, "dh", 16)
    c.wait(nc.sync, "dh")
    c.barrier()
    emit_outproj_residual(c, z_in.ap(), w_ot.ap(), x_sb, "c1")
    emit_ln_transpose(c, None, xln2.ap(), "c1l", x_sb=x_sb)
    emit_ffn_l1(c, xln2.ap(), w1t.ap(), b1cm.ap(), w2t.ap(), b2row.ap(),
                wq1.ap(), wq2.ap(), x_sb, "c1")
    emit_store_x(c, x_sb, x_out.ap())
    c.close()
    return nc



def _gather_xln(outs, key):
    xgs = []
    for b in range(2):
        xg = np.concatenate([outs[4 * b + r][key] for r in range(4)], axis=0)
        xgs.append(np.ascontiguousarray(xg))
    return [xgs[cid // 4] for cid in range(NCORE)]


def _a2a_z(outs, key):
    res = []
    for cid in range(NCORE):
        b, qd = cid // 4, cid % 4
        z = np.concatenate(
            [outs[4 * b + r][key][qd * 256:(qd + 1) * 256, :] for r in range(4)],
            axis=0)
        res.append(np.ascontiguousarray(z))
    return res


def kernel_unfused(**inputs):
    inputs = {k: np.ascontiguousarray(v) for k, v in inputs.items()}
    per_core, shared = host_slices(inputs)
    cores = list(range(NCORE))

    ncA = build_A()
    rA = run_bass_kernel_spmd(ncA, [{"x_in": pc["x0"]} for pc in per_core],
                              core_ids=cores)
    xg = _gather_xln(rA.results, "xln")

    ncB0 = build_B(rope=True, scale=0.125)
    inB0 = [{"xg": bfc(xg[i]), "wq": per_core[i]["wq0"], "wk": per_core[i]["wk0"],
             "wv": per_core[i]["wv0"], "rcos": shared["rope_cos"],
             "rsin": shared["rope_sin"], "p32": shared["p32"]} for i in cores]
    rB0 = run_bass_kernel_spmd(ncB0, inB0, core_ids=cores)
    z0 = _a2a_z(rB0.results, "zo")

    ncC0 = build_C0()
    inC0 = [{"x_in": per_core[i]["x0"], "z_in": z0[i], "w_ot": shared["w2t"],
             "wexpT": shared["wexpT"], "bexp_cm": shared["bexp_cm"],
             "bfw": shared["bfw"], "up_bP": shared["up_bP"],
             "wconP": shared["wconP"], "brow2": shared["brow2"]} for i in cores]
    rC0 = run_bass_kernel_spmd(ncC0, inC0, core_ids=cores)
    xg1 = _gather_xln(rC0.results, "xln_out")

    ncB1 = build_B(rope=False, scale=1.0)
    inB1 = [{"xg": bfc(xg1[i]), "wq": per_core[i]["wq1"], "wk": per_core[i]["wk1"],
             "wv": per_core[i]["wv1"]} for i in cores]
    rB1 = run_bass_kernel_spmd(ncB1, inB1, core_ids=cores)
    z1 = _a2a_z(rB1.results, "zo")

    ncC1 = build_C1()
    inC1 = [{"x_in": rC0.results[i]["x_out"], "z_in": z1[i], "w_ot": shared["ot"],
             "w1t": shared["w1t"], "b1cm": shared["b1cm"],
             "w2t": shared["w2t_ffn"], "b2row": shared["b2row"]} for i in cores]
    rC1 = run_bass_kernel_spmd(ncC1, inC1, core_ids=cores)

    out = np.empty((B, T, E), dtype=np.float32)
    for cid in cores:
        b, qd = cid // 4, cid % 4
        out[b, qd * TO:(qd + 1) * TO, :] = rC1.results[cid]["x_out"]
    return out



def emit_collective(c, kind, in_ap, out_ap, tag):
    nc = c.nc
    cc = nc.gpsimd.collective_compute(
        kind, mybir.AluOpType.bypass,
        replica_groups=[[0, 1, 2, 3], [4, 5, 6, 7]],
        ins=[in_ap], outs=[out_ap])
    c.inc(cc, "cc")
    c.wait(nc.gpsimd, "cc")
    c.barrier()


def build_fused():
    nc = bass.Bass("TRN2", target_bir_lowering=False)
    t_in = {}
    def inp(name, shape, dtype=F32):
        t_in[name] = nc.dram_tensor(name, shape, dtype, kind="ExternalInput")
        return t_in[name].ap()
    x0 = inp("x0", [TO, E])
    wq0 = inp("wq0", [E, 256]); wk0 = inp("wk0", [E, 256]); wv0 = inp("wv0", [E, 256])
    rcos = inp("rcos", [128, T]); rsin = inp("rsin", [128, T]); p32 = inp("p32", [128, 128])
    w2t = inp("w2t", [E, E])
    wexpT = inp("wexpT", [DH, E]); bexp_cm = inp("bexp_cm", [128, 32])
    bfw = inp("bfw", [24, 4, 2048]); up_bP = inp("up_bP", [128, 32])
    wconP = inp("wconP", [DH, E]); brow2 = inp("brow2", [2, E])
    wq1 = inp("wq1", [E, 256]); wk1 = inp("wk1", [E, 256]); wv1 = inp("wv1", [E, 256])
    ot = inp("ot", [E, E])
    w1t = inp("w1t", [E, DH]); b1cm = inp("b1cm", [128, 32])
    w2t_ffn = inp("w2t_ffn", [DH, E]); b2row = inp("b2row", [1, E])
    x_out = nc.dram_tensor("x_out", [TO, E], F32, kind="ExternalOutput")

    def internal(name, shape, dtype):
        return nc.dram_tensor(name, shape, dtype).ap()
    xln_a = internal("xln_a", [E, TO], BF16)
    xg_a = internal("xg_a", [4 * E, TO], BF16)
    q0d = internal("q0d", [256, T], BF16)
    k0d = internal("k0d", [256, T], BF16)
    v0d = internal("v0d", [T, 256], BF16)
    zsh0 = internal("zsh0", [E, TO], BF16)
    zg0 = internal("zg0", [4 * E, TO], BF16)
    xln2 = internal("xln2", [E, TO], BF16)
    w1e = internal("w1e", [32, E, 128], BF16)
    b1e = internal("b1e", [1, DH], F32)
    w2e = internal("w2e", [DH, E], BF16)
    relay = internal("relay", [32, 128, 512], BF16)
    xln_b = internal("xln_b", [E, TO], BF16)
    xg_b = internal("xg_b", [4 * E, TO], BF16)
    q1d = internal("q1d", [256, T], BF16)
    k1d = internal("k1d", [256, T], BF16)
    v1d = internal("v1d", [T, 256], BF16)
    zsh1 = internal("zsh1", [E, TO], BF16)
    zg1 = internal("zg1", [4 * E, TO], BF16)
    xln3 = internal("xln3", [E, TO], BF16)
    wq1i = internal("wq1i", [32, E, 128], BF16)
    wq2i = internal("wq2i", [DH, E], BF16)

    c = Ctx(nc)
    c.sem("cc")
    qd_sv = nc.partition_id() % 4
    x_sb = c.sb("xres", [128, NTT * E], F32)
    ld = nc.sync.dma_start(out=x_sb.ap().rearrange("p (n f) -> p n f", f=E),
                           in_=x0.rearrange("(n p) f -> p n f", p=128))
    c.inc(ld, "dh", 16)
    c.wait(nc.sync, "dh")
    c.barrier()
    emit_ln_transpose(c, None, xln_a, "fa", x_sb=x_sb)
    emit_collective(c, "AllGather", xln_a, xg_a, "ga")
    emit_qkv(c, xg_a, wq0, wk0, wv0, q0d, k0d, v0d, "f0",
             rope_cs=(rcos, rsin, p32))
    emit_attention(c, q0d, k0d, v0d, zsh0, "f0", 0.125)
    emit_collective(c, "AllGather", zsh0, zg0, "za")
    emit_outproj_residual(c, None, w2t, x_sb, "f0", z_gathered=(zg0, qd_sv))
    emit_ln_transpose(c, None, xln2, "f0l", x_sb=x_sb)
    c.push()
    mats = {nm: c.sb(f"mat_{nm}", [128, 32 * 128], BF16)
            for nm in ("M1", "D1", "N1", "N2")}
    _, plans = butterfly_builder_plan(np.zeros((12, 2048, 2, 2), np.float32),
                                     np.zeros((12, 2048, 2, 2), np.float32))
    emit_bfly_matrices(c, bfw, mats, plans, "f0")
    emit_fold_w1(c, wexpT, bexp_cm, mats["M1"], w1e, b1e, "f0")
    emit_fold_w2(c, wconP, mats["N2"], w2e, "f0")
    emit_ffn_l0(c, xln2, w1e, b1e, up_bP, w2e, brow2,
                mats["D1"], mats["N1"], relay, x_sb, "f0")
    c.pop()
    emit_ln_transpose(c, None, xln_b, "fb", x_sb=x_sb)
    emit_collective(c, "AllGather", xln_b, xg_b, "gb")
    emit_qkv(c, xg_b, wq1, wk1, wv1, q1d, k1d, v1d, "f1", rope_cs=None)
    emit_attention(c, q1d, k1d, v1d, zsh1, "f1", 1.0)
    emit_collective(c, "AllGather", zsh1, zg1, "zb")
    emit_outproj_residual(c, None, ot, x_sb, "f1", z_gathered=(zg1, qd_sv))
    emit_ln_transpose(c, None, xln3, "f1l", x_sb=x_sb)
    emit_ffn_l1(c, xln3, w1t, b1cm, w2t_ffn, b2row, wq1i, wq2i, x_sb, "f1")
    emit_store_x(c, x_sb, x_out.ap())
    c.close()
    return nc


def kernel(**inputs):
    inputs = {k: np.ascontiguousarray(v) for k, v in inputs.items()}
    per_core, shared = host_slices(inputs)
    cores = list(range(NCORE))
    nc = build_fused()
    in_maps = []
    for i in cores:
        pc = per_core[i]
        in_maps.append({
            "x0": pc["x0"], "wq0": pc["wq0"], "wk0": pc["wk0"], "wv0": pc["wv0"],
            "rcos": shared["rope_cos"], "rsin": shared["rope_sin"],
            "p32": shared["p32"], "w2t": shared["w2t"],
            "wexpT": shared["wexpT"], "bexp_cm": shared["bexp_cm"],
            "bfw": shared["bfw"], "up_bP": shared["up_bP"],
            "wconP": shared["wconP"], "brow2": shared["brow2"],
            "wq1": pc["wq1"], "wk1": pc["wk1"], "wv1": pc["wv1"],
            "ot": shared["ot"], "w1t": shared["w1t"], "b1cm": shared["b1cm"],
            "w2t_ffn": shared["w2t_ffn"], "b2row": shared["b2row"],
        })
    res = run_bass_kernel_spmd(nc, in_maps, core_ids=cores)
    out = np.empty((B, T, E), dtype=np.float32)
    for cid in cores:
        b, qd = cid // 4, cid % 4
        out[b, qd * TO:(qd + 1) * TO, :] = res.results[cid]["x_out"]
    return out


# version 11
# speedup vs baseline: 1.0373x; 1.0373x over previous
import math
import numpy as np
import ml_dtypes
from contextlib import ExitStack

import concourse.bass as bass
import concourse.mybir as mybir
from concourse.bass_utils import run_bass_kernel_spmd

F32 = mybir.dt.float32
BF16 = mybir.dt.bfloat16
AF = mybir.ActivationFunctionType
ALU = mybir.AluOpType

B, T, E, H, D = 2, 8192, 1024, 16, 64
NCORE = 8
TO = T // 4
HO = H // 4
NTT = TO // 128
NCH = E // 128
DH = 4096
EPS = 1e-5


class Ctx:

    def __init__(self, nc):
        self.nc = nc
        self.es = ExitStack()
        self.sem_es = ExitStack()
        self._scopes = []
        self.sems = {}
        self.counts = {}

    def push(self):
        self._scopes.append(self.es)
        self.es = ExitStack()

    def pop(self):
        self.es.close()
        self.es = self._scopes.pop()

    def sb(self, name, shape, dtype):
        return self.es.enter_context(self.nc.sbuf_tensor(name, shape, dtype))

    def ps(self, name, shape, dtype=F32):
        return self.es.enter_context(self.nc.psum_tensor(name, shape, dtype))

    def sem(self, name):
        if name not in self.sems:
            self.sems[name] = self.sem_es.enter_context(self.nc.semaphore(name))
            self.counts[name] = 0
        return self.sems[name]

    def inc(self, inst, name, n=1):
        inst.then_inc(self.sem(name), n)
        self.counts[name] += n
        return self.counts[name]

    def wait(self, eng, name, val=None):
        if val is None:
            val = self.counts[name]
        eng.wait_ge(self.sem(name), val)

    def barrier(self):
        self.nc.all_engine_barrier()

    def close(self):
        self.es.close()
        self.sem_es.close()


def bfc(x):
    return np.ascontiguousarray(x).astype(ml_dtypes.bfloat16)



def emit_ln_transpose(c, x_src, xln_dst, tag, x_sb=None):
    nc = c.nc
    c.push()
    own_x = x_sb is None
    if own_x:
        x_sb = c.sb(f"x_{tag}", [128, NTT * E], F32)
    xln_tm = c.sb(f"xlntm_{tag}", [128, NTT * E], BF16)
    stats = c.sb(f"stats_{tag}", [128, NTT * 12], F32)
    mv = c.sb(f"mv_{tag}", [128, NTT * 2], F32)
    rstd = c.sb(f"rstd_{tag}", [128, NTT], F32)
    sid = c.sb(f"id_{tag}", [128, 128], BF16)
    xfm = c.sb(f"xfm_{tag}", [128, NCH * TO], BF16)
    ptr = c.ps(f"ptr_{tag}", [128, 2 * 1024], BF16)

    mi = nc.gpsimd.memset(sid[:], 1.0)
    c.inc(mi, "g")
    c.wait(nc.gpsimd, "g")
    ai = nc.gpsimd.affine_select(out=sid[:], in_=sid[:], compare_op=ALU.is_equal,
                                 fill=0.0, base=0, pattern=[[-1, 128]],
                                 channel_multiplier=1)
    c.inc(ai, "g")

    if own_x:
        ld = nc.sync.dma_start(
            out=x_sb.ap().rearrange("p (n f) -> p n f", f=E),
            in_=x_src.rearrange("(n p) f -> p n f", p=128))
        c.inc(ld, "dh", 16)
        c.wait(nc.sync, "dh")
        c.wait(nc.vector, "dh")

    for i in range(NTT):
        xt = x_sb[:, i * E:(i + 1) * E]
        st = stats[:, i * 12:(i + 1) * 12]
        s1 = nc.vector.bn_stats(out=st[:, 0:6], in_=xt[:, 0:512])
        c.inc(s1, "v")
        s2 = nc.vector.bn_stats(out=st[:, 6:12], in_=xt[:, 512:1024])
        c.inc(s2, "v")
        c.wait(nc.vector, "v")
        ag = nc.vector.bn_aggr(out=mv[:, 2 * i:2 * i + 2], in_=st[:])
        c.inc(ag, "v")
    seps = c.sb(f"eps_{tag}", [128, 1], F32)
    me = nc.gpsimd.memset(seps[:], EPS)
    c.inc(me, "g")
    c.wait(nc.scalar, "g")
    c.wait(nc.scalar, "v")
    lg = nc.scalar.activation(rstd.ap().rearrange("p (n one) -> p n one", one=1),
                              mv.ap().rearrange("p (n two) -> p n two", two=2)[:, :, 1:2],
                              AF.Ln, bias=seps[:])
    c.inc(lg, "s")
    c.wait(nc.scalar, "s")
    ex = nc.scalar.activation(rstd[:], rstd[:], AF.Exp, scale=-0.5)
    c.inc(ex, "s")
    c.wait(nc.vector, "s")
    for i in range(NTT):
        tsn = nc.vector.tensor_scalar(
            out=xln_tm[:, i * E:(i + 1) * E], in0=x_sb[:, i * E:(i + 1) * E],
            scalar1=mv[:, 2 * i:2 * i + 1], scalar2=rstd[:, i:i + 1],
            op0=ALU.subtract, op1=ALU.mult)
        c.inc(tsn, "v")
    c.wait(nc.tensor, "v")
    c.wait(nc.tensor, "g")
    c.sem("vtr")
    for i in range(NTT):
        par = i % 2
        bank = ptr[:, par * 1024:(par + 1) * 1024]
        if i >= 2:
            c.wait(nc.tensor, "vtr", c.counts["vtr"] - 1)
        for ch in range(NCH):
            tr = nc.tensor.transpose(
                bank[:, ch * 128:(ch + 1) * 128],
                xln_tm[:, i * E + ch * 128: i * E + (ch + 1) * 128], sid[:])
            if ch == NCH - 1:
                c.inc(tr, "tr")
        c.wait(nc.vector, "tr")
        cp = nc.vector.tensor_copy(
            out=xfm.ap().rearrange("p (ch t) -> p ch t", ch=NCH)[:, :, i * 128:(i + 1) * 128],
            in_=bank.rearrange("p (ch t) -> p ch t", ch=NCH))
        c.inc(cp, "vtr")
    c.wait(nc.sync, "vtr")
    stfm = nc.sync.dma_start(out=xln_dst.rearrange("(ch p) t -> p ch t", p=128),
                             in_=xfm.ap().rearrange("p (ch t) -> p ch t", ch=NCH))
    c.inc(stfm, "dh", 16)
    c.wait(nc.sync, "dh")
    c.barrier()
    c.pop()
    return x_sb


def emit_qkv(c, xln_full, wq, wk, wv, q_dst, k_dst, v_dst, tag, rope_cs=None):
    nc = c.nc
    c.push()
    sw = c.sb(f"w_{tag}", [128, NCH * 3 * 256], BF16)
    sx = c.sb(f"xg_{tag}", [128, 2 * NCH * 512], BF16)
    sqk = c.sb(f"qk_{tag}", [128, 2 * 512], BF16)
    sv = c.sb(f"v_{tag}", [128, 1024], BF16)
    pq = c.ps(f"pq_{tag}", [128, 2 * 512], F32)
    pk = c.ps(f"pk_{tag}", [128, 2 * 512], F32)
    pv = c.ps(f"pv_{tag}", [128, 2 * 512], F32)
    if rope_cs is not None:
        scos = c.sb(f"cos_{tag}", [128, T], BF16)
        ssin = c.sb(f"sin_{tag}", [128, T], BF16)
        sqr = c.sb(f"qr_{tag}", [128, 2048], BF16)
        sp32 = c.sb(f"p32_{tag}", [128, 128], BF16)
        prope = c.ps(f"pr_{tag}", [128, 1024], F32)
        c.inc(nc.gpsimd.dma_start(out=scos[:], in_=rope_cs[0]), "dg", 16)
        c.inc(nc.gpsimd.dma_start(out=ssin[:], in_=rope_cs[1]), "dg", 16)
        c.inc(nc.gpsimd.dma_start(out=sp32[:], in_=rope_cs[2]), "dg", 16)
        c.sem("rp")
    else:
        scos = ssin = sqr = sp32 = prope = None
    for wi, w in enumerate((wq, wk, wv)):
        ld = nc.gpsimd.dma_start(
            out=sw.ap().rearrange("p (ch w s) -> p ch w s", ch=NCH, w=3)[:, :, wi, :],
            in_=w.rearrange("(ch p) s -> p ch s", p=128))
        c.inc(ld, "dg", 16)
    c.wait(nc.tensor, "dg")

    c.sem("mmq"); c.sem("v"); c.sem("dh")
    qfree = kfree = vfree = 0
    for tt in range(16):
        r, tl = tt // 4, (tt % 4) * 512
        buf = tt % 2
        xb = sx[:, buf * NCH * 512:(buf + 1) * NCH * 512]
        if tt >= 2:
            c.wait(nc.sync, "mmq", tt - 1)
        lxs = "lx%d" % buf
        ld = nc.sync.dma_start(
            out=xb.rearrange("p (ch t) -> p ch t", ch=NCH),
            in_=xln_full[r * E:(r + 1) * E, tl:tl + 512].rearrange("(ch p) t -> p ch t", p=128))
        nld = c.inc(ld, lxs, 16)
        c.wait(nc.tensor, lxs, nld)
        if tt >= 1:
            c.wait(nc.tensor, "v", qfree)
        for ch in range(NCH):
            st, sp = (ch == 0), (ch == NCH - 1)
            xt = xb[:, ch * 512:(ch + 1) * 512]
            wch = sw[:, ch * 768:(ch + 1) * 768]
            for half in range(2):
                nc.tensor.matmul(pq[:, half * 512:half * 512 + 512],
                                 wch[:, half * 128:half * 128 + 128], xt,
                                 start=st, stop=sp)
        if tt >= 1:
            c.wait(nc.tensor, "v", kfree)
        for ch in range(NCH):
            st, sp = (ch == 0), (ch == NCH - 1)
            xt = xb[:, ch * 512:(ch + 1) * 512]
            wch = sw[:, ch * 768:(ch + 1) * 768]
            for half in range(2):
                nc.tensor.matmul(pk[:, half * 512:half * 512 + 512],
                                 wch[:, 256 + half * 128:256 + half * 128 + 128], xt,
                                 start=st, stop=sp)
        if tt >= 1:
            c.wait(nc.tensor, "v", vfree)
        for sub in range(4):
            for ch in range(NCH):
                st, sp = (ch == 0), (ch == NCH - 1)
                xt = xb[:, ch * 512:(ch + 1) * 512]
                wch = sw[:, ch * 768:(ch + 1) * 768]
                mm = nc.tensor.matmul(
                    pv[:, sub * 256:(sub + 1) * 256],
                    xt[:, sub * 128:(sub + 1) * 128], wch[:, 512:768],
                    start=st, stop=sp)
        c.inc(mm, "mmq")
        gt = tt * 512
        c.wait(nc.vector, "mmq")
        if tt >= 1:
            c.wait(nc.vector, "dh")
        for half in range(2):
            cp = nc.vector.tensor_copy(out=sqk[:, half * 512:half * 512 + 512],
                                       in_=pq[:, half * 512:half * 512 + 512])
            c.inc(cp, "v")
        qfree = c.counts["v"]
        if rope_cs is not None:
            _emit_rope(c, sqk, sqr, scos, ssin, sp32, prope, gt)
        c.wait(nc.sync, "v")
        st1 = nc.sync.dma_start(
            out=q_dst.rearrange("(h p) t -> p h t", p=128)[:, :, gt:gt + 512],
            in_=sqk.ap().rearrange("p (h t) -> p h t", h=2))
        c.inc(st1, "dh", 16)
        c.wait(nc.vector, "dh")
        for half in range(2):
            cp = nc.vector.tensor_copy(out=sqk[:, half * 512:half * 512 + 512],
                                       in_=pk[:, half * 512:half * 512 + 512])
            c.inc(cp, "v")
        kfree = c.counts["v"]
        if rope_cs is not None:
            _emit_rope(c, sqk, sqr, scos, ssin, sp32, prope, gt)
        c.wait(nc.sync, "v")
        st2 = nc.sync.dma_start(
            out=k_dst.rearrange("(h p) t -> p h t", p=128)[:, :, gt:gt + 512],
            in_=sqk.ap().rearrange("p (h t) -> p h t", h=2))
        c.inc(st2, "dh", 16)
        if tt >= 1:
            c.wait(nc.vector, "dh")
        cpv = nc.vector.tensor_copy(out=sv[:], in_=pv[:])
        vfree = c.inc(cpv, "v")
        c.wait(nc.sync, "v")
        stv = nc.sync.dma_start(
            out=v_dst.rearrange("(n p) d -> p n d", p=128)[:, 4 * tt:4 * tt + 4, :],
            in_=sv.ap().rearrange("p (s d) -> p s d", s=4))
        c.inc(stv, "dh", 16)
    c.wait(nc.sync, "dh")
    c.barrier()
    c.pop()


def _emit_rope(c, sqk, sqr, scos, ssin, p32_sb, prope, gt):
    nc = c.nc
    cosw = scos[:, gt:gt + 512]
    sinw = ssin[:, gt:gt + 512]
    c.wait(nc.vector, "v")
    for half in range(2):
        q = sqk[:, half * 512:half * 512 + 512]
        t1 = sqr[:, half * 512:half * 512 + 512]
        u = sqr[:, 1024 + half * 512:1024 + half * 512 + 512]
        c.inc(nc.vector.tensor_mul(out=t1, in0=q, in1=cosw), "v")
        c.inc(nc.vector.tensor_mul(out=u, in0=q, in1=sinw), "v")
    c.wait(nc.tensor, "v")
    c.wait(nc.tensor, "v", c.counts.get("_ropefree", 0))
    for half in range(2):
        u = sqr[:, 1024 + half * 512:1024 + half * 512 + 512]
        mm = nc.tensor.matmul(prope[:, half * 512:half * 512 + 512],
                              p32_sb[:], u, start=True, stop=True)
        c.inc(mm, "rp")
    c.wait(nc.vector, "rp")
    for half in range(2):
        q = sqk[:, half * 512:half * 512 + 512]
        t1 = sqr[:, half * 512:half * 512 + 512]
        ad = nc.vector.tensor_add(out=q, in0=t1,
                                  in1=prope[:, half * 512:half * 512 + 512])
        c.inc(ad, "v")
    c.counts["_ropefree"] = c.counts["v"]


def emit_attention(c, q_src, k_src, v_src, z_dst, tag, scale):
    nc = c.nc
    c.push()
    NKT = T // 128
    sk = [c.sb(f"ak{x}_{tag}", [128, T], BF16) for x in range(2)]
    sq = [c.sb(f"aq{x}_{tag}", [128, T], BF16) for x in range(2)]
    sva = [c.sb(f"av{x}a_{tag}", [128, NKT * 65], BF16) for x in range(2)]
    svb = [c.sb(f"av{x}b_{tag}", [128, NKT * 65], BF16) for x in range(2)]
    sattn = [c.sb(f"aat{x}_{tag}", [128, 2 * 1024], BF16) for x in range(2)]
    srec = [c.sb(f"arc{x}_{tag}", [1, 2 * 1024], BF16) for x in range(2)]
    srecf = [c.sb(f"arf{x}_{tag}", [1, 1024], F32) for x in range(2)]
    sbc = [c.sb(f"abc{x}_{tag}", [128, 1024], BF16) for x in range(2)]
    szr = [c.sb(f"azr{x}_{tag}", [128, 2 * 1024], BF16) for x in range(2)]
    sones1 = c.sb(f"aon_{tag}", [1, 64], BF16)
    smask = c.sb(f"amk_{tag}", [128, 4 * 512], BF16)
    ps = [c.ps(f"aps{x}_{tag}", [128, 2 * 512], F32) for x in range(2)]
    pz = [c.ps(f"apz{x}_{tag}", [128, 2 * 512], F32) for x in range(2)]

    c.inc(nc.gpsimd.memset(sones1[:], 1.0), "g")
    for m in range(4):
        c.inc(nc.gpsimd.memset(smask[:, m * 512:(m + 1) * 512], 1.0), "g")
    c.wait(nc.gpsimd, "g")
    for m in range(4):
        am = nc.gpsimd.affine_select(
            out=smask[:, m * 512:(m + 1) * 512], in_=smask[:, m * 512:(m + 1) * 512],
            compare_op=ALU.is_ge, fill=0.0, base=-128 * m,
            pattern=[[1, 512]], channel_multiplier=-1)
        c.inc(am, "g")
    for x in range(2):
        c.inc(nc.sync.dma_start(out=sk[x][:], in_=k_src[x * 128:(x + 1) * 128, :]),
              "dh", 16)
        c.inc(nc.sync.dma_start(out=sq[x][:], in_=q_src[x * 128:(x + 1) * 128, :]),
              "dh", 16)
        for hh, sv in ((0, sva[x]), (1, svb[x])):
            ld = nc.sync.dma_start(
                out=sv.ap().rearrange("p (n c) -> p n c", c=65)[:, :, 0:64],
                in_=v_src.rearrange("(n p) c -> p n c", p=128)[
                    :, :, x * 128 + hh * 64:x * 128 + hh * 64 + 64])
            c.inc(ld, "dh", 16)
            c.inc(nc.gpsimd.memset(
                sv.ap().rearrange("p (n c) -> p n c", c=65)[:, :, 64:65], 1.0), "g")
    c.wait(nc.sync, "dh")
    c.wait(nc.gpsimd, "g")
    c.barrier()

    scs = ["scA", "scB"]; exs = ["exA", "exB"]; avs = ["avA", "avB"]
    mks = ["mkA", "mkB"]; rcs = ["rcA", "rcB"]; bcs = ["bcA", "bcB"]
    zes = ["zeA", "zeB"]
    cps = ["cpA", "cpB"]
    for n_ in scs + exs + avs + mks + rcs + bcs + zes + cps + ["zst0", "zst1"]:
        c.sem(n_)
    ex0 = [c.counts[exs[0]], c.counts[exs[1]]]
    mk0 = [c.counts[mks[0]], c.counts[mks[1]]]
    rc_done = [{}, {}]

    def e_scores(x, i, jq, kt):
        if i >= 1:
            c.wait(nc.tensor, exs[x], c.counts[exs[x]])
        if kt == 0:
            c.wait(nc.tensor, cps[x], c.counts[cps[x]])
        qsl = sq[x][:, jq * 512:(jq + 1) * 512]
        ksl = sk[x][:, kt * 128:(kt + 1) * 128]
        nc.tensor.matmul(ps[x][:, 0:512], ksl[0:64, :], qsl[0:64, :],
                         start=True, stop=True)
        mmb = nc.tensor.matmul(ps[x][:, 512:1024], ksl[64:128, :], qsl[64:128, :],
                               start=True, stop=True)
        c.inc(mmb, scs[x])

    def e_exp(x, i, jq, kt):
        par = i % 2
        c.wait(nc.scalar, scs[x])
        e = nc.scalar.activation(sattn[x][:, par * 1024:(par + 1) * 1024],
                                 ps[x][:], AF.Exp, scale=scale)
        c.inc(e, exs[x])

    def e_mask(x, i, jq, kt):
        par = i % 2
        a0 = par * 1024
        m = kt - 4 * jq
        eng = nc.vector if x == 0 else nc.gpsimd
        c.wait(eng, exs[x])
        msl = smask[:, m * 512:(m + 1) * 512]
        eng.tensor_mul(out=sattn[x][:, a0:a0 + 512],
                       in0=sattn[x][:, a0:a0 + 512], in1=msl)
        mb = eng.tensor_mul(out=sattn[x][:, a0 + 512:a0 + 1024],
                            in0=sattn[x][:, a0 + 512:a0 + 1024], in1=msl)
        c.inc(mb, mks[x])

    def e_av(x, i, jq, kt, dcount):
        nk = 4 * (jq + 1)
        if kt == 0:
            c.wait(nc.tensor, zes[x], c.counts[zes[x]])
        if kt >= 4 * jq:
            c.wait(nc.tensor, mks[x], mk0[x] + dcount)
        else:
            c.wait(nc.tensor, exs[x], ex0[x] + i + 1)
        par = i % 2
        a0 = par * 1024
        st, sp = (kt == 0), (kt == nk - 1)
        nc.tensor.matmul(pz[x][0:65, 0:512], sva[x][:, kt * 65:(kt + 1) * 65],
                         sattn[x][:, a0:a0 + 512], start=st, stop=sp,
                         skip_group_check=True)
        mm = nc.tensor.matmul(pz[x][0:65, 512:1024], svb[x][:, kt * 65:(kt + 1) * 65],
                              sattn[x][:, a0 + 512:a0 + 1024], start=st, stop=sp,
                              skip_group_check=True)
        c.inc(mm, avs[x])

    def early_evac(x, jq):
        sl = (jq % 2) * 1024
        c.wait(nc.scalar, avs[x])
        c.wait(nc.scalar, rcs[x])
        cpa = nc.scalar.activation(srecf[x][0:1, :], pz[x][64:65, :],
                                   AF.Identity)
        c.inc(cpa, rcs[x])
        c.wait(nc.vector, avs[x])
        c.wait(nc.vector, "dh")
        zc = nc.vector.tensor_copy(out=szr[x][0:64, sl:sl + 1024],
                                   in_=pz[x][0:64, 0:1024])
        c.inc(zc, zes[x])
        c.wait(nc.vector, rcs[x])
        rc = nc.vector.reciprocal(out=srecf[x][0:1, :], in_=srecf[x][0:1, :])
        c.inc(rc, "v")
        c.wait(nc.vector, "v")
        rc2 = nc.vector.tensor_copy(out=srec[x][0:1, sl:sl + 1024],
                                    in_=srecf[x][0:1, :])
        c.inc(rc2, rcs[x])

    def late_evac(x, jq):
        sl = (jq % 2) * 1024
        c.wait(nc.tensor, rcs[x], rc_done[x][jq])
        c.wait(nc.tensor, exs[x], c.counts[exs[x]])
        c.wait(nc.tensor, cps[x], c.counts[cps[x]])
        nc.tensor.matmul(ps[x][0:64, 0:512], sones1[:], srec[x][0:1, sl:sl + 512],
                         start=True, stop=True)
        bc = nc.tensor.matmul(ps[x][0:64, 512:1024], sones1[:],
                              srec[x][0:1, sl + 512:sl + 1024],
                              start=True, stop=True)
        c.inc(bc, bcs[x])
        c.wait(nc.vector, bcs[x])
        cpc = nc.vector.tensor_copy(out=sbc[x][0:64, :], in_=ps[x][0:64, :])
        c.inc(cpc, cps[x])
        c.wait(nc.vector, cps[x])
        nc.vector.tensor_mul(out=szr[x][0:64, sl:sl + 512],
                             in0=szr[x][0:64, sl:sl + 512],
                             in1=sbc[x][0:64, 0:512])
        zm = nc.vector.tensor_mul(out=szr[x][0:64, sl + 512:sl + 1024],
                                  in0=szr[x][0:64, sl + 512:sl + 1024],
                                  in1=sbc[x][0:64, 512:1024])
        c.inc(zm, "zst%d" % x)
        c.wait(nc.sync, "zst%d" % x, c.counts["zst%d" % x])
        r0 = (jq // 4) * 256 + x * 128
        c0 = (jq % 4) * 512
        st1 = nc.sync.dma_start(out=z_dst[r0:r0 + 64, c0:c0 + 512],
                                in_=szr[x][0:64, sl:sl + 512])
        c.inc(st1, "dh", 16)
        st2 = nc.sync.dma_start(out=z_dst[r0 + 64:r0 + 128, c0:c0 + 512],
                                in_=szr[x][0:64, sl + 512:sl + 1024])
        c.inc(st2, "dh", 16)

    pend = [None, None]
    ndiag = [0, 0]
    stepx = [0, 0]

    def substep(x, jq, kt):
        i = stepx[x]
        e_scores(x, i, jq, kt)
        e_exp(x, i, jq, kt)
        if kt >= 4 * jq:
            ndiag[x] += 1
            e_mask(x, i, jq, kt)
        if pend[x] is not None:
            e_av(x, *pend[x])
        pend[x] = (i, jq, kt, ndiag[x])
        stepx[x] += 1

    def flush(x, jq):
        e_av(x, *pend[x])
        pend[x] = None
        early_evac(x, jq)
        rc_done[x][jq] = c.counts[rcs[x]]

    LAG = 4
    for jq in range(16):
        nk = 4 * (jq + 1)
        for kt in range(nk + LAG):
            if kt < nk:
                if kt == 0 and jq >= 2:
                    late_evac(0, jq - 2)
                substep(0, jq, kt)
            if kt == nk:
                flush(0, jq)
            if kt >= LAG:
                if kt - LAG == 0 and jq >= 2:
                    late_evac(1, jq - 2)
                substep(1, jq, kt - LAG)
        flush(1, jq)
    for jq in (14, 15):
        late_evac(0, jq)
        late_evac(1, jq)
    c.wait(nc.sync, "dh")
    c.barrier()
    c.pop()


def emit_outproj_residual(c, z_own, w_ot, x_sb, tag, z_gathered=None):
    nc = c.nc
    c.push()
    sz = c.sb(f"oz_{tag}", [128, NCH * TO], BF16)
    sw = c.sb(f"ow_{tag}", [128, NCH * E], BF16)
    pp = c.ps(f"op_{tag}", [128, 2 * 512], F32)
    if z_gathered is None:
        ld = nc.sync.dma_start(out=sz.ap().rearrange("p (a t) -> p a t", a=NCH),
                               in_=z_own.rearrange("(a p) t -> p a t", p=128))
        c.inc(ld, "dh", 16)
    else:
        zg, qd = z_gathered
        for ch in range(NCH):
            r, pr = ch // 2, ch % 2
            blk = qd + 4 * r
            ap = zg.rearrange("(b q) t -> b q t", q=256)[bass.ts(blk, 1), :, :]
            ld = nc.sync.dma_start(
                out=sz[:, ch * TO:(ch + 1) * TO],
                in_=ap.squeeze()[pr * 128:(pr + 1) * 128, :])
            c.inc(ld, "dh", 16)
    ld = nc.gpsimd.dma_start(out=sw.ap().rearrange("p (a f) -> p a f", a=NCH),
                             in_=w_ot.rearrange("(a p) f -> p a f", p=128))
    c.inc(ld, "dg", 16)
    c.wait(nc.tensor, "dh")
    c.wait(nc.tensor, "dg")
    c.sem("oev")
    for ts in range(NTT):
        for fh in range(2):
            sl = (ts * 2 + fh) % 2
            if ts * 2 + fh >= 2:
                c.wait(nc.tensor, "oev", c.counts["oev"] - 1)
            for a in range(NCH):
                mm = nc.tensor.matmul(
                    pp[:, sl * 512:sl * 512 + 512],
                    sz[:, a * TO + ts * 128:a * TO + (ts + 1) * 128],
                    sw[:, a * E + fh * 512:a * E + fh * 512 + 512],
                    start=(a == 0), stop=(a == NCH - 1))
            c.inc(mm, "omm")
            c.wait(nc.vector, "omm")
            ad = nc.vector.tensor_add(
                out=x_sb[:, ts * E + fh * 512:ts * E + fh * 512 + 512],
                in0=x_sb[:, ts * E + fh * 512:ts * E + fh * 512 + 512],
                in1=pp[:, sl * 512:sl * 512 + 512])
            c.inc(ad, "oev")
    c.barrier()
    c.pop()


def emit_store_x(c, x_sb, x_dst):
    nc = c.nc
    st = nc.sync.dma_start(out=x_dst.rearrange("(n p) f -> p n f", p=128),
                           in_=x_sb.ap().rearrange("p (n f) -> p n f", f=E))
    c.inc(st, "dh", 16)
    c.wait(nc.sync, "dh")
    c.barrier()



def build_A():
    nc = bass.Bass("TRN2", target_bir_lowering=False)
    x_in = nc.dram_tensor("x_in", [TO, E], F32, kind="ExternalInput")
    xln = nc.dram_tensor("xln", [E, TO], BF16, kind="ExternalOutput")
    c = Ctx(nc)
    emit_ln_transpose(c, x_in.ap(), xln.ap(), "a")
    c.close()
    return nc


def build_B(rope, scale):
    nc = bass.Bass("TRN2", target_bir_lowering=False)
    xg = nc.dram_tensor("xg", [4 * E, TO], BF16, kind="ExternalInput")
    wq = nc.dram_tensor("wq", [E, 256], F32, kind="ExternalInput")
    wk = nc.dram_tensor("wk", [E, 256], F32, kind="ExternalInput")
    wv = nc.dram_tensor("wv", [E, 256], F32, kind="ExternalInput")
    zo = nc.dram_tensor("zo", [E, TO], BF16, kind="ExternalOutput")
    qd_ = nc.dram_tensor("q_i", [256, T], BF16)
    kd_ = nc.dram_tensor("k_i", [256, T], BF16)
    vd_ = nc.dram_tensor("v_i", [T, 256], BF16)
    rope_cs = None
    if rope:
        rc = nc.dram_tensor("rcos", [128, T], F32, kind="ExternalInput")
        rs = nc.dram_tensor("rsin", [128, T], F32, kind="ExternalInput")
        p32 = nc.dram_tensor("p32", [128, 128], F32, kind="ExternalInput")
        rope_cs = (rc.ap(), rs.ap(), p32.ap())
    c = Ctx(nc)
    emit_qkv(c, xg.ap(), wq.ap(), wk.ap(), wv.ap(), qd_.ap(), kd_.ap(), vd_.ap(),
             "b", rope_cs=rope_cs)
    emit_attention(c, qd_.ap(), kd_.ap(), vd_.ap(), zo.ap(), "b", scale)
    c.close()
    return nc


def build_Cpart():
    nc = bass.Bass("TRN2", target_bir_lowering=False)
    x_in = nc.dram_tensor("x_in", [TO, E], F32, kind="ExternalInput")
    z_in = nc.dram_tensor("z_in", [E, TO], BF16, kind="ExternalInput")
    w_ot = nc.dram_tensor("w_ot", [E, E], F32, kind="ExternalInput")
    x_out = nc.dram_tensor("x_out", [TO, E], F32, kind="ExternalOutput")
    c = Ctx(nc)
    x_sb = c.sb("xres", [128, NTT * E], F32)
    ld = nc.sync.dma_start(out=x_sb.ap().rearrange("p (n f) -> p n f", f=E),
                           in_=x_in.ap().rearrange("(n p) f -> p n f", p=128))
    c.inc(ld, "dh", 16)
    c.wait(nc.sync, "dh")
    c.barrier()
    emit_outproj_residual(c, z_in.ap(), w_ot.ap(), x_sb, "c")
    emit_store_x(c, x_sb, x_out.ap())
    c.close()
    return nc



def emit_bfly_matrices(c, bfw, mats, plans, tag):
    nc = c.nc
    c.push()
    Y0 = c.sb(f"y0_{tag}", [128, 4096], F32)
    Y1 = c.sb(f"y1_{tag}", [128, 4096], F32)
    tmp1 = c.sb(f"bt1_{tag}", [128, 2048], F32)
    tmp2 = c.sb(f"bt2_{tag}", [128, 2048], F32)
    wrow = c.sb(f"bw_{tag}", [1, 2 * 2048], BF16)
    wt = c.sb(f"bwt_{tag}", [128, 2 * 2048], F32)
    sone = c.sb(f"bon_{tag}", [1, 128], BF16)
    pwb = c.ps(f"bpw_{tag}", [128, 2048], F32)
    c.inc(nc.gpsimd.memset(sone[:], 1.0), "g")
    c.sem("v"); c.sem("s_pe"); c.sem("dg"); c.sem("g")

    def bcast_pair(base_k, pair):
        c.wait(nc.gpsimd, "v")
        ld = nc.gpsimd.dma_start(out=wrow.ap().rearrange("o (w n) -> o w n", w=2),
                                 in_=bfw[base_k, 2 * pair:2 * pair + 2, :])
        nld = c.inc(ld, "dg", 16)
        c.wait(nc.tensor, "dg", nld)
        c.wait(nc.tensor, "v")
        for w2i in range(2):
            if w2i == 1:
                c.wait(nc.tensor, "v")
            for col in range(4):
                mmb = nc.tensor.matmul(
                    pwb[:, col * 512:(col + 1) * 512], sone[:],
                    wrow[0:1, w2i * 2048 + col * 512:w2i * 2048 + (col + 1) * 512],
                    start=True, stop=True)
            c.inc(mmb, "s_pe")
            c.wait(nc.vector, "s_pe")
            cpw = nc.vector.tensor_copy(out=wt[:, w2i * 2048:(w2i + 1) * 2048],
                                        in_=pwb[:])
            c.inc(cpw, "v")

    base = 0
    order = ["M1", "D1", "N1", "N2"]
    for name in order:
        st = plans[name]
        c.wait(nc.gpsimd, "v")
        c.inc(nc.gpsimd.memset(Y0[:], 1.0), "g")
        c.wait(nc.gpsimd, "g")
        ai = nc.gpsimd.affine_select(out=Y0[:], in_=Y0[:], compare_op=ALU.is_equal,
                                     fill=0.0, base=0, pattern=[[0, 32], [-1, 128]],
                                     channel_multiplier=1)
        c.inc(ai, "g")
        c.wait(nc.vector, "g")
        cur, nxt = Y0, Y1
        for k, s2 in enumerate(st):
            nb = 128 // (2 * s2)

            def vw(t_, half):
                return t_.rearrange("p (blk kb r) -> p blk kb r", blk=32, kb=nb)[
                    :, :, :, half * s2:(half + 1) * s2]

            def cwv(i2):
                return wt.ap()[:, i2 * 2048:(i2 + 1) * 2048].rearrange(
                    "p (blk kb s) -> p blk kb s", blk=32, kb=nb)

            tmp1v = tmp1.ap().rearrange("p (blk kb s) -> p blk kb s", blk=32, kb=nb)
            tmp2v = tmp2.ap().rearrange("p (blk kb s) -> p blk kb s", blk=32, kb=nb)
            a_ap = vw(cur.ap(), 0)
            b_ap = vw(cur.ap(), 1)
            for pair in range(2):
                bcast_pair(base + k, pair)
                c.wait(nc.vector, "v")
                m1_ = nc.vector.tensor_mul(out=tmp1v, in0=a_ap, in1=cwv(0))
                c.inc(m1_, "v")
                m2_ = nc.vector.tensor_mul(out=tmp2v, in0=b_ap, in1=cwv(1))
                c.inc(m2_, "v")
                c.wait(nc.vector, "v")
                ad = nc.vector.tensor_add(out=vw(nxt.ap(), pair), in0=tmp1v,
                                          in1=tmp2v)
                c.inc(ad, "v")
            c.wait(nc.vector, "v")
            cur, nxt = nxt, cur
        base += len(st)
        cc = nc.vector.tensor_copy(out=mats[name][:], in_=cur[:])
        c.inc(cc, "v")
        c.wait(nc.vector, "v")
    c.barrier()
    c.pop()


def emit_fold_w1(c, wexpT, bexp, M1_sb, w1e_dram, b1e_dram, tag):
    nc = c.nc
    c.push()
    swx = c.sb(f"fw_{tag}", [128, 32 * 128], BF16)
    sbe = c.sb(f"fb_{tag}", [128, 32], BF16)
    acc = c.sb(f"fa_{tag}", [128, 4096], BF16)
    brow = c.sb(f"fbr_{tag}", [1, 4096], F32)
    pf = c.ps(f"fp_{tag}", [128, 2 * 512], F32)
    c.sem("v"); c.sem("s_pe"); c.sem("dg"); c.sem("dh")

    ld = nc.gpsimd.dma_start(out=sbe.ap(), in_=bexp)
    c.inc(ld, "dgb", 16)
    c.wait(nc.tensor, "dgb")
    for co in range(32):
        c.wait(nc.tensor, "v", c.counts["v"])
        mm = nc.tensor.matmul(pf[0:1, 0:128], sbe[:, co:co + 1],
                              M1_sb[:, co * 128:(co + 1) * 128],
                              start=True, stop=True)
        c.inc(mm, "s_pe")
        c.wait(nc.vector, "s_pe")
        cp = nc.vector.tensor_copy(out=brow[0:1, co * 128:(co + 1) * 128],
                                   in_=pf[0:1, 0:128])
        c.inc(cp, "v")
    c.wait(nc.sync, "v")
    stb = nc.sync.dma_start(out=b1e_dram, in_=brow.ap())
    c.inc(stb, "dh", 16)
    c.wait(nc.tensor, "v")

    mm_after_ci = 0
    pf_free = [c.counts["v"], c.counts["v"]]
    for ci in range(NCH):
        if ci >= 1:
            c.wait(nc.gpsimd, "s_pe", mm_after_ci)
        ldw = nc.gpsimd.dma_start(
            out=swx.ap().rearrange("p (co q) -> p co q", co=32),
            in_=wexpT[:, ci * 128:(ci + 1) * 128].rearrange("(co p) q -> p co q", p=128))
        nld = c.inc(ldw, "dg", 16)
        c.wait(nc.tensor, "dg", nld)
        c.wait(nc.vector, "dh")
        for co in range(32):
            sl = co % 2
            c.wait(nc.tensor, "v", pf_free[sl])
            mm = nc.tensor.matmul(
                pf[:, sl * 512:sl * 512 + 128],
                swx[:, co * 128:(co + 1) * 128],
                M1_sb[:, co * 128:(co + 1) * 128],
                start=True, stop=True)
            c.inc(mm, "s_pe")
            c.wait(nc.vector, "s_pe")
            cp = nc.vector.tensor_copy(
                out=acc.ap().rearrange("p (c2 hi lo) -> p hi c2 lo",
                                       c2=32, hi=4)[:, :, :, co:co + 1],
                in_=pf[:, sl * 512:sl * 512 + 128].rearrange(
                    "p (hi c2 one) -> p hi c2 one", hi=4, c2=32))
            pf_free[sl] = c.inc(cp, "v")
        mm_after_ci = c.counts["s_pe"]
        c.wait(nc.sync, "v")
        stw = nc.sync.dma_start(
            out=w1e_dram[:, ci * 128:(ci + 1) * 128, :].rearrange("c2 p q -> p c2 q"),
            in_=acc.ap().rearrange("p (c2 q) -> p c2 q", c2=32))
        c.inc(stw, "dh", 16)
    c.wait(nc.sync, "dh")
    c.barrier()
    c.pop()


def emit_fold_w2(c, wconP, N2_sb, w2e_dram, tag):
    nc = c.nc
    c.push()
    swc = c.sb(f"gw_{tag}", [128, 1024], BF16)
    sev = c.sb(f"ge_{tag}", [128, 1024], BF16)
    pg = c.ps(f"gp_{tag}", [128, 2 * 512], F32)
    c.sem("v"); c.sem("s_pe"); c.sem("dg"); c.sem("dh")
    mm_after = 0
    for c2 in range(32):
        if c2 >= 1:
            c.wait(nc.gpsimd, "s_pe", mm_after)
        ldw = nc.gpsimd.dma_start(out=swc.ap(),
                                  in_=wconP[c2 * 128:(c2 + 1) * 128, :])
        nld = c.inc(ldw, "dg", 16)
        c.wait(nc.tensor, "dg", nld)
        c.wait(nc.tensor, "dh")
        for dh_ in range(2):
            mm = nc.tensor.matmul(pg[:, dh_ * 512:(dh_ + 1) * 512],
                                  N2_sb[:, c2 * 128:(c2 + 1) * 128],
                                  swc[:, dh_ * 512:(dh_ + 1) * 512],
                                  start=True, stop=True)
            c.inc(mm, "s_pe")
        mm_after = c.counts["s_pe"]
        c.wait(nc.vector, "s_pe")
        cp = nc.vector.tensor_copy(out=sev[:], in_=pg[:])
        c.inc(cp, "v")
        c.wait(nc.sync, "v")
        for hi in range(4):
            r0 = c2 + 32 * hi
            dst = w2e_dram.rearrange("(co r) d -> r co d", r=128)[r0:r0 + 1, :, :].squeeze()
            st = nc.sync.dma_start(out=dst, in_=sev[hi * 32:(hi + 1) * 32, :])
            c.inc(st, "dh", 16)
    c.wait(nc.sync, "dh")
    c.barrier()
    c.pop()


def emit_ffn_l0(c, xln_fm, w1e_dram, b1e_dram, up_bP, w2e_dram, brow2,
                D1_sb, N1_sb, relay_dram, x_sb, tag):
    nc = c.nc
    c.push()
    sxln = c.sb(f"lx_{tag}", [128, NCH * TO], BF16)
    sw1 = c.sb(f"lw1_{tag}", [128, 2 * NCH * 128], BF16)
    sh1 = c.sb(f"lh1_{tag}", [128, 2 * 512], BF16)
    sh2 = c.sb(f"lh2_{tag}", [128, 2 * 512], BF16)
    sh2o = c.sb(f"lho_{tag}", [128, 2 * 512], BF16)
    sh3 = c.sb(f"lh3_{tag}", [128, 32 * 512], BF16)
    sw2 = c.sb(f"lw2_{tag}", [128, 2 * 512], BF16)
    sb1e = c.sb(f"lb1_{tag}", [128, 32], F32)
    subp = c.sb(f"lbu_{tag}", [128, 32], F32)
    sbr2 = c.sb(f"lbr_{tag}", [2, 1024], BF16)
    sone2 = c.sb(f"lo2_{tag}", [2, 128], BF16)
    psA = c.ps(f"lpa_{tag}", [128, 2 * 512], F32)
    psB = c.ps(f"lpb_{tag}", [128, 2 * 512], F32)
    psY = c.ps(f"lpy_{tag}", [128, 4 * 512], F32)
    c.sem("v"); c.sem("s_pe"); c.sem("s"); c.sem("dg"); c.sem("dh"); c.sem("dgw")

    ld = nc.sync.dma_start(out=sxln.ap().rearrange("p (ch t) -> p ch t", ch=NCH),
                           in_=xln_fm.rearrange("(ch p) t -> p ch t", p=128))
    c.inc(ld, "dh", 16)
    for hi in range(4):
        ldb = nc.gpsimd.dma_start(
            out=sb1e[hi * 32:(hi + 1) * 32, :],
            in_=b1e_dram.rearrange("o (lo r) -> lo (o r)", lo=32)[
                :, 32 * hi:32 * hi + 32])
        c.inc(ldb, "dg", 16)
    ld = nc.gpsimd.dma_start(out=subp.ap(), in_=up_bP)
    c.inc(ld, "dg", 16)
    ld = nc.gpsimd.dma_start(out=sbr2.ap(), in_=brow2)
    c.inc(ld, "dg", 16)
    c.inc(nc.gpsimd.memset(sone2[:], 1.0), "g")
    c.wait(nc.sync, "dh")
    c.wait(nc.gpsimd, "dg")
    c.wait(nc.gpsimd, "g")
    c.barrier()

    w1used = [0, 0]
    w1ld = [0, 0]
    w2ld = [0, 0]
    hold = [0, 0]
    s_after_ev1 = [0, 0]
    s_after_ev2 = [0, 0]
    mmd_cnt = [0, 0]
    st_cnt = [0, 0]
    housed = [0, 0]
    ev3_cnt = [0, 0]
    w2used = [0, 0]
    for tt in range(4):
        t0 = tt * 512
        def ld_w1e(c2_):
            rot_ = c2_ % 2
            wbuf_ = sw1[:, rot_ * NCH * 128:(rot_ + 1) * NCH * 128]
            c.wait(nc.gpsimd, "s_pe", w1used[rot_])
            ldw = nc.gpsimd.dma_start(
                out=wbuf_.rearrange("p (ci q) -> p ci q", ci=NCH),
                in_=w1e_dram[c2_, :, :].rearrange("(ci p) q -> p ci q", p=128))
            w1ld[rot_] = c.inc(ldw, "dgw%d" % rot_, 16)
        ld_w1e(0)
        for c2 in range(32):
            rot = c2 % 2
            wbuf = sw1[:, rot * NCH * 128:(rot + 1) * NCH * 128]
            if c2 + 1 < 32:
                ld_w1e(c2 + 1)
            c.wait(nc.tensor, "dgw%d" % rot, w1ld[rot])
            c.wait(nc.tensor, "s", s_after_ev1[rot])
            for ci in range(NCH):
                mm = nc.tensor.matmul(psA[:, rot * 512:rot * 512 + 512],
                                      wbuf[:, ci * 128:(ci + 1) * 128],
                                      sxln[:, ci * TO + t0:ci * TO + t0 + 512],
                                      start=(ci == 0), stop=(ci == NCH - 1))
            w1used[rot] = c.inc(mm, "s_pe")
            c.wait(nc.scalar, "s_pe")
            c.wait(nc.scalar, "s_pe", mmd_cnt[rot])
            ev1 = nc.scalar.activation(sh1[:, rot * 512:rot * 512 + 512],
                                       psA[:, rot * 512:rot * 512 + 512],
                                       AF.Identity, bias=sb1e[:, c2:c2 + 1])
            s_after_ev1[rot] = c.inc(ev1, "s")
            c.wait(nc.tensor, "s")
            c.wait(nc.tensor, "s", s_after_ev2[rot])
            mmd = nc.tensor.matmul(psB[:, rot * 512:rot * 512 + 512],
                                   D1_sb[:, c2 * 128:(c2 + 1) * 128],
                                   sh1[:, rot * 512:rot * 512 + 512],
                                   start=True, stop=True)
            mmd_cnt[rot] = c.inc(mmd, "s_pe")
            c.wait(nc.scalar, "s_pe")
            c.wait(nc.scalar, "str%d" % rot, st_cnt[rot])
            ev2 = nc.scalar.activation(sh2[:, rot * 512:rot * 512 + 512],
                                       psB[:, rot * 512:rot * 512 + 512],
                                       AF.Gelu, bias=subp[:, c2:c2 + 1])
            s_after_ev2[rot] = c.inc(ev2, "s")
            c.wait(nc.sync, "s")
            str_ = nc.sync.dma_start(out=relay_dram[c2, :, :],
                                     in_=sh2[:, rot * 512:rot * 512 + 512])
            st_cnt[rot] = c.inc(str_, "str%d" % rot, 16)
        c.wait(nc.sync, "str0")
        c.wait(nc.sync, "str1")
        c.barrier()
        def ld_h2o(co_):
            rot_ = co_ % 2
            c.wait(nc.sync, "s_pe", housed[rot_])
            for hi in range(4):
                ldo = nc.sync.dma_start(
                    out=sh2o[hi * 32:(hi + 1) * 32, rot_ * 512:rot_ * 512 + 512],
                    in_=relay_dram[:, co_ + 32 * hi, :])
                hold[rot_] = c.inc(ldo, "dho%d" % rot_, 16)
        ld_h2o(0)
        for co in range(32):
            rot = co % 2
            if co + 1 < 32:
                ld_h2o(co + 1)
            c.wait(nc.tensor, "dho%d" % rot, hold[rot])
            c.wait(nc.tensor, "v", ev3_cnt[rot])
            mmn = nc.tensor.matmul(psA[:, rot * 512:rot * 512 + 512],
                                   N1_sb[:, co * 128:(co + 1) * 128],
                                   sh2o[:, rot * 512:rot * 512 + 512],
                                   start=True, stop=True)
            housed[rot] = c.inc(mmn, "s_pe")
            c.wait(nc.vector, "s_pe")
            ev3 = nc.vector.tensor_copy(out=sh3[:, co * 512:(co + 1) * 512],
                                        in_=psA[:, rot * 512:rot * 512 + 512])
            ev3_cnt[rot] = c.inc(ev3, "v")
        c.barrier()
        for fh in range(2):
            yfree = c.counts["v"]

            def ld_w2e(co_, fh_):
                rot_ = co_ % 2
                c.wait(nc.gpsimd, "s_pe", w2used[rot_])
                ldw = nc.gpsimd.dma_start(
                    out=sw2[:, rot_ * 512:(rot_ + 1) * 512],
                    in_=w2e_dram[co_ * 128:(co_ + 1) * 128,
                                 fh_ * 512:fh_ * 512 + 512])
                w2ld[rot_] = c.inc(ldw, "dgw%d" % rot_, 16)
            if fh == 0:
                ld_w2e(0, 0)
            for co in range(32):
                rot = co % 2
                wb = sw2[:, rot * 512:(rot + 1) * 512]
                if co + 1 < 32:
                    ld_w2e(co + 1, fh)
                elif fh == 0:
                    ld_w2e(0, 1)
                c.wait(nc.tensor, "dgw%d" % rot, w2ld[rot])
                if co == 0:
                    c.wait(nc.tensor, "v", yfree)
                for ts in range(4):
                    mm = nc.tensor.matmul(
                        psY[:, ts * 512:(ts + 1) * 512],
                        sh3[:, co * 512 + ts * 128:co * 512 + (ts + 1) * 128],
                        wb, start=(co == 0), stop=False)
                w2used[rot] = c.inc(mm, "s_pe")
            for ts in range(4):
                mm = nc.tensor.matmul(psY[:, ts * 512:(ts + 1) * 512],
                                      sone2[:], sbr2[:, fh * 512:fh * 512 + 512],
                                      start=False, stop=True)
            c.inc(mm, "s_pe")
            c.wait(nc.vector, "s_pe")
            for ts in range(4):
                gtt = (tt * 4 + ts)
                ad = nc.vector.tensor_add(
                    out=x_sb[:, gtt * E + fh * 512:gtt * E + fh * 512 + 512],
                    in0=x_sb[:, gtt * E + fh * 512:gtt * E + fh * 512 + 512],
                    in1=psY[:, ts * 512:(ts + 1) * 512])
                c.inc(ad, "v")
        c.barrier()
    c.barrier()
    c.pop()


def emit_ffn_l1(c, xln_fm, w1t, b1cm, w2t, b2row, wq1_dram, wq2_dram, x_sb, tag):
    nc = c.nc
    c.push()
    swf = c.sb(f"qw_{tag}", [128, 2 * 4096], F32)
    sabs = c.sb(f"qa_{tag}", [128, NCH + 32], F32)
    sth = c.sb(f"qt_{tag}", [128, 4], F32)
    srow = c.sb(f"qr_{tag}", [1, 8], F32)
    sqt = c.sb(f"qq_{tag}", [128, 4096], BF16)
    stmp = c.sb(f"qtm_{tag}", [128, 4096], F32)
    sone = c.sb(f"qo_{tag}", [1, 128], F32)
    sonec = c.sb(f"qoc_{tag}", [128, 1], F32)
    pq_ = c.ps(f"qp_{tag}", [1, 512], F32)
    pbc = c.ps(f"qb_{tag}", [128, 512], F32)
    c.sem("v"); c.sem("s_pe"); c.sem("s"); c.sem("dg"); c.sem("dh"); c.sem("dgw")
    c.inc(nc.gpsimd.memset(sone[:], 1.0), "g")
    c.inc(nc.gpsimd.memset(sonec[:], 1.0), "g")
    c.wait(nc.gpsimd, "g")
    c.barrier()

    absused = [0, 0]
    gld = [0, 0]
    seqA = [(w1t, ch, 0 + ch, DH) for ch in range(NCH)] + \
           [(w2t, ch, NCH + ch, E) for ch in range(32)]

    def ld_gamma(idx):
        w_ap_, ch_, _, wc_ = seqA[idx]
        rot_ = idx % 2
        c.wait(nc.gpsimd, "v", absused[rot_])
        ldw = nc.gpsimd.dma_start(
            out=swf[:, rot_ * 4096:rot_ * 4096 + wc_],
            in_=w_ap_[ch_ * 128:(ch_ + 1) * 128, :])
        gld[rot_] = c.inc(ldw, "dg%d" % rot_, 16)
    ld_gamma(0)
    for idx in range(len(seqA)):
        rot = idx % 2
        _, _, col_ch, w_cols = seqA[idx]
        if idx + 1 < len(seqA):
            ld_gamma(idx + 1)
        c.wait(nc.vector, "dg%d" % rot, gld[rot])
        rs = nc.vector.reduce_sum(out=sabs[:, col_ch:col_ch + 1],
                                  in_=swf[:, rot * 4096:rot * 4096 + w_cols],
                                  axis=mybir.AxisListType.X,
                                  apply_absolute_value=True)
        absused[rot] = c.inc(rs, "v")
    c.wait(nc.tensor, "v")
    mm1 = nc.tensor.matmul(pq_[0:1, 0:NCH], sonec[:], sabs[:, 0:NCH],
                           start=True, stop=True)
    c.inc(mm1, "s_pe")
    mm2 = nc.tensor.matmul(pq_[0:1, 32:64], sonec[:], sabs[:, NCH:NCH + 32],
                           start=True, stop=True)
    c.inc(mm2, "s_pe")
    c.wait(nc.vector, "s_pe")
    r1 = nc.vector.reduce_sum(out=srow[0:1, 0:1], in_=pq_[0:1, 0:NCH],
                              axis=mybir.AxisListType.X)
    c.inc(r1, "v")
    c.wait(nc.vector, "v")
    r2 = nc.vector.reduce_sum(out=srow[0:1, 1:2], in_=pq_[0:1, 32:64],
                              axis=mybir.AxisListType.X)
    c.inc(r2, "v")
    c.wait(nc.vector, "v")
    ts1 = nc.vector.tensor_scalar(out=srow[0:1, 2:4], in0=srow[0:1, 0:2],
                                  scalar1=1.0 / (DH * E), scalar2=None,
                                  op0=ALU.mult, op1=ALU.bypass)
    c.inc(ts1, "v")
    c.wait(nc.vector, "v")
    ts2 = nc.vector.tensor_scalar(out=srow[0:1, 4:6], in0=srow[0:1, 2:4],
                                  scalar1=EPS, scalar2=0.5,
                                  op0=ALU.add, op1=ALU.mult)
    c.inc(ts2, "v")
    c.wait(nc.vector, "v")
    ts3 = nc.vector.tensor_scalar(out=srow[0:1, 6:8], in0=srow[0:1, 4:6],
                                  scalar1=-1.0, scalar2=None,
                                  op0=ALU.mult, op1=ALU.bypass)
    c.inc(ts3, "v")
    c.wait(nc.tensor, "v")
    mmb = nc.tensor.matmul(pbc[:, 0:4], sone[:], srow[0:1, 4:8],
                           start=True, stop=True)
    c.inc(mmb, "s_pe")
    c.wait(nc.vector, "s_pe")
    cpb = nc.vector.tensor_copy(out=sth[:], in_=pbc[:, 0:4])
    c.inc(cpb, "v")
    c.wait(nc.vector, "v")
    c.barrier()

    qused = [0, 0]
    seqB = [(w1t, ch, 0, DH, 0) for ch in range(NCH)] + \
           [(w2t, ch, 1, E, 1) for ch in range(32)]

    def ld_q(idx):
        w_ap_, ch_, _, wc_, _ = seqB[idx]
        rot_ = idx % 2
        c.wait(nc.gpsimd, "v", qused[rot_])
        ldw = nc.gpsimd.dma_start(
            out=swf[:, rot_ * 4096:rot_ * 4096 + wc_],
            in_=w_ap_[ch_ * 128:(ch_ + 1) * 128, :])
        gld[rot_] = c.inc(ldw, "dg%d" % rot_, 16)
    ld_q(0)
    for idx in range(len(seqB)):
        rot = idx % 2
        _, ch, thc, w_cols, mi = seqB[idx]
        buf = swf[:, rot * 4096:(rot + 1) * 4096]
        if idx + 1 < len(seqB):
            ld_q(idx + 1)
        c.wait(nc.vector, "dg%d" % rot, gld[rot])
        c.wait(nc.vector, "dh")
        tn = nc.vector.tensor_single_scalar(out=stmp[:, 0:w_cols],
                                            in_=buf[:, 0:w_cols],
                                            scalar=sth[:, 2 + thc:3 + thc],
                                            op=ALU.is_lt)
        c.inc(tn, "v")
        c.wait(nc.vector, "v")
        qq = nc.vector.scalar_tensor_tensor(out=sqt[:, 0:w_cols],
                                            in0=buf[:, 0:w_cols],
                                            scalar=sth[:, thc:thc + 1],
                                            in1=stmp[:, 0:w_cols],
                                            op0=ALU.is_gt, op1=ALU.subtract)
        qused[rot] = c.inc(qq, "v")
        c.wait(nc.sync, "v")
        if mi == 0:
            st_ = nc.sync.dma_start(
                out=wq1_dram[:, ch * 128:(ch + 1) * 128, :].rearrange(
                    "fo p q -> p fo q"),
                in_=sqt.ap().rearrange("p (fo q) -> p fo q", fo=32))
        else:
            st_ = nc.sync.dma_start(out=wq2_dram[ch * 128:(ch + 1) * 128, :],
                                    in_=sqt[:, 0:w_cols])
        c.inc(st_, "dh", 16)
    c.wait(nc.sync, "dh")
    c.barrier()
    c.pop()

    c.push()
    sxln = c.sb(f"qx_{tag}", [128, NCH * TO], BF16)
    sw1 = c.sb(f"qw1_{tag}", [128, 2 * NCH * 128], BF16)
    sh = c.sb(f"qh_{tag}", [128, 32 * 512], BF16)
    sw2 = c.sb(f"qw2_{tag}", [128, 2 * 512], BF16)
    sb1 = c.sb(f"qb1_{tag}", [128, 32], F32)
    sbr2 = c.sb(f"qbr_{tag}", [2, 1024], BF16)
    sone2 = c.sb(f"qo2_{tag}", [2, 128], BF16)
    psA = c.ps(f"qpa_{tag}", [128, 2 * 512], F32)
    psY = c.ps(f"qpy_{tag}", [128, 4 * 512], F32)
    ld = nc.sync.dma_start(out=sxln.ap().rearrange("p (ch t) -> p ch t", ch=NCH),
                           in_=xln_fm.rearrange("(ch p) t -> p ch t", p=128))
    c.inc(ld, "dh", 16)
    ldb = nc.gpsimd.dma_start(out=sb1.ap(), in_=b1cm)
    c.inc(ldb, "dg", 16)
    c.inc(nc.gpsimd.memset(sbr2[:], 0.0), "g")
    c.wait(nc.gpsimd, "g")
    ldb = nc.gpsimd.dma_start(out=sbr2[0:1, :], in_=b2row)
    c.inc(ldb, "dg", 16)
    c.inc(nc.gpsimd.memset(sone2[:], 1.0), "g")
    c.wait(nc.gpsimd, "g")
    c.wait(nc.gpsimd, "dg")
    c.wait(nc.sync, "dh")
    c.barrier()
    z1used = [0, 0]
    z2used = [0, 0]
    for tt in range(4):
        t0 = tt * 512
        for fo in range(32):
            rot = fo % 2
            wbuf = sw1[:, rot * NCH * 128:(rot + 1) * NCH * 128]
            c.wait(nc.gpsimd, "s_pe", z1used[rot])
            ldw = nc.gpsimd.dma_start(
                out=wbuf.rearrange("p (ci q) -> p ci q", ci=NCH),
                in_=wq1_dram[fo, :, :].rearrange("(ci p) q -> p ci q", p=128))
            nldw = c.inc(ldw, "dgw%d" % rot, 16)
            c.wait(nc.tensor, "dgw%d" % rot, nldw)
            if c.counts["s"] >= 2:
                c.wait(nc.tensor, "s", c.counts["s"] - 1)
            for ci in range(NCH):
                mm = nc.tensor.matmul(psA[:, rot * 512:rot * 512 + 512],
                                      wbuf[:, ci * 128:(ci + 1) * 128],
                                      sxln[:, ci * TO + t0:ci * TO + t0 + 512],
                                      start=(ci == 0), stop=(ci == NCH - 1))
            z1used[rot] = c.inc(mm, "s_pe")
            c.wait(nc.scalar, "s_pe")
            ev = nc.scalar.activation(sh[:, fo * 512:(fo + 1) * 512],
                                      psA[:, rot * 512:rot * 512 + 512],
                                      AF.Relu, bias=sb1[:, fo:fo + 1])
            c.inc(ev, "s")
        c.wait(nc.tensor, "s")
        c.barrier()
        for fh in range(2):
            yfree = c.counts["v"]
            for co in range(32):
                rot = co % 2
                wb = sw2[:, rot * 512:(rot + 1) * 512]
                c.wait(nc.gpsimd, "s_pe", z2used[rot])
                ldw = nc.gpsimd.dma_start(
                    out=wb, in_=wq2_dram[co * 128:(co + 1) * 128,
                                         fh * 512:fh * 512 + 512])
                nldw = c.inc(ldw, "dgw%d" % rot, 16)
                c.wait(nc.tensor, "dgw%d" % rot, nldw)
                if co == 0:
                    c.wait(nc.tensor, "v", yfree)
                for ts in range(4):
                    mm = nc.tensor.matmul(
                        psY[:, ts * 512:(ts + 1) * 512],
                        sh[:, co * 512 + ts * 128:co * 512 + (ts + 1) * 128],
                        wb, start=(co == 0), stop=False)
                z2used[rot] = c.inc(mm, "s_pe")
            for ts in range(4):
                mm = nc.tensor.matmul(psY[:, ts * 512:(ts + 1) * 512],
                                      sone2[:], sbr2[:, fh * 512:fh * 512 + 512],
                                      start=False, stop=True)
            c.inc(mm, "s_pe")
            c.wait(nc.vector, "s_pe")
            for ts in range(4):
                gtt = (tt * 4 + ts)
                ad = nc.vector.tensor_add(
                    out=x_sb[:, gtt * E + fh * 512:gtt * E + fh * 512 + 512],
                    in0=x_sb[:, gtt * E + fh * 512:gtt * E + fh * 512 + 512],
                    in1=psY[:, ts * 512:(ts + 1) * 512])
                c.inc(ad, "v")
        c.barrier()
    c.barrier()
    c.pop()



def p_of_f(f):
    return (f >> 7) | (((f >> 5) & 3) << 5)


def f_of_p(c2, p2):
    return c2 | ((p2 & 31) << 7) | (((p2 >> 5) & 3) << 5)


def _wvec4(stage_w, s, f_a):
    s2g = 2 ** s
    wr = stage_w.reshape(4096 // (2 * s2g), s2g, 2, 2)
    i = f_a >> (s + 1)
    j = f_a & (s2g - 1)
    w = wr[i, j]
    return np.stack([w[:, 0, 0], w[:, 0, 1], w[:, 1, 0], w[:, 1, 1]])


def _fa_orig(s2):
    co = np.arange(32)
    nb = 128 // (2 * s2)
    kb = np.arange(nb)
    ks = np.arange(s2)
    f = (co[:, None, None] * 128 + kb[None, :, None] * 2 * s2 + ks[None, None, :])
    return f.reshape(-1)


def _fa_P(s2p):
    c2 = np.arange(32)
    nb = 128 // (2 * s2p)
    kb = np.arange(nb)
    ks = np.arange(s2p)
    p2a = kb[:, None] * 2 * s2p + ks[None, :]
    f = f_of_p(c2[:, None, None], p2a[None])
    return f.reshape(-1)


def butterfly_builder_plan(up_w, dn_w):
    rows = []
    plans = {}
    st = []
    for s in range(7):
        s2 = 2 ** s
        rows.append(_wvec4(up_w[s], s, _fa_orig(s2)))
        st.append(s2)
    plans["M1"] = st
    st = []
    for s in range(7, 12):
        s2p = 2 ** (s - 7)
        rows.append(_wvec4(up_w[s], s, _fa_P(s2p)))
        st.append(s2p)
    plans["D1"] = st
    st = []
    for s in range(7):
        s2 = 2 ** s
        rows.append(_wvec4(dn_w[s], s, _fa_orig(s2)))
        st.append(s2)
    plans["N1"] = st
    st = []
    for s in range(7, 12):
        s2p = 2 ** (s - 7)
        rows.append(_wvec4(dn_w[s], s, _fa_P(s2p)))
        st.append(s2p)
    plans["N2"] = st
    bfw = np.stack(rows).astype(np.float32)
    return bfw, plans



def host_slices(inputs):
    per_core = []
    x = inputs["L0_attn_x"]
    W1 = inputs["L0_attn_W1"]
    Q1, K1, V1 = inputs["L1_attn_Q"], inputs["L1_attn_K"], inputs["L1_attn_V"]
    for cid in range(NCORE):
        b, qd = cid // 4, cid % 4
        hs = list(range(4 * qd, 4 * qd + 4))
        d = {}
        d["x0"] = np.ascontiguousarray(x[b, qd * TO:(qd + 1) * TO, :])
        d["wq0"] = np.ascontiguousarray(
            np.concatenate([W1[h * D:(h + 1) * D, :].T for h in hs], axis=1))
        d["wk0"] = np.ascontiguousarray(
            np.concatenate([W1[E + h * D:E + (h + 1) * D, :].T for h in hs], axis=1))
        d["wv0"] = np.ascontiguousarray(
            np.concatenate([W1[2 * E + h * D:2 * E + (h + 1) * D, :].T for h in hs], axis=1))
        d["wq1"] = np.ascontiguousarray(Q1[hs].transpose(2, 0, 1).reshape(E, 4 * D))
        d["wk1"] = np.ascontiguousarray(K1[hs].transpose(2, 0, 1).reshape(E, 4 * D))
        d["wv1"] = np.ascontiguousarray(V1[hs].transpose(2, 0, 1).reshape(E, 4 * D))
        per_core.append(d)
    shared = {}
    shared["w2t"] = np.ascontiguousarray(inputs["L0_attn_W2"].T)
    shared["ot"] = np.ascontiguousarray(inputs["L1_attn_O"].T)
    shared["wexpT"] = np.ascontiguousarray(inputs["L0_ffn_W_exp"].T)
    shared["bexp_cm"] = np.ascontiguousarray(
        inputs["L0_ffn_b_exp"].reshape(32, 128).T)
    bfw, _plans = butterfly_builder_plan(inputs["L0_ffn_up_weights"],
                                         inputs["L0_ffn_down_weights"])
    shared["bfw"] = bfw
    fidx = np.array([f_of_p(i // 128, i % 128) for i in range(4096)])
    ub = inputs["L0_ffn_up_bias"]
    up_bP = ub[fidx].reshape(32, 128).T
    shared["up_bP"] = np.ascontiguousarray(up_bP.astype(np.float32))
    shared["wconP"] = np.ascontiguousarray(inputs["L0_ffn_W_con"][fidx])
    shared["brow2"] = np.ascontiguousarray(
        np.stack([inputs["L0_ffn_b_con"], inputs["L0_ffn_down_bias"]]))
    shared["w1t"] = np.ascontiguousarray(inputs["L1_ffn_W1"].T)
    shared["b1cm"] = np.ascontiguousarray(inputs["L1_ffn_b1"].reshape(32, 128).T)
    shared["w2t_ffn"] = np.ascontiguousarray(inputs["L1_ffn_W2"].T)
    shared["b2row"] = np.ascontiguousarray(inputs["L1_ffn_b2"][None, :])
    p = np.arange(128)
    inv = (1.0 / 10000.0 ** (np.arange(0, D, 2, dtype=np.float32) / D))
    ang = np.arange(T, dtype=np.float32)[None, :] * inv[p % 32][:, None]
    cos = np.cos(ang).astype(np.float32)
    sin = np.sin(ang).astype(np.float32)
    sgn = np.where(((p ^ 32) % 64) < 32, -1.0, 1.0).astype(np.float32)[:, None]
    shared["rope_cos"] = cos
    shared["rope_sin"] = (sin * sgn)
    p32 = np.zeros((128, 128), dtype=np.float32)
    p32[p, p ^ 32] = 1.0
    shared["p32"] = p32
    return per_core, shared


def build_C0():
    nc = bass.Bass("TRN2", target_bir_lowering=False)
    x_in = nc.dram_tensor("x_in", [TO, E], F32, kind="ExternalInput")
    z_in = nc.dram_tensor("z_in", [E, TO], BF16, kind="ExternalInput")
    w_ot = nc.dram_tensor("w_ot", [E, E], F32, kind="ExternalInput")
    wexpT = nc.dram_tensor("wexpT", [DH, E], F32, kind="ExternalInput")
    bexp_cm = nc.dram_tensor("bexp_cm", [128, 32], F32, kind="ExternalInput")
    bfw = nc.dram_tensor("bfw", [24, 4, 2048], F32, kind="ExternalInput")
    up_bP = nc.dram_tensor("up_bP", [128, 32], F32, kind="ExternalInput")
    wconP = nc.dram_tensor("wconP", [DH, E], F32, kind="ExternalInput")
    brow2 = nc.dram_tensor("brow2", [2, E], F32, kind="ExternalInput")
    x_out = nc.dram_tensor("x_out", [TO, E], F32, kind="ExternalOutput")
    xln_out = nc.dram_tensor("xln_out", [E, TO], BF16, kind="ExternalOutput")
    xln2 = nc.dram_tensor("xln2_i", [E, TO], BF16)
    w1e = nc.dram_tensor("w1e_i", [32, E, 128], BF16)
    b1e = nc.dram_tensor("b1e_i", [1, DH], F32)
    w2e = nc.dram_tensor("w2e_i", [DH, E], BF16)
    relay = nc.dram_tensor("relay_i", [32, 128, 512], BF16)
    c = Ctx(nc)
    x_sb = c.sb("xres", [128, NTT * E], F32)
    mats = {nm: c.sb(f"mat_{nm}", [128, 32 * 128], BF16)
            for nm in ("M1", "D1", "N1", "N2")}
    _, plans = butterfly_builder_plan(np.zeros((12, 2048, 2, 2), np.float32),
                                     np.zeros((12, 2048, 2, 2), np.float32))
    ld = nc.sync.dma_start(out=x_sb.ap().rearrange("p (n f) -> p n f", f=E),
                           in_=x_in.ap().rearrange("(n p) f -> p n f", p=128))
    c.inc(ld, "dh", 16)
    c.wait(nc.sync, "dh")
    c.barrier()
    emit_outproj_residual(c, z_in.ap(), w_ot.ap(), x_sb, "c0")
    emit_ln_transpose(c, None, xln2.ap(), "c0l", x_sb=x_sb)
    emit_bfly_matrices(c, bfw.ap(), mats, plans, "c0")
    emit_fold_w1(c, wexpT.ap(), bexp_cm.ap(), mats["M1"], w1e.ap(), b1e.ap(), "c0")
    emit_fold_w2(c, wconP.ap(), mats["N2"], w2e.ap(), "c0")
    emit_ffn_l0(c, xln2.ap(), w1e.ap(), b1e.ap(), up_bP.ap(), w2e.ap(),
                brow2.ap(), mats["D1"], mats["N1"], relay.ap(), x_sb, "c0")
    emit_ln_transpose(c, None, xln_out.ap(), "c0m", x_sb=x_sb)
    emit_store_x(c, x_sb, x_out.ap())
    c.close()
    return nc


def build_C1():
    nc = bass.Bass("TRN2", target_bir_lowering=False)
    x_in = nc.dram_tensor("x_in", [TO, E], F32, kind="ExternalInput")
    z_in = nc.dram_tensor("z_in", [E, TO], BF16, kind="ExternalInput")
    w_ot = nc.dram_tensor("w_ot", [E, E], F32, kind="ExternalInput")
    w1t = nc.dram_tensor("w1t", [E, DH], F32, kind="ExternalInput")
    b1cm = nc.dram_tensor("b1cm", [128, 32], F32, kind="ExternalInput")
    w2t = nc.dram_tensor("w2t", [DH, E], F32, kind="ExternalInput")
    b2row = nc.dram_tensor("b2row", [1, E], F32, kind="ExternalInput")
    x_out = nc.dram_tensor("x_out", [TO, E], F32, kind="ExternalOutput")
    xln2 = nc.dram_tensor("xln2_i", [E, TO], BF16)
    wq1 = nc.dram_tensor("wq1_i", [32, E, 128], BF16)
    wq2 = nc.dram_tensor("wq2_i", [DH, E], BF16)
    c = Ctx(nc)
    x_sb = c.sb("xres", [128, NTT * E], F32)
    ld = nc.sync.dma_start(out=x_sb.ap().rearrange("p (n f) -> p n f", f=E),
                           in_=x_in.ap().rearrange("(n p) f -> p n f", p=128))
    c.inc(ld, "dh", 16)
    c.wait(nc.sync, "dh")
    c.barrier()
    emit_outproj_residual(c, z_in.ap(), w_ot.ap(), x_sb, "c1")
    emit_ln_transpose(c, None, xln2.ap(), "c1l", x_sb=x_sb)
    emit_ffn_l1(c, xln2.ap(), w1t.ap(), b1cm.ap(), w2t.ap(), b2row.ap(),
                wq1.ap(), wq2.ap(), x_sb, "c1")
    emit_store_x(c, x_sb, x_out.ap())
    c.close()
    return nc



def _gather_xln(outs, key):
    xgs = []
    for b in range(2):
        xg = np.concatenate([outs[4 * b + r][key] for r in range(4)], axis=0)
        xgs.append(np.ascontiguousarray(xg))
    return [xgs[cid // 4] for cid in range(NCORE)]


def _a2a_z(outs, key):
    res = []
    for cid in range(NCORE):
        b, qd = cid // 4, cid % 4
        z = np.concatenate(
            [outs[4 * b + r][key][qd * 256:(qd + 1) * 256, :] for r in range(4)],
            axis=0)
        res.append(np.ascontiguousarray(z))
    return res


def kernel_unfused(**inputs):
    inputs = {k: np.ascontiguousarray(v) for k, v in inputs.items()}
    per_core, shared = host_slices(inputs)
    cores = list(range(NCORE))

    ncA = build_A()
    rA = run_bass_kernel_spmd(ncA, [{"x_in": pc["x0"]} for pc in per_core],
                              core_ids=cores)
    xg = _gather_xln(rA.results, "xln")

    ncB0 = build_B(rope=True, scale=0.125)
    inB0 = [{"xg": bfc(xg[i]), "wq": per_core[i]["wq0"], "wk": per_core[i]["wk0"],
             "wv": per_core[i]["wv0"], "rcos": shared["rope_cos"],
             "rsin": shared["rope_sin"], "p32": shared["p32"]} for i in cores]
    rB0 = run_bass_kernel_spmd(ncB0, inB0, core_ids=cores)
    z0 = _a2a_z(rB0.results, "zo")

    ncC0 = build_C0()
    inC0 = [{"x_in": per_core[i]["x0"], "z_in": z0[i], "w_ot": shared["w2t"],
             "wexpT": shared["wexpT"], "bexp_cm": shared["bexp_cm"],
             "bfw": shared["bfw"], "up_bP": shared["up_bP"],
             "wconP": shared["wconP"], "brow2": shared["brow2"]} for i in cores]
    rC0 = run_bass_kernel_spmd(ncC0, inC0, core_ids=cores)
    xg1 = _gather_xln(rC0.results, "xln_out")

    ncB1 = build_B(rope=False, scale=1.0)
    inB1 = [{"xg": bfc(xg1[i]), "wq": per_core[i]["wq1"], "wk": per_core[i]["wk1"],
             "wv": per_core[i]["wv1"]} for i in cores]
    rB1 = run_bass_kernel_spmd(ncB1, inB1, core_ids=cores)
    z1 = _a2a_z(rB1.results, "zo")

    ncC1 = build_C1()
    inC1 = [{"x_in": rC0.results[i]["x_out"], "z_in": z1[i], "w_ot": shared["ot"],
             "w1t": shared["w1t"], "b1cm": shared["b1cm"],
             "w2t": shared["w2t_ffn"], "b2row": shared["b2row"]} for i in cores]
    rC1 = run_bass_kernel_spmd(ncC1, inC1, core_ids=cores)

    out = np.empty((B, T, E), dtype=np.float32)
    for cid in cores:
        b, qd = cid // 4, cid % 4
        out[b, qd * TO:(qd + 1) * TO, :] = rC1.results[cid]["x_out"]
    return out



def emit_collective(c, kind, in_ap, out_ap, tag):
    nc = c.nc
    cc = nc.gpsimd.collective_compute(
        kind, mybir.AluOpType.bypass,
        replica_groups=[[0, 1, 2, 3], [4, 5, 6, 7]],
        ins=[in_ap], outs=[out_ap])
    c.inc(cc, "cc")
    c.wait(nc.gpsimd, "cc")
    c.barrier()


def build_fused():
    nc = bass.Bass("TRN2", target_bir_lowering=False)
    t_in = {}
    def inp(name, shape, dtype=F32):
        t_in[name] = nc.dram_tensor(name, shape, dtype, kind="ExternalInput")
        return t_in[name].ap()
    x0 = inp("x0", [TO, E])
    wq0 = inp("wq0", [E, 256]); wk0 = inp("wk0", [E, 256]); wv0 = inp("wv0", [E, 256])
    rcos = inp("rcos", [128, T]); rsin = inp("rsin", [128, T]); p32 = inp("p32", [128, 128])
    w2t = inp("w2t", [E, E])
    wexpT = inp("wexpT", [DH, E]); bexp_cm = inp("bexp_cm", [128, 32])
    bfw = inp("bfw", [24, 4, 2048]); up_bP = inp("up_bP", [128, 32])
    wconP = inp("wconP", [DH, E]); brow2 = inp("brow2", [2, E])
    wq1 = inp("wq1", [E, 256]); wk1 = inp("wk1", [E, 256]); wv1 = inp("wv1", [E, 256])
    ot = inp("ot", [E, E])
    w1t = inp("w1t", [E, DH]); b1cm = inp("b1cm", [128, 32])
    w2t_ffn = inp("w2t_ffn", [DH, E]); b2row = inp("b2row", [1, E])
    x_out = nc.dram_tensor("x_out", [TO, E], F32, kind="ExternalOutput")

    def internal(name, shape, dtype):
        return nc.dram_tensor(name, shape, dtype).ap()
    xln_a = internal("xln_a", [E, TO], BF16)
    xg_a = internal("xg_a", [4 * E, TO], BF16)
    q0d = internal("q0d", [256, T], BF16)
    k0d = internal("k0d", [256, T], BF16)
    v0d = internal("v0d", [T, 256], BF16)
    zsh0 = internal("zsh0", [E, TO], BF16)
    zg0 = internal("zg0", [4 * E, TO], BF16)
    xln2 = internal("xln2", [E, TO], BF16)
    w1e = internal("w1e", [32, E, 128], BF16)
    b1e = internal("b1e", [1, DH], F32)
    w2e = internal("w2e", [DH, E], BF16)
    relay = internal("relay", [32, 128, 512], BF16)
    xln_b = internal("xln_b", [E, TO], BF16)
    xg_b = internal("xg_b", [4 * E, TO], BF16)
    q1d = internal("q1d", [256, T], BF16)
    k1d = internal("k1d", [256, T], BF16)
    v1d = internal("v1d", [T, 256], BF16)
    zsh1 = internal("zsh1", [E, TO], BF16)
    zg1 = internal("zg1", [4 * E, TO], BF16)
    xln3 = internal("xln3", [E, TO], BF16)
    wq1i = internal("wq1i", [32, E, 128], BF16)
    wq2i = internal("wq2i", [DH, E], BF16)

    c = Ctx(nc)
    c.sem("cc")
    qd_sv = nc.partition_id() % 4
    x_sb = c.sb("xres", [128, NTT * E], F32)
    ld = nc.sync.dma_start(out=x_sb.ap().rearrange("p (n f) -> p n f", f=E),
                           in_=x0.rearrange("(n p) f -> p n f", p=128))
    c.inc(ld, "dh", 16)
    c.wait(nc.sync, "dh")
    c.barrier()
    emit_ln_transpose(c, None, xln_a, "fa", x_sb=x_sb)
    emit_collective(c, "AllGather", xln_a, xg_a, "ga")
    emit_qkv(c, xg_a, wq0, wk0, wv0, q0d, k0d, v0d, "f0",
             rope_cs=(rcos, rsin, p32))
    emit_attention(c, q0d, k0d, v0d, zsh0, "f0", 0.125)
    emit_collective(c, "AllGather", zsh0, zg0, "za")
    emit_outproj_residual(c, None, w2t, x_sb, "f0", z_gathered=(zg0, qd_sv))
    emit_ln_transpose(c, None, xln2, "f0l", x_sb=x_sb)
    c.push()
    mats = {nm: c.sb(f"mat_{nm}", [128, 32 * 128], BF16)
            for nm in ("M1", "D1", "N1", "N2")}
    _, plans = butterfly_builder_plan(np.zeros((12, 2048, 2, 2), np.float32),
                                     np.zeros((12, 2048, 2, 2), np.float32))
    emit_bfly_matrices(c, bfw, mats, plans, "f0")
    emit_fold_w1(c, wexpT, bexp_cm, mats["M1"], w1e, b1e, "f0")
    emit_fold_w2(c, wconP, mats["N2"], w2e, "f0")
    emit_ffn_l0(c, xln2, w1e, b1e, up_bP, w2e, brow2,
                mats["D1"], mats["N1"], relay, x_sb, "f0")
    c.pop()
    emit_ln_transpose(c, None, xln_b, "fb", x_sb=x_sb)
    emit_collective(c, "AllGather", xln_b, xg_b, "gb")
    emit_qkv(c, xg_b, wq1, wk1, wv1, q1d, k1d, v1d, "f1", rope_cs=None)
    emit_attention(c, q1d, k1d, v1d, zsh1, "f1", 1.0)
    emit_collective(c, "AllGather", zsh1, zg1, "zb")
    emit_outproj_residual(c, None, ot, x_sb, "f1", z_gathered=(zg1, qd_sv))
    emit_ln_transpose(c, None, xln3, "f1l", x_sb=x_sb)
    emit_ffn_l1(c, xln3, w1t, b1cm, w2t_ffn, b2row, wq1i, wq2i, x_sb, "f1")
    emit_store_x(c, x_sb, x_out.ap())
    c.close()
    return nc


def kernel(**inputs):
    inputs = {k: np.ascontiguousarray(v) for k, v in inputs.items()}
    per_core, shared = host_slices(inputs)
    cores = list(range(NCORE))
    nc = build_fused()
    in_maps = []
    for i in cores:
        pc = per_core[i]
        in_maps.append({
            "x0": pc["x0"], "wq0": pc["wq0"], "wk0": pc["wk0"], "wv0": pc["wv0"],
            "rcos": shared["rope_cos"], "rsin": shared["rope_sin"],
            "p32": shared["p32"], "w2t": shared["w2t"],
            "wexpT": shared["wexpT"], "bexp_cm": shared["bexp_cm"],
            "bfw": shared["bfw"], "up_bP": shared["up_bP"],
            "wconP": shared["wconP"], "brow2": shared["brow2"],
            "wq1": pc["wq1"], "wk1": pc["wk1"], "wv1": pc["wv1"],
            "ot": shared["ot"], "w1t": shared["w1t"], "b1cm": shared["b1cm"],
            "w2t_ffn": shared["w2t_ffn"], "b2row": shared["b2row"],
        })
    res = run_bass_kernel_spmd(nc, in_maps, core_ids=cores)
    out = np.empty((B, T, E), dtype=np.float32)
    for cid in cores:
        b, qd = cid // 4, cid % 4
        out[b, qd * TO:(qd + 1) * TO, :] = res.results[cid]["x_out"]
    return out


# version 13
# speedup vs baseline: 1.0789x; 1.0401x over previous
import math
import numpy as np
import ml_dtypes
from contextlib import ExitStack

import concourse.bass as bass
import concourse.mybir as mybir
from concourse.bass_utils import run_bass_kernel_spmd

F32 = mybir.dt.float32
BF16 = mybir.dt.bfloat16
AF = mybir.ActivationFunctionType
ALU = mybir.AluOpType

B, T, E, H, D = 2, 8192, 1024, 16, 64
NCORE = 8
TO = T // 4
HO = H // 4
NTT = TO // 128
NCH = E // 128
DH = 4096
EPS = 1e-5


class Ctx:

    def __init__(self, nc):
        self.nc = nc
        self.es = ExitStack()
        self.sem_es = ExitStack()
        self._scopes = []
        self.sems = {}
        self.counts = {}

    def push(self):
        self._scopes.append(self.es)
        self.es = ExitStack()

    def pop(self):
        self.es.close()
        self.es = self._scopes.pop()

    def sb(self, name, shape, dtype):
        return self.es.enter_context(self.nc.sbuf_tensor(name, shape, dtype))

    def ps(self, name, shape, dtype=F32):
        return self.es.enter_context(self.nc.psum_tensor(name, shape, dtype))

    def sem(self, name):
        if name not in self.sems:
            self.sems[name] = self.sem_es.enter_context(self.nc.semaphore(name))
            self.counts[name] = 0
        return self.sems[name]

    def inc(self, inst, name, n=1):
        inst.then_inc(self.sem(name), n)
        self.counts[name] += n
        return self.counts[name]

    def wait(self, eng, name, val=None):
        if val is None:
            val = self.counts[name]
        eng.wait_ge(self.sem(name), val)

    def barrier(self):
        self.nc.all_engine_barrier()

    def close(self):
        self.es.close()
        self.sem_es.close()


def bfc(x):
    return np.ascontiguousarray(x).astype(ml_dtypes.bfloat16)



def emit_ln_transpose(c, x_src, xln_dst, tag, x_sb=None):
    nc = c.nc
    c.push()
    own_x = x_sb is None
    if own_x:
        x_sb = c.sb(f"x_{tag}", [128, NTT * E], F32)
    xln_tm = c.sb(f"xlntm_{tag}", [128, NTT * E], BF16)
    stats = c.sb(f"stats_{tag}", [128, NTT * 12], F32)
    mv = c.sb(f"mv_{tag}", [128, NTT * 2], F32)
    rstd = c.sb(f"rstd_{tag}", [128, NTT], F32)
    sid = c.sb(f"id_{tag}", [128, 128], BF16)
    xfm = c.sb(f"xfm_{tag}", [128, NCH * TO], BF16)
    ptr = c.ps(f"ptr_{tag}", [128, 2 * 1024], BF16)

    mi = nc.gpsimd.memset(sid[:], 1.0)
    c.inc(mi, "g")
    c.wait(nc.gpsimd, "g")
    ai = nc.gpsimd.affine_select(out=sid[:], in_=sid[:], compare_op=ALU.is_equal,
                                 fill=0.0, base=0, pattern=[[-1, 128]],
                                 channel_multiplier=1)
    c.inc(ai, "g")

    if own_x:
        ld = nc.sync.dma_start(
            out=x_sb.ap().rearrange("p (n f) -> p n f", f=E),
            in_=x_src.rearrange("(n p) f -> p n f", p=128))
        c.inc(ld, "dh", 16)
        c.wait(nc.sync, "dh")
        c.wait(nc.vector, "dh")

    for i in range(NTT):
        xt = x_sb[:, i * E:(i + 1) * E]
        st = stats[:, i * 12:(i + 1) * 12]
        s1 = nc.vector.bn_stats(out=st[:, 0:6], in_=xt[:, 0:512])
        c.inc(s1, "v")
        s2 = nc.vector.bn_stats(out=st[:, 6:12], in_=xt[:, 512:1024])
        c.inc(s2, "v")
        c.wait(nc.vector, "v")
        ag = nc.vector.bn_aggr(out=mv[:, 2 * i:2 * i + 2], in_=st[:])
        c.inc(ag, "v")
    seps = c.sb(f"eps_{tag}", [128, 1], F32)
    me = nc.gpsimd.memset(seps[:], EPS)
    c.inc(me, "g")
    c.wait(nc.scalar, "g")
    c.wait(nc.scalar, "v")
    lg = nc.scalar.activation(rstd.ap().rearrange("p (n one) -> p n one", one=1),
                              mv.ap().rearrange("p (n two) -> p n two", two=2)[:, :, 1:2],
                              AF.Ln, bias=seps[:])
    c.inc(lg, "s")
    c.wait(nc.scalar, "s")
    ex = nc.scalar.activation(rstd[:], rstd[:], AF.Exp, scale=-0.5)
    c.inc(ex, "s")
    c.wait(nc.vector, "s")
    for i in range(NTT):
        tsn = nc.vector.tensor_scalar(
            out=xln_tm[:, i * E:(i + 1) * E], in0=x_sb[:, i * E:(i + 1) * E],
            scalar1=mv[:, 2 * i:2 * i + 1], scalar2=rstd[:, i:i + 1],
            op0=ALU.subtract, op1=ALU.mult)
        c.inc(tsn, "v")
    c.wait(nc.tensor, "v")
    c.wait(nc.tensor, "g")
    c.sem("vtr")
    for i in range(NTT):
        par = i % 2
        bank = ptr[:, par * 1024:(par + 1) * 1024]
        if i >= 2:
            c.wait(nc.tensor, "vtr", c.counts["vtr"] - 1)
        for ch in range(NCH):
            tr = nc.tensor.transpose(
                bank[:, ch * 128:(ch + 1) * 128],
                xln_tm[:, i * E + ch * 128: i * E + (ch + 1) * 128], sid[:])
            if ch == NCH - 1:
                c.inc(tr, "tr")
        c.wait(nc.vector, "tr")
        cp = nc.vector.tensor_copy(
            out=xfm.ap().rearrange("p (ch t) -> p ch t", ch=NCH)[:, :, i * 128:(i + 1) * 128],
            in_=bank.rearrange("p (ch t) -> p ch t", ch=NCH))
        c.inc(cp, "vtr")
    c.wait(nc.sync, "vtr")
    stfm = nc.sync.dma_start(out=xln_dst.rearrange("(ch p) t -> p ch t", p=128),
                             in_=xfm.ap().rearrange("p (ch t) -> p ch t", ch=NCH))
    c.inc(stfm, "dh", 16)
    c.wait(nc.sync, "dh")
    c.barrier()
    c.pop()
    return x_sb


def emit_qkv(c, xln_full, wq, wk, wv, q_dst, k_dst, v_dst, tag, rope_cs=None):
    nc = c.nc
    c.push()
    sw = c.sb(f"w_{tag}", [128, NCH * 3 * 256], BF16)
    sx = c.sb(f"xg_{tag}", [128, 2 * NCH * 512], BF16)
    sqk = c.sb(f"qk_{tag}", [128, 2 * 512], BF16)
    sv = c.sb(f"v_{tag}", [128, 1024], BF16)
    pq = c.ps(f"pq_{tag}", [128, 2 * 512], F32)
    pk = c.ps(f"pk_{tag}", [128, 2 * 512], F32)
    pv = c.ps(f"pv_{tag}", [128, 2 * 512], F32)
    if rope_cs is not None:
        scos = c.sb(f"cos_{tag}", [128, T], BF16)
        ssin = c.sb(f"sin_{tag}", [128, T], BF16)
        sqr = c.sb(f"qr_{tag}", [128, 2048], BF16)
        sp32 = c.sb(f"p32_{tag}", [128, 128], BF16)
        prope = c.ps(f"pr_{tag}", [128, 1024], F32)
        c.inc(nc.gpsimd.dma_start(out=scos[:], in_=rope_cs[0]), "dg", 16)
        c.inc(nc.gpsimd.dma_start(out=ssin[:], in_=rope_cs[1]), "dg", 16)
        c.inc(nc.gpsimd.dma_start(out=sp32[:], in_=rope_cs[2]), "dg", 16)
        c.sem("rp")
    else:
        scos = ssin = sqr = sp32 = prope = None
    for wi, w in enumerate((wq, wk, wv)):
        ld = nc.gpsimd.dma_start(
            out=sw.ap().rearrange("p (ch w s) -> p ch w s", ch=NCH, w=3)[:, :, wi, :],
            in_=w.rearrange("(ch p) s -> p ch s", p=128))
        c.inc(ld, "dg", 16)
    c.wait(nc.tensor, "dg")

    c.sem("mmq"); c.sem("v"); c.sem("dh")
    qfree = kfree = vfree = 0
    for tt in range(16):
        r, tl = tt // 4, (tt % 4) * 512
        buf = tt % 2
        xb = sx[:, buf * NCH * 512:(buf + 1) * NCH * 512]
        if tt >= 2:
            c.wait(nc.sync, "mmq", tt - 1)
        lxs = "lx%d" % buf
        ld = nc.sync.dma_start(
            out=xb.rearrange("p (ch t) -> p ch t", ch=NCH),
            in_=xln_full[r * E:(r + 1) * E, tl:tl + 512].rearrange("(ch p) t -> p ch t", p=128))
        nld = c.inc(ld, lxs, 16)
        c.wait(nc.tensor, lxs, nld)
        if tt >= 1:
            c.wait(nc.tensor, "v", qfree)
        for ch in range(NCH):
            st, sp = (ch == 0), (ch == NCH - 1)
            xt = xb[:, ch * 512:(ch + 1) * 512]
            wch = sw[:, ch * 768:(ch + 1) * 768]
            for half in range(2):
                nc.tensor.matmul(pq[:, half * 512:half * 512 + 512],
                                 wch[:, half * 128:half * 128 + 128], xt,
                                 start=st, stop=sp)
        if tt >= 1:
            c.wait(nc.tensor, "v", kfree)
        for ch in range(NCH):
            st, sp = (ch == 0), (ch == NCH - 1)
            xt = xb[:, ch * 512:(ch + 1) * 512]
            wch = sw[:, ch * 768:(ch + 1) * 768]
            for half in range(2):
                nc.tensor.matmul(pk[:, half * 512:half * 512 + 512],
                                 wch[:, 256 + half * 128:256 + half * 128 + 128], xt,
                                 start=st, stop=sp)
        if tt >= 1:
            c.wait(nc.tensor, "v", vfree)
        for sub in range(4):
            for ch in range(NCH):
                st, sp = (ch == 0), (ch == NCH - 1)
                xt = xb[:, ch * 512:(ch + 1) * 512]
                wch = sw[:, ch * 768:(ch + 1) * 768]
                mm = nc.tensor.matmul(
                    pv[:, sub * 256:(sub + 1) * 256],
                    xt[:, sub * 128:(sub + 1) * 128], wch[:, 512:768],
                    start=st, stop=sp)
        c.inc(mm, "mmq")
        gt = tt * 512
        c.wait(nc.vector, "mmq")
        if tt >= 1:
            c.wait(nc.vector, "dh")
        for half in range(2):
            cp = nc.vector.tensor_copy(out=sqk[:, half * 512:half * 512 + 512],
                                       in_=pq[:, half * 512:half * 512 + 512])
            c.inc(cp, "v")
        qfree = c.counts["v"]
        if rope_cs is not None:
            _emit_rope(c, sqk, sqr, scos, ssin, sp32, prope, gt)
        c.wait(nc.sync, "v")
        st1 = nc.sync.dma_start(
            out=q_dst.rearrange("(h p) t -> p h t", p=128)[:, :, gt:gt + 512],
            in_=sqk.ap().rearrange("p (h t) -> p h t", h=2))
        c.inc(st1, "dh", 16)
        c.wait(nc.vector, "dh")
        for half in range(2):
            cp = nc.vector.tensor_copy(out=sqk[:, half * 512:half * 512 + 512],
                                       in_=pk[:, half * 512:half * 512 + 512])
            c.inc(cp, "v")
        kfree = c.counts["v"]
        if rope_cs is not None:
            _emit_rope(c, sqk, sqr, scos, ssin, sp32, prope, gt)
        c.wait(nc.sync, "v")
        st2 = nc.sync.dma_start(
            out=k_dst.rearrange("(h p) t -> p h t", p=128)[:, :, gt:gt + 512],
            in_=sqk.ap().rearrange("p (h t) -> p h t", h=2))
        c.inc(st2, "dh", 16)
        if tt >= 1:
            c.wait(nc.vector, "dh")
        cpv = nc.vector.tensor_copy(out=sv[:], in_=pv[:])
        vfree = c.inc(cpv, "v")
        c.wait(nc.sync, "v")
        stv = nc.sync.dma_start(
            out=v_dst.rearrange("(n p) d -> p n d", p=128)[:, 4 * tt:4 * tt + 4, :],
            in_=sv.ap().rearrange("p (s d) -> p s d", s=4))
        c.inc(stv, "dh", 16)
    c.wait(nc.sync, "dh")
    c.barrier()
    c.pop()


def _emit_rope(c, sqk, sqr, scos, ssin, p32_sb, prope, gt):
    nc = c.nc
    cosw = scos[:, gt:gt + 512]
    sinw = ssin[:, gt:gt + 512]
    c.wait(nc.vector, "v")
    for half in range(2):
        q = sqk[:, half * 512:half * 512 + 512]
        t1 = sqr[:, half * 512:half * 512 + 512]
        u = sqr[:, 1024 + half * 512:1024 + half * 512 + 512]
        c.inc(nc.vector.tensor_mul(out=t1, in0=q, in1=cosw), "v")
        c.inc(nc.vector.tensor_mul(out=u, in0=q, in1=sinw), "v")
    c.wait(nc.tensor, "v")
    c.wait(nc.tensor, "v", c.counts.get("_ropefree", 0))
    for half in range(2):
        u = sqr[:, 1024 + half * 512:1024 + half * 512 + 512]
        mm = nc.tensor.matmul(prope[:, half * 512:half * 512 + 512],
                              p32_sb[:], u, start=True, stop=True)
        c.inc(mm, "rp")
    c.wait(nc.vector, "rp")
    for half in range(2):
        q = sqk[:, half * 512:half * 512 + 512]
        t1 = sqr[:, half * 512:half * 512 + 512]
        ad = nc.vector.tensor_add(out=q, in0=t1,
                                  in1=prope[:, half * 512:half * 512 + 512])
        c.inc(ad, "v")
    c.counts["_ropefree"] = c.counts["v"]


def emit_attention(c, q_src, k_src, v_src, z_dst, tag, scale):
    nc = c.nc
    c.push()
    NKT = T // 128
    sk = [c.sb(f"ak{x}_{tag}", [128, T], BF16) for x in range(2)]
    sq = [c.sb(f"aq{x}_{tag}", [128, T], BF16) for x in range(2)]
    sva = [c.sb(f"av{x}a_{tag}", [128, NKT * 65], BF16) for x in range(2)]
    svb = [c.sb(f"av{x}b_{tag}", [128, NKT * 65], BF16) for x in range(2)]
    sattn = [c.sb(f"aat{x}_{tag}", [128, 3 * 1024], BF16) for x in range(2)]
    srec = [c.sb(f"arc{x}_{tag}", [1, 2 * 1024], BF16) for x in range(2)]
    srecf = [c.sb(f"arf{x}_{tag}", [1, 1024], F32) for x in range(2)]
    sbc = [c.sb(f"abc{x}_{tag}", [128, 1024], BF16) for x in range(2)]
    szr = [c.sb(f"azr{x}_{tag}", [128, 2 * 1024], BF16) for x in range(2)]
    sones1 = c.sb(f"aon_{tag}", [1, 64], BF16)
    smask = c.sb(f"amk_{tag}", [128, 4 * 512], BF16)
    ps = [c.ps(f"aps{x}_{tag}", [128, 2 * 512], F32) for x in range(2)]
    pz = [c.ps(f"apz{x}_{tag}", [128, 2 * 512], F32) for x in range(2)]

    c.inc(nc.gpsimd.memset(sones1[:], 1.0), "g")
    for m in range(4):
        c.inc(nc.gpsimd.memset(smask[:, m * 512:(m + 1) * 512], 1.0), "g")
    c.wait(nc.gpsimd, "g")
    for m in range(4):
        am = nc.gpsimd.affine_select(
            out=smask[:, m * 512:(m + 1) * 512], in_=smask[:, m * 512:(m + 1) * 512],
            compare_op=ALU.is_ge, fill=0.0, base=-128 * m,
            pattern=[[1, 512]], channel_multiplier=-1)
        c.inc(am, "g")
    for x in range(2):
        c.inc(nc.sync.dma_start(out=sk[x][:], in_=k_src[x * 128:(x + 1) * 128, :]),
              "dh", 16)
        c.inc(nc.sync.dma_start(out=sq[x][:], in_=q_src[x * 128:(x + 1) * 128, :]),
              "dh", 16)
        for hh, sv in ((0, sva[x]), (1, svb[x])):
            ld = nc.sync.dma_start(
                out=sv.ap().rearrange("p (n c) -> p n c", c=65)[:, :, 0:64],
                in_=v_src.rearrange("(n p) c -> p n c", p=128)[
                    :, :, x * 128 + hh * 64:x * 128 + hh * 64 + 64])
            c.inc(ld, "dh", 16)
            c.inc(nc.gpsimd.memset(
                sv.ap().rearrange("p (n c) -> p n c", c=65)[:, :, 64:65], 1.0), "g")
    c.wait(nc.sync, "dh")
    c.wait(nc.gpsimd, "g")
    c.barrier()

    scs = ["scA", "scB"]; exs = ["exA", "exB"]; avs = ["avA", "avB"]
    mks = ["mkA", "mkB"]; rcs = ["rcA", "rcB"]; bcs = ["bcA", "bcB"]
    zes = ["zeA", "zeB"]
    cps = ["cpA", "cpB"]
    for n_ in scs + exs + avs + mks + rcs + bcs + zes + cps + ["zst0", "zst1"]:
        c.sem(n_)
    ex0 = [c.counts[exs[0]], c.counts[exs[1]]]
    mk0 = [c.counts[mks[0]], c.counts[mks[1]]]
    rc_done = [{}, {}]

    def e_scores(x, i, jq, kt):
        if i >= 1:
            c.wait(nc.tensor, exs[x], c.counts[exs[x]])
        if kt == 0:
            c.wait(nc.tensor, cps[x], c.counts[cps[x]])
        qsl = sq[x][:, jq * 512:(jq + 1) * 512]
        ksl = sk[x][:, kt * 128:(kt + 1) * 128]
        nc.tensor.matmul(ps[x][:, 0:512], ksl[0:64, :], qsl[0:64, :],
                         start=True, stop=True)
        mmb = nc.tensor.matmul(ps[x][:, 512:1024], ksl[64:128, :], qsl[64:128, :],
                               start=True, stop=True)
        c.inc(mmb, scs[x])

    def e_exp(x, i, jq, kt):
        par = i % 3
        c.wait(nc.scalar, scs[x])
        e = nc.scalar.activation(sattn[x][:, par * 1024:(par + 1) * 1024],
                                 ps[x][:], AF.Exp, scale=scale)
        c.inc(e, exs[x])

    def e_mask(x, i, jq, kt):
        par = i % 3
        a0 = par * 1024
        m = kt - 4 * jq
        eng = nc.vector if x == 0 else nc.gpsimd
        c.wait(eng, exs[x])
        msl = smask[:, m * 512:(m + 1) * 512]
        eng.tensor_mul(out=sattn[x][:, a0:a0 + 512],
                       in0=sattn[x][:, a0:a0 + 512], in1=msl)
        mb = eng.tensor_mul(out=sattn[x][:, a0 + 512:a0 + 1024],
                            in0=sattn[x][:, a0 + 512:a0 + 1024], in1=msl)
        c.inc(mb, mks[x])

    def e_av(x, i, jq, kt, dcount):
        nk = 4 * (jq + 1)
        if kt == 0:
            c.wait(nc.tensor, zes[x], c.counts[zes[x]])
        if kt >= 4 * jq:
            c.wait(nc.tensor, mks[x], mk0[x] + dcount)
        else:
            c.wait(nc.tensor, exs[x], ex0[x] + i + 1)
        par = i % 3
        a0 = par * 1024
        st, sp = (kt == 0), (kt == nk - 1)
        nc.tensor.matmul(pz[x][0:65, 0:512], sva[x][:, kt * 65:(kt + 1) * 65],
                         sattn[x][:, a0:a0 + 512], start=st, stop=sp,
                         skip_group_check=True)
        mm = nc.tensor.matmul(pz[x][0:65, 512:1024], svb[x][:, kt * 65:(kt + 1) * 65],
                              sattn[x][:, a0 + 512:a0 + 1024], start=st, stop=sp,
                              skip_group_check=True)
        c.inc(mm, avs[x])

    def early_evac(x, jq):
        sl = (jq % 2) * 1024
        c.wait(nc.scalar, avs[x])
        c.wait(nc.scalar, rcs[x])
        cpa = nc.scalar.activation(srecf[x][0:1, :], pz[x][64:65, :],
                                   AF.Identity)
        c.inc(cpa, rcs[x])
        c.wait(nc.vector, avs[x])
        c.wait(nc.vector, "dh")
        zc = nc.vector.tensor_copy(out=szr[x][0:64, sl:sl + 1024],
                                   in_=pz[x][0:64, 0:1024])
        c.inc(zc, zes[x])
        c.wait(nc.vector, rcs[x])
        rc = nc.vector.reciprocal(out=srecf[x][0:1, :], in_=srecf[x][0:1, :])
        c.inc(rc, "v")
        c.wait(nc.vector, "v")
        rc2 = nc.vector.tensor_copy(out=srec[x][0:1, sl:sl + 1024],
                                    in_=srecf[x][0:1, :])
        c.inc(rc2, rcs[x])

    def late_evac(x, jq):
        sl = (jq % 2) * 1024
        c.wait(nc.tensor, rcs[x], rc_done[x][jq])
        c.wait(nc.tensor, exs[x], c.counts[exs[x]])
        c.wait(nc.tensor, cps[x], c.counts[cps[x]])
        nc.tensor.matmul(ps[x][0:64, 0:512], sones1[:], srec[x][0:1, sl:sl + 512],
                         start=True, stop=True)
        bc = nc.tensor.matmul(ps[x][0:64, 512:1024], sones1[:],
                              srec[x][0:1, sl + 512:sl + 1024],
                              start=True, stop=True)
        c.inc(bc, bcs[x])
        c.wait(nc.scalar, bcs[x])
        c.wait(nc.scalar, "zst%d" % x, c.counts["zst%d" % x])
        cpc = nc.scalar.activation(sbc[x][0:64, :], ps[x][0:64, :], AF.Identity)
        c.inc(cpc, cps[x])
        c.wait(nc.vector, cps[x])
        nc.vector.tensor_mul(out=szr[x][0:64, sl:sl + 512],
                             in0=szr[x][0:64, sl:sl + 512],
                             in1=sbc[x][0:64, 0:512])
        zm = nc.vector.tensor_mul(out=szr[x][0:64, sl + 512:sl + 1024],
                                  in0=szr[x][0:64, sl + 512:sl + 1024],
                                  in1=sbc[x][0:64, 512:1024])
        c.inc(zm, "zst%d" % x)
        c.wait(nc.sync, "zst%d" % x, c.counts["zst%d" % x])
        r0 = (jq // 4) * 256 + x * 128
        c0 = (jq % 4) * 512
        st1 = nc.sync.dma_start(out=z_dst[r0:r0 + 64, c0:c0 + 512],
                                in_=szr[x][0:64, sl:sl + 512])
        c.inc(st1, "dh", 16)
        st2 = nc.sync.dma_start(out=z_dst[r0 + 64:r0 + 128, c0:c0 + 512],
                                in_=szr[x][0:64, sl + 512:sl + 1024])
        c.inc(st2, "dh", 16)

    pend = [[], []]
    ndiag = [0, 0]
    stepx = [0, 0]

    def substep(x, jq, kt):
        i = stepx[x]
        e_scores(x, i, jq, kt)
        e_exp(x, i, jq, kt)
        if kt >= 4 * jq:
            ndiag[x] += 1
            e_mask(x, i, jq, kt)
        pend[x].append((i, jq, kt, ndiag[x]))
        if len(pend[x]) > 2:
            e_av(x, *pend[x].pop(0))
        stepx[x] += 1

    def flush(x, jq):
        while pend[x]:
            e_av(x, *pend[x].pop(0))
        early_evac(x, jq)
        rc_done[x][jq] = c.counts[rcs[x]]

    LAG = 4
    for jq in range(16):
        nk = 4 * (jq + 1)
        for kt in range(nk + LAG):
            if kt < nk:
                if kt == 0 and jq >= 2:
                    late_evac(0, jq - 2)
                if kt == 2 and jq >= 1:
                    flush(1, jq - 1)
                substep(0, jq, kt)
            if kt == nk + 2:
                flush(0, jq)
            if kt >= LAG:
                if kt - LAG == 0 and jq >= 2:
                    late_evac(1, jq - 2)
                substep(1, jq, kt - LAG)
    flush(1, 15)
    for jq in (14, 15):
        late_evac(0, jq)
        late_evac(1, jq)
    c.wait(nc.sync, "dh")
    c.barrier()
    c.pop()


def emit_outproj_residual(c, z_own, w_ot, x_sb, tag, z_gathered=None):
    nc = c.nc
    c.push()
    sz = c.sb(f"oz_{tag}", [128, NCH * TO], BF16)
    sw = c.sb(f"ow_{tag}", [128, NCH * E], BF16)
    pp = c.ps(f"op_{tag}", [128, 2 * 512], F32)
    if z_gathered is None:
        ld = nc.sync.dma_start(out=sz.ap().rearrange("p (a t) -> p a t", a=NCH),
                               in_=z_own.rearrange("(a p) t -> p a t", p=128))
        c.inc(ld, "dh", 16)
    else:
        zg, qd = z_gathered
        for ch in range(NCH):
            r, pr = ch // 2, ch % 2
            blk = qd + 4 * r
            ap = zg.rearrange("(b q) t -> b q t", q=256)[bass.ts(blk, 1), :, :]
            ld = nc.sync.dma_start(
                out=sz[:, ch * TO:(ch + 1) * TO],
                in_=ap.squeeze()[pr * 128:(pr + 1) * 128, :])
            c.inc(ld, "dh", 16)
    ld = nc.gpsimd.dma_start(out=sw.ap().rearrange("p (a f) -> p a f", a=NCH),
                             in_=w_ot.rearrange("(a p) f -> p a f", p=128))
    c.inc(ld, "dg", 16)
    c.wait(nc.tensor, "dh")
    c.wait(nc.tensor, "dg")
    c.sem("oev")
    for ts in range(NTT):
        for fh in range(2):
            sl = (ts * 2 + fh) % 2
            if ts * 2 + fh >= 2:
                c.wait(nc.tensor, "oev", c.counts["oev"] - 1)
            for a in range(NCH):
                mm = nc.tensor.matmul(
                    pp[:, sl * 512:sl * 512 + 512],
                    sz[:, a * TO + ts * 128:a * TO + (ts + 1) * 128],
                    sw[:, a * E + fh * 512:a * E + fh * 512 + 512],
                    start=(a == 0), stop=(a == NCH - 1))
            c.inc(mm, "omm")
            c.wait(nc.vector, "omm")
            ad = nc.vector.tensor_add(
                out=x_sb[:, ts * E + fh * 512:ts * E + fh * 512 + 512],
                in0=x_sb[:, ts * E + fh * 512:ts * E + fh * 512 + 512],
                in1=pp[:, sl * 512:sl * 512 + 512])
            c.inc(ad, "oev")
    c.barrier()
    c.pop()


def emit_store_x(c, x_sb, x_dst):
    nc = c.nc
    st = nc.sync.dma_start(out=x_dst.rearrange("(n p) f -> p n f", p=128),
                           in_=x_sb.ap().rearrange("p (n f) -> p n f", f=E))
    c.inc(st, "dh", 16)
    c.wait(nc.sync, "dh")
    c.barrier()



def build_A():
    nc = bass.Bass("TRN2", target_bir_lowering=False)
    x_in = nc.dram_tensor("x_in", [TO, E], F32, kind="ExternalInput")
    xln = nc.dram_tensor("xln", [E, TO], BF16, kind="ExternalOutput")
    c = Ctx(nc)
    emit_ln_transpose(c, x_in.ap(), xln.ap(), "a")
    c.close()
    return nc


def build_B(rope, scale):
    nc = bass.Bass("TRN2", target_bir_lowering=False)
    xg = nc.dram_tensor("xg", [4 * E, TO], BF16, kind="ExternalInput")
    wq = nc.dram_tensor("wq", [E, 256], F32, kind="ExternalInput")
    wk = nc.dram_tensor("wk", [E, 256], F32, kind="ExternalInput")
    wv = nc.dram_tensor("wv", [E, 256], F32, kind="ExternalInput")
    zo = nc.dram_tensor("zo", [E, TO], BF16, kind="ExternalOutput")
    qd_ = nc.dram_tensor("q_i", [256, T], BF16)
    kd_ = nc.dram_tensor("k_i", [256, T], BF16)
    vd_ = nc.dram_tensor("v_i", [T, 256], BF16)
    rope_cs = None
    if rope:
        rc = nc.dram_tensor("rcos", [128, T], F32, kind="ExternalInput")
        rs = nc.dram_tensor("rsin", [128, T], F32, kind="ExternalInput")
        p32 = nc.dram_tensor("p32", [128, 128], F32, kind="ExternalInput")
        rope_cs = (rc.ap(), rs.ap(), p32.ap())
    c = Ctx(nc)
    emit_qkv(c, xg.ap(), wq.ap(), wk.ap(), wv.ap(), qd_.ap(), kd_.ap(), vd_.ap(),
             "b", rope_cs=rope_cs)
    emit_attention(c, qd_.ap(), kd_.ap(), vd_.ap(), zo.ap(), "b", scale)
    c.close()
    return nc


def build_Cpart():
    nc = bass.Bass("TRN2", target_bir_lowering=False)
    x_in = nc.dram_tensor("x_in", [TO, E], F32, kind="ExternalInput")
    z_in = nc.dram_tensor("z_in", [E, TO], BF16, kind="ExternalInput")
    w_ot = nc.dram_tensor("w_ot", [E, E], F32, kind="ExternalInput")
    x_out = nc.dram_tensor("x_out", [TO, E], F32, kind="ExternalOutput")
    c = Ctx(nc)
    x_sb = c.sb("xres", [128, NTT * E], F32)
    ld = nc.sync.dma_start(out=x_sb.ap().rearrange("p (n f) -> p n f", f=E),
                           in_=x_in.ap().rearrange("(n p) f -> p n f", p=128))
    c.inc(ld, "dh", 16)
    c.wait(nc.sync, "dh")
    c.barrier()
    emit_outproj_residual(c, z_in.ap(), w_ot.ap(), x_sb, "c")
    emit_store_x(c, x_sb, x_out.ap())
    c.close()
    return nc



def emit_bfly_matrices(c, bfw, mats, plans, tag):
    nc = c.nc
    c.push()
    Y0 = c.sb(f"y0_{tag}", [128, 4096], F32)
    Y1 = c.sb(f"y1_{tag}", [128, 4096], F32)
    tmp1 = c.sb(f"bt1_{tag}", [128, 2048], F32)
    tmp2 = c.sb(f"bt2_{tag}", [128, 2048], F32)
    wrow = c.sb(f"bw_{tag}", [1, 2 * 2048], BF16)
    wt = c.sb(f"bwt_{tag}", [128, 2 * 2048], F32)
    sone = c.sb(f"bon_{tag}", [1, 128], BF16)
    pwb = c.ps(f"bpw_{tag}", [128, 2048], F32)
    c.inc(nc.gpsimd.memset(sone[:], 1.0), "g")
    c.sem("v"); c.sem("s_pe"); c.sem("dg"); c.sem("g")

    def bcast_pair(base_k, pair):
        c.wait(nc.gpsimd, "v")
        ld = nc.gpsimd.dma_start(out=wrow.ap().rearrange("o (w n) -> o w n", w=2),
                                 in_=bfw[base_k, 2 * pair:2 * pair + 2, :])
        nld = c.inc(ld, "dg", 16)
        c.wait(nc.tensor, "dg", nld)
        c.wait(nc.tensor, "v")
        for w2i in range(2):
            if w2i == 1:
                c.wait(nc.tensor, "v")
            for col in range(4):
                mmb = nc.tensor.matmul(
                    pwb[:, col * 512:(col + 1) * 512], sone[:],
                    wrow[0:1, w2i * 2048 + col * 512:w2i * 2048 + (col + 1) * 512],
                    start=True, stop=True)
            c.inc(mmb, "s_pe")
            c.wait(nc.vector, "s_pe")
            cpw = nc.vector.tensor_copy(out=wt[:, w2i * 2048:(w2i + 1) * 2048],
                                        in_=pwb[:])
            c.inc(cpw, "v")

    base = 0
    order = ["M1", "D1", "N1", "N2"]
    for name in order:
        st = plans[name]
        c.wait(nc.gpsimd, "v")
        c.inc(nc.gpsimd.memset(Y0[:], 1.0), "g")
        c.wait(nc.gpsimd, "g")
        ai = nc.gpsimd.affine_select(out=Y0[:], in_=Y0[:], compare_op=ALU.is_equal,
                                     fill=0.0, base=0, pattern=[[0, 32], [-1, 128]],
                                     channel_multiplier=1)
        c.inc(ai, "g")
        c.wait(nc.vector, "g")
        cur, nxt = Y0, Y1
        for k, s2 in enumerate(st):
            nb = 128 // (2 * s2)

            def vw(t_, half):
                return t_.rearrange("p (blk kb r) -> p blk kb r", blk=32, kb=nb)[
                    :, :, :, half * s2:(half + 1) * s2]

            def cwv(i2):
                return wt.ap()[:, i2 * 2048:(i2 + 1) * 2048].rearrange(
                    "p (blk kb s) -> p blk kb s", blk=32, kb=nb)

            tmp1v = tmp1.ap().rearrange("p (blk kb s) -> p blk kb s", blk=32, kb=nb)
            tmp2v = tmp2.ap().rearrange("p (blk kb s) -> p blk kb s", blk=32, kb=nb)
            a_ap = vw(cur.ap(), 0)
            b_ap = vw(cur.ap(), 1)
            for pair in range(2):
                bcast_pair(base + k, pair)
                c.wait(nc.vector, "v")
                m1_ = nc.vector.tensor_mul(out=tmp1v, in0=a_ap, in1=cwv(0))
                c.inc(m1_, "v")
                m2_ = nc.vector.tensor_mul(out=tmp2v, in0=b_ap, in1=cwv(1))
                c.inc(m2_, "v")
                c.wait(nc.vector, "v")
                ad = nc.vector.tensor_add(out=vw(nxt.ap(), pair), in0=tmp1v,
                                          in1=tmp2v)
                c.inc(ad, "v")
            c.wait(nc.vector, "v")
            cur, nxt = nxt, cur
        base += len(st)
        cc = nc.vector.tensor_copy(out=mats[name][:], in_=cur[:])
        c.inc(cc, "v")
        c.wait(nc.vector, "v")
    c.barrier()
    c.pop()


def emit_fold_w1(c, wexpT, bexp, M1_sb, w1e_dram, b1e_dram, tag):
    nc = c.nc
    c.push()
    swx = c.sb(f"fw_{tag}", [128, 32 * 128], BF16)
    sbe = c.sb(f"fb_{tag}", [128, 32], BF16)
    acc = c.sb(f"fa_{tag}", [128, 4096], BF16)
    brow = c.sb(f"fbr_{tag}", [1, 4096], F32)
    pf = c.ps(f"fp_{tag}", [128, 2 * 512], F32)
    c.sem("v"); c.sem("s_pe"); c.sem("dg"); c.sem("dh")

    ld = nc.gpsimd.dma_start(out=sbe.ap(), in_=bexp)
    c.inc(ld, "dgb", 16)
    c.wait(nc.tensor, "dgb")
    for co in range(32):
        c.wait(nc.tensor, "v", c.counts["v"])
        mm = nc.tensor.matmul(pf[0:1, 0:128], sbe[:, co:co + 1],
                              M1_sb[:, co * 128:(co + 1) * 128],
                              start=True, stop=True)
        c.inc(mm, "s_pe")
        c.wait(nc.vector, "s_pe")
        cp = nc.vector.tensor_copy(out=brow[0:1, co * 128:(co + 1) * 128],
                                   in_=pf[0:1, 0:128])
        c.inc(cp, "v")
    c.wait(nc.sync, "v")
    stb = nc.sync.dma_start(out=b1e_dram, in_=brow.ap())
    c.inc(stb, "dh", 16)
    c.wait(nc.tensor, "v")

    mm_after_ci = 0
    pf_free = [c.counts["v"], c.counts["v"]]
    for ci in range(NCH):
        if ci >= 1:
            c.wait(nc.gpsimd, "s_pe", mm_after_ci)
        ldw = nc.gpsimd.dma_start(
            out=swx.ap().rearrange("p (co q) -> p co q", co=32),
            in_=wexpT[:, ci * 128:(ci + 1) * 128].rearrange("(co p) q -> p co q", p=128))
        nld = c.inc(ldw, "dg", 16)
        c.wait(nc.tensor, "dg", nld)
        c.wait(nc.vector, "dh")
        for co in range(32):
            sl = co % 2
            c.wait(nc.tensor, "v", pf_free[sl])
            mm = nc.tensor.matmul(
                pf[:, sl * 512:sl * 512 + 128],
                swx[:, co * 128:(co + 1) * 128],
                M1_sb[:, co * 128:(co + 1) * 128],
                start=True, stop=True)
            c.inc(mm, "s_pe")
            c.wait(nc.vector, "s_pe")
            cp = nc.vector.tensor_copy(
                out=acc.ap().rearrange("p (c2 hi lo) -> p hi c2 lo",
                                       c2=32, hi=4)[:, :, :, co:co + 1],
                in_=pf[:, sl * 512:sl * 512 + 128].rearrange(
                    "p (hi c2 one) -> p hi c2 one", hi=4, c2=32))
            pf_free[sl] = c.inc(cp, "v")
        mm_after_ci = c.counts["s_pe"]
        c.wait(nc.sync, "v")
        stw = nc.sync.dma_start(
            out=w1e_dram[:, ci * 128:(ci + 1) * 128, :].rearrange("c2 p q -> p c2 q"),
            in_=acc.ap().rearrange("p (c2 q) -> p c2 q", c2=32))
        c.inc(stw, "dh", 16)
    c.wait(nc.sync, "dh")
    c.barrier()
    c.pop()


def emit_fold_w2(c, wconP, N2_sb, w2e_dram, tag):
    nc = c.nc
    c.push()
    swc = c.sb(f"gw_{tag}", [128, 1024], BF16)
    sev = c.sb(f"ge_{tag}", [128, 1024], BF16)
    pg = c.ps(f"gp_{tag}", [128, 2 * 512], F32)
    c.sem("v"); c.sem("s_pe"); c.sem("dg"); c.sem("dh")
    mm_after = 0
    for c2 in range(32):
        if c2 >= 1:
            c.wait(nc.gpsimd, "s_pe", mm_after)
        ldw = nc.gpsimd.dma_start(out=swc.ap(),
                                  in_=wconP[c2 * 128:(c2 + 1) * 128, :])
        nld = c.inc(ldw, "dg", 16)
        c.wait(nc.tensor, "dg", nld)
        c.wait(nc.tensor, "dh")
        for dh_ in range(2):
            mm = nc.tensor.matmul(pg[:, dh_ * 512:(dh_ + 1) * 512],
                                  N2_sb[:, c2 * 128:(c2 + 1) * 128],
                                  swc[:, dh_ * 512:(dh_ + 1) * 512],
                                  start=True, stop=True)
            c.inc(mm, "s_pe")
        mm_after = c.counts["s_pe"]
        c.wait(nc.vector, "s_pe")
        cp = nc.vector.tensor_copy(out=sev[:], in_=pg[:])
        c.inc(cp, "v")
        c.wait(nc.sync, "v")
        for hi in range(4):
            r0 = c2 + 32 * hi
            dst = w2e_dram.rearrange("(co r) d -> r co d", r=128)[r0:r0 + 1, :, :].squeeze()
            st = nc.sync.dma_start(out=dst, in_=sev[hi * 32:(hi + 1) * 32, :])
            c.inc(st, "dh", 16)
    c.wait(nc.sync, "dh")
    c.barrier()
    c.pop()


def emit_ffn_l0(c, xln_fm, w1e_dram, b1e_dram, up_bP, w2e_dram, brow2,
                D1_sb, N1_sb, relay_dram, x_sb, tag):
    nc = c.nc
    c.push()
    sxln = c.sb(f"lx_{tag}", [128, NCH * TO], BF16)
    sw1 = c.sb(f"lw1_{tag}", [128, 2 * NCH * 128], BF16)
    sh1 = c.sb(f"lh1_{tag}", [128, 2 * 512], BF16)
    sh2 = c.sb(f"lh2_{tag}", [128, 2 * 512], BF16)
    sh2o = c.sb(f"lho_{tag}", [128, 2 * 512], BF16)
    sh3 = c.sb(f"lh3_{tag}", [128, 32 * 512], BF16)
    sw2 = c.sb(f"lw2_{tag}", [128, 2 * 512], BF16)
    sb1e = c.sb(f"lb1_{tag}", [128, 32], F32)
    subp = c.sb(f"lbu_{tag}", [128, 32], F32)
    sbr2 = c.sb(f"lbr_{tag}", [2, 1024], BF16)
    sone2 = c.sb(f"lo2_{tag}", [2, 128], BF16)
    psA = c.ps(f"lpa_{tag}", [128, 2 * 512], F32)
    psB = c.ps(f"lpb_{tag}", [128, 2 * 512], F32)
    psY = c.ps(f"lpy_{tag}", [128, 4 * 512], F32)
    c.sem("v"); c.sem("s_pe"); c.sem("s"); c.sem("dg"); c.sem("dh"); c.sem("dgw")

    ld = nc.sync.dma_start(out=sxln.ap().rearrange("p (ch t) -> p ch t", ch=NCH),
                           in_=xln_fm.rearrange("(ch p) t -> p ch t", p=128))
    c.inc(ld, "dh", 16)
    for hi in range(4):
        ldb = nc.gpsimd.dma_start(
            out=sb1e[hi * 32:(hi + 1) * 32, :],
            in_=b1e_dram.rearrange("o (lo r) -> lo (o r)", lo=32)[
                :, 32 * hi:32 * hi + 32])
        c.inc(ldb, "dg", 16)
    ld = nc.gpsimd.dma_start(out=subp.ap(), in_=up_bP)
    c.inc(ld, "dg", 16)
    ld = nc.gpsimd.dma_start(out=sbr2.ap(), in_=brow2)
    c.inc(ld, "dg", 16)
    c.inc(nc.gpsimd.memset(sone2[:], 1.0), "g")
    c.wait(nc.sync, "dh")
    c.wait(nc.gpsimd, "dg")
    c.wait(nc.gpsimd, "g")
    c.barrier()

    w1used = [0, 0]
    w1ld = [0, 0]
    w2ld = [0, 0]
    hold = [0, 0]
    s_after_ev1 = [0, 0]
    s_after_ev2 = [0, 0]
    mmd_cnt = [0, 0]
    st_cnt = [0, 0]
    housed = [0, 0]
    ev3_cnt = [0, 0]
    w2used = [0, 0]
    for tt in range(4):
        t0 = tt * 512
        def ld_w1e(c2_):
            rot_ = c2_ % 2
            wbuf_ = sw1[:, rot_ * NCH * 128:(rot_ + 1) * NCH * 128]
            c.wait(nc.gpsimd, "s_pe", w1used[rot_])
            ldw = nc.gpsimd.dma_start(
                out=wbuf_.rearrange("p (ci q) -> p ci q", ci=NCH),
                in_=w1e_dram[c2_, :, :].rearrange("(ci p) q -> p ci q", p=128))
            w1ld[rot_] = c.inc(ldw, "dgw%d" % rot_, 16)
        ld_w1e(0)
        for c2 in range(32):
            rot = c2 % 2
            wbuf = sw1[:, rot * NCH * 128:(rot + 1) * NCH * 128]
            if c2 + 1 < 32:
                ld_w1e(c2 + 1)
            c.wait(nc.tensor, "dgw%d" % rot, w1ld[rot])
            c.wait(nc.tensor, "s", s_after_ev1[rot])
            for ci in range(NCH):
                mm = nc.tensor.matmul(psA[:, rot * 512:rot * 512 + 512],
                                      wbuf[:, ci * 128:(ci + 1) * 128],
                                      sxln[:, ci * TO + t0:ci * TO + t0 + 512],
                                      start=(ci == 0), stop=(ci == NCH - 1))
            w1used[rot] = c.inc(mm, "s_pe")
            c.wait(nc.scalar, "s_pe")
            c.wait(nc.scalar, "s_pe", mmd_cnt[rot])
            ev1 = nc.scalar.activation(sh1[:, rot * 512:rot * 512 + 512],
                                       psA[:, rot * 512:rot * 512 + 512],
                                       AF.Identity, bias=sb1e[:, c2:c2 + 1])
            s_after_ev1[rot] = c.inc(ev1, "s")
            c.wait(nc.tensor, "s")
            c.wait(nc.tensor, "s", s_after_ev2[rot])
            mmd = nc.tensor.matmul(psB[:, rot * 512:rot * 512 + 512],
                                   D1_sb[:, c2 * 128:(c2 + 1) * 128],
                                   sh1[:, rot * 512:rot * 512 + 512],
                                   start=True, stop=True)
            mmd_cnt[rot] = c.inc(mmd, "s_pe")
            c.wait(nc.scalar, "s_pe")
            c.wait(nc.scalar, "str%d" % rot, st_cnt[rot])
            ev2 = nc.scalar.activation(sh2[:, rot * 512:rot * 512 + 512],
                                       psB[:, rot * 512:rot * 512 + 512],
                                       AF.Gelu, bias=subp[:, c2:c2 + 1])
            s_after_ev2[rot] = c.inc(ev2, "s")
            c.wait(nc.sync, "s")
            str_ = nc.sync.dma_start(out=relay_dram[c2, :, :],
                                     in_=sh2[:, rot * 512:rot * 512 + 512])
            st_cnt[rot] = c.inc(str_, "str%d" % rot, 16)
        c.wait(nc.sync, "str0")
        c.wait(nc.sync, "str1")
        c.barrier()
        def ld_h2o(co_):
            rot_ = co_ % 2
            c.wait(nc.sync, "s_pe", housed[rot_])
            for hi in range(4):
                ldo = nc.sync.dma_start(
                    out=sh2o[hi * 32:(hi + 1) * 32, rot_ * 512:rot_ * 512 + 512],
                    in_=relay_dram[:, co_ + 32 * hi, :])
                hold[rot_] = c.inc(ldo, "dho%d" % rot_, 16)
        ld_h2o(0)
        for co in range(32):
            rot = co % 2
            if co + 1 < 32:
                ld_h2o(co + 1)
            c.wait(nc.tensor, "dho%d" % rot, hold[rot])
            c.wait(nc.tensor, "v", ev3_cnt[rot])
            mmn = nc.tensor.matmul(psA[:, rot * 512:rot * 512 + 512],
                                   N1_sb[:, co * 128:(co + 1) * 128],
                                   sh2o[:, rot * 512:rot * 512 + 512],
                                   start=True, stop=True)
            housed[rot] = c.inc(mmn, "s_pe")
            c.wait(nc.vector, "s_pe")
            ev3 = nc.vector.tensor_copy(out=sh3[:, co * 512:(co + 1) * 512],
                                        in_=psA[:, rot * 512:rot * 512 + 512])
            ev3_cnt[rot] = c.inc(ev3, "v")
        c.barrier()
        for fh in range(2):
            yfree = c.counts["v"]

            def ld_w2e(co_, fh_):
                rot_ = co_ % 2
                c.wait(nc.gpsimd, "s_pe", w2used[rot_])
                ldw = nc.gpsimd.dma_start(
                    out=sw2[:, rot_ * 512:(rot_ + 1) * 512],
                    in_=w2e_dram[co_ * 128:(co_ + 1) * 128,
                                 fh_ * 512:fh_ * 512 + 512])
                w2ld[rot_] = c.inc(ldw, "dgw%d" % rot_, 16)
            if fh == 0:
                ld_w2e(0, 0)
            for co in range(32):
                rot = co % 2
                wb = sw2[:, rot * 512:(rot + 1) * 512]
                if co + 1 < 32:
                    ld_w2e(co + 1, fh)
                elif fh == 0:
                    ld_w2e(0, 1)
                c.wait(nc.tensor, "dgw%d" % rot, w2ld[rot])
                if co == 0:
                    c.wait(nc.tensor, "v", yfree)
                for ts in range(4):
                    mm = nc.tensor.matmul(
                        psY[:, ts * 512:(ts + 1) * 512],
                        sh3[:, co * 512 + ts * 128:co * 512 + (ts + 1) * 128],
                        wb, start=(co == 0), stop=False)
                w2used[rot] = c.inc(mm, "s_pe")
            for ts in range(4):
                mm = nc.tensor.matmul(psY[:, ts * 512:(ts + 1) * 512],
                                      sone2[:], sbr2[:, fh * 512:fh * 512 + 512],
                                      start=False, stop=True)
            c.inc(mm, "s_pe")
            c.wait(nc.vector, "s_pe")
            for ts in range(4):
                gtt = (tt * 4 + ts)
                ad = nc.vector.tensor_add(
                    out=x_sb[:, gtt * E + fh * 512:gtt * E + fh * 512 + 512],
                    in0=x_sb[:, gtt * E + fh * 512:gtt * E + fh * 512 + 512],
                    in1=psY[:, ts * 512:(ts + 1) * 512])
                c.inc(ad, "v")
        c.barrier()
    c.barrier()
    c.pop()


def emit_ffn_l1(c, xln_fm, w1t, b1cm, w2t, b2row, wq1_dram, wq2_dram, x_sb, tag):
    nc = c.nc
    c.push()
    swf = c.sb(f"qw_{tag}", [128, 2 * 4096], F32)
    sabs = c.sb(f"qa_{tag}", [128, NCH + 32], F32)
    sth = c.sb(f"qt_{tag}", [128, 4], F32)
    srow = c.sb(f"qr_{tag}", [1, 8], F32)
    sqt = c.sb(f"qq_{tag}", [128, 4096], BF16)
    stmp = c.sb(f"qtm_{tag}", [128, 4096], F32)
    sone = c.sb(f"qo_{tag}", [1, 128], F32)
    sonec = c.sb(f"qoc_{tag}", [128, 1], F32)
    pq_ = c.ps(f"qp_{tag}", [1, 512], F32)
    pbc = c.ps(f"qb_{tag}", [128, 512], F32)
    c.sem("v"); c.sem("s_pe"); c.sem("s"); c.sem("dg"); c.sem("dh"); c.sem("dgw")
    c.inc(nc.gpsimd.memset(sone[:], 1.0), "g")
    c.inc(nc.gpsimd.memset(sonec[:], 1.0), "g")
    c.wait(nc.gpsimd, "g")
    c.barrier()

    absused = [0, 0]
    gld = [0, 0]
    seqA = [(w1t, ch, 0 + ch, DH) for ch in range(NCH)] + \
           [(w2t, ch, NCH + ch, E) for ch in range(32)]

    def ld_gamma(idx):
        w_ap_, ch_, _, wc_ = seqA[idx]
        rot_ = idx % 2
        c.wait(nc.gpsimd, "v", absused[rot_])
        ldw = nc.gpsimd.dma_start(
            out=swf[:, rot_ * 4096:rot_ * 4096 + wc_],
            in_=w_ap_[ch_ * 128:(ch_ + 1) * 128, :])
        gld[rot_] = c.inc(ldw, "dg%d" % rot_, 16)
    ld_gamma(0)
    for idx in range(len(seqA)):
        rot = idx % 2
        _, _, col_ch, w_cols = seqA[idx]
        if idx + 1 < len(seqA):
            ld_gamma(idx + 1)
        c.wait(nc.vector, "dg%d" % rot, gld[rot])
        rs = nc.vector.reduce_sum(out=sabs[:, col_ch:col_ch + 1],
                                  in_=swf[:, rot * 4096:rot * 4096 + w_cols],
                                  axis=mybir.AxisListType.X,
                                  apply_absolute_value=True)
        absused[rot] = c.inc(rs, "v")
    c.wait(nc.tensor, "v")
    mm1 = nc.tensor.matmul(pq_[0:1, 0:NCH], sonec[:], sabs[:, 0:NCH],
                           start=True, stop=True)
    c.inc(mm1, "s_pe")
    mm2 = nc.tensor.matmul(pq_[0:1, 32:64], sonec[:], sabs[:, NCH:NCH + 32],
                           start=True, stop=True)
    c.inc(mm2, "s_pe")
    c.wait(nc.vector, "s_pe")
    r1 = nc.vector.reduce_sum(out=srow[0:1, 0:1], in_=pq_[0:1, 0:NCH],
                              axis=mybir.AxisListType.X)
    c.inc(r1, "v")
    c.wait(nc.vector, "v")
    r2 = nc.vector.reduce_sum(out=srow[0:1, 1:2], in_=pq_[0:1, 32:64],
                              axis=mybir.AxisListType.X)
    c.inc(r2, "v")
    c.wait(nc.vector, "v")
    ts1 = nc.vector.tensor_scalar(out=srow[0:1, 2:4], in0=srow[0:1, 0:2],
                                  scalar1=1.0 / (DH * E), scalar2=None,
                                  op0=ALU.mult, op1=ALU.bypass)
    c.inc(ts1, "v")
    c.wait(nc.vector, "v")
    ts2 = nc.vector.tensor_scalar(out=srow[0:1, 4:6], in0=srow[0:1, 2:4],
                                  scalar1=EPS, scalar2=0.5,
                                  op0=ALU.add, op1=ALU.mult)
    c.inc(ts2, "v")
    c.wait(nc.vector, "v")
    ts3 = nc.vector.tensor_scalar(out=srow[0:1, 6:8], in0=srow[0:1, 4:6],
                                  scalar1=-1.0, scalar2=None,
                                  op0=ALU.mult, op1=ALU.bypass)
    c.inc(ts3, "v")
    c.wait(nc.tensor, "v")
    mmb = nc.tensor.matmul(pbc[:, 0:4], sone[:], srow[0:1, 4:8],
                           start=True, stop=True)
    c.inc(mmb, "s_pe")
    c.wait(nc.vector, "s_pe")
    cpb = nc.vector.tensor_copy(out=sth[:], in_=pbc[:, 0:4])
    c.inc(cpb, "v")
    c.wait(nc.vector, "v")
    c.barrier()

    qused = [0, 0]
    seqB = [(w1t, ch, 0, DH, 0) for ch in range(NCH)] + \
           [(w2t, ch, 1, E, 1) for ch in range(32)]

    def ld_q(idx):
        w_ap_, ch_, _, wc_, _ = seqB[idx]
        rot_ = idx % 2
        c.wait(nc.gpsimd, "v", qused[rot_])
        ldw = nc.gpsimd.dma_start(
            out=swf[:, rot_ * 4096:rot_ * 4096 + wc_],
            in_=w_ap_[ch_ * 128:(ch_ + 1) * 128, :])
        gld[rot_] = c.inc(ldw, "dg%d" % rot_, 16)
    ld_q(0)
    for idx in range(len(seqB)):
        rot = idx % 2
        _, ch, thc, w_cols, mi = seqB[idx]
        buf = swf[:, rot * 4096:(rot + 1) * 4096]
        if idx + 1 < len(seqB):
            ld_q(idx + 1)
        c.wait(nc.vector, "dg%d" % rot, gld[rot])
        c.wait(nc.vector, "dh")
        tn = nc.vector.tensor_single_scalar(out=stmp[:, 0:w_cols],
                                            in_=buf[:, 0:w_cols],
                                            scalar=sth[:, 2 + thc:3 + thc],
                                            op=ALU.is_lt)
        c.inc(tn, "v")
        c.wait(nc.vector, "v")
        qq = nc.vector.scalar_tensor_tensor(out=sqt[:, 0:w_cols],
                                            in0=buf[:, 0:w_cols],
                                            scalar=sth[:, thc:thc + 1],
                                            in1=stmp[:, 0:w_cols],
                                            op0=ALU.is_gt, op1=ALU.subtract)
        qused[rot] = c.inc(qq, "v")
        c.wait(nc.sync, "v")
        if mi == 0:
            st_ = nc.sync.dma_start(
                out=wq1_dram[:, ch * 128:(ch + 1) * 128, :].rearrange(
                    "fo p q -> p fo q"),
                in_=sqt.ap().rearrange("p (fo q) -> p fo q", fo=32))
        else:
            st_ = nc.sync.dma_start(out=wq2_dram[ch * 128:(ch + 1) * 128, :],
                                    in_=sqt[:, 0:w_cols])
        c.inc(st_, "dh", 16)
    c.wait(nc.sync, "dh")
    c.barrier()
    c.pop()

    c.push()
    sxln = c.sb(f"qx_{tag}", [128, NCH * TO], BF16)
    sw1 = c.sb(f"qw1_{tag}", [128, 2 * NCH * 128], BF16)
    sh = c.sb(f"qh_{tag}", [128, 32 * 512], BF16)
    sw2 = c.sb(f"qw2_{tag}", [128, 2 * 512], BF16)
    sb1 = c.sb(f"qb1_{tag}", [128, 32], F32)
    sbr2 = c.sb(f"qbr_{tag}", [2, 1024], BF16)
    sone2 = c.sb(f"qo2_{tag}", [2, 128], BF16)
    psA = c.ps(f"qpa_{tag}", [128, 2 * 512], F32)
    psY = c.ps(f"qpy_{tag}", [128, 4 * 512], F32)
    ld = nc.sync.dma_start(out=sxln.ap().rearrange("p (ch t) -> p ch t", ch=NCH),
                           in_=xln_fm.rearrange("(ch p) t -> p ch t", p=128))
    c.inc(ld, "dh", 16)
    ldb = nc.gpsimd.dma_start(out=sb1.ap(), in_=b1cm)
    c.inc(ldb, "dg", 16)
    c.inc(nc.gpsimd.memset(sbr2[:], 0.0), "g")
    c.wait(nc.gpsimd, "g")
    ldb = nc.gpsimd.dma_start(out=sbr2[0:1, :], in_=b2row)
    c.inc(ldb, "dg", 16)
    c.inc(nc.gpsimd.memset(sone2[:], 1.0), "g")
    c.wait(nc.gpsimd, "g")
    c.wait(nc.gpsimd, "dg")
    c.wait(nc.sync, "dh")
    c.barrier()
    z1used = [0, 0]
    z2used = [0, 0]
    for tt in range(4):
        t0 = tt * 512
        for fo in range(32):
            rot = fo % 2
            wbuf = sw1[:, rot * NCH * 128:(rot + 1) * NCH * 128]
            c.wait(nc.gpsimd, "s_pe", z1used[rot])
            ldw = nc.gpsimd.dma_start(
                out=wbuf.rearrange("p (ci q) -> p ci q", ci=NCH),
                in_=wq1_dram[fo, :, :].rearrange("(ci p) q -> p ci q", p=128))
            nldw = c.inc(ldw, "dgw%d" % rot, 16)
            c.wait(nc.tensor, "dgw%d" % rot, nldw)
            if c.counts["s"] >= 2:
                c.wait(nc.tensor, "s", c.counts["s"] - 1)
            for ci in range(NCH):
                mm = nc.tensor.matmul(psA[:, rot * 512:rot * 512 + 512],
                                      wbuf[:, ci * 128:(ci + 1) * 128],
                                      sxln[:, ci * TO + t0:ci * TO + t0 + 512],
                                      start=(ci == 0), stop=(ci == NCH - 1))
            z1used[rot] = c.inc(mm, "s_pe")
            c.wait(nc.scalar, "s_pe")
            ev = nc.scalar.activation(sh[:, fo * 512:(fo + 1) * 512],
                                      psA[:, rot * 512:rot * 512 + 512],
                                      AF.Relu, bias=sb1[:, fo:fo + 1])
            c.inc(ev, "s")
        c.wait(nc.tensor, "s")
        c.barrier()
        for fh in range(2):
            yfree = c.counts["v"]
            for co in range(32):
                rot = co % 2
                wb = sw2[:, rot * 512:(rot + 1) * 512]
                c.wait(nc.gpsimd, "s_pe", z2used[rot])
                ldw = nc.gpsimd.dma_start(
                    out=wb, in_=wq2_dram[co * 128:(co + 1) * 128,
                                         fh * 512:fh * 512 + 512])
                nldw = c.inc(ldw, "dgw%d" % rot, 16)
                c.wait(nc.tensor, "dgw%d" % rot, nldw)
                if co == 0:
                    c.wait(nc.tensor, "v", yfree)
                for ts in range(4):
                    mm = nc.tensor.matmul(
                        psY[:, ts * 512:(ts + 1) * 512],
                        sh[:, co * 512 + ts * 128:co * 512 + (ts + 1) * 128],
                        wb, start=(co == 0), stop=False)
                z2used[rot] = c.inc(mm, "s_pe")
            for ts in range(4):
                mm = nc.tensor.matmul(psY[:, ts * 512:(ts + 1) * 512],
                                      sone2[:], sbr2[:, fh * 512:fh * 512 + 512],
                                      start=False, stop=True)
            c.inc(mm, "s_pe")
            c.wait(nc.vector, "s_pe")
            for ts in range(4):
                gtt = (tt * 4 + ts)
                ad = nc.vector.tensor_add(
                    out=x_sb[:, gtt * E + fh * 512:gtt * E + fh * 512 + 512],
                    in0=x_sb[:, gtt * E + fh * 512:gtt * E + fh * 512 + 512],
                    in1=psY[:, ts * 512:(ts + 1) * 512])
                c.inc(ad, "v")
        c.barrier()
    c.barrier()
    c.pop()



def p_of_f(f):
    return (f >> 7) | (((f >> 5) & 3) << 5)


def f_of_p(c2, p2):
    return c2 | ((p2 & 31) << 7) | (((p2 >> 5) & 3) << 5)


def _wvec4(stage_w, s, f_a):
    s2g = 2 ** s
    wr = stage_w.reshape(4096 // (2 * s2g), s2g, 2, 2)
    i = f_a >> (s + 1)
    j = f_a & (s2g - 1)
    w = wr[i, j]
    return np.stack([w[:, 0, 0], w[:, 0, 1], w[:, 1, 0], w[:, 1, 1]])


def _fa_orig(s2):
    co = np.arange(32)
    nb = 128 // (2 * s2)
    kb = np.arange(nb)
    ks = np.arange(s2)
    f = (co[:, None, None] * 128 + kb[None, :, None] * 2 * s2 + ks[None, None, :])
    return f.reshape(-1)


def _fa_P(s2p):
    c2 = np.arange(32)
    nb = 128 // (2 * s2p)
    kb = np.arange(nb)
    ks = np.arange(s2p)
    p2a = kb[:, None] * 2 * s2p + ks[None, :]
    f = f_of_p(c2[:, None, None], p2a[None])
    return f.reshape(-1)


def butterfly_builder_plan(up_w, dn_w):
    rows = []
    plans = {}
    st = []
    for s in range(7):
        s2 = 2 ** s
        rows.append(_wvec4(up_w[s], s, _fa_orig(s2)))
        st.append(s2)
    plans["M1"] = st
    st = []
    for s in range(7, 12):
        s2p = 2 ** (s - 7)
        rows.append(_wvec4(up_w[s], s, _fa_P(s2p)))
        st.append(s2p)
    plans["D1"] = st
    st = []
    for s in range(7):
        s2 = 2 ** s
        rows.append(_wvec4(dn_w[s], s, _fa_orig(s2)))
        st.append(s2)
    plans["N1"] = st
    st = []
    for s in range(7, 12):
        s2p = 2 ** (s - 7)
        rows.append(_wvec4(dn_w[s], s, _fa_P(s2p)))
        st.append(s2p)
    plans["N2"] = st
    bfw = np.stack(rows).astype(np.float32)
    return bfw, plans



def host_slices(inputs):
    per_core = []
    x = inputs["L0_attn_x"]
    W1 = inputs["L0_attn_W1"]
    Q1, K1, V1 = inputs["L1_attn_Q"], inputs["L1_attn_K"], inputs["L1_attn_V"]
    for cid in range(NCORE):
        b, qd = cid // 4, cid % 4
        hs = list(range(4 * qd, 4 * qd + 4))
        d = {}
        d["x0"] = np.ascontiguousarray(x[b, qd * TO:(qd + 1) * TO, :])
        d["wq0"] = np.ascontiguousarray(
            np.concatenate([W1[h * D:(h + 1) * D, :].T for h in hs], axis=1))
        d["wk0"] = np.ascontiguousarray(
            np.concatenate([W1[E + h * D:E + (h + 1) * D, :].T for h in hs], axis=1))
        d["wv0"] = np.ascontiguousarray(
            np.concatenate([W1[2 * E + h * D:2 * E + (h + 1) * D, :].T for h in hs], axis=1))
        d["wq1"] = np.ascontiguousarray(Q1[hs].transpose(2, 0, 1).reshape(E, 4 * D))
        d["wk1"] = np.ascontiguousarray(K1[hs].transpose(2, 0, 1).reshape(E, 4 * D))
        d["wv1"] = np.ascontiguousarray(V1[hs].transpose(2, 0, 1).reshape(E, 4 * D))
        per_core.append(d)
    shared = {}
    shared["w2t"] = np.ascontiguousarray(inputs["L0_attn_W2"].T)
    shared["ot"] = np.ascontiguousarray(inputs["L1_attn_O"].T)
    shared["wexpT"] = np.ascontiguousarray(inputs["L0_ffn_W_exp"].T)
    shared["bexp_cm"] = np.ascontiguousarray(
        inputs["L0_ffn_b_exp"].reshape(32, 128).T)
    bfw, _plans = butterfly_builder_plan(inputs["L0_ffn_up_weights"],
                                         inputs["L0_ffn_down_weights"])
    shared["bfw"] = bfw
    fidx = np.array([f_of_p(i // 128, i % 128) for i in range(4096)])
    ub = inputs["L0_ffn_up_bias"]
    up_bP = ub[fidx].reshape(32, 128).T
    shared["up_bP"] = np.ascontiguousarray(up_bP.astype(np.float32))
    shared["wconP"] = np.ascontiguousarray(inputs["L0_ffn_W_con"][fidx])
    shared["brow2"] = np.ascontiguousarray(
        np.stack([inputs["L0_ffn_b_con"], inputs["L0_ffn_down_bias"]]))
    shared["w1t"] = np.ascontiguousarray(inputs["L1_ffn_W1"].T)
    shared["b1cm"] = np.ascontiguousarray(inputs["L1_ffn_b1"].reshape(32, 128).T)
    shared["w2t_ffn"] = np.ascontiguousarray(inputs["L1_ffn_W2"].T)
    shared["b2row"] = np.ascontiguousarray(inputs["L1_ffn_b2"][None, :])
    p = np.arange(128)
    inv = (1.0 / 10000.0 ** (np.arange(0, D, 2, dtype=np.float32) / D))
    ang = np.arange(T, dtype=np.float32)[None, :] * inv[p % 32][:, None]
    cos = np.cos(ang).astype(np.float32)
    sin = np.sin(ang).astype(np.float32)
    sgn = np.where(((p ^ 32) % 64) < 32, -1.0, 1.0).astype(np.float32)[:, None]
    shared["rope_cos"] = cos
    shared["rope_sin"] = (sin * sgn)
    p32 = np.zeros((128, 128), dtype=np.float32)
    p32[p, p ^ 32] = 1.0
    shared["p32"] = p32
    return per_core, shared


def build_C0():
    nc = bass.Bass("TRN2", target_bir_lowering=False)
    x_in = nc.dram_tensor("x_in", [TO, E], F32, kind="ExternalInput")
    z_in = nc.dram_tensor("z_in", [E, TO], BF16, kind="ExternalInput")
    w_ot = nc.dram_tensor("w_ot", [E, E], F32, kind="ExternalInput")
    wexpT = nc.dram_tensor("wexpT", [DH, E], F32, kind="ExternalInput")
    bexp_cm = nc.dram_tensor("bexp_cm", [128, 32], F32, kind="ExternalInput")
    bfw = nc.dram_tensor("bfw", [24, 4, 2048], F32, kind="ExternalInput")
    up_bP = nc.dram_tensor("up_bP", [128, 32], F32, kind="ExternalInput")
    wconP = nc.dram_tensor("wconP", [DH, E], F32, kind="ExternalInput")
    brow2 = nc.dram_tensor("brow2", [2, E], F32, kind="ExternalInput")
    x_out = nc.dram_tensor("x_out", [TO, E], F32, kind="ExternalOutput")
    xln_out = nc.dram_tensor("xln_out", [E, TO], BF16, kind="ExternalOutput")
    xln2 = nc.dram_tensor("xln2_i", [E, TO], BF16)
    w1e = nc.dram_tensor("w1e_i", [32, E, 128], BF16)
    b1e = nc.dram_tensor("b1e_i", [1, DH], F32)
    w2e = nc.dram_tensor("w2e_i", [DH, E], BF16)
    relay = nc.dram_tensor("relay_i", [32, 128, 512], BF16)
    c = Ctx(nc)
    x_sb = c.sb("xres", [128, NTT * E], F32)
    mats = {nm: c.sb(f"mat_{nm}", [128, 32 * 128], BF16)
            for nm in ("M1", "D1", "N1", "N2")}
    _, plans = butterfly_builder_plan(np.zeros((12, 2048, 2, 2), np.float32),
                                     np.zeros((12, 2048, 2, 2), np.float32))
    ld = nc.sync.dma_start(out=x_sb.ap().rearrange("p (n f) -> p n f", f=E),
                           in_=x_in.ap().rearrange("(n p) f -> p n f", p=128))
    c.inc(ld, "dh", 16)
    c.wait(nc.sync, "dh")
    c.barrier()
    emit_outproj_residual(c, z_in.ap(), w_ot.ap(), x_sb, "c0")
    emit_ln_transpose(c, None, xln2.ap(), "c0l", x_sb=x_sb)
    emit_bfly_matrices(c, bfw.ap(), mats, plans, "c0")
    emit_fold_w1(c, wexpT.ap(), bexp_cm.ap(), mats["M1"], w1e.ap(), b1e.ap(), "c0")
    emit_fold_w2(c, wconP.ap(), mats["N2"], w2e.ap(), "c0")
    emit_ffn_l0(c, xln2.ap(), w1e.ap(), b1e.ap(), up_bP.ap(), w2e.ap(),
                brow2.ap(), mats["D1"], mats["N1"], relay.ap(), x_sb, "c0")
    emit_ln_transpose(c, None, xln_out.ap(), "c0m", x_sb=x_sb)
    emit_store_x(c, x_sb, x_out.ap())
    c.close()
    return nc


def build_C1():
    nc = bass.Bass("TRN2", target_bir_lowering=False)
    x_in = nc.dram_tensor("x_in", [TO, E], F32, kind="ExternalInput")
    z_in = nc.dram_tensor("z_in", [E, TO], BF16, kind="ExternalInput")
    w_ot = nc.dram_tensor("w_ot", [E, E], F32, kind="ExternalInput")
    w1t = nc.dram_tensor("w1t", [E, DH], F32, kind="ExternalInput")
    b1cm = nc.dram_tensor("b1cm", [128, 32], F32, kind="ExternalInput")
    w2t = nc.dram_tensor("w2t", [DH, E], F32, kind="ExternalInput")
    b2row = nc.dram_tensor("b2row", [1, E], F32, kind="ExternalInput")
    x_out = nc.dram_tensor("x_out", [TO, E], F32, kind="ExternalOutput")
    xln2 = nc.dram_tensor("xln2_i", [E, TO], BF16)
    wq1 = nc.dram_tensor("wq1_i", [32, E, 128], BF16)
    wq2 = nc.dram_tensor("wq2_i", [DH, E], BF16)
    c = Ctx(nc)
    x_sb = c.sb("xres", [128, NTT * E], F32)
    ld = nc.sync.dma_start(out=x_sb.ap().rearrange("p (n f) -> p n f", f=E),
                           in_=x_in.ap().rearrange("(n p) f -> p n f", p=128))
    c.inc(ld, "dh", 16)
    c.wait(nc.sync, "dh")
    c.barrier()
    emit_outproj_residual(c, z_in.ap(), w_ot.ap(), x_sb, "c1")
    emit_ln_transpose(c, None, xln2.ap(), "c1l", x_sb=x_sb)
    emit_ffn_l1(c, xln2.ap(), w1t.ap(), b1cm.ap(), w2t.ap(), b2row.ap(),
                wq1.ap(), wq2.ap(), x_sb, "c1")
    emit_store_x(c, x_sb, x_out.ap())
    c.close()
    return nc



def _gather_xln(outs, key):
    xgs = []
    for b in range(2):
        xg = np.concatenate([outs[4 * b + r][key] for r in range(4)], axis=0)
        xgs.append(np.ascontiguousarray(xg))
    return [xgs[cid // 4] for cid in range(NCORE)]


def _a2a_z(outs, key):
    res = []
    for cid in range(NCORE):
        b, qd = cid // 4, cid % 4
        z = np.concatenate(
            [outs[4 * b + r][key][qd * 256:(qd + 1) * 256, :] for r in range(4)],
            axis=0)
        res.append(np.ascontiguousarray(z))
    return res


def kernel_unfused(**inputs):
    inputs = {k: np.ascontiguousarray(v) for k, v in inputs.items()}
    per_core, shared = host_slices(inputs)
    cores = list(range(NCORE))

    ncA = build_A()
    rA = run_bass_kernel_spmd(ncA, [{"x_in": pc["x0"]} for pc in per_core],
                              core_ids=cores)
    xg = _gather_xln(rA.results, "xln")

    ncB0 = build_B(rope=True, scale=0.125)
    inB0 = [{"xg": bfc(xg[i]), "wq": per_core[i]["wq0"], "wk": per_core[i]["wk0"],
             "wv": per_core[i]["wv0"], "rcos": shared["rope_cos"],
             "rsin": shared["rope_sin"], "p32": shared["p32"]} for i in cores]
    rB0 = run_bass_kernel_spmd(ncB0, inB0, core_ids=cores)
    z0 = _a2a_z(rB0.results, "zo")

    ncC0 = build_C0()
    inC0 = [{"x_in": per_core[i]["x0"], "z_in": z0[i], "w_ot": shared["w2t"],
             "wexpT": shared["wexpT"], "bexp_cm": shared["bexp_cm"],
             "bfw": shared["bfw"], "up_bP": shared["up_bP"],
             "wconP": shared["wconP"], "brow2": shared["brow2"]} for i in cores]
    rC0 = run_bass_kernel_spmd(ncC0, inC0, core_ids=cores)
    xg1 = _gather_xln(rC0.results, "xln_out")

    ncB1 = build_B(rope=False, scale=1.0)
    inB1 = [{"xg": bfc(xg1[i]), "wq": per_core[i]["wq1"], "wk": per_core[i]["wk1"],
             "wv": per_core[i]["wv1"]} for i in cores]
    rB1 = run_bass_kernel_spmd(ncB1, inB1, core_ids=cores)
    z1 = _a2a_z(rB1.results, "zo")

    ncC1 = build_C1()
    inC1 = [{"x_in": rC0.results[i]["x_out"], "z_in": z1[i], "w_ot": shared["ot"],
             "w1t": shared["w1t"], "b1cm": shared["b1cm"],
             "w2t": shared["w2t_ffn"], "b2row": shared["b2row"]} for i in cores]
    rC1 = run_bass_kernel_spmd(ncC1, inC1, core_ids=cores)

    out = np.empty((B, T, E), dtype=np.float32)
    for cid in cores:
        b, qd = cid // 4, cid % 4
        out[b, qd * TO:(qd + 1) * TO, :] = rC1.results[cid]["x_out"]
    return out



def emit_collective(c, kind, in_ap, out_ap, tag):
    nc = c.nc
    cc = nc.gpsimd.collective_compute(
        kind, mybir.AluOpType.bypass,
        replica_groups=[[0, 1, 2, 3], [4, 5, 6, 7]],
        ins=[in_ap], outs=[out_ap])
    c.inc(cc, "cc")
    c.wait(nc.gpsimd, "cc")
    c.barrier()


def build_fused():
    nc = bass.Bass("TRN2", target_bir_lowering=False)
    t_in = {}
    def inp(name, shape, dtype=F32):
        t_in[name] = nc.dram_tensor(name, shape, dtype, kind="ExternalInput")
        return t_in[name].ap()
    x0 = inp("x0", [TO, E])
    wq0 = inp("wq0", [E, 256]); wk0 = inp("wk0", [E, 256]); wv0 = inp("wv0", [E, 256])
    rcos = inp("rcos", [128, T]); rsin = inp("rsin", [128, T]); p32 = inp("p32", [128, 128])
    w2t = inp("w2t", [E, E])
    wexpT = inp("wexpT", [DH, E]); bexp_cm = inp("bexp_cm", [128, 32])
    bfw = inp("bfw", [24, 4, 2048]); up_bP = inp("up_bP", [128, 32])
    wconP = inp("wconP", [DH, E]); brow2 = inp("brow2", [2, E])
    wq1 = inp("wq1", [E, 256]); wk1 = inp("wk1", [E, 256]); wv1 = inp("wv1", [E, 256])
    ot = inp("ot", [E, E])
    w1t = inp("w1t", [E, DH]); b1cm = inp("b1cm", [128, 32])
    w2t_ffn = inp("w2t_ffn", [DH, E]); b2row = inp("b2row", [1, E])
    x_out = nc.dram_tensor("x_out", [TO, E], F32, kind="ExternalOutput")

    def internal(name, shape, dtype):
        return nc.dram_tensor(name, shape, dtype).ap()
    xln_a = internal("xln_a", [E, TO], BF16)
    xg_a = internal("xg_a", [4 * E, TO], BF16)
    q0d = internal("q0d", [256, T], BF16)
    k0d = internal("k0d", [256, T], BF16)
    v0d = internal("v0d", [T, 256], BF16)
    zsh0 = internal("zsh0", [E, TO], BF16)
    zg0 = internal("zg0", [4 * E, TO], BF16)
    xln2 = internal("xln2", [E, TO], BF16)
    w1e = internal("w1e", [32, E, 128], BF16)
    b1e = internal("b1e", [1, DH], F32)
    w2e = internal("w2e", [DH, E], BF16)
    relay = internal("relay", [32, 128, 512], BF16)
    xln_b = internal("xln_b", [E, TO], BF16)
    xg_b = internal("xg_b", [4 * E, TO], BF16)
    q1d = internal("q1d", [256, T], BF16)
    k1d = internal("k1d", [256, T], BF16)
    v1d = internal("v1d", [T, 256], BF16)
    zsh1 = internal("zsh1", [E, TO], BF16)
    zg1 = internal("zg1", [4 * E, TO], BF16)
    xln3 = internal("xln3", [E, TO], BF16)
    wq1i = internal("wq1i", [32, E, 128], BF16)
    wq2i = internal("wq2i", [DH, E], BF16)

    c = Ctx(nc)
    c.sem("cc")
    qd_sv = nc.partition_id() % 4
    x_sb = c.sb("xres", [128, NTT * E], F32)
    ld = nc.sync.dma_start(out=x_sb.ap().rearrange("p (n f) -> p n f", f=E),
                           in_=x0.rearrange("(n p) f -> p n f", p=128))
    c.inc(ld, "dh", 16)
    c.wait(nc.sync, "dh")
    c.barrier()
    emit_ln_transpose(c, None, xln_a, "fa", x_sb=x_sb)
    emit_collective(c, "AllGather", xln_a, xg_a, "ga")
    emit_qkv(c, xg_a, wq0, wk0, wv0, q0d, k0d, v0d, "f0",
             rope_cs=(rcos, rsin, p32))
    emit_attention(c, q0d, k0d, v0d, zsh0, "f0", 0.125)
    emit_collective(c, "AllGather", zsh0, zg0, "za")
    emit_outproj_residual(c, None, w2t, x_sb, "f0", z_gathered=(zg0, qd_sv))
    emit_ln_transpose(c, None, xln2, "f0l", x_sb=x_sb)
    c.push()
    mats = {nm: c.sb(f"mat_{nm}", [128, 32 * 128], BF16)
            for nm in ("M1", "D1", "N1", "N2")}
    _, plans = butterfly_builder_plan(np.zeros((12, 2048, 2, 2), np.float32),
                                     np.zeros((12, 2048, 2, 2), np.float32))
    emit_bfly_matrices(c, bfw, mats, plans, "f0")
    emit_fold_w1(c, wexpT, bexp_cm, mats["M1"], w1e, b1e, "f0")
    emit_fold_w2(c, wconP, mats["N2"], w2e, "f0")
    emit_ffn_l0(c, xln2, w1e, b1e, up_bP, w2e, brow2,
                mats["D1"], mats["N1"], relay, x_sb, "f0")
    c.pop()
    emit_ln_transpose(c, None, xln_b, "fb", x_sb=x_sb)
    emit_collective(c, "AllGather", xln_b, xg_b, "gb")
    emit_qkv(c, xg_b, wq1, wk1, wv1, q1d, k1d, v1d, "f1", rope_cs=None)
    emit_attention(c, q1d, k1d, v1d, zsh1, "f1", 1.0)
    emit_collective(c, "AllGather", zsh1, zg1, "zb")
    emit_outproj_residual(c, None, ot, x_sb, "f1", z_gathered=(zg1, qd_sv))
    emit_ln_transpose(c, None, xln3, "f1l", x_sb=x_sb)
    emit_ffn_l1(c, xln3, w1t, b1cm, w2t_ffn, b2row, wq1i, wq2i, x_sb, "f1")
    emit_store_x(c, x_sb, x_out.ap())
    c.close()
    return nc


def kernel(**inputs):
    inputs = {k: np.ascontiguousarray(v) for k, v in inputs.items()}
    per_core, shared = host_slices(inputs)
    cores = list(range(NCORE))
    nc = build_fused()
    in_maps = []
    for i in cores:
        pc = per_core[i]
        in_maps.append({
            "x0": pc["x0"], "wq0": pc["wq0"], "wk0": pc["wk0"], "wv0": pc["wv0"],
            "rcos": shared["rope_cos"], "rsin": shared["rope_sin"],
            "p32": shared["p32"], "w2t": shared["w2t"],
            "wexpT": shared["wexpT"], "bexp_cm": shared["bexp_cm"],
            "bfw": shared["bfw"], "up_bP": shared["up_bP"],
            "wconP": shared["wconP"], "brow2": shared["brow2"],
            "wq1": pc["wq1"], "wk1": pc["wk1"], "wv1": pc["wv1"],
            "ot": shared["ot"], "w1t": shared["w1t"], "b1cm": shared["b1cm"],
            "w2t_ffn": shared["w2t_ffn"], "b2row": shared["b2row"],
        })
    res = run_bass_kernel_spmd(nc, in_maps, core_ids=cores)
    out = np.empty((B, T, E), dtype=np.float32)
    for cid in cores:
        b, qd = cid // 4, cid % 4
        out[b, qd * TO:(qd + 1) * TO, :] = res.results[cid]["x_out"]
    return out


# version 14
# speedup vs baseline: 1.1413x; 1.0578x over previous
import math
import numpy as np
import ml_dtypes
from contextlib import ExitStack

import concourse.bass as bass
import concourse.mybir as mybir
from concourse.bass_utils import run_bass_kernel_spmd

F32 = mybir.dt.float32
BF16 = mybir.dt.bfloat16
AF = mybir.ActivationFunctionType
ALU = mybir.AluOpType

B, T, E, H, D = 2, 8192, 1024, 16, 64
NCORE = 8
TO = T // 4
HO = H // 4
NTT = TO // 128
NCH = E // 128
DH = 4096
EPS = 1e-5


class Ctx:

    def __init__(self, nc):
        self.nc = nc
        self.es = ExitStack()
        self.sem_es = ExitStack()
        self._scopes = []
        self.sems = {}
        self.counts = {}

    def push(self):
        self._scopes.append(self.es)
        self.es = ExitStack()

    def pop(self):
        self.es.close()
        self.es = self._scopes.pop()

    def sb(self, name, shape, dtype):
        return self.es.enter_context(self.nc.sbuf_tensor(name, shape, dtype))

    def ps(self, name, shape, dtype=F32):
        return self.es.enter_context(self.nc.psum_tensor(name, shape, dtype))

    def sem(self, name):
        if name not in self.sems:
            self.sems[name] = self.sem_es.enter_context(self.nc.semaphore(name))
            self.counts[name] = 0
        return self.sems[name]

    def inc(self, inst, name, n=1):
        inst.then_inc(self.sem(name), n)
        self.counts[name] += n
        return self.counts[name]

    def wait(self, eng, name, val=None):
        if val is None:
            val = self.counts[name]
        eng.wait_ge(self.sem(name), val)

    def barrier(self):
        self.nc.all_engine_barrier()

    def close(self):
        self.es.close()
        self.sem_es.close()


def bfc(x):
    return np.ascontiguousarray(x).astype(ml_dtypes.bfloat16)



def emit_ln_transpose(c, x_src, xln_dst, tag, x_sb=None):
    nc = c.nc
    c.push()
    own_x = x_sb is None
    if own_x:
        x_sb = c.sb(f"x_{tag}", [128, NTT * E], F32)
    xln_tm = c.sb(f"xlntm_{tag}", [128, NTT * E], BF16)
    stats = c.sb(f"stats_{tag}", [128, NTT * 12], F32)
    mv = c.sb(f"mv_{tag}", [128, NTT * 2], F32)
    rstd = c.sb(f"rstd_{tag}", [128, NTT], F32)
    sid = c.sb(f"id_{tag}", [128, 128], BF16)
    xfm = c.sb(f"xfm_{tag}", [128, NCH * TO], BF16)
    ptr = c.ps(f"ptr_{tag}", [128, 2 * 1024], BF16)

    mi = nc.gpsimd.memset(sid[:], 1.0)
    c.inc(mi, "g")
    c.wait(nc.gpsimd, "g")
    ai = nc.gpsimd.affine_select(out=sid[:], in_=sid[:], compare_op=ALU.is_equal,
                                 fill=0.0, base=0, pattern=[[-1, 128]],
                                 channel_multiplier=1)
    c.inc(ai, "g")

    if own_x:
        ld = nc.sync.dma_start(
            out=x_sb.ap().rearrange("p (n f) -> p n f", f=E),
            in_=x_src.rearrange("(n p) f -> p n f", p=128))
        c.inc(ld, "dh", 16)
        c.wait(nc.sync, "dh")
        c.wait(nc.vector, "dh")

    for i in range(NTT):
        xt = x_sb[:, i * E:(i + 1) * E]
        st = stats[:, i * 12:(i + 1) * 12]
        s1 = nc.vector.bn_stats(out=st[:, 0:6], in_=xt[:, 0:512])
        c.inc(s1, "v")
        s2 = nc.vector.bn_stats(out=st[:, 6:12], in_=xt[:, 512:1024])
        c.inc(s2, "v")
        c.wait(nc.vector, "v")
        ag = nc.vector.bn_aggr(out=mv[:, 2 * i:2 * i + 2], in_=st[:])
        c.inc(ag, "v")
    seps = c.sb(f"eps_{tag}", [128, 1], F32)
    me = nc.gpsimd.memset(seps[:], EPS)
    c.inc(me, "g")
    c.wait(nc.scalar, "g")
    c.wait(nc.scalar, "v")
    lg = nc.scalar.activation(rstd.ap().rearrange("p (n one) -> p n one", one=1),
                              mv.ap().rearrange("p (n two) -> p n two", two=2)[:, :, 1:2],
                              AF.Ln, bias=seps[:])
    c.inc(lg, "s")
    c.wait(nc.scalar, "s")
    ex = nc.scalar.activation(rstd[:], rstd[:], AF.Exp, scale=-0.5)
    c.inc(ex, "s")
    c.wait(nc.vector, "s")
    for i in range(NTT):
        tsn = nc.vector.tensor_scalar(
            out=xln_tm[:, i * E:(i + 1) * E], in0=x_sb[:, i * E:(i + 1) * E],
            scalar1=mv[:, 2 * i:2 * i + 1], scalar2=rstd[:, i:i + 1],
            op0=ALU.subtract, op1=ALU.mult)
        c.inc(tsn, "v")
    c.wait(nc.tensor, "v")
    c.wait(nc.tensor, "g")
    c.sem("vtr")
    for i in range(NTT):
        par = i % 2
        bank = ptr[:, par * 1024:(par + 1) * 1024]
        if i >= 2:
            c.wait(nc.tensor, "vtr", c.counts["vtr"] - 1)
        for ch in range(NCH):
            tr = nc.tensor.transpose(
                bank[:, ch * 128:(ch + 1) * 128],
                xln_tm[:, i * E + ch * 128: i * E + (ch + 1) * 128], sid[:])
            if ch == NCH - 1:
                c.inc(tr, "tr")
        c.wait(nc.vector, "tr")
        cp = nc.vector.tensor_copy(
            out=xfm.ap().rearrange("p (ch t) -> p ch t", ch=NCH)[:, :, i * 128:(i + 1) * 128],
            in_=bank.rearrange("p (ch t) -> p ch t", ch=NCH))
        c.inc(cp, "vtr")
    c.wait(nc.sync, "vtr")
    stfm = nc.sync.dma_start(out=xln_dst.rearrange("(ch p) t -> p ch t", p=128),
                             in_=xfm.ap().rearrange("p (ch t) -> p ch t", ch=NCH))
    c.inc(stfm, "dh", 16)
    c.wait(nc.sync, "dh")
    c.barrier()
    c.pop()
    return x_sb


def emit_qkv(c, xln_full, wq, wk, wv, q_dst, k_dst, v_dst, tag, rope_cs=None):
    nc = c.nc
    c.push()
    sw = c.sb(f"w_{tag}", [128, NCH * 3 * 256], BF16)
    sx = c.sb(f"xg_{tag}", [128, 2 * NCH * 512], BF16)
    sqk = c.sb(f"qk_{tag}", [128, 2 * 512], BF16)
    sv = c.sb(f"v_{tag}", [128, 1024], BF16)
    pq = c.ps(f"pq_{tag}", [128, 2 * 512], F32)
    pk = c.ps(f"pk_{tag}", [128, 2 * 512], F32)
    pv = c.ps(f"pv_{tag}", [128, 2 * 512], F32)
    if rope_cs is not None:
        scos = c.sb(f"cos_{tag}", [128, T], BF16)
        ssin = c.sb(f"sin_{tag}", [128, T], BF16)
        sqr = c.sb(f"qr_{tag}", [128, 2048], BF16)
        sp32 = c.sb(f"p32_{tag}", [128, 128], BF16)
        prope = c.ps(f"pr_{tag}", [128, 1024], F32)
        c.inc(nc.gpsimd.dma_start(out=scos[:], in_=rope_cs[0]), "dg", 16)
        c.inc(nc.gpsimd.dma_start(out=ssin[:], in_=rope_cs[1]), "dg", 16)
        c.inc(nc.gpsimd.dma_start(out=sp32[:], in_=rope_cs[2]), "dg", 16)
        c.sem("rp")
    else:
        scos = ssin = sqr = sp32 = prope = None
    for wi, w in enumerate((wq, wk, wv)):
        ld = nc.gpsimd.dma_start(
            out=sw.ap().rearrange("p (ch w s) -> p ch w s", ch=NCH, w=3)[:, :, wi, :],
            in_=w.rearrange("(ch p) s -> p ch s", p=128))
        c.inc(ld, "dg", 16)
    c.wait(nc.tensor, "dg")

    c.sem("mmq"); c.sem("v"); c.sem("dh")
    qfree = kfree = vfree = 0
    for tt in range(16):
        r, tl = tt // 4, (tt % 4) * 512
        buf = tt % 2
        xb = sx[:, buf * NCH * 512:(buf + 1) * NCH * 512]
        if tt >= 2:
            c.wait(nc.sync, "mmq", tt - 1)
        lxs = "lx%d" % buf
        ld = nc.sync.dma_start(
            out=xb.rearrange("p (ch t) -> p ch t", ch=NCH),
            in_=xln_full[r * E:(r + 1) * E, tl:tl + 512].rearrange("(ch p) t -> p ch t", p=128))
        nld = c.inc(ld, lxs, 16)
        c.wait(nc.tensor, lxs, nld)
        if tt >= 1:
            c.wait(nc.tensor, "v", qfree)
        for ch in range(NCH):
            st, sp = (ch == 0), (ch == NCH - 1)
            xt = xb[:, ch * 512:(ch + 1) * 512]
            wch = sw[:, ch * 768:(ch + 1) * 768]
            for half in range(2):
                nc.tensor.matmul(pq[:, half * 512:half * 512 + 512],
                                 wch[:, half * 128:half * 128 + 128], xt,
                                 start=st, stop=sp)
        if tt >= 1:
            c.wait(nc.tensor, "v", kfree)
        for ch in range(NCH):
            st, sp = (ch == 0), (ch == NCH - 1)
            xt = xb[:, ch * 512:(ch + 1) * 512]
            wch = sw[:, ch * 768:(ch + 1) * 768]
            for half in range(2):
                nc.tensor.matmul(pk[:, half * 512:half * 512 + 512],
                                 wch[:, 256 + half * 128:256 + half * 128 + 128], xt,
                                 start=st, stop=sp)
        if tt >= 1:
            c.wait(nc.tensor, "v", vfree)
        for sub in range(4):
            for ch in range(NCH):
                st, sp = (ch == 0), (ch == NCH - 1)
                xt = xb[:, ch * 512:(ch + 1) * 512]
                wch = sw[:, ch * 768:(ch + 1) * 768]
                mm = nc.tensor.matmul(
                    pv[:, sub * 256:(sub + 1) * 256],
                    xt[:, sub * 128:(sub + 1) * 128], wch[:, 512:768],
                    start=st, stop=sp)
        c.inc(mm, "mmq")
        gt = tt * 512
        c.wait(nc.vector, "mmq")
        if tt >= 1:
            c.wait(nc.vector, "dh")
        for half in range(2):
            cp = nc.vector.tensor_copy(out=sqk[:, half * 512:half * 512 + 512],
                                       in_=pq[:, half * 512:half * 512 + 512])
            c.inc(cp, "v")
        qfree = c.counts["v"]
        if rope_cs is not None:
            _emit_rope(c, sqk, sqr, scos, ssin, sp32, prope, gt)
        c.wait(nc.sync, "v")
        st1 = nc.sync.dma_start(
            out=q_dst.rearrange("(h p) t -> p h t", p=128)[:, :, gt:gt + 512],
            in_=sqk.ap().rearrange("p (h t) -> p h t", h=2))
        c.inc(st1, "dh", 16)
        c.wait(nc.vector, "dh")
        for half in range(2):
            cp = nc.vector.tensor_copy(out=sqk[:, half * 512:half * 512 + 512],
                                       in_=pk[:, half * 512:half * 512 + 512])
            c.inc(cp, "v")
        kfree = c.counts["v"]
        if rope_cs is not None:
            _emit_rope(c, sqk, sqr, scos, ssin, sp32, prope, gt)
        c.wait(nc.sync, "v")
        st2 = nc.sync.dma_start(
            out=k_dst.rearrange("(h p) t -> p h t", p=128)[:, :, gt:gt + 512],
            in_=sqk.ap().rearrange("p (h t) -> p h t", h=2))
        c.inc(st2, "dh", 16)
        if tt >= 1:
            c.wait(nc.vector, "dh")
        cpv = nc.vector.tensor_copy(out=sv[:], in_=pv[:])
        vfree = c.inc(cpv, "v")
        c.wait(nc.sync, "v")
        stv = nc.sync.dma_start(
            out=v_dst.rearrange("(n p) d -> p n d", p=128)[:, 4 * tt:4 * tt + 4, :],
            in_=sv.ap().rearrange("p (s d) -> p s d", s=4))
        c.inc(stv, "dh", 16)
    c.wait(nc.sync, "dh")
    c.barrier()
    c.pop()


def _emit_rope(c, sqk, sqr, scos, ssin, p32_sb, prope, gt):
    nc = c.nc
    cosw = scos[:, gt:gt + 512]
    sinw = ssin[:, gt:gt + 512]
    c.wait(nc.vector, "v")
    for half in range(2):
        q = sqk[:, half * 512:half * 512 + 512]
        t1 = sqr[:, half * 512:half * 512 + 512]
        u = sqr[:, 1024 + half * 512:1024 + half * 512 + 512]
        c.inc(nc.vector.tensor_mul(out=t1, in0=q, in1=cosw), "v")
        c.inc(nc.vector.tensor_mul(out=u, in0=q, in1=sinw), "v")
    c.wait(nc.tensor, "v")
    c.wait(nc.tensor, "v", c.counts.get("_ropefree", 0))
    for half in range(2):
        u = sqr[:, 1024 + half * 512:1024 + half * 512 + 512]
        mm = nc.tensor.matmul(prope[:, half * 512:half * 512 + 512],
                              p32_sb[:], u, start=True, stop=True)
        c.inc(mm, "rp")
    c.wait(nc.vector, "rp")
    for half in range(2):
        q = sqk[:, half * 512:half * 512 + 512]
        t1 = sqr[:, half * 512:half * 512 + 512]
        ad = nc.vector.tensor_add(out=q, in0=t1,
                                  in1=prope[:, half * 512:half * 512 + 512])
        c.inc(ad, "v")
    c.counts["_ropefree"] = c.counts["v"]


def emit_attention(c, q_src, k_src, v_src, z_dst, tag, scale):
    nc = c.nc
    c.push()
    NKT = T // 128
    sk = [c.sb(f"ak{x}_{tag}", [128, T], BF16) for x in range(2)]
    sq = [c.sb(f"aq{x}_{tag}", [128, T], BF16) for x in range(2)]
    sva = [c.sb(f"av{x}a_{tag}", [128, NKT * 65], BF16) for x in range(2)]
    svb = [c.sb(f"av{x}b_{tag}", [128, NKT * 65], BF16) for x in range(2)]
    sattn = [c.sb(f"aat{x}_{tag}", [128, 3 * 1024], BF16) for x in range(2)]
    srec = [c.sb(f"arc{x}_{tag}", [1, 2 * 1024], BF16) for x in range(2)]
    srecf = [c.sb(f"arf{x}_{tag}", [1, 1024], F32) for x in range(2)]
    sbc = [c.sb(f"abc{x}_{tag}", [128, 1024], BF16) for x in range(2)]
    szr = [c.sb(f"azr{x}_{tag}", [128, 2 * 1024], BF16) for x in range(2)]
    sones1 = c.sb(f"aon_{tag}", [1, 64], BF16)
    smask = c.sb(f"amk_{tag}", [128, 4 * 512], BF16)
    ps = [c.ps(f"aps{x}_{tag}", [128, 2 * 512], F32) for x in range(2)]
    pz = [c.ps(f"apz{x}_{tag}", [128, 2 * 512], F32) for x in range(2)]

    c.inc(nc.gpsimd.memset(sones1[:], 1.0), "g")
    for m in range(4):
        c.inc(nc.gpsimd.memset(smask[:, m * 512:(m + 1) * 512], 1.0), "g")
    c.wait(nc.gpsimd, "g")
    for m in range(4):
        am = nc.gpsimd.affine_select(
            out=smask[:, m * 512:(m + 1) * 512], in_=smask[:, m * 512:(m + 1) * 512],
            compare_op=ALU.is_ge, fill=0.0, base=-128 * m,
            pattern=[[1, 512]], channel_multiplier=-1)
        c.inc(am, "g")
    for x in range(2):
        c.inc(nc.sync.dma_start(out=sk[x][:], in_=k_src[x * 128:(x + 1) * 128, :]),
              "dh", 16)
        c.inc(nc.sync.dma_start(out=sq[x][:], in_=q_src[x * 128:(x + 1) * 128, :]),
              "dh", 16)
        for hh, sv in ((0, sva[x]), (1, svb[x])):
            ld = nc.sync.dma_start(
                out=sv.ap().rearrange("p (n c) -> p n c", c=65)[:, :, 0:64],
                in_=v_src.rearrange("(n p) c -> p n c", p=128)[
                    :, :, x * 128 + hh * 64:x * 128 + hh * 64 + 64])
            c.inc(ld, "dh", 16)
            c.inc(nc.gpsimd.memset(
                sv.ap().rearrange("p (n c) -> p n c", c=65)[:, :, 64:65], 1.0), "g")
    c.wait(nc.sync, "dh")
    c.wait(nc.gpsimd, "g")
    c.barrier()

    scs = ["scA", "scB"]; exs = ["exA", "exB"]; avs = ["avA", "avB"]
    mks = ["mkA", "mkB"]; rcs = ["rcA", "rcB"]; bcs = ["bcA", "bcB"]
    zes = ["zeA", "zeB"]
    cps = ["cpA", "cpB"]
    for n_ in scs + exs + avs + mks + rcs + bcs + zes + cps + ["zst0", "zst1"]:
        c.sem(n_)
    ex0 = [c.counts[exs[0]], c.counts[exs[1]]]
    mk0 = [c.counts[mks[0]], c.counts[mks[1]]]
    rc_done = [{}, {}]

    def e_scores(x, i, jq, kt):
        if i >= 1:
            c.wait(nc.tensor, exs[x], c.counts[exs[x]])
        if kt == 0:
            c.wait(nc.tensor, cps[x], c.counts[cps[x]])
        qsl = sq[x][:, jq * 512:(jq + 1) * 512]
        ksl = sk[x][:, kt * 128:(kt + 1) * 128]
        nc.tensor.matmul(ps[x][:, 0:512], ksl[0:64, :], qsl[0:64, :],
                         start=True, stop=True)
        mmb = nc.tensor.matmul(ps[x][:, 512:1024], ksl[64:128, :], qsl[64:128, :],
                               start=True, stop=True)
        c.inc(mmb, scs[x])

    def e_exp(x, i, jq, kt):
        par = i % 3
        c.wait(nc.scalar, scs[x])
        e = nc.scalar.activation(sattn[x][:, par * 1024:(par + 1) * 1024],
                                 ps[x][:], AF.Exp, scale=scale)
        c.inc(e, exs[x])

    def e_mask(x, i, jq, kt):
        par = i % 3
        a0 = par * 1024
        m = kt - 4 * jq
        eng = nc.vector if x == 0 else nc.gpsimd
        c.wait(eng, exs[x])
        msl = smask[:, m * 512:(m + 1) * 512]
        eng.tensor_mul(out=sattn[x][:, a0:a0 + 512],
                       in0=sattn[x][:, a0:a0 + 512], in1=msl)
        mb = eng.tensor_mul(out=sattn[x][:, a0 + 512:a0 + 1024],
                            in0=sattn[x][:, a0 + 512:a0 + 1024], in1=msl)
        c.inc(mb, mks[x])

    def e_av(x, i, jq, kt, dcount):
        nk = 4 * (jq + 1)
        if kt == 0:
            c.wait(nc.tensor, zes[x], c.counts[zes[x]])
        if kt >= 4 * jq:
            c.wait(nc.tensor, mks[x], mk0[x] + dcount)
        else:
            c.wait(nc.tensor, exs[x], ex0[x] + i + 1)
        par = i % 3
        a0 = par * 1024
        st, sp = (kt == 0), (kt == nk - 1)
        nc.tensor.matmul(pz[x][0:65, 0:512], sva[x][:, kt * 65:(kt + 1) * 65],
                         sattn[x][:, a0:a0 + 512], start=st, stop=sp,
                         skip_group_check=True)
        mm = nc.tensor.matmul(pz[x][0:65, 512:1024], svb[x][:, kt * 65:(kt + 1) * 65],
                              sattn[x][:, a0 + 512:a0 + 1024], start=st, stop=sp,
                              skip_group_check=True)
        c.inc(mm, avs[x])

    def early_evac(x, jq):
        sl = (jq % 2) * 1024
        c.wait(nc.scalar, avs[x])
        c.wait(nc.scalar, rcs[x])
        cpa = nc.scalar.activation(srecf[x][0:1, :], pz[x][64:65, :],
                                   AF.Identity)
        c.inc(cpa, rcs[x])
        c.wait(nc.vector, avs[x])
        c.wait(nc.vector, "dh")
        zc = nc.vector.tensor_copy(out=szr[x][0:64, sl:sl + 1024],
                                   in_=pz[x][0:64, 0:1024])
        c.inc(zc, zes[x])
        c.wait(nc.vector, rcs[x])
        rc = nc.vector.reciprocal(out=srecf[x][0:1, :], in_=srecf[x][0:1, :])
        c.inc(rc, "v")
        c.wait(nc.vector, "v")
        rc2 = nc.vector.tensor_copy(out=srec[x][0:1, sl:sl + 1024],
                                    in_=srecf[x][0:1, :])
        c.inc(rc2, rcs[x])

    def late_evac(x, jq):
        sl = (jq % 2) * 1024
        c.wait(nc.tensor, rcs[x], rc_done[x][jq])
        c.wait(nc.tensor, exs[x], c.counts[exs[x]])
        c.wait(nc.tensor, cps[x], c.counts[cps[x]])
        nc.tensor.matmul(ps[x][0:64, 0:512], sones1[:], srec[x][0:1, sl:sl + 512],
                         start=True, stop=True)
        bc = nc.tensor.matmul(ps[x][0:64, 512:1024], sones1[:],
                              srec[x][0:1, sl + 512:sl + 1024],
                              start=True, stop=True)
        c.inc(bc, bcs[x])
        c.wait(nc.scalar, bcs[x])
        c.wait(nc.scalar, "zst%d" % x, c.counts["zst%d" % x])
        cpc = nc.scalar.activation(sbc[x][0:64, :], ps[x][0:64, :], AF.Identity)
        c.inc(cpc, cps[x])
        c.wait(nc.vector, cps[x])
        nc.vector.tensor_mul(out=szr[x][0:64, sl:sl + 512],
                             in0=szr[x][0:64, sl:sl + 512],
                             in1=sbc[x][0:64, 0:512])
        zm = nc.vector.tensor_mul(out=szr[x][0:64, sl + 512:sl + 1024],
                                  in0=szr[x][0:64, sl + 512:sl + 1024],
                                  in1=sbc[x][0:64, 512:1024])
        c.inc(zm, "zst%d" % x)
        c.wait(nc.sync, "zst%d" % x, c.counts["zst%d" % x])
        r0 = (jq // 4) * 256 + x * 128
        c0 = (jq % 4) * 512
        st1 = nc.sync.dma_start(out=z_dst[r0:r0 + 64, c0:c0 + 512],
                                in_=szr[x][0:64, sl:sl + 512])
        c.inc(st1, "dh", 16)
        st2 = nc.sync.dma_start(out=z_dst[r0 + 64:r0 + 128, c0:c0 + 512],
                                in_=szr[x][0:64, sl + 512:sl + 1024])
        c.inc(st2, "dh", 16)

    pend = [[], []]
    ndiag = [0, 0]
    stepx = [0, 0]

    def substep(x, jq, kt):
        i = stepx[x]
        e_scores(x, i, jq, kt)
        e_exp(x, i, jq, kt)
        if kt >= 4 * jq:
            ndiag[x] += 1
            e_mask(x, i, jq, kt)
        pend[x].append((i, jq, kt, ndiag[x]))
        if len(pend[x]) > 2:
            e_av(x, *pend[x].pop(0))
        stepx[x] += 1

    def flush(x, jq):
        while pend[x]:
            e_av(x, *pend[x].pop(0))
        early_evac(x, jq)
        rc_done[x][jq] = c.counts[rcs[x]]

    LAG = 4
    for jq in range(16):
        nk = 4 * (jq + 1)
        for kt in range(nk + LAG):
            if kt < nk:
                if kt == 0 and jq >= 2:
                    late_evac(0, jq - 2)
                if kt == 2 and jq >= 1:
                    flush(1, jq - 1)
                substep(0, jq, kt)
            if kt == nk + 2:
                flush(0, jq)
            if kt >= LAG:
                if kt - LAG == 0 and jq >= 2:
                    late_evac(1, jq - 2)
                substep(1, jq, kt - LAG)
    flush(1, 15)
    for jq in (14, 15):
        late_evac(0, jq)
        late_evac(1, jq)
    c.wait(nc.sync, "dh")
    c.barrier()
    c.pop()


def emit_outproj_residual(c, z_own, w_ot, x_sb, tag, z_gathered=None):
    nc = c.nc
    c.push()
    sz = c.sb(f"oz_{tag}", [128, NCH * TO], BF16)
    sw = c.sb(f"ow_{tag}", [128, NCH * E], BF16)
    pp = c.ps(f"op_{tag}", [128, 2 * 512], F32)
    if z_gathered is None:
        ld = nc.sync.dma_start(out=sz.ap().rearrange("p (a t) -> p a t", a=NCH),
                               in_=z_own.rearrange("(a p) t -> p a t", p=128))
        c.inc(ld, "dh", 16)
    else:
        zg, qd = z_gathered
        for ch in range(NCH):
            r, pr = ch // 2, ch % 2
            blk = qd + 4 * r
            ap = zg.rearrange("(b q) t -> b q t", q=256)[bass.ts(blk, 1), :, :]
            ld = nc.sync.dma_start(
                out=sz[:, ch * TO:(ch + 1) * TO],
                in_=ap.squeeze()[pr * 128:(pr + 1) * 128, :])
            c.inc(ld, "dh", 16)
    ld = nc.gpsimd.dma_start(out=sw.ap().rearrange("p (a f) -> p a f", a=NCH),
                             in_=w_ot.rearrange("(a p) f -> p a f", p=128))
    c.inc(ld, "dg", 16)
    c.wait(nc.tensor, "dh")
    c.wait(nc.tensor, "dg")
    c.sem("oev")
    for ts in range(NTT):
        for fh in range(2):
            sl = (ts * 2 + fh) % 2
            if ts * 2 + fh >= 2:
                c.wait(nc.tensor, "oev", c.counts["oev"] - 1)
            for a in range(NCH):
                mm = nc.tensor.matmul(
                    pp[:, sl * 512:sl * 512 + 512],
                    sz[:, a * TO + ts * 128:a * TO + (ts + 1) * 128],
                    sw[:, a * E + fh * 512:a * E + fh * 512 + 512],
                    start=(a == 0), stop=(a == NCH - 1))
            c.inc(mm, "omm")
            c.wait(nc.vector, "omm")
            ad = nc.vector.tensor_add(
                out=x_sb[:, ts * E + fh * 512:ts * E + fh * 512 + 512],
                in0=x_sb[:, ts * E + fh * 512:ts * E + fh * 512 + 512],
                in1=pp[:, sl * 512:sl * 512 + 512])
            c.inc(ad, "oev")
    c.barrier()
    c.pop()


def emit_store_x(c, x_sb, x_dst):
    nc = c.nc
    st = nc.sync.dma_start(out=x_dst.rearrange("(n p) f -> p n f", p=128),
                           in_=x_sb.ap().rearrange("p (n f) -> p n f", f=E))
    c.inc(st, "dh", 16)
    c.wait(nc.sync, "dh")
    c.barrier()



def build_A():
    nc = bass.Bass("TRN2", target_bir_lowering=False)
    x_in = nc.dram_tensor("x_in", [TO, E], F32, kind="ExternalInput")
    xln = nc.dram_tensor("xln", [E, TO], BF16, kind="ExternalOutput")
    c = Ctx(nc)
    emit_ln_transpose(c, x_in.ap(), xln.ap(), "a")
    c.close()
    return nc


def build_B(rope, scale):
    nc = bass.Bass("TRN2", target_bir_lowering=False)
    xg = nc.dram_tensor("xg", [4 * E, TO], BF16, kind="ExternalInput")
    wq = nc.dram_tensor("wq", [E, 256], F32, kind="ExternalInput")
    wk = nc.dram_tensor("wk", [E, 256], F32, kind="ExternalInput")
    wv = nc.dram_tensor("wv", [E, 256], F32, kind="ExternalInput")
    zo = nc.dram_tensor("zo", [E, TO], BF16, kind="ExternalOutput")
    qd_ = nc.dram_tensor("q_i", [256, T], BF16)
    kd_ = nc.dram_tensor("k_i", [256, T], BF16)
    vd_ = nc.dram_tensor("v_i", [T, 256], BF16)
    rope_cs = None
    if rope:
        rc = nc.dram_tensor("rcos", [128, T], F32, kind="ExternalInput")
        rs = nc.dram_tensor("rsin", [128, T], F32, kind="ExternalInput")
        p32 = nc.dram_tensor("p32", [128, 128], F32, kind="ExternalInput")
        rope_cs = (rc.ap(), rs.ap(), p32.ap())
    c = Ctx(nc)
    emit_qkv(c, xg.ap(), wq.ap(), wk.ap(), wv.ap(), qd_.ap(), kd_.ap(), vd_.ap(),
             "b", rope_cs=rope_cs)
    emit_attention(c, qd_.ap(), kd_.ap(), vd_.ap(), zo.ap(), "b", scale)
    c.close()
    return nc


def build_Cpart():
    nc = bass.Bass("TRN2", target_bir_lowering=False)
    x_in = nc.dram_tensor("x_in", [TO, E], F32, kind="ExternalInput")
    z_in = nc.dram_tensor("z_in", [E, TO], BF16, kind="ExternalInput")
    w_ot = nc.dram_tensor("w_ot", [E, E], F32, kind="ExternalInput")
    x_out = nc.dram_tensor("x_out", [TO, E], F32, kind="ExternalOutput")
    c = Ctx(nc)
    x_sb = c.sb("xres", [128, NTT * E], F32)
    ld = nc.sync.dma_start(out=x_sb.ap().rearrange("p (n f) -> p n f", f=E),
                           in_=x_in.ap().rearrange("(n p) f -> p n f", p=128))
    c.inc(ld, "dh", 16)
    c.wait(nc.sync, "dh")
    c.barrier()
    emit_outproj_residual(c, z_in.ap(), w_ot.ap(), x_sb, "c")
    emit_store_x(c, x_sb, x_out.ap())
    c.close()
    return nc



def emit_bfly_matrices(c, bfw, mats, plans, tag):
    nc = c.nc
    c.push()
    Y0 = c.sb(f"y0_{tag}", [128, 4096], F32)
    Y1 = c.sb(f"y1_{tag}", [128, 4096], F32)
    tmp1 = c.sb(f"bt1_{tag}", [128, 2048], F32)
    tmp2 = c.sb(f"bt2_{tag}", [128, 2048], F32)
    wrow = [c.sb(f"bw{i}_{tag}", [1, 2 * 2048], BF16) for i in range(2)]
    wt = [c.sb(f"bwt{i}_{tag}", [128, 2 * 2048], F32) for i in range(2)]
    sone = c.sb(f"bon_{tag}", [1, 128], BF16)
    pwb = c.ps(f"bpw_{tag}", [128, 2 * 2048], F32)
    c.inc(nc.gpsimd.memset(sone[:], 1.0), "g")
    c.sem("v"); c.sem("s_pe"); c.sem("g"); c.sem("a")
    c.sem("bld0"); c.sem("bld1")

    state = {"pair": 0, "pwb_free": 0, "matcopy": 0}
    wrow_free = [0, 0]
    wt_ready = [0, 0]
    wt_users = [0, 0]

    def bcast_pair(base_k, pair):
        bi = state["pair"] % 2
        c.wait(nc.gpsimd, "s_pe", wrow_free[bi])
        ld = nc.gpsimd.dma_start(
            out=wrow[bi].ap().rearrange("o (w n) -> o w n", w=2),
            in_=bfw[base_k, 2 * pair:2 * pair + 2, :])
        nld = c.inc(ld, "bld%d" % bi, 16)
        c.wait(nc.tensor, "bld%d" % bi, nld)
        c.wait(nc.tensor, "a", state["pwb_free"])
        first = True
        for w2i in range(2):
            for col in range(4):
                a0 = w2i * 2048 + col * 512
                mmb = nc.tensor.matmul(pwb[:, a0:a0 + 512], sone[:],
                                       wrow[bi][0:1, a0:a0 + 512],
                                       start=True, stop=True)
            c.inc(mmb, "s_pe")
            c.wait(nc.scalar, "s_pe")
            if first:
                c.wait(nc.scalar, "v", wt_users[bi])
                first = False
            cpw = nc.scalar.activation(wt[bi][:, w2i * 2048:(w2i + 1) * 2048],
                                       pwb[:, w2i * 2048:(w2i + 1) * 2048],
                                       AF.Identity)
            c.inc(cpw, "a")
        wrow_free[bi] = c.counts["s_pe"]
        wt_ready[bi] = c.counts["a"]
        state["pwb_free"] = c.counts["a"]
        state["pair"] += 1
        return bi

    base = 0
    order = ["M1", "D1", "N1", "N2"]
    for name in order:
        st = plans[name]
        c.wait(nc.gpsimd, "v", c.counts["v"])
        c.inc(nc.gpsimd.memset(Y0[:], 1.0), "g")
        c.wait(nc.gpsimd, "g")
        ai = nc.gpsimd.affine_select(out=Y0[:], in_=Y0[:], compare_op=ALU.is_equal,
                                     fill=0.0, base=0, pattern=[[0, 32], [-1, 128]],
                                     channel_multiplier=1)
        c.inc(ai, "g")
        gp_init = c.counts["g"]
        cur, nxt = Y0, Y1
        for k, s2 in enumerate(st):
            nb = 128 // (2 * s2)

            def vw(t_, half):
                return t_.rearrange("p (blk kb r) -> p blk kb r", blk=32, kb=nb)[
                    :, :, :, half * s2:(half + 1) * s2]

            def cwv(bi_, i2):
                return wt[bi_].ap()[:, i2 * 2048:(i2 + 1) * 2048].rearrange(
                    "p (blk kb s) -> p blk kb s", blk=32, kb=nb)

            tmp1v = tmp1.ap().rearrange("p (blk kb s) -> p blk kb s", blk=32, kb=nb)
            tmp2v = tmp2.ap().rearrange("p (blk kb s) -> p blk kb s", blk=32, kb=nb)
            a_ap = vw(cur.ap(), 0)
            b_ap = vw(cur.ap(), 1)
            for pair in range(2):
                bi = bcast_pair(base + k, pair)
                c.wait(nc.vector, "a", wt_ready[bi])
                if k == 0:
                    c.wait(nc.vector, "g", gp_init)
                    c.wait(nc.vector, "a", state["matcopy"])
                c.wait(nc.vector, "v", c.counts["v"])
                m1_ = nc.vector.tensor_mul(out=tmp1v, in0=a_ap, in1=cwv(bi, 0))
                c.inc(m1_, "v")
                m2_ = nc.vector.tensor_mul(out=tmp2v, in0=b_ap, in1=cwv(bi, 1))
                c.inc(m2_, "v")
                c.wait(nc.vector, "v", c.counts["v"])
                ad = nc.vector.tensor_add(out=vw(nxt.ap(), pair), in0=tmp1v,
                                          in1=tmp2v)
                c.inc(ad, "v")
                wt_users[bi] = c.counts["v"]
            cur, nxt = nxt, cur
        base += len(st)
        c.wait(nc.scalar, "v", c.counts["v"])
        cc = nc.scalar.activation(mats[name][:], cur[:], AF.Identity)
        state["matcopy"] = c.inc(cc, "a")
    c.wait(nc.vector, "a", state["matcopy"])
    c.barrier()
    c.pop()


def emit_fold_w1(c, wexpT, bexp, M1_sb, w1e_dram, b1e_dram, tag):
    nc = c.nc
    c.push()
    swx = c.sb(f"fw_{tag}", [128, 32 * 128], BF16)
    sbe = c.sb(f"fb_{tag}", [128, 32], BF16)
    acc = c.sb(f"fa_{tag}", [128, 4096], BF16)
    brow = c.sb(f"fbr_{tag}", [1, 4096], F32)
    pf = c.ps(f"fp_{tag}", [128, 2 * 512], F32)
    c.sem("v"); c.sem("s_pe"); c.sem("dg"); c.sem("dh")

    ld = nc.gpsimd.dma_start(out=sbe.ap(), in_=bexp)
    c.inc(ld, "dgb", 16)
    c.wait(nc.tensor, "dgb")
    for co in range(32):
        c.wait(nc.tensor, "v", c.counts["v"])
        mm = nc.tensor.matmul(pf[0:1, 0:128], sbe[:, co:co + 1],
                              M1_sb[:, co * 128:(co + 1) * 128],
                              start=True, stop=True)
        c.inc(mm, "s_pe")
        c.wait(nc.vector, "s_pe")
        cp = nc.vector.tensor_copy(out=brow[0:1, co * 128:(co + 1) * 128],
                                   in_=pf[0:1, 0:128])
        c.inc(cp, "v")
    c.wait(nc.sync, "v")
    stb = nc.sync.dma_start(out=b1e_dram, in_=brow.ap())
    c.inc(stb, "dh", 16)
    c.wait(nc.tensor, "v")

    mm_after_ci = 0
    pf_free = [c.counts["v"], c.counts["v"]]
    for ci in range(NCH):
        if ci >= 1:
            c.wait(nc.gpsimd, "s_pe", mm_after_ci)
        ldw = nc.gpsimd.dma_start(
            out=swx.ap().rearrange("p (co q) -> p co q", co=32),
            in_=wexpT[:, ci * 128:(ci + 1) * 128].rearrange("(co p) q -> p co q", p=128))
        nld = c.inc(ldw, "dg", 16)
        c.wait(nc.tensor, "dg", nld)
        c.wait(nc.vector, "dh")
        for co in range(32):
            sl = co % 2
            c.wait(nc.tensor, "v", pf_free[sl])
            mm = nc.tensor.matmul(
                pf[:, sl * 512:sl * 512 + 128],
                swx[:, co * 128:(co + 1) * 128],
                M1_sb[:, co * 128:(co + 1) * 128],
                start=True, stop=True)
            c.inc(mm, "s_pe")
            c.wait(nc.vector, "s_pe")
            cp = nc.vector.tensor_copy(
                out=acc.ap().rearrange("p (c2 hi lo) -> p hi c2 lo",
                                       c2=32, hi=4)[:, :, :, co:co + 1],
                in_=pf[:, sl * 512:sl * 512 + 128].rearrange(
                    "p (hi c2 one) -> p hi c2 one", hi=4, c2=32))
            pf_free[sl] = c.inc(cp, "v")
        mm_after_ci = c.counts["s_pe"]
        c.wait(nc.sync, "v")
        stw = nc.sync.dma_start(
            out=w1e_dram[:, ci * 128:(ci + 1) * 128, :].rearrange("c2 p q -> p c2 q"),
            in_=acc.ap().rearrange("p (c2 q) -> p c2 q", c2=32))
        c.inc(stw, "dh", 16)
    c.wait(nc.sync, "dh")
    c.barrier()
    c.pop()


def emit_fold_w2(c, wconP, N2_sb, w2e_dram, tag):
    nc = c.nc
    c.push()
    swc = c.sb(f"gw_{tag}", [128, 1024], BF16)
    sev = c.sb(f"ge_{tag}", [128, 1024], BF16)
    pg = c.ps(f"gp_{tag}", [128, 2 * 512], F32)
    c.sem("v"); c.sem("s_pe"); c.sem("dg"); c.sem("dh")
    mm_after = 0
    for c2 in range(32):
        if c2 >= 1:
            c.wait(nc.gpsimd, "s_pe", mm_after)
        ldw = nc.gpsimd.dma_start(out=swc.ap(),
                                  in_=wconP[c2 * 128:(c2 + 1) * 128, :])
        nld = c.inc(ldw, "dg", 16)
        c.wait(nc.tensor, "dg", nld)
        c.wait(nc.tensor, "dh")
        for dh_ in range(2):
            mm = nc.tensor.matmul(pg[:, dh_ * 512:(dh_ + 1) * 512],
                                  N2_sb[:, c2 * 128:(c2 + 1) * 128],
                                  swc[:, dh_ * 512:(dh_ + 1) * 512],
                                  start=True, stop=True)
            c.inc(mm, "s_pe")
        mm_after = c.counts["s_pe"]
        c.wait(nc.vector, "s_pe")
        cp = nc.vector.tensor_copy(out=sev[:], in_=pg[:])
        c.inc(cp, "v")
        c.wait(nc.sync, "v")
        for hi in range(4):
            r0 = c2 + 32 * hi
            dst = w2e_dram.rearrange("(co r) d -> r co d", r=128)[r0:r0 + 1, :, :].squeeze()
            st = nc.sync.dma_start(out=dst, in_=sev[hi * 32:(hi + 1) * 32, :])
            c.inc(st, "dh", 16)
    c.wait(nc.sync, "dh")
    c.barrier()
    c.pop()


def emit_ffn_l0(c, xln_fm, w1e_dram, b1e_dram, up_bP, w2e_dram, brow2,
                D1_sb, N1_sb, relay_dram, x_sb, tag):
    nc = c.nc
    c.push()
    sxln = c.sb(f"lx_{tag}", [128, NCH * TO], BF16)
    sw1 = c.sb(f"lw1_{tag}", [128, 2 * NCH * 128], BF16)
    sh1 = c.sb(f"lh1_{tag}", [128, 2 * 512], BF16)
    sh2 = c.sb(f"lh2_{tag}", [128, 2 * 512], BF16)
    sh2o = c.sb(f"lho_{tag}", [128, 2 * 512], BF16)
    sh3 = c.sb(f"lh3_{tag}", [128, 32 * 512], BF16)
    sw2 = c.sb(f"lw2_{tag}", [128, 2 * 512], BF16)
    sb1e = c.sb(f"lb1_{tag}", [128, 32], F32)
    subp = c.sb(f"lbu_{tag}", [128, 32], F32)
    sbr2 = c.sb(f"lbr_{tag}", [2, 1024], BF16)
    sone2 = c.sb(f"lo2_{tag}", [2, 128], BF16)
    psA = c.ps(f"lpa_{tag}", [128, 2 * 512], F32)
    psB = c.ps(f"lpb_{tag}", [128, 2 * 512], F32)
    psY = c.ps(f"lpy_{tag}", [128, 4 * 512], F32)
    c.sem("v"); c.sem("s_pe"); c.sem("s"); c.sem("dg"); c.sem("dh"); c.sem("dgw")

    ld = nc.sync.dma_start(out=sxln.ap().rearrange("p (ch t) -> p ch t", ch=NCH),
                           in_=xln_fm.rearrange("(ch p) t -> p ch t", p=128))
    c.inc(ld, "dh", 16)
    for hi in range(4):
        ldb = nc.gpsimd.dma_start(
            out=sb1e[hi * 32:(hi + 1) * 32, :],
            in_=b1e_dram.rearrange("o (lo r) -> lo (o r)", lo=32)[
                :, 32 * hi:32 * hi + 32])
        c.inc(ldb, "dg", 16)
    ld = nc.gpsimd.dma_start(out=subp.ap(), in_=up_bP)
    c.inc(ld, "dg", 16)
    ld = nc.gpsimd.dma_start(out=sbr2.ap(), in_=brow2)
    c.inc(ld, "dg", 16)
    c.inc(nc.gpsimd.memset(sone2[:], 1.0), "g")
    c.wait(nc.sync, "dh")
    c.wait(nc.gpsimd, "dg")
    c.wait(nc.gpsimd, "g")
    c.barrier()

    w1used = [0, 0]
    w1ld = [0, 0]
    w2ld = [0, 0]
    hold = [0, 0]
    s_after_ev1 = [0, 0]
    s_after_ev2 = [0, 0]
    mmd_cnt = [0, 0]
    st_cnt = [0, 0]
    housed = [0, 0]
    ev3_cnt = [0, 0]
    w2used = [0, 0]
    for tt in range(4):
        t0 = tt * 512
        def ld_w1e(c2_):
            rot_ = c2_ % 2
            wbuf_ = sw1[:, rot_ * NCH * 128:(rot_ + 1) * NCH * 128]
            c.wait(nc.gpsimd, "s_pe", w1used[rot_])
            ldw = nc.gpsimd.dma_start(
                out=wbuf_.rearrange("p (ci q) -> p ci q", ci=NCH),
                in_=w1e_dram[c2_, :, :].rearrange("(ci p) q -> p ci q", p=128))
            w1ld[rot_] = c.inc(ldw, "dgw%d" % rot_, 16)
        ld_w1e(0)
        for c2 in range(32):
            rot = c2 % 2
            wbuf = sw1[:, rot * NCH * 128:(rot + 1) * NCH * 128]
            if c2 + 1 < 32:
                ld_w1e(c2 + 1)
            c.wait(nc.tensor, "dgw%d" % rot, w1ld[rot])
            c.wait(nc.tensor, "s", s_after_ev1[rot])
            for ci in range(NCH):
                mm = nc.tensor.matmul(psA[:, rot * 512:rot * 512 + 512],
                                      wbuf[:, ci * 128:(ci + 1) * 128],
                                      sxln[:, ci * TO + t0:ci * TO + t0 + 512],
                                      start=(ci == 0), stop=(ci == NCH - 1))
            w1used[rot] = c.inc(mm, "s_pe")
            c.wait(nc.scalar, "s_pe")
            c.wait(nc.scalar, "s_pe", mmd_cnt[rot])
            ev1 = nc.scalar.activation(sh1[:, rot * 512:rot * 512 + 512],
                                       psA[:, rot * 512:rot * 512 + 512],
                                       AF.Identity, bias=sb1e[:, c2:c2 + 1])
            s_after_ev1[rot] = c.inc(ev1, "s")
            c.wait(nc.tensor, "s")
            c.wait(nc.tensor, "s", s_after_ev2[rot])
            mmd = nc.tensor.matmul(psB[:, rot * 512:rot * 512 + 512],
                                   D1_sb[:, c2 * 128:(c2 + 1) * 128],
                                   sh1[:, rot * 512:rot * 512 + 512],
                                   start=True, stop=True)
            mmd_cnt[rot] = c.inc(mmd, "s_pe")
            c.wait(nc.scalar, "s_pe")
            c.wait(nc.scalar, "str%d" % rot, st_cnt[rot])
            ev2 = nc.scalar.activation(sh2[:, rot * 512:rot * 512 + 512],
                                       psB[:, rot * 512:rot * 512 + 512],
                                       AF.Gelu, bias=subp[:, c2:c2 + 1])
            s_after_ev2[rot] = c.inc(ev2, "s")
            c.wait(nc.sync, "s")
            str_ = nc.sync.dma_start(out=relay_dram[c2, :, :],
                                     in_=sh2[:, rot * 512:rot * 512 + 512])
            st_cnt[rot] = c.inc(str_, "str%d" % rot, 16)
        c.wait(nc.sync, "str0")
        c.wait(nc.sync, "str1")
        c.barrier()
        def ld_h2o(co_):
            rot_ = co_ % 2
            c.wait(nc.sync, "s_pe", housed[rot_])
            for hi in range(4):
                ldo = nc.sync.dma_start(
                    out=sh2o[hi * 32:(hi + 1) * 32, rot_ * 512:rot_ * 512 + 512],
                    in_=relay_dram[:, co_ + 32 * hi, :])
                hold[rot_] = c.inc(ldo, "dho%d" % rot_, 16)
        ld_h2o(0)
        for co in range(32):
            rot = co % 2
            if co + 1 < 32:
                ld_h2o(co + 1)
            c.wait(nc.tensor, "dho%d" % rot, hold[rot])
            c.wait(nc.tensor, "v", ev3_cnt[rot])
            mmn = nc.tensor.matmul(psA[:, rot * 512:rot * 512 + 512],
                                   N1_sb[:, co * 128:(co + 1) * 128],
                                   sh2o[:, rot * 512:rot * 512 + 512],
                                   start=True, stop=True)
            housed[rot] = c.inc(mmn, "s_pe")
            c.wait(nc.vector, "s_pe")
            ev3 = nc.vector.tensor_copy(out=sh3[:, co * 512:(co + 1) * 512],
                                        in_=psA[:, rot * 512:rot * 512 + 512])
            ev3_cnt[rot] = c.inc(ev3, "v")
        c.barrier()
        for fh in range(2):
            yfree = c.counts["v"]

            def ld_w2e(co_, fh_):
                rot_ = co_ % 2
                c.wait(nc.gpsimd, "s_pe", w2used[rot_])
                ldw = nc.gpsimd.dma_start(
                    out=sw2[:, rot_ * 512:(rot_ + 1) * 512],
                    in_=w2e_dram[co_ * 128:(co_ + 1) * 128,
                                 fh_ * 512:fh_ * 512 + 512])
                w2ld[rot_] = c.inc(ldw, "dgw%d" % rot_, 16)
            if fh == 0:
                ld_w2e(0, 0)
            for co in range(32):
                rot = co % 2
                wb = sw2[:, rot * 512:(rot + 1) * 512]
                if co + 1 < 32:
                    ld_w2e(co + 1, fh)
                elif fh == 0:
                    ld_w2e(0, 1)
                c.wait(nc.tensor, "dgw%d" % rot, w2ld[rot])
                if co == 0:
                    c.wait(nc.tensor, "v", yfree)
                for ts in range(4):
                    mm = nc.tensor.matmul(
                        psY[:, ts * 512:(ts + 1) * 512],
                        sh3[:, co * 512 + ts * 128:co * 512 + (ts + 1) * 128],
                        wb, start=(co == 0), stop=False)
                w2used[rot] = c.inc(mm, "s_pe")
            for ts in range(4):
                mm = nc.tensor.matmul(psY[:, ts * 512:(ts + 1) * 512],
                                      sone2[:], sbr2[:, fh * 512:fh * 512 + 512],
                                      start=False, stop=True)
            c.inc(mm, "s_pe")
            c.wait(nc.vector, "s_pe")
            for ts in range(4):
                gtt = (tt * 4 + ts)
                ad = nc.vector.tensor_add(
                    out=x_sb[:, gtt * E + fh * 512:gtt * E + fh * 512 + 512],
                    in0=x_sb[:, gtt * E + fh * 512:gtt * E + fh * 512 + 512],
                    in1=psY[:, ts * 512:(ts + 1) * 512])
                c.inc(ad, "v")
        c.barrier()
    c.barrier()
    c.pop()


def emit_ffn_l1(c, xln_fm, w1t, b1cm, w2t, b2row, wq1_dram, wq2_dram, x_sb, tag):
    nc = c.nc
    c.push()
    swf = c.sb(f"qw_{tag}", [128, 2 * 4096], F32)
    sabs = c.sb(f"qa_{tag}", [128, NCH + 32], F32)
    sth = c.sb(f"qt_{tag}", [128, 4], F32)
    srow = c.sb(f"qr_{tag}", [1, 8], F32)
    sqt = c.sb(f"qq_{tag}", [128, 4096], BF16)
    stmp = c.sb(f"qtm_{tag}", [128, 4096], F32)
    sone = c.sb(f"qo_{tag}", [1, 128], F32)
    sonec = c.sb(f"qoc_{tag}", [128, 1], F32)
    pq_ = c.ps(f"qp_{tag}", [1, 512], F32)
    pbc = c.ps(f"qb_{tag}", [128, 512], F32)
    c.sem("v"); c.sem("s_pe"); c.sem("s"); c.sem("dg"); c.sem("dh"); c.sem("dgw")
    c.inc(nc.gpsimd.memset(sone[:], 1.0), "g")
    c.inc(nc.gpsimd.memset(sonec[:], 1.0), "g")
    c.wait(nc.gpsimd, "g")
    c.barrier()

    absused = [0, 0]
    gld = [0, 0]
    seqA = [(w1t, ch, 0 + ch, DH) for ch in range(NCH)] + \
           [(w2t, ch, NCH + ch, E) for ch in range(32)]

    def ld_gamma(idx):
        w_ap_, ch_, _, wc_ = seqA[idx]
        rot_ = idx % 2
        c.wait(nc.gpsimd, "v", absused[rot_])
        ldw = nc.gpsimd.dma_start(
            out=swf[:, rot_ * 4096:rot_ * 4096 + wc_],
            in_=w_ap_[ch_ * 128:(ch_ + 1) * 128, :])
        gld[rot_] = c.inc(ldw, "dg%d" % rot_, 16)
    ld_gamma(0)
    for idx in range(len(seqA)):
        rot = idx % 2
        _, _, col_ch, w_cols = seqA[idx]
        if idx + 1 < len(seqA):
            ld_gamma(idx + 1)
        c.wait(nc.vector, "dg%d" % rot, gld[rot])
        rs = nc.vector.reduce_sum(out=sabs[:, col_ch:col_ch + 1],
                                  in_=swf[:, rot * 4096:rot * 4096 + w_cols],
                                  axis=mybir.AxisListType.X,
                                  apply_absolute_value=True)
        absused[rot] = c.inc(rs, "v")
    c.wait(nc.tensor, "v")
    mm1 = nc.tensor.matmul(pq_[0:1, 0:NCH], sonec[:], sabs[:, 0:NCH],
                           start=True, stop=True)
    c.inc(mm1, "s_pe")
    mm2 = nc.tensor.matmul(pq_[0:1, 32:64], sonec[:], sabs[:, NCH:NCH + 32],
                           start=True, stop=True)
    c.inc(mm2, "s_pe")
    c.wait(nc.vector, "s_pe")
    r1 = nc.vector.reduce_sum(out=srow[0:1, 0:1], in_=pq_[0:1, 0:NCH],
                              axis=mybir.AxisListType.X)
    c.inc(r1, "v")
    c.wait(nc.vector, "v")
    r2 = nc.vector.reduce_sum(out=srow[0:1, 1:2], in_=pq_[0:1, 32:64],
                              axis=mybir.AxisListType.X)
    c.inc(r2, "v")
    c.wait(nc.vector, "v")
    ts1 = nc.vector.tensor_scalar(out=srow[0:1, 2:4], in0=srow[0:1, 0:2],
                                  scalar1=1.0 / (DH * E), scalar2=None,
                                  op0=ALU.mult, op1=ALU.bypass)
    c.inc(ts1, "v")
    c.wait(nc.vector, "v")
    ts2 = nc.vector.tensor_scalar(out=srow[0:1, 4:6], in0=srow[0:1, 2:4],
                                  scalar1=EPS, scalar2=0.5,
                                  op0=ALU.add, op1=ALU.mult)
    c.inc(ts2, "v")
    c.wait(nc.vector, "v")
    ts3 = nc.vector.tensor_scalar(out=srow[0:1, 6:8], in0=srow[0:1, 4:6],
                                  scalar1=-1.0, scalar2=None,
                                  op0=ALU.mult, op1=ALU.bypass)
    c.inc(ts3, "v")
    c.wait(nc.tensor, "v")
    mmb = nc.tensor.matmul(pbc[:, 0:4], sone[:], srow[0:1, 4:8],
                           start=True, stop=True)
    c.inc(mmb, "s_pe")
    c.wait(nc.vector, "s_pe")
    cpb = nc.vector.tensor_copy(out=sth[:], in_=pbc[:, 0:4])
    c.inc(cpb, "v")
    c.wait(nc.vector, "v")
    c.barrier()

    qused = [0, 0]
    seqB = [(w1t, ch, 0, DH, 0) for ch in range(NCH)] + \
           [(w2t, ch, 1, E, 1) for ch in range(32)]

    def ld_q(idx):
        w_ap_, ch_, _, wc_, _ = seqB[idx]
        rot_ = idx % 2
        c.wait(nc.gpsimd, "v", qused[rot_])
        ldw = nc.gpsimd.dma_start(
            out=swf[:, rot_ * 4096:rot_ * 4096 + wc_],
            in_=w_ap_[ch_ * 128:(ch_ + 1) * 128, :])
        gld[rot_] = c.inc(ldw, "dg%d" % rot_, 16)
    ld_q(0)
    for idx in range(len(seqB)):
        rot = idx % 2
        _, ch, thc, w_cols, mi = seqB[idx]
        buf = swf[:, rot * 4096:(rot + 1) * 4096]
        if idx + 1 < len(seqB):
            ld_q(idx + 1)
        c.wait(nc.vector, "dg%d" % rot, gld[rot])
        c.wait(nc.vector, "dh")
        tn = nc.vector.tensor_single_scalar(out=stmp[:, 0:w_cols],
                                            in_=buf[:, 0:w_cols],
                                            scalar=sth[:, 2 + thc:3 + thc],
                                            op=ALU.is_lt)
        c.inc(tn, "v")
        c.wait(nc.vector, "v")
        qq = nc.vector.scalar_tensor_tensor(out=sqt[:, 0:w_cols],
                                            in0=buf[:, 0:w_cols],
                                            scalar=sth[:, thc:thc + 1],
                                            in1=stmp[:, 0:w_cols],
                                            op0=ALU.is_gt, op1=ALU.subtract)
        qused[rot] = c.inc(qq, "v")
        c.wait(nc.sync, "v")
        if mi == 0:
            st_ = nc.sync.dma_start(
                out=wq1_dram[:, ch * 128:(ch + 1) * 128, :].rearrange(
                    "fo p q -> p fo q"),
                in_=sqt.ap().rearrange("p (fo q) -> p fo q", fo=32))
        else:
            st_ = nc.sync.dma_start(out=wq2_dram[ch * 128:(ch + 1) * 128, :],
                                    in_=sqt[:, 0:w_cols])
        c.inc(st_, "dh", 16)
    c.wait(nc.sync, "dh")
    c.barrier()
    c.pop()

    c.push()
    sxln = c.sb(f"qx_{tag}", [128, NCH * TO], BF16)
    sw1 = c.sb(f"qw1_{tag}", [128, 2 * NCH * 128], BF16)
    sh = c.sb(f"qh_{tag}", [128, 32 * 512], BF16)
    sw2 = c.sb(f"qw2_{tag}", [128, 2 * 512], BF16)
    sb1 = c.sb(f"qb1_{tag}", [128, 32], F32)
    sbr2 = c.sb(f"qbr_{tag}", [2, 1024], BF16)
    sone2 = c.sb(f"qo2_{tag}", [2, 128], BF16)
    psA = c.ps(f"qpa_{tag}", [128, 2 * 512], F32)
    psY = c.ps(f"qpy_{tag}", [128, 4 * 512], F32)
    ld = nc.sync.dma_start(out=sxln.ap().rearrange("p (ch t) -> p ch t", ch=NCH),
                           in_=xln_fm.rearrange("(ch p) t -> p ch t", p=128))
    c.inc(ld, "dh", 16)
    ldb = nc.gpsimd.dma_start(out=sb1.ap(), in_=b1cm)
    c.inc(ldb, "dg", 16)
    c.inc(nc.gpsimd.memset(sbr2[:], 0.0), "g")
    c.wait(nc.gpsimd, "g")
    ldb = nc.gpsimd.dma_start(out=sbr2[0:1, :], in_=b2row)
    c.inc(ldb, "dg", 16)
    c.inc(nc.gpsimd.memset(sone2[:], 1.0), "g")
    c.wait(nc.gpsimd, "g")
    c.wait(nc.gpsimd, "dg")
    c.wait(nc.sync, "dh")
    c.barrier()
    z1used = [0, 0]
    z2used = [0, 0]
    for tt in range(4):
        t0 = tt * 512
        for fo in range(32):
            rot = fo % 2
            wbuf = sw1[:, rot * NCH * 128:(rot + 1) * NCH * 128]
            c.wait(nc.gpsimd, "s_pe", z1used[rot])
            ldw = nc.gpsimd.dma_start(
                out=wbuf.rearrange("p (ci q) -> p ci q", ci=NCH),
                in_=wq1_dram[fo, :, :].rearrange("(ci p) q -> p ci q", p=128))
            nldw = c.inc(ldw, "dgw%d" % rot, 16)
            c.wait(nc.tensor, "dgw%d" % rot, nldw)
            if c.counts["s"] >= 2:
                c.wait(nc.tensor, "s", c.counts["s"] - 1)
            for ci in range(NCH):
                mm = nc.tensor.matmul(psA[:, rot * 512:rot * 512 + 512],
                                      wbuf[:, ci * 128:(ci + 1) * 128],
                                      sxln[:, ci * TO + t0:ci * TO + t0 + 512],
                                      start=(ci == 0), stop=(ci == NCH - 1))
            z1used[rot] = c.inc(mm, "s_pe")
            c.wait(nc.scalar, "s_pe")
            ev = nc.scalar.activation(sh[:, fo * 512:(fo + 1) * 512],
                                      psA[:, rot * 512:rot * 512 + 512],
                                      AF.Relu, bias=sb1[:, fo:fo + 1])
            c.inc(ev, "s")
        c.wait(nc.tensor, "s")
        c.barrier()
        for fh in range(2):
            yfree = c.counts["v"]
            for co in range(32):
                rot = co % 2
                wb = sw2[:, rot * 512:(rot + 1) * 512]
                c.wait(nc.gpsimd, "s_pe", z2used[rot])
                ldw = nc.gpsimd.dma_start(
                    out=wb, in_=wq2_dram[co * 128:(co + 1) * 128,
                                         fh * 512:fh * 512 + 512])
                nldw = c.inc(ldw, "dgw%d" % rot, 16)
                c.wait(nc.tensor, "dgw%d" % rot, nldw)
                if co == 0:
                    c.wait(nc.tensor, "v", yfree)
                for ts in range(4):
                    mm = nc.tensor.matmul(
                        psY[:, ts * 512:(ts + 1) * 512],
                        sh[:, co * 512 + ts * 128:co * 512 + (ts + 1) * 128],
                        wb, start=(co == 0), stop=False)
                z2used[rot] = c.inc(mm, "s_pe")
            for ts in range(4):
                mm = nc.tensor.matmul(psY[:, ts * 512:(ts + 1) * 512],
                                      sone2[:], sbr2[:, fh * 512:fh * 512 + 512],
                                      start=False, stop=True)
            c.inc(mm, "s_pe")
            c.wait(nc.vector, "s_pe")
            for ts in range(4):
                gtt = (tt * 4 + ts)
                ad = nc.vector.tensor_add(
                    out=x_sb[:, gtt * E + fh * 512:gtt * E + fh * 512 + 512],
                    in0=x_sb[:, gtt * E + fh * 512:gtt * E + fh * 512 + 512],
                    in1=psY[:, ts * 512:(ts + 1) * 512])
                c.inc(ad, "v")
        c.barrier()
    c.barrier()
    c.pop()



def p_of_f(f):
    return (f >> 7) | (((f >> 5) & 3) << 5)


def f_of_p(c2, p2):
    return c2 | ((p2 & 31) << 7) | (((p2 >> 5) & 3) << 5)


def _wvec4(stage_w, s, f_a):
    s2g = 2 ** s
    wr = stage_w.reshape(4096 // (2 * s2g), s2g, 2, 2)
    i = f_a >> (s + 1)
    j = f_a & (s2g - 1)
    w = wr[i, j]
    return np.stack([w[:, 0, 0], w[:, 0, 1], w[:, 1, 0], w[:, 1, 1]])


def _fa_orig(s2):
    co = np.arange(32)
    nb = 128 // (2 * s2)
    kb = np.arange(nb)
    ks = np.arange(s2)
    f = (co[:, None, None] * 128 + kb[None, :, None] * 2 * s2 + ks[None, None, :])
    return f.reshape(-1)


def _fa_P(s2p):
    c2 = np.arange(32)
    nb = 128 // (2 * s2p)
    kb = np.arange(nb)
    ks = np.arange(s2p)
    p2a = kb[:, None] * 2 * s2p + ks[None, :]
    f = f_of_p(c2[:, None, None], p2a[None])
    return f.reshape(-1)


def butterfly_builder_plan(up_w, dn_w):
    rows = []
    plans = {}
    st = []
    for s in range(7):
        s2 = 2 ** s
        rows.append(_wvec4(up_w[s], s, _fa_orig(s2)))
        st.append(s2)
    plans["M1"] = st
    st = []
    for s in range(7, 12):
        s2p = 2 ** (s - 7)
        rows.append(_wvec4(up_w[s], s, _fa_P(s2p)))
        st.append(s2p)
    plans["D1"] = st
    st = []
    for s in range(7):
        s2 = 2 ** s
        rows.append(_wvec4(dn_w[s], s, _fa_orig(s2)))
        st.append(s2)
    plans["N1"] = st
    st = []
    for s in range(7, 12):
        s2p = 2 ** (s - 7)
        rows.append(_wvec4(dn_w[s], s, _fa_P(s2p)))
        st.append(s2p)
    plans["N2"] = st
    bfw = np.stack(rows).astype(np.float32)
    return bfw, plans



def host_slices(inputs):
    per_core = []
    x = inputs["L0_attn_x"]
    W1 = inputs["L0_attn_W1"]
    Q1, K1, V1 = inputs["L1_attn_Q"], inputs["L1_attn_K"], inputs["L1_attn_V"]
    for cid in range(NCORE):
        b, qd = cid // 4, cid % 4
        hs = list(range(4 * qd, 4 * qd + 4))
        d = {}
        d["x0"] = np.ascontiguousarray(x[b, qd * TO:(qd + 1) * TO, :])
        d["wq0"] = np.ascontiguousarray(
            np.concatenate([W1[h * D:(h + 1) * D, :].T for h in hs], axis=1))
        d["wk0"] = np.ascontiguousarray(
            np.concatenate([W1[E + h * D:E + (h + 1) * D, :].T for h in hs], axis=1))
        d["wv0"] = np.ascontiguousarray(
            np.concatenate([W1[2 * E + h * D:2 * E + (h + 1) * D, :].T for h in hs], axis=1))
        d["wq1"] = np.ascontiguousarray(Q1[hs].transpose(2, 0, 1).reshape(E, 4 * D))
        d["wk1"] = np.ascontiguousarray(K1[hs].transpose(2, 0, 1).reshape(E, 4 * D))
        d["wv1"] = np.ascontiguousarray(V1[hs].transpose(2, 0, 1).reshape(E, 4 * D))
        per_core.append(d)
    shared = {}
    shared["w2t"] = np.ascontiguousarray(inputs["L0_attn_W2"].T)
    shared["ot"] = np.ascontiguousarray(inputs["L1_attn_O"].T)
    shared["wexpT"] = np.ascontiguousarray(inputs["L0_ffn_W_exp"].T)
    shared["bexp_cm"] = np.ascontiguousarray(
        inputs["L0_ffn_b_exp"].reshape(32, 128).T)
    bfw, _plans = butterfly_builder_plan(inputs["L0_ffn_up_weights"],
                                         inputs["L0_ffn_down_weights"])
    shared["bfw"] = bfw
    fidx = np.array([f_of_p(i // 128, i % 128) for i in range(4096)])
    ub = inputs["L0_ffn_up_bias"]
    up_bP = ub[fidx].reshape(32, 128).T
    shared["up_bP"] = np.ascontiguousarray(up_bP.astype(np.float32))
    shared["wconP"] = np.ascontiguousarray(inputs["L0_ffn_W_con"][fidx])
    shared["brow2"] = np.ascontiguousarray(
        np.stack([inputs["L0_ffn_b_con"], inputs["L0_ffn_down_bias"]]))
    shared["w1t"] = np.ascontiguousarray(inputs["L1_ffn_W1"].T)
    shared["b1cm"] = np.ascontiguousarray(inputs["L1_ffn_b1"].reshape(32, 128).T)
    shared["w2t_ffn"] = np.ascontiguousarray(inputs["L1_ffn_W2"].T)
    shared["b2row"] = np.ascontiguousarray(inputs["L1_ffn_b2"][None, :])
    p = np.arange(128)
    inv = (1.0 / 10000.0 ** (np.arange(0, D, 2, dtype=np.float32) / D))
    ang = np.arange(T, dtype=np.float32)[None, :] * inv[p % 32][:, None]
    cos = np.cos(ang).astype(np.float32)
    sin = np.sin(ang).astype(np.float32)
    sgn = np.where(((p ^ 32) % 64) < 32, -1.0, 1.0).astype(np.float32)[:, None]
    shared["rope_cos"] = cos
    shared["rope_sin"] = (sin * sgn)
    p32 = np.zeros((128, 128), dtype=np.float32)
    p32[p, p ^ 32] = 1.0
    shared["p32"] = p32
    return per_core, shared


def build_C0():
    nc = bass.Bass("TRN2", target_bir_lowering=False)
    x_in = nc.dram_tensor("x_in", [TO, E], F32, kind="ExternalInput")
    z_in = nc.dram_tensor("z_in", [E, TO], BF16, kind="ExternalInput")
    w_ot = nc.dram_tensor("w_ot", [E, E], F32, kind="ExternalInput")
    wexpT = nc.dram_tensor("wexpT", [DH, E], F32, kind="ExternalInput")
    bexp_cm = nc.dram_tensor("bexp_cm", [128, 32], F32, kind="ExternalInput")
    bfw = nc.dram_tensor("bfw", [24, 4, 2048], F32, kind="ExternalInput")
    up_bP = nc.dram_tensor("up_bP", [128, 32], F32, kind="ExternalInput")
    wconP = nc.dram_tensor("wconP", [DH, E], F32, kind="ExternalInput")
    brow2 = nc.dram_tensor("brow2", [2, E], F32, kind="ExternalInput")
    x_out = nc.dram_tensor("x_out", [TO, E], F32, kind="ExternalOutput")
    xln_out = nc.dram_tensor("xln_out", [E, TO], BF16, kind="ExternalOutput")
    xln2 = nc.dram_tensor("xln2_i", [E, TO], BF16)
    w1e = nc.dram_tensor("w1e_i", [32, E, 128], BF16)
    b1e = nc.dram_tensor("b1e_i", [1, DH], F32)
    w2e = nc.dram_tensor("w2e_i", [DH, E], BF16)
    relay = nc.dram_tensor("relay_i", [32, 128, 512], BF16)
    c = Ctx(nc)
    x_sb = c.sb("xres", [128, NTT * E], F32)
    mats = {nm: c.sb(f"mat_{nm}", [128, 32 * 128], BF16)
            for nm in ("M1", "D1", "N1", "N2")}
    _, plans = butterfly_builder_plan(np.zeros((12, 2048, 2, 2), np.float32),
                                     np.zeros((12, 2048, 2, 2), np.float32))
    ld = nc.sync.dma_start(out=x_sb.ap().rearrange("p (n f) -> p n f", f=E),
                           in_=x_in.ap().rearrange("(n p) f -> p n f", p=128))
    c.inc(ld, "dh", 16)
    c.wait(nc.sync, "dh")
    c.barrier()
    emit_outproj_residual(c, z_in.ap(), w_ot.ap(), x_sb, "c0")
    emit_ln_transpose(c, None, xln2.ap(), "c0l", x_sb=x_sb)
    emit_bfly_matrices(c, bfw.ap(), mats, plans, "c0")
    emit_fold_w1(c, wexpT.ap(), bexp_cm.ap(), mats["M1"], w1e.ap(), b1e.ap(), "c0")
    emit_fold_w2(c, wconP.ap(), mats["N2"], w2e.ap(), "c0")
    emit_ffn_l0(c, xln2.ap(), w1e.ap(), b1e.ap(), up_bP.ap(), w2e.ap(),
                brow2.ap(), mats["D1"], mats["N1"], relay.ap(), x_sb, "c0")
    emit_ln_transpose(c, None, xln_out.ap(), "c0m", x_sb=x_sb)
    emit_store_x(c, x_sb, x_out.ap())
    c.close()
    return nc


def build_C1():
    nc = bass.Bass("TRN2", target_bir_lowering=False)
    x_in = nc.dram_tensor("x_in", [TO, E], F32, kind="ExternalInput")
    z_in = nc.dram_tensor("z_in", [E, TO], BF16, kind="ExternalInput")
    w_ot = nc.dram_tensor("w_ot", [E, E], F32, kind="ExternalInput")
    w1t = nc.dram_tensor("w1t", [E, DH], F32, kind="ExternalInput")
    b1cm = nc.dram_tensor("b1cm", [128, 32], F32, kind="ExternalInput")
    w2t = nc.dram_tensor("w2t", [DH, E], F32, kind="ExternalInput")
    b2row = nc.dram_tensor("b2row", [1, E], F32, kind="ExternalInput")
    x_out = nc.dram_tensor("x_out", [TO, E], F32, kind="ExternalOutput")
    xln2 = nc.dram_tensor("xln2_i", [E, TO], BF16)
    wq1 = nc.dram_tensor("wq1_i", [32, E, 128], BF16)
    wq2 = nc.dram_tensor("wq2_i", [DH, E], BF16)
    c = Ctx(nc)
    x_sb = c.sb("xres", [128, NTT * E], F32)
    ld = nc.sync.dma_start(out=x_sb.ap().rearrange("p (n f) -> p n f", f=E),
                           in_=x_in.ap().rearrange("(n p) f -> p n f", p=128))
    c.inc(ld, "dh", 16)
    c.wait(nc.sync, "dh")
    c.barrier()
    emit_outproj_residual(c, z_in.ap(), w_ot.ap(), x_sb, "c1")
    emit_ln_transpose(c, None, xln2.ap(), "c1l", x_sb=x_sb)
    emit_ffn_l1(c, xln2.ap(), w1t.ap(), b1cm.ap(), w2t.ap(), b2row.ap(),
                wq1.ap(), wq2.ap(), x_sb, "c1")
    emit_store_x(c, x_sb, x_out.ap())
    c.close()
    return nc



def _gather_xln(outs, key):
    xgs = []
    for b in range(2):
        xg = np.concatenate([outs[4 * b + r][key] for r in range(4)], axis=0)
        xgs.append(np.ascontiguousarray(xg))
    return [xgs[cid // 4] for cid in range(NCORE)]


def _a2a_z(outs, key):
    res = []
    for cid in range(NCORE):
        b, qd = cid // 4, cid % 4
        z = np.concatenate(
            [outs[4 * b + r][key][qd * 256:(qd + 1) * 256, :] for r in range(4)],
            axis=0)
        res.append(np.ascontiguousarray(z))
    return res


def kernel_unfused(**inputs):
    inputs = {k: np.ascontiguousarray(v) for k, v in inputs.items()}
    per_core, shared = host_slices(inputs)
    cores = list(range(NCORE))

    ncA = build_A()
    rA = run_bass_kernel_spmd(ncA, [{"x_in": pc["x0"]} for pc in per_core],
                              core_ids=cores)
    xg = _gather_xln(rA.results, "xln")

    ncB0 = build_B(rope=True, scale=0.125)
    inB0 = [{"xg": bfc(xg[i]), "wq": per_core[i]["wq0"], "wk": per_core[i]["wk0"],
             "wv": per_core[i]["wv0"], "rcos": shared["rope_cos"],
             "rsin": shared["rope_sin"], "p32": shared["p32"]} for i in cores]
    rB0 = run_bass_kernel_spmd(ncB0, inB0, core_ids=cores)
    z0 = _a2a_z(rB0.results, "zo")

    ncC0 = build_C0()
    inC0 = [{"x_in": per_core[i]["x0"], "z_in": z0[i], "w_ot": shared["w2t"],
             "wexpT": shared["wexpT"], "bexp_cm": shared["bexp_cm"],
             "bfw": shared["bfw"], "up_bP": shared["up_bP"],
             "wconP": shared["wconP"], "brow2": shared["brow2"]} for i in cores]
    rC0 = run_bass_kernel_spmd(ncC0, inC0, core_ids=cores)
    xg1 = _gather_xln(rC0.results, "xln_out")

    ncB1 = build_B(rope=False, scale=1.0)
    inB1 = [{"xg": bfc(xg1[i]), "wq": per_core[i]["wq1"], "wk": per_core[i]["wk1"],
             "wv": per_core[i]["wv1"]} for i in cores]
    rB1 = run_bass_kernel_spmd(ncB1, inB1, core_ids=cores)
    z1 = _a2a_z(rB1.results, "zo")

    ncC1 = build_C1()
    inC1 = [{"x_in": rC0.results[i]["x_out"], "z_in": z1[i], "w_ot": shared["ot"],
             "w1t": shared["w1t"], "b1cm": shared["b1cm"],
             "w2t": shared["w2t_ffn"], "b2row": shared["b2row"]} for i in cores]
    rC1 = run_bass_kernel_spmd(ncC1, inC1, core_ids=cores)

    out = np.empty((B, T, E), dtype=np.float32)
    for cid in cores:
        b, qd = cid // 4, cid % 4
        out[b, qd * TO:(qd + 1) * TO, :] = rC1.results[cid]["x_out"]
    return out



def emit_collective(c, kind, in_ap, out_ap, tag):
    nc = c.nc
    cc = nc.gpsimd.collective_compute(
        kind, mybir.AluOpType.bypass,
        replica_groups=[[0, 1, 2, 3], [4, 5, 6, 7]],
        ins=[in_ap], outs=[out_ap])
    c.inc(cc, "cc")
    c.wait(nc.gpsimd, "cc")
    c.barrier()


def build_fused():
    nc = bass.Bass("TRN2", target_bir_lowering=False)
    t_in = {}
    def inp(name, shape, dtype=F32):
        t_in[name] = nc.dram_tensor(name, shape, dtype, kind="ExternalInput")
        return t_in[name].ap()
    x0 = inp("x0", [TO, E])
    wq0 = inp("wq0", [E, 256]); wk0 = inp("wk0", [E, 256]); wv0 = inp("wv0", [E, 256])
    rcos = inp("rcos", [128, T]); rsin = inp("rsin", [128, T]); p32 = inp("p32", [128, 128])
    w2t = inp("w2t", [E, E])
    wexpT = inp("wexpT", [DH, E]); bexp_cm = inp("bexp_cm", [128, 32])
    bfw = inp("bfw", [24, 4, 2048]); up_bP = inp("up_bP", [128, 32])
    wconP = inp("wconP", [DH, E]); brow2 = inp("brow2", [2, E])
    wq1 = inp("wq1", [E, 256]); wk1 = inp("wk1", [E, 256]); wv1 = inp("wv1", [E, 256])
    ot = inp("ot", [E, E])
    w1t = inp("w1t", [E, DH]); b1cm = inp("b1cm", [128, 32])
    w2t_ffn = inp("w2t_ffn", [DH, E]); b2row = inp("b2row", [1, E])
    x_out = nc.dram_tensor("x_out", [TO, E], F32, kind="ExternalOutput")

    def internal(name, shape, dtype):
        return nc.dram_tensor(name, shape, dtype).ap()
    xln_a = internal("xln_a", [E, TO], BF16)
    xg_a = internal("xg_a", [4 * E, TO], BF16)
    q0d = internal("q0d", [256, T], BF16)
    k0d = internal("k0d", [256, T], BF16)
    v0d = internal("v0d", [T, 256], BF16)
    zsh0 = internal("zsh0", [E, TO], BF16)
    zg0 = internal("zg0", [4 * E, TO], BF16)
    xln2 = internal("xln2", [E, TO], BF16)
    w1e = internal("w1e", [32, E, 128], BF16)
    b1e = internal("b1e", [1, DH], F32)
    w2e = internal("w2e", [DH, E], BF16)
    relay = internal("relay", [32, 128, 512], BF16)
    xln_b = internal("xln_b", [E, TO], BF16)
    xg_b = internal("xg_b", [4 * E, TO], BF16)
    q1d = internal("q1d", [256, T], BF16)
    k1d = internal("k1d", [256, T], BF16)
    v1d = internal("v1d", [T, 256], BF16)
    zsh1 = internal("zsh1", [E, TO], BF16)
    zg1 = internal("zg1", [4 * E, TO], BF16)
    xln3 = internal("xln3", [E, TO], BF16)
    wq1i = internal("wq1i", [32, E, 128], BF16)
    wq2i = internal("wq2i", [DH, E], BF16)

    c = Ctx(nc)
    c.sem("cc")
    qd_sv = nc.partition_id() % 4
    x_sb = c.sb("xres", [128, NTT * E], F32)
    ld = nc.sync.dma_start(out=x_sb.ap().rearrange("p (n f) -> p n f", f=E),
                           in_=x0.rearrange("(n p) f -> p n f", p=128))
    c.inc(ld, "dh", 16)
    c.wait(nc.sync, "dh")
    c.barrier()
    emit_ln_transpose(c, None, xln_a, "fa", x_sb=x_sb)
    emit_collective(c, "AllGather", xln_a, xg_a, "ga")
    emit_qkv(c, xg_a, wq0, wk0, wv0, q0d, k0d, v0d, "f0",
             rope_cs=(rcos, rsin, p32))
    emit_attention(c, q0d, k0d, v0d, zsh0, "f0", 0.125)
    emit_collective(c, "AllGather", zsh0, zg0, "za")
    emit_outproj_residual(c, None, w2t, x_sb, "f0", z_gathered=(zg0, qd_sv))
    emit_ln_transpose(c, None, xln2, "f0l", x_sb=x_sb)
    c.push()
    mats = {nm: c.sb(f"mat_{nm}", [128, 32 * 128], BF16)
            for nm in ("M1", "D1", "N1", "N2")}
    _, plans = butterfly_builder_plan(np.zeros((12, 2048, 2, 2), np.float32),
                                     np.zeros((12, 2048, 2, 2), np.float32))
    emit_bfly_matrices(c, bfw, mats, plans, "f0")
    emit_fold_w1(c, wexpT, bexp_cm, mats["M1"], w1e, b1e, "f0")
    emit_fold_w2(c, wconP, mats["N2"], w2e, "f0")
    emit_ffn_l0(c, xln2, w1e, b1e, up_bP, w2e, brow2,
                mats["D1"], mats["N1"], relay, x_sb, "f0")
    c.pop()
    emit_ln_transpose(c, None, xln_b, "fb", x_sb=x_sb)
    emit_collective(c, "AllGather", xln_b, xg_b, "gb")
    emit_qkv(c, xg_b, wq1, wk1, wv1, q1d, k1d, v1d, "f1", rope_cs=None)
    emit_attention(c, q1d, k1d, v1d, zsh1, "f1", 1.0)
    emit_collective(c, "AllGather", zsh1, zg1, "zb")
    emit_outproj_residual(c, None, ot, x_sb, "f1", z_gathered=(zg1, qd_sv))
    emit_ln_transpose(c, None, xln3, "f1l", x_sb=x_sb)
    emit_ffn_l1(c, xln3, w1t, b1cm, w2t_ffn, b2row, wq1i, wq2i, x_sb, "f1")
    emit_store_x(c, x_sb, x_out.ap())
    c.close()
    return nc


def kernel(**inputs):
    inputs = {k: np.ascontiguousarray(v) for k, v in inputs.items()}
    per_core, shared = host_slices(inputs)
    cores = list(range(NCORE))
    nc = build_fused()
    in_maps = []
    for i in cores:
        pc = per_core[i]
        in_maps.append({
            "x0": pc["x0"], "wq0": pc["wq0"], "wk0": pc["wk0"], "wv0": pc["wv0"],
            "rcos": shared["rope_cos"], "rsin": shared["rope_sin"],
            "p32": shared["p32"], "w2t": shared["w2t"],
            "wexpT": shared["wexpT"], "bexp_cm": shared["bexp_cm"],
            "bfw": shared["bfw"], "up_bP": shared["up_bP"],
            "wconP": shared["wconP"], "brow2": shared["brow2"],
            "wq1": pc["wq1"], "wk1": pc["wk1"], "wv1": pc["wv1"],
            "ot": shared["ot"], "w1t": shared["w1t"], "b1cm": shared["b1cm"],
            "w2t_ffn": shared["w2t_ffn"], "b2row": shared["b2row"],
        })
    res = run_bass_kernel_spmd(nc, in_maps, core_ids=cores)
    out = np.empty((B, T, E), dtype=np.float32)
    for cid in cores:
        b, qd = cid // 4, cid % 4
        out[b, qd * TO:(qd + 1) * TO, :] = res.results[cid]["x_out"]
    return out


# version 16
# speedup vs baseline: 1.2005x; 1.0518x over previous
import math
import numpy as np
import ml_dtypes
from contextlib import ExitStack

import concourse.bass as bass
import concourse.mybir as mybir
from concourse.bass_utils import run_bass_kernel_spmd

F32 = mybir.dt.float32
BF16 = mybir.dt.bfloat16
AF = mybir.ActivationFunctionType
ALU = mybir.AluOpType

B, T, E, H, D = 2, 8192, 1024, 16, 64
NCORE = 8
TO = T // 4
HO = H // 4
NTT = TO // 128
NCH = E // 128
DH = 4096
EPS = 1e-5


class Ctx:

    def __init__(self, nc):
        self.nc = nc
        self.es = ExitStack()
        self.sem_es = ExitStack()
        self._scopes = []
        self.sems = {}
        self.counts = {}

    def push(self):
        self._scopes.append(self.es)
        self.es = ExitStack()

    def pop(self):
        self.es.close()
        self.es = self._scopes.pop()

    def sb(self, name, shape, dtype):
        return self.es.enter_context(self.nc.sbuf_tensor(name, shape, dtype))

    def ps(self, name, shape, dtype=F32):
        return self.es.enter_context(self.nc.psum_tensor(name, shape, dtype))

    def sem(self, name):
        if name not in self.sems:
            self.sems[name] = self.sem_es.enter_context(self.nc.semaphore(name))
            self.counts[name] = 0
        return self.sems[name]

    def inc(self, inst, name, n=1):
        inst.then_inc(self.sem(name), n)
        self.counts[name] += n
        return self.counts[name]

    def wait(self, eng, name, val=None):
        if val is None:
            val = self.counts[name]
        eng.wait_ge(self.sem(name), val)

    def barrier(self):
        self.nc.all_engine_barrier()

    def close(self):
        self.es.close()
        self.sem_es.close()


def bfc(x):
    return np.ascontiguousarray(x).astype(ml_dtypes.bfloat16)



def emit_ln_transpose(c, x_src, xln_dst, tag, x_sb=None, gather=None):
    nc = c.nc
    c.push()
    own_x = x_sb is None
    if own_x:
        x_sb = c.sb(f"x_{tag}", [128, NTT * E], F32)
    xln_tm = c.sb(f"xlntm_{tag}", [128, NTT * E], BF16)
    stats = c.sb(f"stats_{tag}", [128, NTT * 12], F32)
    mv = c.sb(f"mv_{tag}", [128, NTT * 2], F32)
    rstd = c.sb(f"rstd_{tag}", [128, NTT], F32)
    sid = c.sb(f"id_{tag}", [128, 128], BF16)
    xfm = c.sb(f"xfm_{tag}", [128, NCH * TO], BF16)
    ptr = c.ps(f"ptr_{tag}", [128, 2 * 1024], BF16)

    mi = nc.gpsimd.memset(sid[:], 1.0)
    c.inc(mi, "g")
    c.wait(nc.gpsimd, "g")
    ai = nc.gpsimd.affine_select(out=sid[:], in_=sid[:], compare_op=ALU.is_equal,
                                 fill=0.0, base=0, pattern=[[-1, 128]],
                                 channel_multiplier=1)
    c.inc(ai, "g")

    if own_x:
        ld = nc.sync.dma_start(
            out=x_sb.ap().rearrange("p (n f) -> p n f", f=E),
            in_=x_src.rearrange("(n p) f -> p n f", p=128))
        c.inc(ld, "dh", 16)
        c.wait(nc.sync, "dh")
        c.wait(nc.vector, "dh")

    for i in range(NTT):
        xt = x_sb[:, i * E:(i + 1) * E]
        st = stats[:, i * 12:(i + 1) * 12]
        s1 = nc.vector.bn_stats(out=st[:, 0:6], in_=xt[:, 0:512])
        c.inc(s1, "v")
        s2 = nc.vector.bn_stats(out=st[:, 6:12], in_=xt[:, 512:1024])
        c.inc(s2, "v")
        c.wait(nc.vector, "v")
        ag = nc.vector.bn_aggr(out=mv[:, 2 * i:2 * i + 2], in_=st[:])
        c.inc(ag, "v")
    seps = c.sb(f"eps_{tag}", [128, 1], F32)
    me = nc.gpsimd.memset(seps[:], EPS)
    c.inc(me, "g")
    c.wait(nc.scalar, "g")
    c.wait(nc.scalar, "v")
    lg = nc.scalar.activation(rstd.ap().rearrange("p (n one) -> p n one", one=1),
                              mv.ap().rearrange("p (n two) -> p n two", two=2)[:, :, 1:2],
                              AF.Ln, bias=seps[:])
    c.inc(lg, "s")
    c.wait(nc.scalar, "s")
    ex = nc.scalar.activation(rstd[:], rstd[:], AF.Exp, scale=-0.5)
    c.inc(ex, "s")
    c.wait(nc.vector, "s")
    for i in range(NTT):
        tsn = nc.vector.tensor_scalar(
            out=xln_tm[:, i * E:(i + 1) * E], in0=x_sb[:, i * E:(i + 1) * E],
            scalar1=mv[:, 2 * i:2 * i + 1], scalar2=rstd[:, i:i + 1],
            op0=ALU.subtract, op1=ALU.mult)
        c.inc(tsn, "v")
    c.wait(nc.tensor, "v")
    c.wait(nc.tensor, "g")
    c.sem("vtr")
    for i in range(NTT):
        par = i % 2
        bank = ptr[:, par * 1024:(par + 1) * 1024]
        if i >= 2:
            c.wait(nc.tensor, "vtr", c.counts["vtr"] - 1)
        for ch in range(NCH):
            tr = nc.tensor.transpose(
                bank[:, ch * 128:(ch + 1) * 128],
                xln_tm[:, i * E + ch * 128: i * E + (ch + 1) * 128], sid[:])
            if ch == NCH - 1:
                c.inc(tr, "tr")
        c.wait(nc.vector, "tr")
        cp = nc.vector.tensor_copy(
            out=xfm.ap().rearrange("p (ch t) -> p ch t", ch=NCH)[:, :, i * 128:(i + 1) * 128],
            in_=bank.rearrange("p (ch t) -> p ch t", ch=NCH))
        c.inc(cp, "vtr")
        if gather is not None and i % 4 == 3:
            g = i // 4
            xlv = xln_dst.rearrange("g (ch p) t -> (g ch) p t", p=128)
            c.wait(nc.sync, "vtr")
            st = nc.sync.dma_start(
                out=xlv[g * NCH:(g + 1) * NCH, :, :].rearrange("c p t -> p c t"),
                in_=xfm.ap().rearrange("p (ch t) -> p ch t", ch=NCH)[
                    :, :, g * 512:(g + 1) * 512])
            nst = c.inc(st, "dh", 16)
            c.wait(nc.gpsimd, "dh", nst)
            cc = nc.gpsimd.collective_compute(
                "AllGather", mybir.AluOpType.bypass,
                replica_groups=[[0, 1, 2, 3], [4, 5, 6, 7]],
                ins=[xln_dst[g, :, :].squeeze()],
                outs=[gather[g, :, :].squeeze()])
            c.inc(cc, "cc")
    if gather is None:
        c.wait(nc.sync, "vtr")
        stfm = nc.sync.dma_start(out=xln_dst.rearrange("(ch p) t -> p ch t", p=128),
                                 in_=xfm.ap().rearrange("p (ch t) -> p ch t", ch=NCH))
        c.inc(stfm, "dh", 16)
    c.wait(nc.sync, "dh")
    c.barrier()
    c.pop()
    return x_sb


def emit_qkv(c, xln_full, wq, wk, wv, q_dst, k_dst, v_dst, tag, rope_cs=None,
             cc_base=0):
    nc = c.nc
    c.push()
    sw = c.sb(f"w_{tag}", [128, NCH * 3 * 256], BF16)
    sx = c.sb(f"xg_{tag}", [128, 2 * NCH * 512], BF16)
    sqk = c.sb(f"qk_{tag}", [128, 2 * 512], BF16)
    sv = c.sb(f"v_{tag}", [128, 1024], BF16)
    pq = c.ps(f"pq_{tag}", [128, 2 * 512], F32)
    pk = c.ps(f"pk_{tag}", [128, 2 * 512], F32)
    pv = c.ps(f"pv_{tag}", [128, 2 * 512], F32)
    if rope_cs is not None:
        scos = c.sb(f"cos_{tag}", [128, T], BF16)
        ssin = c.sb(f"sin_{tag}", [128, T], BF16)
        sqr = c.sb(f"qr_{tag}", [128, 2048], BF16)
        sp32 = c.sb(f"p32_{tag}", [128, 128], BF16)
        prope = c.ps(f"pr_{tag}", [128, 1024], F32)
        c.inc(nc.gpsimd.dma_start(out=scos[:], in_=rope_cs[0]), "dg", 16)
        c.inc(nc.gpsimd.dma_start(out=ssin[:], in_=rope_cs[1]), "dg", 16)
        c.inc(nc.gpsimd.dma_start(out=sp32[:], in_=rope_cs[2]), "dg", 16)
        c.sem("rp")
    else:
        scos = ssin = sqr = sp32 = prope = None
    for wi, w in enumerate((wq, wk, wv)):
        ld = nc.gpsimd.dma_start(
            out=sw.ap().rearrange("p (ch w s) -> p ch w s", ch=NCH, w=3)[:, :, wi, :],
            in_=w.rearrange("(ch p) s -> p ch s", p=128))
        c.inc(ld, "dg", 16)
    c.wait(nc.tensor, "dg")

    c.sem("mmq"); c.sem("v"); c.sem("dh")
    qfree = kfree = vfree = 0
    xgv = xln_full.rearrange("g (ch p) t -> (g ch) p t", p=128)
    for idx in range(16):
        g, r = idx // 4, idx % 4
        tt = r * 4 + g
        buf = idx % 2
        xb = sx[:, buf * NCH * 512:(buf + 1) * NCH * 512]
        if idx >= 2:
            c.wait(nc.sync, "mmq", idx - 1)
        if r == 0:
            c.wait(nc.sync, "cc", cc_base + g + 1)
        lxs = "lx%d" % buf
        base = (g * 4 + r) * NCH
        ld = nc.sync.dma_start(
            out=xb.rearrange("p (ch t) -> p ch t", ch=NCH),
            in_=xgv[base:base + NCH, :, :].rearrange("c p t -> p c t"))
        nld = c.inc(ld, lxs, 16)
        c.wait(nc.tensor, lxs, nld)
        if idx >= 1:
            c.wait(nc.tensor, "v", qfree)
        for ch in range(NCH):
            st, sp = (ch == 0), (ch == NCH - 1)
            xt = xb[:, ch * 512:(ch + 1) * 512]
            wch = sw[:, ch * 768:(ch + 1) * 768]
            for half in range(2):
                nc.tensor.matmul(pq[:, half * 512:half * 512 + 512],
                                 wch[:, half * 128:half * 128 + 128], xt,
                                 start=st, stop=sp)
        if idx >= 1:
            c.wait(nc.tensor, "v", kfree)
        for ch in range(NCH):
            st, sp = (ch == 0), (ch == NCH - 1)
            xt = xb[:, ch * 512:(ch + 1) * 512]
            wch = sw[:, ch * 768:(ch + 1) * 768]
            for half in range(2):
                nc.tensor.matmul(pk[:, half * 512:half * 512 + 512],
                                 wch[:, 256 + half * 128:256 + half * 128 + 128], xt,
                                 start=st, stop=sp)
        if idx >= 1:
            c.wait(nc.tensor, "v", vfree)
        for sub in range(4):
            for ch in range(NCH):
                st, sp = (ch == 0), (ch == NCH - 1)
                xt = xb[:, ch * 512:(ch + 1) * 512]
                wch = sw[:, ch * 768:(ch + 1) * 768]
                mm = nc.tensor.matmul(
                    pv[:, sub * 256:(sub + 1) * 256],
                    xt[:, sub * 128:(sub + 1) * 128], wch[:, 512:768],
                    start=st, stop=sp)
        c.inc(mm, "mmq")
        gt = tt * 512
        c.wait(nc.vector, "mmq")
        if idx >= 1:
            c.wait(nc.vector, "dh")
        for half in range(2):
            cp = nc.vector.tensor_copy(out=sqk[:, half * 512:half * 512 + 512],
                                       in_=pq[:, half * 512:half * 512 + 512])
            c.inc(cp, "v")
        qfree = c.counts["v"]
        if rope_cs is not None:
            _emit_rope(c, sqk, sqr, scos, ssin, sp32, prope, gt)
        c.wait(nc.sync, "v")
        st1 = nc.sync.dma_start(
            out=q_dst.rearrange("(h p) t -> p h t", p=128)[:, :, gt:gt + 512],
            in_=sqk.ap().rearrange("p (h t) -> p h t", h=2))
        c.inc(st1, "dh", 16)
        c.wait(nc.vector, "dh")
        for half in range(2):
            cp = nc.vector.tensor_copy(out=sqk[:, half * 512:half * 512 + 512],
                                       in_=pk[:, half * 512:half * 512 + 512])
            c.inc(cp, "v")
        kfree = c.counts["v"]
        if rope_cs is not None:
            _emit_rope(c, sqk, sqr, scos, ssin, sp32, prope, gt)
        c.wait(nc.sync, "v")
        st2 = nc.sync.dma_start(
            out=k_dst.rearrange("(h p) t -> p h t", p=128)[:, :, gt:gt + 512],
            in_=sqk.ap().rearrange("p (h t) -> p h t", h=2))
        c.inc(st2, "dh", 16)
        if idx >= 1:
            c.wait(nc.vector, "dh")
        cpv = nc.vector.tensor_copy(out=sv[:], in_=pv[:])
        vfree = c.inc(cpv, "v")
        c.wait(nc.sync, "v")
        stv = nc.sync.dma_start(
            out=v_dst.rearrange("(n p) d -> p n d", p=128)[:, 4 * tt:4 * tt + 4, :],
            in_=sv.ap().rearrange("p (s d) -> p s d", s=4))
        c.inc(stv, "dh", 16)
    c.wait(nc.sync, "dh")
    c.barrier()
    c.pop()


def _emit_rope(c, sqk, sqr, scos, ssin, p32_sb, prope, gt):
    nc = c.nc
    cosw = scos[:, gt:gt + 512]
    sinw = ssin[:, gt:gt + 512]
    c.wait(nc.vector, "v")
    for half in range(2):
        q = sqk[:, half * 512:half * 512 + 512]
        t1 = sqr[:, half * 512:half * 512 + 512]
        u = sqr[:, 1024 + half * 512:1024 + half * 512 + 512]
        c.inc(nc.vector.tensor_mul(out=t1, in0=q, in1=cosw), "v")
        c.inc(nc.vector.tensor_mul(out=u, in0=q, in1=sinw), "v")
    c.wait(nc.tensor, "v")
    c.wait(nc.tensor, "v", c.counts.get("_ropefree", 0))
    for half in range(2):
        u = sqr[:, 1024 + half * 512:1024 + half * 512 + 512]
        mm = nc.tensor.matmul(prope[:, half * 512:half * 512 + 512],
                              p32_sb[:], u, start=True, stop=True)
        c.inc(mm, "rp")
    c.wait(nc.vector, "rp")
    for half in range(2):
        q = sqk[:, half * 512:half * 512 + 512]
        t1 = sqr[:, half * 512:half * 512 + 512]
        ad = nc.vector.tensor_add(out=q, in0=t1,
                                  in1=prope[:, half * 512:half * 512 + 512])
        c.inc(ad, "v")
    c.counts["_ropefree"] = c.counts["v"]


def emit_attention(c, q_src, k_src, v_src, z_dst, tag, scale, zg_out=None):
    nc = c.nc
    c.push()
    NKT = T // 128
    sk = [c.sb(f"ak{x}_{tag}", [128, T], BF16) for x in range(2)]
    sq = [c.sb(f"aq{x}_{tag}", [128, T], BF16) for x in range(2)]
    sva = [c.sb(f"av{x}a_{tag}", [128, NKT * 65], BF16) for x in range(2)]
    svb = [c.sb(f"av{x}b_{tag}", [128, NKT * 65], BF16) for x in range(2)]
    sattn = [c.sb(f"aat{x}_{tag}", [128, 3 * 1024], BF16) for x in range(2)]
    srec = [c.sb(f"arc{x}_{tag}", [1, 2 * 1024], BF16) for x in range(2)]
    srecf = [c.sb(f"arf{x}_{tag}", [1, 1024], F32) for x in range(2)]
    sbc = [c.sb(f"abc{x}_{tag}", [128, 1024], BF16) for x in range(2)]
    szr = [c.sb(f"azr{x}_{tag}", [128, 2 * 1024], BF16) for x in range(2)]
    sones1 = c.sb(f"aon_{tag}", [1, 64], BF16)
    smask = c.sb(f"amk_{tag}", [128, 4 * 512], BF16)
    ps = [c.ps(f"aps{x}_{tag}", [128, 2 * 512], F32) for x in range(2)]
    pz = [c.ps(f"apz{x}_{tag}", [128, 2 * 512], F32) for x in range(2)]

    c.inc(nc.gpsimd.memset(sones1[:], 1.0), "g")
    for m in range(4):
        c.inc(nc.gpsimd.memset(smask[:, m * 512:(m + 1) * 512], 1.0), "g")
    c.wait(nc.gpsimd, "g")
    for m in range(4):
        am = nc.gpsimd.affine_select(
            out=smask[:, m * 512:(m + 1) * 512], in_=smask[:, m * 512:(m + 1) * 512],
            compare_op=ALU.is_ge, fill=0.0, base=-128 * m,
            pattern=[[1, 512]], channel_multiplier=-1)
        c.inc(am, "g")
    for x in range(2):
        c.inc(nc.sync.dma_start(out=sk[x][:], in_=k_src[x * 128:(x + 1) * 128, :]),
              "dh", 16)
        c.inc(nc.sync.dma_start(out=sq[x][:], in_=q_src[x * 128:(x + 1) * 128, :]),
              "dh", 16)
        for hh, sv in ((0, sva[x]), (1, svb[x])):
            ld = nc.sync.dma_start(
                out=sv.ap().rearrange("p (n c) -> p n c", c=65)[:, :, 0:64],
                in_=v_src.rearrange("(n p) c -> p n c", p=128)[
                    :, :, x * 128 + hh * 64:x * 128 + hh * 64 + 64])
            c.inc(ld, "dh", 16)
            c.inc(nc.gpsimd.memset(
                sv.ap().rearrange("p (n c) -> p n c", c=65)[:, :, 64:65], 1.0), "g")
    c.wait(nc.sync, "dh")
    c.wait(nc.gpsimd, "g")
    c.barrier()

    scs = ["scA", "scB"]; exs = ["exA", "exB"]; avs = ["avA", "avB"]
    mks = ["mkA", "mkB"]; rcs = ["rcA", "rcB"]; bcs = ["bcA", "bcB"]
    zes = ["zeA", "zeB"]
    cps = ["cpA", "cpB"]
    for n_ in scs + exs + avs + mks + rcs + bcs + zes + cps + ["zst0", "zst1"]:
        c.sem(n_)
    ex0 = [c.counts[exs[0]], c.counts[exs[1]]]
    mk0 = [c.counts[mks[0]], c.counts[mks[1]]]
    rc_done = [{}, {}]

    def e_scores(x, i, jq, kt):
        if i >= 1:
            c.wait(nc.tensor, exs[x], c.counts[exs[x]])
        if kt == 0:
            c.wait(nc.tensor, cps[x], c.counts[cps[x]])
        qsl = sq[x][:, jq * 512:(jq + 1) * 512]
        ksl = sk[x][:, kt * 128:(kt + 1) * 128]
        nc.tensor.matmul(ps[x][:, 0:512], ksl[0:64, :], qsl[0:64, :],
                         start=True, stop=True)
        mmb = nc.tensor.matmul(ps[x][:, 512:1024], ksl[64:128, :], qsl[64:128, :],
                               start=True, stop=True)
        c.inc(mmb, scs[x])

    def e_exp(x, i, jq, kt):
        par = i % 3
        c.wait(nc.scalar, scs[x])
        e = nc.scalar.activation(sattn[x][:, par * 1024:(par + 1) * 1024],
                                 ps[x][:], AF.Exp, scale=scale)
        c.inc(e, exs[x])

    def e_mask(x, i, jq, kt):
        par = i % 3
        a0 = par * 1024
        m = kt - 4 * jq
        eng = nc.vector if x == 0 else nc.gpsimd
        c.wait(eng, exs[x])
        msl = smask[:, m * 512:(m + 1) * 512]
        eng.tensor_mul(out=sattn[x][:, a0:a0 + 512],
                       in0=sattn[x][:, a0:a0 + 512], in1=msl)
        mb = eng.tensor_mul(out=sattn[x][:, a0 + 512:a0 + 1024],
                            in0=sattn[x][:, a0 + 512:a0 + 1024], in1=msl)
        c.inc(mb, mks[x])

    def e_av(x, i, jq, kt, dcount):
        nk = 4 * (jq + 1)
        if kt == 0:
            c.wait(nc.tensor, zes[x], c.counts[zes[x]])
        if kt >= 4 * jq:
            c.wait(nc.tensor, mks[x], mk0[x] + dcount)
        else:
            c.wait(nc.tensor, exs[x], ex0[x] + i + 1)
        par = i % 3
        a0 = par * 1024
        st, sp = (kt == 0), (kt == nk - 1)
        nc.tensor.matmul(pz[x][0:65, 0:512], sva[x][:, kt * 65:(kt + 1) * 65],
                         sattn[x][:, a0:a0 + 512], start=st, stop=sp,
                         skip_group_check=True)
        mm = nc.tensor.matmul(pz[x][0:65, 512:1024], svb[x][:, kt * 65:(kt + 1) * 65],
                              sattn[x][:, a0 + 512:a0 + 1024], start=st, stop=sp,
                              skip_group_check=True)
        c.inc(mm, avs[x])

    def early_evac(x, jq):
        sl = (jq % 2) * 1024
        c.wait(nc.scalar, avs[x])
        c.wait(nc.scalar, rcs[x])
        cpa = nc.scalar.activation(srecf[x][0:1, :], pz[x][64:65, :],
                                   AF.Identity)
        c.inc(cpa, rcs[x])
        c.wait(nc.vector, avs[x])
        c.wait(nc.vector, "dh")
        zc = nc.vector.tensor_copy(out=szr[x][0:64, sl:sl + 1024],
                                   in_=pz[x][0:64, 0:1024])
        c.inc(zc, zes[x])
        c.wait(nc.vector, rcs[x])
        rc = nc.vector.reciprocal(out=srecf[x][0:1, :], in_=srecf[x][0:1, :])
        c.inc(rc, "v")
        c.wait(nc.vector, "v")
        rc2 = nc.vector.tensor_copy(out=srec[x][0:1, sl:sl + 1024],
                                    in_=srecf[x][0:1, :])
        c.inc(rc2, rcs[x])

    def late_evac(x, jq):
        sl = (jq % 2) * 1024
        c.wait(nc.tensor, rcs[x], rc_done[x][jq])
        c.wait(nc.tensor, exs[x], c.counts[exs[x]])
        c.wait(nc.tensor, cps[x], c.counts[cps[x]])
        nc.tensor.matmul(ps[x][0:64, 0:512], sones1[:], srec[x][0:1, sl:sl + 512],
                         start=True, stop=True)
        bc = nc.tensor.matmul(ps[x][0:64, 512:1024], sones1[:],
                              srec[x][0:1, sl + 512:sl + 1024],
                              start=True, stop=True)
        c.inc(bc, bcs[x])
        c.wait(nc.scalar, bcs[x])
        c.wait(nc.scalar, "zst%d" % x, c.counts["zst%d" % x])
        cpc = nc.scalar.activation(sbc[x][0:64, :], ps[x][0:64, :], AF.Identity)
        c.inc(cpc, cps[x])
        c.wait(nc.vector, cps[x])
        nc.vector.tensor_mul(out=szr[x][0:64, sl:sl + 512],
                             in0=szr[x][0:64, sl:sl + 512],
                             in1=sbc[x][0:64, 0:512])
        zm = nc.vector.tensor_mul(out=szr[x][0:64, sl + 512:sl + 1024],
                                  in0=szr[x][0:64, sl + 512:sl + 1024],
                                  in1=sbc[x][0:64, 512:1024])
        c.inc(zm, "zst%d" % x)
        c.wait(nc.sync, "zst%d" % x, c.counts["zst%d" % x])
        r0 = (jq // 4) * 256 + x * 128
        c0 = (jq % 4) * 512
        st1 = nc.sync.dma_start(out=z_dst[r0:r0 + 64, c0:c0 + 512],
                                in_=szr[x][0:64, sl:sl + 512])
        c.inc(st1, "dh", 16)
        st2 = nc.sync.dma_start(out=z_dst[r0 + 64:r0 + 128, c0:c0 + 512],
                                in_=szr[x][0:64, sl + 512:sl + 1024])
        c.inc(st2, "dh", 16)

    pend = [[], []]
    ndiag = [0, 0]
    stepx = [0, 0]

    def substep(x, jq, kt):
        i = stepx[x]
        e_scores(x, i, jq, kt)
        e_exp(x, i, jq, kt)
        if kt >= 4 * jq:
            ndiag[x] += 1
            e_mask(x, i, jq, kt)
        pend[x].append((i, jq, kt, ndiag[x]))
        if len(pend[x]) > 2:
            e_av(x, *pend[x].pop(0))
        stepx[x] += 1

    def flush(x, jq):
        while pend[x]:
            e_av(x, *pend[x].pop(0))
        early_evac(x, jq)
        rc_done[x][jq] = c.counts[rcs[x]]

    def issue_zchunk(r):
        c.wait(nc.gpsimd, "dh", c.counts["dh"])
        cc = nc.gpsimd.collective_compute(
            "AllGather", mybir.AluOpType.bypass,
            replica_groups=[[0, 1, 2, 3], [4, 5, 6, 7]],
            ins=[z_dst[r * 256:(r + 1) * 256, :]],
            outs=[zg_out[r, :, :].squeeze()])
        c.inc(cc, "cc")

    LAG = 4
    for jq in range(16):
        nk = 4 * (jq + 1)
        for kt in range(nk + LAG):
            if kt < nk:
                if kt == 0 and jq >= 2:
                    late_evac(0, jq - 2)
                if kt == 2 and jq >= 1:
                    flush(1, jq - 1)
                substep(0, jq, kt)
            if kt == nk + 2:
                flush(0, jq)
            if kt >= LAG:
                if kt - LAG == 0 and jq >= 2:
                    late_evac(1, jq - 2)
                substep(1, jq, kt - LAG)
                if (zg_out is not None and kt == LAG and jq >= 5
                        and (jq - 5) % 4 == 0):
                    issue_zchunk((jq - 5) // 4)
    flush(1, 15)
    for jq in (14, 15):
        late_evac(0, jq)
        late_evac(1, jq)
    if zg_out is not None:
        issue_zchunk(3)
    c.wait(nc.sync, "dh")
    c.barrier()
    c.pop()


def emit_outproj_residual(c, z_own, w_ot, x_sb, tag, z_gathered=None):
    nc = c.nc
    c.push()
    sz = c.sb(f"oz_{tag}", [128, NCH * TO], BF16)
    sw = c.sb(f"ow_{tag}", [128, NCH * E], BF16)
    pp = c.ps(f"op_{tag}", [128, 2 * 512], F32)
    if z_gathered is None:
        ld = nc.sync.dma_start(out=sz.ap().rearrange("p (a t) -> p a t", a=NCH),
                               in_=z_own.rearrange("(a p) t -> p a t", p=128))
        c.inc(ld, "dh", 16)
    else:
        zg, qd, cc_need = z_gathered
        zgv = zg.rearrange("c (ch p) t -> (c ch) p t", p=128)
        c.wait(nc.sync, "cc", cc_need)
        for ch in range(NCH):
            ld = nc.sync.dma_start(
                out=sz[:, ch * TO:(ch + 1) * TO],
                in_=zgv[bass.ts(qd * NCH + ch, 1), :, :].squeeze())
            c.inc(ld, "dh", 16)
    ld = nc.gpsimd.dma_start(out=sw.ap().rearrange("p (a f) -> p a f", a=NCH),
                             in_=w_ot.rearrange("(a p) f -> p a f", p=128))
    c.inc(ld, "dg", 16)
    c.wait(nc.tensor, "dh")
    c.wait(nc.tensor, "dg")
    c.sem("oev")
    for ts in range(NTT):
        for fh in range(2):
            sl = (ts * 2 + fh) % 2
            if ts * 2 + fh >= 2:
                c.wait(nc.tensor, "oev", c.counts["oev"] - 1)
            for a in range(NCH):
                mm = nc.tensor.matmul(
                    pp[:, sl * 512:sl * 512 + 512],
                    sz[:, a * TO + ts * 128:a * TO + (ts + 1) * 128],
                    sw[:, a * E + fh * 512:a * E + fh * 512 + 512],
                    start=(a == 0), stop=(a == NCH - 1))
            c.inc(mm, "omm")
            c.wait(nc.vector, "omm")
            ad = nc.vector.tensor_add(
                out=x_sb[:, ts * E + fh * 512:ts * E + fh * 512 + 512],
                in0=x_sb[:, ts * E + fh * 512:ts * E + fh * 512 + 512],
                in1=pp[:, sl * 512:sl * 512 + 512])
            c.inc(ad, "oev")
    c.barrier()
    c.pop()


def emit_store_x(c, x_sb, x_dst):
    nc = c.nc
    st = nc.sync.dma_start(out=x_dst.rearrange("(n p) f -> p n f", p=128),
                           in_=x_sb.ap().rearrange("p (n f) -> p n f", f=E))
    c.inc(st, "dh", 16)
    c.wait(nc.sync, "dh")
    c.barrier()



def build_A():
    nc = bass.Bass("TRN2", target_bir_lowering=False)
    x_in = nc.dram_tensor("x_in", [TO, E], F32, kind="ExternalInput")
    xln = nc.dram_tensor("xln", [E, TO], BF16, kind="ExternalOutput")
    c = Ctx(nc)
    emit_ln_transpose(c, x_in.ap(), xln.ap(), "a")
    c.close()
    return nc


def build_B(rope, scale):
    nc = bass.Bass("TRN2", target_bir_lowering=False)
    xg = nc.dram_tensor("xg", [4 * E, TO], BF16, kind="ExternalInput")
    wq = nc.dram_tensor("wq", [E, 256], F32, kind="ExternalInput")
    wk = nc.dram_tensor("wk", [E, 256], F32, kind="ExternalInput")
    wv = nc.dram_tensor("wv", [E, 256], F32, kind="ExternalInput")
    zo = nc.dram_tensor("zo", [E, TO], BF16, kind="ExternalOutput")
    qd_ = nc.dram_tensor("q_i", [256, T], BF16)
    kd_ = nc.dram_tensor("k_i", [256, T], BF16)
    vd_ = nc.dram_tensor("v_i", [T, 256], BF16)
    rope_cs = None
    if rope:
        rc = nc.dram_tensor("rcos", [128, T], F32, kind="ExternalInput")
        rs = nc.dram_tensor("rsin", [128, T], F32, kind="ExternalInput")
        p32 = nc.dram_tensor("p32", [128, 128], F32, kind="ExternalInput")
        rope_cs = (rc.ap(), rs.ap(), p32.ap())
    c = Ctx(nc)
    emit_qkv(c, xg.ap(), wq.ap(), wk.ap(), wv.ap(), qd_.ap(), kd_.ap(), vd_.ap(),
             "b", rope_cs=rope_cs)
    emit_attention(c, qd_.ap(), kd_.ap(), vd_.ap(), zo.ap(), "b", scale)
    c.close()
    return nc


def build_Cpart():
    nc = bass.Bass("TRN2", target_bir_lowering=False)
    x_in = nc.dram_tensor("x_in", [TO, E], F32, kind="ExternalInput")
    z_in = nc.dram_tensor("z_in", [E, TO], BF16, kind="ExternalInput")
    w_ot = nc.dram_tensor("w_ot", [E, E], F32, kind="ExternalInput")
    x_out = nc.dram_tensor("x_out", [TO, E], F32, kind="ExternalOutput")
    c = Ctx(nc)
    x_sb = c.sb("xres", [128, NTT * E], F32)
    ld = nc.sync.dma_start(out=x_sb.ap().rearrange("p (n f) -> p n f", f=E),
                           in_=x_in.ap().rearrange("(n p) f -> p n f", p=128))
    c.inc(ld, "dh", 16)
    c.wait(nc.sync, "dh")
    c.barrier()
    emit_outproj_residual(c, z_in.ap(), w_ot.ap(), x_sb, "c")
    emit_store_x(c, x_sb, x_out.ap())
    c.close()
    return nc



def emit_bfly_matrices(c, bfw, mats, plans, tag):
    nc = c.nc
    c.push()
    Y0 = c.sb(f"y0_{tag}", [128, 4096], F32)
    Y1 = c.sb(f"y1_{tag}", [128, 4096], F32)
    tmp1 = c.sb(f"bt1_{tag}", [128, 2048], F32)
    tmp2 = c.sb(f"bt2_{tag}", [128, 2048], F32)
    wrow = [c.sb(f"bw{i}_{tag}", [1, 2 * 2048], BF16) for i in range(2)]
    wt = [c.sb(f"bwt{i}_{tag}", [128, 2 * 2048], F32) for i in range(2)]
    sone = c.sb(f"bon_{tag}", [1, 128], BF16)
    pwb = c.ps(f"bpw_{tag}", [128, 2 * 2048], F32)
    c.inc(nc.gpsimd.memset(sone[:], 1.0), "g")
    c.sem("v"); c.sem("s_pe"); c.sem("g"); c.sem("a")
    c.sem("bld0"); c.sem("bld1")

    state = {"pair": 0, "pwb_free": 0, "matcopy": 0}
    wrow_free = [0, 0]
    wt_ready = [0, 0]
    wt_users = [0, 0]

    def bcast_pair(base_k, pair):
        bi = state["pair"] % 2
        c.wait(nc.gpsimd, "s_pe", wrow_free[bi])
        ld = nc.gpsimd.dma_start(
            out=wrow[bi].ap().rearrange("o (w n) -> o w n", w=2),
            in_=bfw[base_k, 2 * pair:2 * pair + 2, :])
        nld = c.inc(ld, "bld%d" % bi, 16)
        c.wait(nc.tensor, "bld%d" % bi, nld)
        c.wait(nc.tensor, "a", state["pwb_free"])
        first = True
        for w2i in range(2):
            for col in range(4):
                a0 = w2i * 2048 + col * 512
                mmb = nc.tensor.matmul(pwb[:, a0:a0 + 512], sone[:],
                                       wrow[bi][0:1, a0:a0 + 512],
                                       start=True, stop=True)
            c.inc(mmb, "s_pe")
            c.wait(nc.scalar, "s_pe")
            if first:
                c.wait(nc.scalar, "v", wt_users[bi])
                first = False
            cpw = nc.scalar.activation(wt[bi][:, w2i * 2048:(w2i + 1) * 2048],
                                       pwb[:, w2i * 2048:(w2i + 1) * 2048],
                                       AF.Identity)
            c.inc(cpw, "a")
        wrow_free[bi] = c.counts["s_pe"]
        wt_ready[bi] = c.counts["a"]
        state["pwb_free"] = c.counts["a"]
        state["pair"] += 1
        return bi

    base = 0
    order = ["M1", "D1", "N1", "N2"]
    for name in order:
        st = plans[name]
        c.wait(nc.gpsimd, "v", c.counts["v"])
        c.inc(nc.gpsimd.memset(Y0[:], 1.0), "g")
        c.wait(nc.gpsimd, "g")
        ai = nc.gpsimd.affine_select(out=Y0[:], in_=Y0[:], compare_op=ALU.is_equal,
                                     fill=0.0, base=0, pattern=[[0, 32], [-1, 128]],
                                     channel_multiplier=1)
        c.inc(ai, "g")
        gp_init = c.counts["g"]
        cur, nxt = Y0, Y1
        for k, s2 in enumerate(st):
            nb = 128 // (2 * s2)

            def vw(t_, half):
                return t_.rearrange("p (blk kb r) -> p blk kb r", blk=32, kb=nb)[
                    :, :, :, half * s2:(half + 1) * s2]

            def cwv(bi_, i2):
                return wt[bi_].ap()[:, i2 * 2048:(i2 + 1) * 2048].rearrange(
                    "p (blk kb s) -> p blk kb s", blk=32, kb=nb)

            tmp1v = tmp1.ap().rearrange("p (blk kb s) -> p blk kb s", blk=32, kb=nb)
            tmp2v = tmp2.ap().rearrange("p (blk kb s) -> p blk kb s", blk=32, kb=nb)
            a_ap = vw(cur.ap(), 0)
            b_ap = vw(cur.ap(), 1)
            for pair in range(2):
                bi = bcast_pair(base + k, pair)
                c.wait(nc.vector, "a", wt_ready[bi])
                if k == 0:
                    c.wait(nc.vector, "g", gp_init)
                    c.wait(nc.vector, "a", state["matcopy"])
                c.wait(nc.vector, "v", c.counts["v"])
                m1_ = nc.vector.tensor_mul(out=tmp1v, in0=a_ap, in1=cwv(bi, 0))
                c.inc(m1_, "v")
                m2_ = nc.vector.tensor_mul(out=tmp2v, in0=b_ap, in1=cwv(bi, 1))
                c.inc(m2_, "v")
                c.wait(nc.vector, "v", c.counts["v"])
                ad = nc.vector.tensor_add(out=vw(nxt.ap(), pair), in0=tmp1v,
                                          in1=tmp2v)
                c.inc(ad, "v")
                wt_users[bi] = c.counts["v"]
            cur, nxt = nxt, cur
        base += len(st)
        c.wait(nc.scalar, "v", c.counts["v"])
        cc = nc.scalar.activation(mats[name][:], cur[:], AF.Identity)
        state["matcopy"] = c.inc(cc, "a")
    c.wait(nc.vector, "a", state["matcopy"])
    c.barrier()
    c.pop()


def emit_fold_w1(c, wexpT, bexp, M1_sb, w1e_dram, b1e_dram, tag):
    nc = c.nc
    c.push()
    swx = c.sb(f"fw_{tag}", [128, 32 * 128], BF16)
    sbe = c.sb(f"fb_{tag}", [128, 32], BF16)
    acc = c.sb(f"fa_{tag}", [128, 4096], BF16)
    brow = c.sb(f"fbr_{tag}", [1, 4096], F32)
    pf = c.ps(f"fp_{tag}", [128, 2 * 512], F32)
    c.sem("v"); c.sem("s_pe"); c.sem("dg"); c.sem("dh")

    ld = nc.gpsimd.dma_start(out=sbe.ap(), in_=bexp)
    c.inc(ld, "dgb", 16)
    c.wait(nc.tensor, "dgb")
    for co in range(32):
        c.wait(nc.tensor, "v", c.counts["v"])
        mm = nc.tensor.matmul(pf[0:1, 0:128], sbe[:, co:co + 1],
                              M1_sb[:, co * 128:(co + 1) * 128],
                              start=True, stop=True)
        c.inc(mm, "s_pe")
        c.wait(nc.vector, "s_pe")
        cp = nc.vector.tensor_copy(out=brow[0:1, co * 128:(co + 1) * 128],
                                   in_=pf[0:1, 0:128])
        c.inc(cp, "v")
    c.wait(nc.sync, "v")
    stb = nc.sync.dma_start(out=b1e_dram, in_=brow.ap())
    c.inc(stb, "dh", 16)
    c.wait(nc.tensor, "v")

    mm_after_ci = 0
    pf_free = [c.counts["v"], c.counts["v"]]
    for ci in range(NCH):
        if ci >= 1:
            c.wait(nc.gpsimd, "s_pe", mm_after_ci)
        ldw = nc.gpsimd.dma_start(
            out=swx.ap().rearrange("p (co q) -> p co q", co=32),
            in_=wexpT[:, ci * 128:(ci + 1) * 128].rearrange("(co p) q -> p co q", p=128))
        nld = c.inc(ldw, "dg", 16)
        c.wait(nc.tensor, "dg", nld)
        c.wait(nc.vector, "dh")
        for co in range(32):
            sl = co % 2
            c.wait(nc.tensor, "v", pf_free[sl])
            mm = nc.tensor.matmul(
                pf[:, sl * 512:sl * 512 + 128],
                swx[:, co * 128:(co + 1) * 128],
                M1_sb[:, co * 128:(co + 1) * 128],
                start=True, stop=True)
            c.inc(mm, "s_pe")
            c.wait(nc.vector, "s_pe")
            cp = nc.vector.tensor_copy(
                out=acc.ap().rearrange("p (c2 hi lo) -> p hi c2 lo",
                                       c2=32, hi=4)[:, :, :, co:co + 1],
                in_=pf[:, sl * 512:sl * 512 + 128].rearrange(
                    "p (hi c2 one) -> p hi c2 one", hi=4, c2=32))
            pf_free[sl] = c.inc(cp, "v")
        mm_after_ci = c.counts["s_pe"]
        c.wait(nc.sync, "v")
        stw = nc.sync.dma_start(
            out=w1e_dram[:, ci * 128:(ci + 1) * 128, :].rearrange("c2 p q -> p c2 q"),
            in_=acc.ap().rearrange("p (c2 q) -> p c2 q", c2=32))
        c.inc(stw, "dh", 16)
    c.wait(nc.sync, "dh")
    c.barrier()
    c.pop()


def emit_fold_w2(c, wconP, N2_sb, w2e_dram, tag):
    nc = c.nc
    c.push()
    swc = c.sb(f"gw_{tag}", [128, 1024], BF16)
    sev = c.sb(f"ge_{tag}", [128, 1024], BF16)
    pg = c.ps(f"gp_{tag}", [128, 2 * 512], F32)
    c.sem("v"); c.sem("s_pe"); c.sem("dg"); c.sem("dh")
    mm_after = 0
    for c2 in range(32):
        if c2 >= 1:
            c.wait(nc.gpsimd, "s_pe", mm_after)
        ldw = nc.gpsimd.dma_start(out=swc.ap(),
                                  in_=wconP[c2 * 128:(c2 + 1) * 128, :])
        nld = c.inc(ldw, "dg", 16)
        c.wait(nc.tensor, "dg", nld)
        c.wait(nc.tensor, "dh")
        for dh_ in range(2):
            mm = nc.tensor.matmul(pg[:, dh_ * 512:(dh_ + 1) * 512],
                                  N2_sb[:, c2 * 128:(c2 + 1) * 128],
                                  swc[:, dh_ * 512:(dh_ + 1) * 512],
                                  start=True, stop=True)
            c.inc(mm, "s_pe")
        mm_after = c.counts["s_pe"]
        c.wait(nc.vector, "s_pe")
        cp = nc.vector.tensor_copy(out=sev[:], in_=pg[:])
        c.inc(cp, "v")
        c.wait(nc.sync, "v")
        for hi in range(4):
            r0 = c2 + 32 * hi
            dst = w2e_dram.rearrange("(co r) d -> r co d", r=128)[r0:r0 + 1, :, :].squeeze()
            st = nc.sync.dma_start(out=dst, in_=sev[hi * 32:(hi + 1) * 32, :])
            c.inc(st, "dh", 16)
    c.wait(nc.sync, "dh")
    c.barrier()
    c.pop()


def emit_ffn_l0(c, xln_fm, w1e_dram, b1e_dram, up_bP, w2e_dram, brow2,
                D1_sb, N1_sb, relay_dram, x_sb, tag):
    nc = c.nc
    c.push()
    sxln = c.sb(f"lx_{tag}", [128, NCH * TO], BF16)
    sw1 = c.sb(f"lw1_{tag}", [128, 2 * NCH * 128], BF16)
    sh1 = c.sb(f"lh1_{tag}", [128, 2 * 512], BF16)
    sh2 = c.sb(f"lh2_{tag}", [128, 2 * 512], BF16)
    sh2o = c.sb(f"lho_{tag}", [128, 2 * 512], BF16)
    sh3 = c.sb(f"lh3_{tag}", [128, 32 * 512], BF16)
    sw2 = c.sb(f"lw2_{tag}", [128, 2 * 512], BF16)
    sb1e = c.sb(f"lb1_{tag}", [128, 32], F32)
    subp = c.sb(f"lbu_{tag}", [128, 32], F32)
    sbr2 = c.sb(f"lbr_{tag}", [2, 1024], BF16)
    sone2 = c.sb(f"lo2_{tag}", [2, 128], BF16)
    psA = c.ps(f"lpa_{tag}", [128, 2 * 512], F32)
    psB = c.ps(f"lpb_{tag}", [128, 2 * 512], F32)
    psY = c.ps(f"lpy_{tag}", [128, 4 * 512], F32)
    c.sem("v"); c.sem("s_pe"); c.sem("s"); c.sem("dg"); c.sem("dh"); c.sem("dgw")

    ld = nc.sync.dma_start(out=sxln.ap().rearrange("p (ch t) -> p ch t", ch=NCH),
                           in_=xln_fm.rearrange("(ch p) t -> p ch t", p=128))
    c.inc(ld, "dh", 16)
    for hi in range(4):
        ldb = nc.gpsimd.dma_start(
            out=sb1e[hi * 32:(hi + 1) * 32, :],
            in_=b1e_dram.rearrange("o (lo r) -> lo (o r)", lo=32)[
                :, 32 * hi:32 * hi + 32])
        c.inc(ldb, "dg", 16)
    ld = nc.gpsimd.dma_start(out=subp.ap(), in_=up_bP)
    c.inc(ld, "dg", 16)
    ld = nc.gpsimd.dma_start(out=sbr2.ap(), in_=brow2)
    c.inc(ld, "dg", 16)
    c.inc(nc.gpsimd.memset(sone2[:], 1.0), "g")
    c.wait(nc.sync, "dh")
    c.wait(nc.gpsimd, "dg")
    c.wait(nc.gpsimd, "g")
    c.barrier()

    w1used = [0, 0]
    w1ld = [0, 0]
    w2ld = [0, 0]
    hold = [0, 0]
    hgld = [0, 0]
    s_after_ev1 = {}
    s_after_ev2 = [0, 0]
    mmd_cnt = [0, 0]
    st_cnt = [0, 0]
    housed = [0, 0]
    ev3_cnt = [0, 0]
    w2used = [0, 0]
    qeng = [nc.gpsimd, nc.sync]
    for tt in range(4):
        t0 = tt * 512
        def ld_w1e(c2_):
            rot_ = c2_ % 2
            eng = qeng[rot_]
            wbuf_ = sw1[:, rot_ * NCH * 128:(rot_ + 1) * NCH * 128]
            c.wait(eng, "s_pe", w1used[rot_])
            ldw = eng.dma_start(
                out=wbuf_.rearrange("p (ci q) -> p ci q", ci=NCH),
                in_=w1e_dram[c2_, :, :].rearrange("(ci p) q -> p ci q", p=128))
            w1ld[rot_] = c.inc(ldw, "dl0w%d" % rot_, 16)

        def d1_tail(c2_):
            rot_ = c2_ % 2
            c.wait(nc.tensor, "s", s_after_ev1[c2_])
            c.wait(nc.tensor, "s", s_after_ev2[rot_])
            mmd = nc.tensor.matmul(psB[:, rot_ * 512:rot_ * 512 + 512],
                                   D1_sb[:, c2_ * 128:(c2_ + 1) * 128],
                                   sh1[:, rot_ * 512:rot_ * 512 + 512],
                                   start=True, stop=True)
            mmd_cnt[rot_] = c.inc(mmd, "s_pe")
            c.wait(nc.scalar, "s_pe")
            c.wait(nc.scalar, "str%d" % rot_, st_cnt[rot_])
            ev2 = nc.scalar.activation(sh2[:, rot_ * 512:rot_ * 512 + 512],
                                       psB[:, rot_ * 512:rot_ * 512 + 512],
                                       AF.Gelu, bias=subp[:, c2_:c2_ + 1])
            s_after_ev2[rot_] = c.inc(ev2, "s")
            eng = qeng[rot_]
            c.wait(eng, "s", s_after_ev2[rot_])
            str_ = eng.dma_start(out=relay_dram[c2_, :, :],
                                 in_=sh2[:, rot_ * 512:rot_ * 512 + 512])
            st_cnt[rot_] = c.inc(str_, "str%d" % rot_, 16)

        ld_w1e(0)
        pendP = []
        for c2 in range(32):
            rot = c2 % 2
            wbuf = sw1[:, rot * NCH * 128:(rot + 1) * NCH * 128]
            if c2 + 1 < 32:
                ld_w1e(c2 + 1)
            c.wait(nc.tensor, "dl0w%d" % rot, w1ld[rot])
            c.wait(nc.tensor, "s", s_after_ev1.get(c2 - 2, 0))
            for ci in range(NCH):
                mm = nc.tensor.matmul(psA[:, rot * 512:rot * 512 + 512],
                                      wbuf[:, ci * 128:(ci + 1) * 128],
                                      sxln[:, ci * TO + t0:ci * TO + t0 + 512],
                                      start=(ci == 0), stop=(ci == NCH - 1))
            w1used[rot] = c.inc(mm, "s_pe")
            c.wait(nc.scalar, "s_pe")
            c.wait(nc.scalar, "s_pe", mmd_cnt[rot])
            ev1 = nc.scalar.activation(sh1[:, rot * 512:rot * 512 + 512],
                                       psA[:, rot * 512:rot * 512 + 512],
                                       AF.Identity, bias=sb1e[:, c2:c2 + 1])
            s_after_ev1[c2] = c.inc(ev1, "s")
            pendP.append(c2)
            if len(pendP) > 1:
                d1_tail(pendP.pop(0))
        d1_tail(pendP.pop(0))
        c.wait(nc.sync, "str0")
        c.wait(nc.sync, "str1")
        c.barrier()
        def ld_h2o(co_):
            rot_ = co_ % 2
            c.wait(nc.sync, "s_pe", housed[rot_])
            c.wait(nc.gpsimd, "s_pe", housed[rot_])
            for hi in range(4):
                eng = qeng[hi % 2]
                sname = ("dhg%d" if hi % 2 == 0 else "dho%d") % rot_
                ldo = eng.dma_start(
                    out=sh2o[hi * 32:(hi + 1) * 32, rot_ * 512:rot_ * 512 + 512],
                    in_=relay_dram[:, co_ + 32 * hi, :])
                n_ = c.inc(ldo, sname, 16)
                if hi % 2 == 0:
                    hgld[rot_] = n_
                else:
                    hold[rot_] = n_
        ld_h2o(0)
        for co in range(32):
            rot = co % 2
            if co + 1 < 32:
                ld_h2o(co + 1)
            c.wait(nc.tensor, "dho%d" % rot, hold[rot])
            c.wait(nc.tensor, "dhg%d" % rot, hgld[rot])
            c.wait(nc.tensor, "v", ev3_cnt[rot])
            mmn = nc.tensor.matmul(psA[:, rot * 512:rot * 512 + 512],
                                   N1_sb[:, co * 128:(co + 1) * 128],
                                   sh2o[:, rot * 512:rot * 512 + 512],
                                   start=True, stop=True)
            housed[rot] = c.inc(mmn, "s_pe")
            c.wait(nc.vector, "s_pe")
            ev3 = nc.vector.tensor_copy(out=sh3[:, co * 512:(co + 1) * 512],
                                        in_=psA[:, rot * 512:rot * 512 + 512])
            ev3_cnt[rot] = c.inc(ev3, "v")
        c.barrier()
        for fh in range(2):
            yfree = c.counts["v"]

            def ld_w2e(co_, fh_):
                rot_ = co_ % 2
                eng = qeng[rot_]
                c.wait(eng, "s_pe", w2used[rot_])
                ldw = eng.dma_start(
                    out=sw2[:, rot_ * 512:(rot_ + 1) * 512],
                    in_=w2e_dram[co_ * 128:(co_ + 1) * 128,
                                 fh_ * 512:fh_ * 512 + 512])
                w2ld[rot_] = c.inc(ldw, "dl0w%d" % rot_, 16)
            if fh == 0:
                ld_w2e(0, 0)
            for co in range(32):
                rot = co % 2
                wb = sw2[:, rot * 512:(rot + 1) * 512]
                if co + 1 < 32:
                    ld_w2e(co + 1, fh)
                elif fh == 0:
                    ld_w2e(0, 1)
                c.wait(nc.tensor, "dl0w%d" % rot, w2ld[rot])
                if co == 0:
                    c.wait(nc.tensor, "v", yfree)
                for ts in range(4):
                    mm = nc.tensor.matmul(
                        psY[:, ts * 512:(ts + 1) * 512],
                        sh3[:, co * 512 + ts * 128:co * 512 + (ts + 1) * 128],
                        wb, start=(co == 0), stop=False)
                w2used[rot] = c.inc(mm, "s_pe")
            for ts in range(4):
                mm = nc.tensor.matmul(psY[:, ts * 512:(ts + 1) * 512],
                                      sone2[:], sbr2[:, fh * 512:fh * 512 + 512],
                                      start=False, stop=True)
            c.inc(mm, "s_pe")
            c.wait(nc.vector, "s_pe")
            for ts in range(4):
                gtt = (tt * 4 + ts)
                ad = nc.vector.tensor_add(
                    out=x_sb[:, gtt * E + fh * 512:gtt * E + fh * 512 + 512],
                    in0=x_sb[:, gtt * E + fh * 512:gtt * E + fh * 512 + 512],
                    in1=psY[:, ts * 512:(ts + 1) * 512])
                c.inc(ad, "v")
        c.barrier()
    c.barrier()
    c.pop()


def emit_ffn_l1(c, xln_fm, w1t, b1cm, w2t, b2row, wq1_dram, wq2_dram, x_sb, tag):
    nc = c.nc
    c.push()
    swf = c.sb(f"qw_{tag}", [128, 2 * 4096], F32)
    sabs = c.sb(f"qa_{tag}", [128, NCH + 32], F32)
    sth = c.sb(f"qt_{tag}", [128, 4], F32)
    srow = c.sb(f"qr_{tag}", [1, 8], F32)
    sqt = c.sb(f"qq_{tag}", [128, 4096], BF16)
    stmp = c.sb(f"qtm_{tag}", [128, 4096], F32)
    sone = c.sb(f"qo_{tag}", [1, 128], F32)
    sonec = c.sb(f"qoc_{tag}", [128, 1], F32)
    pq_ = c.ps(f"qp_{tag}", [1, 512], F32)
    pbc = c.ps(f"qb_{tag}", [128, 512], F32)
    c.sem("v"); c.sem("s_pe"); c.sem("s"); c.sem("dg"); c.sem("dh"); c.sem("dgw")
    c.inc(nc.gpsimd.memset(sone[:], 1.0), "g")
    c.inc(nc.gpsimd.memset(sonec[:], 1.0), "g")
    c.wait(nc.gpsimd, "g")
    c.barrier()

    absused = [0, 0]
    gld = [0, 0]
    seqA = [(w1t, ch, 0 + ch, DH) for ch in range(NCH)] + \
           [(w2t, ch, NCH + ch, E) for ch in range(32)]

    def ld_gamma(idx):
        w_ap_, ch_, _, wc_ = seqA[idx]
        rot_ = idx % 2
        c.wait(nc.gpsimd, "v", absused[rot_])
        ldw = nc.gpsimd.dma_start(
            out=swf[:, rot_ * 4096:rot_ * 4096 + wc_],
            in_=w_ap_[ch_ * 128:(ch_ + 1) * 128, :])
        gld[rot_] = c.inc(ldw, "dg%d" % rot_, 16)
    ld_gamma(0)
    for idx in range(len(seqA)):
        rot = idx % 2
        _, _, col_ch, w_cols = seqA[idx]
        if idx + 1 < len(seqA):
            ld_gamma(idx + 1)
        c.wait(nc.vector, "dg%d" % rot, gld[rot])
        rs = nc.vector.reduce_sum(out=sabs[:, col_ch:col_ch + 1],
                                  in_=swf[:, rot * 4096:rot * 4096 + w_cols],
                                  axis=mybir.AxisListType.X,
                                  apply_absolute_value=True)
        absused[rot] = c.inc(rs, "v")
    c.wait(nc.tensor, "v")
    mm1 = nc.tensor.matmul(pq_[0:1, 0:NCH], sonec[:], sabs[:, 0:NCH],
                           start=True, stop=True)
    c.inc(mm1, "s_pe")
    mm2 = nc.tensor.matmul(pq_[0:1, 32:64], sonec[:], sabs[:, NCH:NCH + 32],
                           start=True, stop=True)
    c.inc(mm2, "s_pe")
    c.wait(nc.vector, "s_pe")
    r1 = nc.vector.reduce_sum(out=srow[0:1, 0:1], in_=pq_[0:1, 0:NCH],
                              axis=mybir.AxisListType.X)
    c.inc(r1, "v")
    c.wait(nc.vector, "v")
    r2 = nc.vector.reduce_sum(out=srow[0:1, 1:2], in_=pq_[0:1, 32:64],
                              axis=mybir.AxisListType.X)
    c.inc(r2, "v")
    c.wait(nc.vector, "v")
    ts1 = nc.vector.tensor_scalar(out=srow[0:1, 2:4], in0=srow[0:1, 0:2],
                                  scalar1=1.0 / (DH * E), scalar2=None,
                                  op0=ALU.mult, op1=ALU.bypass)
    c.inc(ts1, "v")
    c.wait(nc.vector, "v")
    ts2 = nc.vector.tensor_scalar(out=srow[0:1, 4:6], in0=srow[0:1, 2:4],
                                  scalar1=EPS, scalar2=0.5,
                                  op0=ALU.add, op1=ALU.mult)
    c.inc(ts2, "v")
    c.wait(nc.vector, "v")
    ts3 = nc.vector.tensor_scalar(out=srow[0:1, 6:8], in0=srow[0:1, 4:6],
                                  scalar1=-1.0, scalar2=None,
                                  op0=ALU.mult, op1=ALU.bypass)
    c.inc(ts3, "v")
    c.wait(nc.tensor, "v")
    mmb = nc.tensor.matmul(pbc[:, 0:4], sone[:], srow[0:1, 4:8],
                           start=True, stop=True)
    c.inc(mmb, "s_pe")
    c.wait(nc.vector, "s_pe")
    cpb = nc.vector.tensor_copy(out=sth[:], in_=pbc[:, 0:4])
    c.inc(cpb, "v")
    c.wait(nc.vector, "v")
    c.barrier()

    qused = [0, 0]
    seqB = [(w1t, ch, 0, DH, 0) for ch in range(NCH)] + \
           [(w2t, ch, 1, E, 1) for ch in range(32)]

    def ld_q(idx):
        w_ap_, ch_, _, wc_, _ = seqB[idx]
        rot_ = idx % 2
        c.wait(nc.gpsimd, "v", qused[rot_])
        ldw = nc.gpsimd.dma_start(
            out=swf[:, rot_ * 4096:rot_ * 4096 + wc_],
            in_=w_ap_[ch_ * 128:(ch_ + 1) * 128, :])
        gld[rot_] = c.inc(ldw, "dg%d" % rot_, 16)
    ld_q(0)
    for idx in range(len(seqB)):
        rot = idx % 2
        _, ch, thc, w_cols, mi = seqB[idx]
        buf = swf[:, rot * 4096:(rot + 1) * 4096]
        if idx + 1 < len(seqB):
            ld_q(idx + 1)
        c.wait(nc.vector, "dg%d" % rot, gld[rot])
        c.wait(nc.vector, "dh")
        tn = nc.vector.tensor_single_scalar(out=stmp[:, 0:w_cols],
                                            in_=buf[:, 0:w_cols],
                                            scalar=sth[:, 2 + thc:3 + thc],
                                            op=ALU.is_lt)
        c.inc(tn, "v")
        c.wait(nc.vector, "v")
        qq = nc.vector.scalar_tensor_tensor(out=sqt[:, 0:w_cols],
                                            in0=buf[:, 0:w_cols],
                                            scalar=sth[:, thc:thc + 1],
                                            in1=stmp[:, 0:w_cols],
                                            op0=ALU.is_gt, op1=ALU.subtract)
        qused[rot] = c.inc(qq, "v")
        c.wait(nc.sync, "v")
        if mi == 0:
            st_ = nc.sync.dma_start(
                out=wq1_dram[:, ch * 128:(ch + 1) * 128, :].rearrange(
                    "fo p q -> p fo q"),
                in_=sqt.ap().rearrange("p (fo q) -> p fo q", fo=32))
        else:
            st_ = nc.sync.dma_start(out=wq2_dram[ch * 128:(ch + 1) * 128, :],
                                    in_=sqt[:, 0:w_cols])
        c.inc(st_, "dh", 16)
    c.wait(nc.sync, "dh")
    c.barrier()
    c.pop()

    c.push()
    sxln = c.sb(f"qx_{tag}", [128, NCH * TO], BF16)
    sw1 = c.sb(f"qw1_{tag}", [128, 2 * NCH * 128], BF16)
    sh = c.sb(f"qh_{tag}", [128, 32 * 512], BF16)
    sw2 = c.sb(f"qw2_{tag}", [128, 2 * 512], BF16)
    sb1 = c.sb(f"qb1_{tag}", [128, 32], F32)
    sbr2 = c.sb(f"qbr_{tag}", [2, 1024], BF16)
    sone2 = c.sb(f"qo2_{tag}", [2, 128], BF16)
    psA = c.ps(f"qpa_{tag}", [128, 2 * 512], F32)
    psY = c.ps(f"qpy_{tag}", [128, 4 * 512], F32)
    ld = nc.sync.dma_start(out=sxln.ap().rearrange("p (ch t) -> p ch t", ch=NCH),
                           in_=xln_fm.rearrange("(ch p) t -> p ch t", p=128))
    c.inc(ld, "dh", 16)
    ldb = nc.gpsimd.dma_start(out=sb1.ap(), in_=b1cm)
    c.inc(ldb, "dg", 16)
    c.inc(nc.gpsimd.memset(sbr2[:], 0.0), "g")
    c.wait(nc.gpsimd, "g")
    ldb = nc.gpsimd.dma_start(out=sbr2[0:1, :], in_=b2row)
    c.inc(ldb, "dg", 16)
    c.inc(nc.gpsimd.memset(sone2[:], 1.0), "g")
    c.wait(nc.gpsimd, "g")
    c.wait(nc.gpsimd, "dg")
    c.wait(nc.sync, "dh")
    c.barrier()
    z1used = [0, 0]
    z2used = [0, 0]
    for tt in range(4):
        t0 = tt * 512
        for fo in range(32):
            rot = fo % 2
            wbuf = sw1[:, rot * NCH * 128:(rot + 1) * NCH * 128]
            c.wait(nc.gpsimd, "s_pe", z1used[rot])
            ldw = nc.gpsimd.dma_start(
                out=wbuf.rearrange("p (ci q) -> p ci q", ci=NCH),
                in_=wq1_dram[fo, :, :].rearrange("(ci p) q -> p ci q", p=128))
            nldw = c.inc(ldw, "dgw%d" % rot, 16)
            c.wait(nc.tensor, "dgw%d" % rot, nldw)
            if c.counts["s"] >= 2:
                c.wait(nc.tensor, "s", c.counts["s"] - 1)
            for ci in range(NCH):
                mm = nc.tensor.matmul(psA[:, rot * 512:rot * 512 + 512],
                                      wbuf[:, ci * 128:(ci + 1) * 128],
                                      sxln[:, ci * TO + t0:ci * TO + t0 + 512],
                                      start=(ci == 0), stop=(ci == NCH - 1))
            z1used[rot] = c.inc(mm, "s_pe")
            c.wait(nc.scalar, "s_pe")
            ev = nc.scalar.activation(sh[:, fo * 512:(fo + 1) * 512],
                                      psA[:, rot * 512:rot * 512 + 512],
                                      AF.Relu, bias=sb1[:, fo:fo + 1])
            c.inc(ev, "s")
        c.wait(nc.tensor, "s")
        c.barrier()
        for fh in range(2):
            yfree = c.counts["v"]
            for co in range(32):
                rot = co % 2
                wb = sw2[:, rot * 512:(rot + 1) * 512]
                c.wait(nc.gpsimd, "s_pe", z2used[rot])
                ldw = nc.gpsimd.dma_start(
                    out=wb, in_=wq2_dram[co * 128:(co + 1) * 128,
                                         fh * 512:fh * 512 + 512])
                nldw = c.inc(ldw, "dgw%d" % rot, 16)
                c.wait(nc.tensor, "dgw%d" % rot, nldw)
                if co == 0:
                    c.wait(nc.tensor, "v", yfree)
                for ts in range(4):
                    mm = nc.tensor.matmul(
                        psY[:, ts * 512:(ts + 1) * 512],
                        sh[:, co * 512 + ts * 128:co * 512 + (ts + 1) * 128],
                        wb, start=(co == 0), stop=False)
                z2used[rot] = c.inc(mm, "s_pe")
            for ts in range(4):
                mm = nc.tensor.matmul(psY[:, ts * 512:(ts + 1) * 512],
                                      sone2[:], sbr2[:, fh * 512:fh * 512 + 512],
                                      start=False, stop=True)
            c.inc(mm, "s_pe")
            c.wait(nc.vector, "s_pe")
            for ts in range(4):
                gtt = (tt * 4 + ts)
                ad = nc.vector.tensor_add(
                    out=x_sb[:, gtt * E + fh * 512:gtt * E + fh * 512 + 512],
                    in0=x_sb[:, gtt * E + fh * 512:gtt * E + fh * 512 + 512],
                    in1=psY[:, ts * 512:(ts + 1) * 512])
                c.inc(ad, "v")
        c.barrier()
    c.barrier()
    c.pop()



def p_of_f(f):
    return (f >> 7) | (((f >> 5) & 3) << 5)


def f_of_p(c2, p2):
    return c2 | ((p2 & 31) << 7) | (((p2 >> 5) & 3) << 5)


def _wvec4(stage_w, s, f_a):
    s2g = 2 ** s
    wr = stage_w.reshape(4096 // (2 * s2g), s2g, 2, 2)
    i = f_a >> (s + 1)
    j = f_a & (s2g - 1)
    w = wr[i, j]
    return np.stack([w[:, 0, 0], w[:, 0, 1], w[:, 1, 0], w[:, 1, 1]])


def _fa_orig(s2):
    co = np.arange(32)
    nb = 128 // (2 * s2)
    kb = np.arange(nb)
    ks = np.arange(s2)
    f = (co[:, None, None] * 128 + kb[None, :, None] * 2 * s2 + ks[None, None, :])
    return f.reshape(-1)


def _fa_P(s2p):
    c2 = np.arange(32)
    nb = 128 // (2 * s2p)
    kb = np.arange(nb)
    ks = np.arange(s2p)
    p2a = kb[:, None] * 2 * s2p + ks[None, :]
    f = f_of_p(c2[:, None, None], p2a[None])
    return f.reshape(-1)


def butterfly_builder_plan(up_w, dn_w):
    rows = []
    plans = {}
    st = []
    for s in range(7):
        s2 = 2 ** s
        rows.append(_wvec4(up_w[s], s, _fa_orig(s2)))
        st.append(s2)
    plans["M1"] = st
    st = []
    for s in range(7, 12):
        s2p = 2 ** (s - 7)
        rows.append(_wvec4(up_w[s], s, _fa_P(s2p)))
        st.append(s2p)
    plans["D1"] = st
    st = []
    for s in range(7):
        s2 = 2 ** s
        rows.append(_wvec4(dn_w[s], s, _fa_orig(s2)))
        st.append(s2)
    plans["N1"] = st
    st = []
    for s in range(7, 12):
        s2p = 2 ** (s - 7)
        rows.append(_wvec4(dn_w[s], s, _fa_P(s2p)))
        st.append(s2p)
    plans["N2"] = st
    bfw = np.stack(rows).astype(np.float32)
    return bfw, plans



def host_slices(inputs):
    per_core = []
    x = inputs["L0_attn_x"]
    W1 = inputs["L0_attn_W1"]
    Q1, K1, V1 = inputs["L1_attn_Q"], inputs["L1_attn_K"], inputs["L1_attn_V"]
    for cid in range(NCORE):
        b, qd = cid // 4, cid % 4
        hs = list(range(4 * qd, 4 * qd + 4))
        d = {}
        d["x0"] = np.ascontiguousarray(x[b, qd * TO:(qd + 1) * TO, :])
        d["wq0"] = np.ascontiguousarray(
            np.concatenate([W1[h * D:(h + 1) * D, :].T for h in hs], axis=1))
        d["wk0"] = np.ascontiguousarray(
            np.concatenate([W1[E + h * D:E + (h + 1) * D, :].T for h in hs], axis=1))
        d["wv0"] = np.ascontiguousarray(
            np.concatenate([W1[2 * E + h * D:2 * E + (h + 1) * D, :].T for h in hs], axis=1))
        d["wq1"] = np.ascontiguousarray(Q1[hs].transpose(2, 0, 1).reshape(E, 4 * D))
        d["wk1"] = np.ascontiguousarray(K1[hs].transpose(2, 0, 1).reshape(E, 4 * D))
        d["wv1"] = np.ascontiguousarray(V1[hs].transpose(2, 0, 1).reshape(E, 4 * D))
        per_core.append(d)
    shared = {}
    shared["w2t"] = np.ascontiguousarray(inputs["L0_attn_W2"].T)
    shared["ot"] = np.ascontiguousarray(inputs["L1_attn_O"].T)
    shared["wexpT"] = np.ascontiguousarray(inputs["L0_ffn_W_exp"].T)
    shared["bexp_cm"] = np.ascontiguousarray(
        inputs["L0_ffn_b_exp"].reshape(32, 128).T)
    bfw, _plans = butterfly_builder_plan(inputs["L0_ffn_up_weights"],
                                         inputs["L0_ffn_down_weights"])
    shared["bfw"] = bfw
    fidx = np.array([f_of_p(i // 128, i % 128) for i in range(4096)])
    ub = inputs["L0_ffn_up_bias"]
    up_bP = ub[fidx].reshape(32, 128).T
    shared["up_bP"] = np.ascontiguousarray(up_bP.astype(np.float32))
    shared["wconP"] = np.ascontiguousarray(inputs["L0_ffn_W_con"][fidx])
    shared["brow2"] = np.ascontiguousarray(
        np.stack([inputs["L0_ffn_b_con"], inputs["L0_ffn_down_bias"]]))
    shared["w1t"] = np.ascontiguousarray(inputs["L1_ffn_W1"].T)
    shared["b1cm"] = np.ascontiguousarray(inputs["L1_ffn_b1"].reshape(32, 128).T)
    shared["w2t_ffn"] = np.ascontiguousarray(inputs["L1_ffn_W2"].T)
    shared["b2row"] = np.ascontiguousarray(inputs["L1_ffn_b2"][None, :])
    p = np.arange(128)
    inv = (1.0 / 10000.0 ** (np.arange(0, D, 2, dtype=np.float32) / D))
    ang = np.arange(T, dtype=np.float32)[None, :] * inv[p % 32][:, None]
    cos = np.cos(ang).astype(np.float32)
    sin = np.sin(ang).astype(np.float32)
    sgn = np.where(((p ^ 32) % 64) < 32, -1.0, 1.0).astype(np.float32)[:, None]
    shared["rope_cos"] = cos
    shared["rope_sin"] = (sin * sgn)
    p32 = np.zeros((128, 128), dtype=np.float32)
    p32[p, p ^ 32] = 1.0
    shared["p32"] = p32
    return per_core, shared


def build_C0():
    nc = bass.Bass("TRN2", target_bir_lowering=False)
    x_in = nc.dram_tensor("x_in", [TO, E], F32, kind="ExternalInput")
    z_in = nc.dram_tensor("z_in", [E, TO], BF16, kind="ExternalInput")
    w_ot = nc.dram_tensor("w_ot", [E, E], F32, kind="ExternalInput")
    wexpT = nc.dram_tensor("wexpT", [DH, E], F32, kind="ExternalInput")
    bexp_cm = nc.dram_tensor("bexp_cm", [128, 32], F32, kind="ExternalInput")
    bfw = nc.dram_tensor("bfw", [24, 4, 2048], F32, kind="ExternalInput")
    up_bP = nc.dram_tensor("up_bP", [128, 32], F32, kind="ExternalInput")
    wconP = nc.dram_tensor("wconP", [DH, E], F32, kind="ExternalInput")
    brow2 = nc.dram_tensor("brow2", [2, E], F32, kind="ExternalInput")
    x_out = nc.dram_tensor("x_out", [TO, E], F32, kind="ExternalOutput")
    xln_out = nc.dram_tensor("xln_out", [E, TO], BF16, kind="ExternalOutput")
    xln2 = nc.dram_tensor("xln2_i", [E, TO], BF16)
    w1e = nc.dram_tensor("w1e_i", [32, E, 128], BF16)
    b1e = nc.dram_tensor("b1e_i", [1, DH], F32)
    w2e = nc.dram_tensor("w2e_i", [DH, E], BF16)
    relay = nc.dram_tensor("relay_i", [32, 128, 512], BF16)
    c = Ctx(nc)
    x_sb = c.sb("xres", [128, NTT * E], F32)
    mats = {nm: c.sb(f"mat_{nm}", [128, 32 * 128], BF16)
            for nm in ("M1", "D1", "N1", "N2")}
    _, plans = butterfly_builder_plan(np.zeros((12, 2048, 2, 2), np.float32),
                                     np.zeros((12, 2048, 2, 2), np.float32))
    ld = nc.sync.dma_start(out=x_sb.ap().rearrange("p (n f) -> p n f", f=E),
                           in_=x_in.ap().rearrange("(n p) f -> p n f", p=128))
    c.inc(ld, "dh", 16)
    c.wait(nc.sync, "dh")
    c.barrier()
    emit_outproj_residual(c, z_in.ap(), w_ot.ap(), x_sb, "c0")
    emit_ln_transpose(c, None, xln2.ap(), "c0l", x_sb=x_sb)
    emit_bfly_matrices(c, bfw.ap(), mats, plans, "c0")
    emit_fold_w1(c, wexpT.ap(), bexp_cm.ap(), mats["M1"], w1e.ap(), b1e.ap(), "c0")
    emit_fold_w2(c, wconP.ap(), mats["N2"], w2e.ap(), "c0")
    emit_ffn_l0(c, xln2.ap(), w1e.ap(), b1e.ap(), up_bP.ap(), w2e.ap(),
                brow2.ap(), mats["D1"], mats["N1"], relay.ap(), x_sb, "c0")
    emit_ln_transpose(c, None, xln_out.ap(), "c0m", x_sb=x_sb)
    emit_store_x(c, x_sb, x_out.ap())
    c.close()
    return nc


def build_C1():
    nc = bass.Bass("TRN2", target_bir_lowering=False)
    x_in = nc.dram_tensor("x_in", [TO, E], F32, kind="ExternalInput")
    z_in = nc.dram_tensor("z_in", [E, TO], BF16, kind="ExternalInput")
    w_ot = nc.dram_tensor("w_ot", [E, E], F32, kind="ExternalInput")
    w1t = nc.dram_tensor("w1t", [E, DH], F32, kind="ExternalInput")
    b1cm = nc.dram_tensor("b1cm", [128, 32], F32, kind="ExternalInput")
    w2t = nc.dram_tensor("w2t", [DH, E], F32, kind="ExternalInput")
    b2row = nc.dram_tensor("b2row", [1, E], F32, kind="ExternalInput")
    x_out = nc.dram_tensor("x_out", [TO, E], F32, kind="ExternalOutput")
    xln2 = nc.dram_tensor("xln2_i", [E, TO], BF16)
    wq1 = nc.dram_tensor("wq1_i", [32, E, 128], BF16)
    wq2 = nc.dram_tensor("wq2_i", [DH, E], BF16)
    c = Ctx(nc)
    x_sb = c.sb("xres", [128, NTT * E], F32)
    ld = nc.sync.dma_start(out=x_sb.ap().rearrange("p (n f) -> p n f", f=E),
                           in_=x_in.ap().rearrange("(n p) f -> p n f", p=128))
    c.inc(ld, "dh", 16)
    c.wait(nc.sync, "dh")
    c.barrier()
    emit_outproj_residual(c, z_in.ap(), w_ot.ap(), x_sb, "c1")
    emit_ln_transpose(c, None, xln2.ap(), "c1l", x_sb=x_sb)
    emit_ffn_l1(c, xln2.ap(), w1t.ap(), b1cm.ap(), w2t.ap(), b2row.ap(),
                wq1.ap(), wq2.ap(), x_sb, "c1")
    emit_store_x(c, x_sb, x_out.ap())
    c.close()
    return nc



def _gather_xln(outs, key):
    xgs = []
    for b in range(2):
        xg = np.concatenate([outs[4 * b + r][key] for r in range(4)], axis=0)
        xgs.append(np.ascontiguousarray(xg))
    return [xgs[cid // 4] for cid in range(NCORE)]


def _a2a_z(outs, key):
    res = []
    for cid in range(NCORE):
        b, qd = cid // 4, cid % 4
        z = np.concatenate(
            [outs[4 * b + r][key][qd * 256:(qd + 1) * 256, :] for r in range(4)],
            axis=0)
        res.append(np.ascontiguousarray(z))
    return res


def kernel_unfused(**inputs):
    inputs = {k: np.ascontiguousarray(v) for k, v in inputs.items()}
    per_core, shared = host_slices(inputs)
    cores = list(range(NCORE))

    ncA = build_A()
    rA = run_bass_kernel_spmd(ncA, [{"x_in": pc["x0"]} for pc in per_core],
                              core_ids=cores)
    xg = _gather_xln(rA.results, "xln")

    ncB0 = build_B(rope=True, scale=0.125)
    inB0 = [{"xg": bfc(xg[i]), "wq": per_core[i]["wq0"], "wk": per_core[i]["wk0"],
             "wv": per_core[i]["wv0"], "rcos": shared["rope_cos"],
             "rsin": shared["rope_sin"], "p32": shared["p32"]} for i in cores]
    rB0 = run_bass_kernel_spmd(ncB0, inB0, core_ids=cores)
    z0 = _a2a_z(rB0.results, "zo")

    ncC0 = build_C0()
    inC0 = [{"x_in": per_core[i]["x0"], "z_in": z0[i], "w_ot": shared["w2t"],
             "wexpT": shared["wexpT"], "bexp_cm": shared["bexp_cm"],
             "bfw": shared["bfw"], "up_bP": shared["up_bP"],
             "wconP": shared["wconP"], "brow2": shared["brow2"]} for i in cores]
    rC0 = run_bass_kernel_spmd(ncC0, inC0, core_ids=cores)
    xg1 = _gather_xln(rC0.results, "xln_out")

    ncB1 = build_B(rope=False, scale=1.0)
    inB1 = [{"xg": bfc(xg1[i]), "wq": per_core[i]["wq1"], "wk": per_core[i]["wk1"],
             "wv": per_core[i]["wv1"]} for i in cores]
    rB1 = run_bass_kernel_spmd(ncB1, inB1, core_ids=cores)
    z1 = _a2a_z(rB1.results, "zo")

    ncC1 = build_C1()
    inC1 = [{"x_in": rC0.results[i]["x_out"], "z_in": z1[i], "w_ot": shared["ot"],
             "w1t": shared["w1t"], "b1cm": shared["b1cm"],
             "w2t": shared["w2t_ffn"], "b2row": shared["b2row"]} for i in cores]
    rC1 = run_bass_kernel_spmd(ncC1, inC1, core_ids=cores)

    out = np.empty((B, T, E), dtype=np.float32)
    for cid in cores:
        b, qd = cid // 4, cid % 4
        out[b, qd * TO:(qd + 1) * TO, :] = rC1.results[cid]["x_out"]
    return out



def emit_collective(c, kind, in_ap, out_ap, tag):
    nc = c.nc
    cc = nc.gpsimd.collective_compute(
        kind, mybir.AluOpType.bypass,
        replica_groups=[[0, 1, 2, 3], [4, 5, 6, 7]],
        ins=[in_ap], outs=[out_ap])
    c.inc(cc, "cc")
    c.wait(nc.gpsimd, "cc")
    c.barrier()


def build_fused():
    nc = bass.Bass("TRN2", target_bir_lowering=False)
    t_in = {}
    def inp(name, shape, dtype=F32):
        t_in[name] = nc.dram_tensor(name, shape, dtype, kind="ExternalInput")
        return t_in[name].ap()
    x0 = inp("x0", [TO, E])
    wq0 = inp("wq0", [E, 256]); wk0 = inp("wk0", [E, 256]); wv0 = inp("wv0", [E, 256])
    rcos = inp("rcos", [128, T]); rsin = inp("rsin", [128, T]); p32 = inp("p32", [128, 128])
    w2t = inp("w2t", [E, E])
    wexpT = inp("wexpT", [DH, E]); bexp_cm = inp("bexp_cm", [128, 32])
    bfw = inp("bfw", [24, 4, 2048]); up_bP = inp("up_bP", [128, 32])
    wconP = inp("wconP", [DH, E]); brow2 = inp("brow2", [2, E])
    wq1 = inp("wq1", [E, 256]); wk1 = inp("wk1", [E, 256]); wv1 = inp("wv1", [E, 256])
    ot = inp("ot", [E, E])
    w1t = inp("w1t", [E, DH]); b1cm = inp("b1cm", [128, 32])
    w2t_ffn = inp("w2t_ffn", [DH, E]); b2row = inp("b2row", [1, E])
    x_out = nc.dram_tensor("x_out", [TO, E], F32, kind="ExternalOutput")

    def internal(name, shape, dtype):
        return nc.dram_tensor(name, shape, dtype).ap()
    xln_a = internal("xln_a", [4, E, 512], BF16)
    xg_a = internal("xg_a", [4, 4 * E, 512], BF16)
    q0d = internal("q0d", [256, T], BF16)
    k0d = internal("k0d", [256, T], BF16)
    v0d = internal("v0d", [T, 256], BF16)
    zsh0 = internal("zsh0", [E, TO], BF16)
    zg0 = internal("zg0", [4, 1024, TO], BF16)
    xln2 = internal("xln2", [E, TO], BF16)
    w1e = internal("w1e", [32, E, 128], BF16)
    b1e = internal("b1e", [1, DH], F32)
    w2e = internal("w2e", [DH, E], BF16)
    relay = internal("relay", [32, 128, 512], BF16)
    xln_b = internal("xln_b", [4, E, 512], BF16)
    xg_b = internal("xg_b", [4, 4 * E, 512], BF16)
    q1d = internal("q1d", [256, T], BF16)
    k1d = internal("k1d", [256, T], BF16)
    v1d = internal("v1d", [T, 256], BF16)
    zsh1 = internal("zsh1", [E, TO], BF16)
    zg1 = internal("zg1", [4, 1024, TO], BF16)
    xln3 = internal("xln3", [E, TO], BF16)
    wq1i = internal("wq1i", [32, E, 128], BF16)
    wq2i = internal("wq2i", [DH, E], BF16)

    c = Ctx(nc)
    c.sem("cc")
    qd_sv = nc.partition_id() % 4
    x_sb = c.sb("xres", [128, NTT * E], F32)
    ld = nc.sync.dma_start(out=x_sb.ap().rearrange("p (n f) -> p n f", f=E),
                           in_=x0.rearrange("(n p) f -> p n f", p=128))
    c.inc(ld, "dh", 16)
    c.wait(nc.sync, "dh")
    c.barrier()
    cc0 = c.counts.get("cc", 0)
    emit_ln_transpose(c, None, xln_a, "fa", x_sb=x_sb, gather=xg_a)
    emit_qkv(c, xg_a, wq0, wk0, wv0, q0d, k0d, v0d, "f0",
             rope_cs=(rcos, rsin, p32), cc_base=cc0)
    emit_attention(c, q0d, k0d, v0d, zsh0, "f0", 0.125, zg_out=zg0)
    emit_outproj_residual(c, None, w2t, x_sb, "f0",
                          z_gathered=(zg0, qd_sv, c.counts["cc"]))
    emit_ln_transpose(c, None, xln2, "f0l", x_sb=x_sb)
    c.push()
    mats = {nm: c.sb(f"mat_{nm}", [128, 32 * 128], BF16)
            for nm in ("M1", "D1", "N1", "N2")}
    _, plans = butterfly_builder_plan(np.zeros((12, 2048, 2, 2), np.float32),
                                     np.zeros((12, 2048, 2, 2), np.float32))
    emit_bfly_matrices(c, bfw, mats, plans, "f0")
    emit_fold_w1(c, wexpT, bexp_cm, mats["M1"], w1e, b1e, "f0")
    emit_fold_w2(c, wconP, mats["N2"], w2e, "f0")
    emit_ffn_l0(c, xln2, w1e, b1e, up_bP, w2e, brow2,
                mats["D1"], mats["N1"], relay, x_sb, "f0")
    c.pop()
    cc1 = c.counts["cc"]
    emit_ln_transpose(c, None, xln_b, "fb", x_sb=x_sb, gather=xg_b)
    emit_qkv(c, xg_b, wq1, wk1, wv1, q1d, k1d, v1d, "f1", rope_cs=None,
             cc_base=cc1)
    emit_attention(c, q1d, k1d, v1d, zsh1, "f1", 1.0, zg_out=zg1)
    emit_outproj_residual(c, None, ot, x_sb, "f1",
                          z_gathered=(zg1, qd_sv, c.counts["cc"]))
    emit_ln_transpose(c, None, xln3, "f1l", x_sb=x_sb)
    emit_ffn_l1(c, xln3, w1t, b1cm, w2t_ffn, b2row, wq1i, wq2i, x_sb, "f1")
    emit_store_x(c, x_sb, x_out.ap())
    c.close()
    return nc


def kernel(**inputs):
    inputs = {k: np.ascontiguousarray(v) for k, v in inputs.items()}
    per_core, shared = host_slices(inputs)
    cores = list(range(NCORE))
    nc = build_fused()
    in_maps = []
    for i in cores:
        pc = per_core[i]
        in_maps.append({
            "x0": pc["x0"], "wq0": pc["wq0"], "wk0": pc["wk0"], "wv0": pc["wv0"],
            "rcos": shared["rope_cos"], "rsin": shared["rope_sin"],
            "p32": shared["p32"], "w2t": shared["w2t"],
            "wexpT": shared["wexpT"], "bexp_cm": shared["bexp_cm"],
            "bfw": shared["bfw"], "up_bP": shared["up_bP"],
            "wconP": shared["wconP"], "brow2": shared["brow2"],
            "wq1": pc["wq1"], "wk1": pc["wk1"], "wv1": pc["wv1"],
            "ot": shared["ot"], "w1t": shared["w1t"], "b1cm": shared["b1cm"],
            "w2t_ffn": shared["w2t_ffn"], "b2row": shared["b2row"],
        })
    res = run_bass_kernel_spmd(nc, in_maps, core_ids=cores)
    out = np.empty((B, T, E), dtype=np.float32)
    for cid in cores:
        b, qd = cid // 4, cid % 4
        out[b, qd * TO:(qd + 1) * TO, :] = res.results[cid]["x_out"]
    return out


# version 19
# speedup vs baseline: 1.2308x; 1.0253x over previous
import math
import numpy as np
import ml_dtypes
from contextlib import ExitStack

import concourse.bass as bass
import concourse.mybir as mybir
from concourse.bass_utils import run_bass_kernel_spmd

F32 = mybir.dt.float32
BF16 = mybir.dt.bfloat16
AF = mybir.ActivationFunctionType
ALU = mybir.AluOpType

B, T, E, H, D = 2, 8192, 1024, 16, 64
NCORE = 8
TO = T // 4
HO = H // 4
NTT = TO // 128
NCH = E // 128
DH = 4096
EPS = 1e-5


class Ctx:

    def __init__(self, nc):
        self.nc = nc
        self.es = ExitStack()
        self.sem_es = ExitStack()
        self._scopes = []
        self.sems = {}
        self.counts = {}

    def push(self):
        self._scopes.append(self.es)
        self.es = ExitStack()

    def pop(self):
        self.es.close()
        self.es = self._scopes.pop()

    def sb(self, name, shape, dtype):
        return self.es.enter_context(self.nc.sbuf_tensor(name, shape, dtype))

    def ps(self, name, shape, dtype=F32):
        return self.es.enter_context(self.nc.psum_tensor(name, shape, dtype))

    def sem(self, name):
        if name not in self.sems:
            self.sems[name] = self.sem_es.enter_context(self.nc.semaphore(name))
            self.counts[name] = 0
        return self.sems[name]

    def inc(self, inst, name, n=1):
        inst.then_inc(self.sem(name), n)
        self.counts[name] += n
        return self.counts[name]

    def wait(self, eng, name, val=None):
        if val is None:
            val = self.counts[name]
        eng.wait_ge(self.sem(name), val)

    def barrier(self):
        self.nc.all_engine_barrier()

    def close(self):
        self.es.close()
        self.sem_es.close()


def bfc(x):
    return np.ascontiguousarray(x).astype(ml_dtypes.bfloat16)



def emit_ln_transpose(c, x_src, xln_dst, tag, x_sb=None, gather=None):
    nc = c.nc
    c.push()
    own_x = x_sb is None
    if own_x:
        x_sb = c.sb(f"x_{tag}", [128, NTT * E], F32)
    xln_tm = c.sb(f"xlntm_{tag}", [128, NTT * E], BF16)
    stats = c.sb(f"stats_{tag}", [128, NTT * 12], F32)
    mv = c.sb(f"mv_{tag}", [128, NTT * 2], F32)
    rstd = c.sb(f"rstd_{tag}", [128, NTT], F32)
    sid = c.sb(f"id_{tag}", [128, 128], BF16)
    xfm = c.sb(f"xfm_{tag}", [128, NCH * TO], BF16)
    ptr = c.ps(f"ptr_{tag}", [128, 2 * 1024], BF16)

    mi = nc.gpsimd.memset(sid[:], 1.0)
    c.inc(mi, "g")
    c.wait(nc.gpsimd, "g")
    ai = nc.gpsimd.affine_select(out=sid[:], in_=sid[:], compare_op=ALU.is_equal,
                                 fill=0.0, base=0, pattern=[[-1, 128]],
                                 channel_multiplier=1)
    c.inc(ai, "g")

    if own_x:
        ld = nc.sync.dma_start(
            out=x_sb.ap().rearrange("p (n f) -> p n f", f=E),
            in_=x_src.rearrange("(n p) f -> p n f", p=128))
        c.inc(ld, "dh", 16)
        c.wait(nc.sync, "dh")
        c.wait(nc.vector, "dh")

    for i in range(NTT):
        xt = x_sb[:, i * E:(i + 1) * E]
        st = stats[:, i * 12:(i + 1) * 12]
        s1 = nc.vector.bn_stats(out=st[:, 0:6], in_=xt[:, 0:512])
        c.inc(s1, "v")
        s2 = nc.vector.bn_stats(out=st[:, 6:12], in_=xt[:, 512:1024])
        c.inc(s2, "v")
        c.wait(nc.vector, "v")
        ag = nc.vector.bn_aggr(out=mv[:, 2 * i:2 * i + 2], in_=st[:])
        c.inc(ag, "v")
    seps = c.sb(f"eps_{tag}", [128, 1], F32)
    me = nc.gpsimd.memset(seps[:], EPS)
    c.inc(me, "g")
    c.wait(nc.scalar, "g")
    c.wait(nc.scalar, "v")
    lg = nc.scalar.activation(rstd.ap().rearrange("p (n one) -> p n one", one=1),
                              mv.ap().rearrange("p (n two) -> p n two", two=2)[:, :, 1:2],
                              AF.Ln, bias=seps[:])
    c.inc(lg, "s")
    c.wait(nc.scalar, "s")
    ex = nc.scalar.activation(rstd[:], rstd[:], AF.Exp, scale=-0.5)
    c.inc(ex, "s")
    c.wait(nc.vector, "s")
    for i in range(NTT):
        tsn = nc.vector.tensor_scalar(
            out=xln_tm[:, i * E:(i + 1) * E], in0=x_sb[:, i * E:(i + 1) * E],
            scalar1=mv[:, 2 * i:2 * i + 1], scalar2=rstd[:, i:i + 1],
            op0=ALU.subtract, op1=ALU.mult)
        c.inc(tsn, "v")
    c.wait(nc.tensor, "v")
    c.wait(nc.tensor, "g")
    c.sem("vtr")
    for i in range(NTT):
        par = i % 2
        bank = ptr[:, par * 1024:(par + 1) * 1024]
        if i >= 2:
            c.wait(nc.tensor, "vtr", c.counts["vtr"] - 1)
        for ch in range(NCH):
            tr = nc.tensor.transpose(
                bank[:, ch * 128:(ch + 1) * 128],
                xln_tm[:, i * E + ch * 128: i * E + (ch + 1) * 128], sid[:])
            if ch == NCH - 1:
                c.inc(tr, "tr")
        c.wait(nc.vector, "tr")
        cp = nc.vector.tensor_copy(
            out=xfm.ap().rearrange("p (ch t) -> p ch t", ch=NCH)[:, :, i * 128:(i + 1) * 128],
            in_=bank.rearrange("p (ch t) -> p ch t", ch=NCH))
        c.inc(cp, "vtr")
        if gather is not None and i % 4 == 3:
            g = i // 4
            xlv = xln_dst.rearrange("g (ch p) t -> (g ch) p t", p=128)
            c.wait(nc.sync, "vtr")
            st = nc.sync.dma_start(
                out=xlv[g * NCH:(g + 1) * NCH, :, :].rearrange("c p t -> p c t"),
                in_=xfm.ap().rearrange("p (ch t) -> p ch t", ch=NCH)[
                    :, :, g * 512:(g + 1) * 512])
            nst = c.inc(st, "dh", 16)
            c.wait(nc.gpsimd, "dh", nst)
            cc = nc.gpsimd.collective_compute(
                "AllGather", mybir.AluOpType.bypass,
                replica_groups=[[0, 1, 2, 3], [4, 5, 6, 7]],
                ins=[xln_dst[g, :, :].squeeze()],
                outs=[gather[g, :, :].squeeze()])
            c.inc(cc, "cc")
    if gather is None:
        c.wait(nc.sync, "vtr")
        stfm = nc.sync.dma_start(out=xln_dst.rearrange("(ch p) t -> p ch t", p=128),
                                 in_=xfm.ap().rearrange("p (ch t) -> p ch t", ch=NCH))
        c.inc(stfm, "dh", 16)
    c.wait(nc.sync, "dh")
    c.barrier()
    c.pop()
    return x_sb


def emit_qkv(c, xln_full, wq, wk, wv, q_dst, k_dst, v_dst, tag, rope_cs=None,
             cc_base=0):
    nc = c.nc
    c.push()
    sw = c.sb(f"w_{tag}", [128, NCH * 3 * 256], BF16)
    sx = c.sb(f"xg_{tag}", [128, 2 * NCH * 512], BF16)
    sqk = c.sb(f"qk_{tag}", [128, 2 * 512], BF16)
    sv = c.sb(f"v_{tag}", [128, 1024], BF16)
    pq = c.ps(f"pq_{tag}", [128, 2 * 512], F32)
    pk = c.ps(f"pk_{tag}", [128, 2 * 512], F32)
    pv = c.ps(f"pv_{tag}", [128, 2 * 512], F32)
    if rope_cs is not None:
        scos = c.sb(f"cos_{tag}", [128, T], BF16)
        ssin = c.sb(f"sin_{tag}", [128, T], BF16)
        sqr = c.sb(f"qr_{tag}", [128, 2048], BF16)
        sp32 = c.sb(f"p32_{tag}", [128, 128], BF16)
        prope = c.ps(f"pr_{tag}", [128, 1024], F32)
        c.inc(nc.gpsimd.dma_start(out=scos[:], in_=rope_cs[0]), "dg", 16)
        c.inc(nc.gpsimd.dma_start(out=ssin[:], in_=rope_cs[1]), "dg", 16)
        c.inc(nc.gpsimd.dma_start(out=sp32[:], in_=rope_cs[2]), "dg", 16)
        c.sem("rp")
    else:
        scos = ssin = sqr = sp32 = prope = None
    for wi, w in enumerate((wq, wk, wv)):
        ld = nc.gpsimd.dma_start(
            out=sw.ap().rearrange("p (ch w s) -> p ch w s", ch=NCH, w=3)[:, :, wi, :],
            in_=w.rearrange("(ch p) s -> p ch s", p=128))
        c.inc(ld, "dg", 16)
    c.wait(nc.tensor, "dg")

    c.sem("mmq"); c.sem("v"); c.sem("dh")
    qfree = kfree = vfree = 0
    xgv = xln_full.rearrange("g (ch p) t -> (g ch) p t", p=128)
    for idx in range(16):
        g, r = idx // 4, idx % 4
        tt = r * 4 + g
        buf = idx % 2
        xb = sx[:, buf * NCH * 512:(buf + 1) * NCH * 512]
        if idx >= 2:
            c.wait(nc.sync, "mmq", idx - 1)
        if r == 0:
            c.wait(nc.sync, "cc", cc_base + g + 1)
        lxs = "lx%d" % buf
        base = (g * 4 + r) * NCH
        ld = nc.sync.dma_start(
            out=xb.rearrange("p (ch t) -> p ch t", ch=NCH),
            in_=xgv[base:base + NCH, :, :].rearrange("c p t -> p c t"))
        nld = c.inc(ld, lxs, 16)
        c.wait(nc.tensor, lxs, nld)
        if idx >= 1:
            c.wait(nc.tensor, "v", qfree)
        for ch in range(NCH):
            st, sp = (ch == 0), (ch == NCH - 1)
            xt = xb[:, ch * 512:(ch + 1) * 512]
            wch = sw[:, ch * 768:(ch + 1) * 768]
            for half in range(2):
                nc.tensor.matmul(pq[:, half * 512:half * 512 + 512],
                                 wch[:, half * 128:half * 128 + 128], xt,
                                 start=st, stop=sp)
        if idx >= 1:
            c.wait(nc.tensor, "v", kfree)
        for ch in range(NCH):
            st, sp = (ch == 0), (ch == NCH - 1)
            xt = xb[:, ch * 512:(ch + 1) * 512]
            wch = sw[:, ch * 768:(ch + 1) * 768]
            for half in range(2):
                nc.tensor.matmul(pk[:, half * 512:half * 512 + 512],
                                 wch[:, 256 + half * 128:256 + half * 128 + 128], xt,
                                 start=st, stop=sp)
        if idx >= 1:
            c.wait(nc.tensor, "v", vfree)
        for sub in range(4):
            for ch in range(NCH):
                st, sp = (ch == 0), (ch == NCH - 1)
                xt = xb[:, ch * 512:(ch + 1) * 512]
                wch = sw[:, ch * 768:(ch + 1) * 768]
                mm = nc.tensor.matmul(
                    pv[:, sub * 256:(sub + 1) * 256],
                    xt[:, sub * 128:(sub + 1) * 128], wch[:, 512:768],
                    start=st, stop=sp)
        c.inc(mm, "mmq")
        gt = tt * 512
        c.wait(nc.vector, "mmq")
        if idx >= 1:
            c.wait(nc.vector, "dh")
        for half in range(2):
            cp = nc.vector.tensor_copy(out=sqk[:, half * 512:half * 512 + 512],
                                       in_=pq[:, half * 512:half * 512 + 512])
            c.inc(cp, "v")
        qfree = c.counts["v"]
        if rope_cs is not None:
            _emit_rope(c, sqk, sqr, scos, ssin, sp32, prope, gt)
        c.wait(nc.sync, "v")
        st1 = nc.sync.dma_start(
            out=q_dst.rearrange("(h p) t -> p h t", p=128)[:, :, gt:gt + 512],
            in_=sqk.ap().rearrange("p (h t) -> p h t", h=2))
        c.inc(st1, "dh", 16)
        c.wait(nc.vector, "dh")
        for half in range(2):
            cp = nc.vector.tensor_copy(out=sqk[:, half * 512:half * 512 + 512],
                                       in_=pk[:, half * 512:half * 512 + 512])
            c.inc(cp, "v")
        kfree = c.counts["v"]
        if rope_cs is not None:
            _emit_rope(c, sqk, sqr, scos, ssin, sp32, prope, gt)
        c.wait(nc.sync, "v")
        st2 = nc.sync.dma_start(
            out=k_dst.rearrange("(h p) t -> p h t", p=128)[:, :, gt:gt + 512],
            in_=sqk.ap().rearrange("p (h t) -> p h t", h=2))
        c.inc(st2, "dh", 16)
        if idx >= 1:
            c.wait(nc.vector, "dh")
        cpv = nc.vector.tensor_copy(out=sv[:], in_=pv[:])
        vfree = c.inc(cpv, "v")
        c.wait(nc.sync, "v")
        stv = nc.sync.dma_start(
            out=v_dst.rearrange("(n p) d -> p n d", p=128)[:, 4 * tt:4 * tt + 4, :],
            in_=sv.ap().rearrange("p (s d) -> p s d", s=4))
        c.inc(stv, "dh", 16)
    c.wait(nc.sync, "dh")
    c.barrier()
    c.pop()


def _emit_rope(c, sqk, sqr, scos, ssin, p32_sb, prope, gt):
    nc = c.nc
    cosw = scos[:, gt:gt + 512]
    sinw = ssin[:, gt:gt + 512]
    c.wait(nc.vector, "v")
    for half in range(2):
        q = sqk[:, half * 512:half * 512 + 512]
        t1 = sqr[:, half * 512:half * 512 + 512]
        u = sqr[:, 1024 + half * 512:1024 + half * 512 + 512]
        c.inc(nc.vector.tensor_mul(out=t1, in0=q, in1=cosw), "v")
        c.inc(nc.vector.tensor_mul(out=u, in0=q, in1=sinw), "v")
    c.wait(nc.tensor, "v")
    c.wait(nc.tensor, "v", c.counts.get("_ropefree", 0))
    for half in range(2):
        u = sqr[:, 1024 + half * 512:1024 + half * 512 + 512]
        mm = nc.tensor.matmul(prope[:, half * 512:half * 512 + 512],
                              p32_sb[:], u, start=True, stop=True)
        c.inc(mm, "rp")
    c.wait(nc.vector, "rp")
    for half in range(2):
        q = sqk[:, half * 512:half * 512 + 512]
        t1 = sqr[:, half * 512:half * 512 + 512]
        ad = nc.vector.tensor_add(out=q, in0=t1,
                                  in1=prope[:, half * 512:half * 512 + 512])
        c.inc(ad, "v")
    c.counts["_ropefree"] = c.counts["v"]


def emit_attention(c, q_src, k_src, v_src, z_dst, tag, scale, zg_out=None):
    nc = c.nc
    c.push()
    NKT = T // 128
    sk = [c.sb(f"ak{x}_{tag}", [128, T], BF16) for x in range(2)]
    sq = [c.sb(f"aq{x}_{tag}", [128, T], BF16) for x in range(2)]
    sva = [c.sb(f"av{x}a_{tag}", [128, NKT * 65], BF16) for x in range(2)]
    svb = [c.sb(f"av{x}b_{tag}", [128, NKT * 65], BF16) for x in range(2)]
    sattn = [c.sb(f"aat{x}_{tag}", [128, 3 * 1024], BF16) for x in range(2)]
    srec = [c.sb(f"arc{x}_{tag}", [1, 2 * 1024], BF16) for x in range(2)]
    srecf = [c.sb(f"arf{x}_{tag}", [1, 1024], F32) for x in range(2)]
    sbc = [c.sb(f"abc{x}_{tag}", [128, 1024], BF16) for x in range(2)]
    szr = [c.sb(f"azr{x}_{tag}", [128, 2 * 1024], BF16) for x in range(2)]
    sones1 = c.sb(f"aon_{tag}", [1, 64], BF16)
    smask = c.sb(f"amk_{tag}", [128, 4 * 512], BF16)
    ps = [c.ps(f"aps{x}_{tag}", [128, 2 * 512], F32) for x in range(2)]
    pz = [c.ps(f"apz{x}_{tag}", [128, 2 * 512], F32) for x in range(2)]

    c.inc(nc.gpsimd.memset(sones1[:], 1.0), "g")
    for m in range(4):
        c.inc(nc.gpsimd.memset(smask[:, m * 512:(m + 1) * 512], 1.0), "g")
    c.wait(nc.gpsimd, "g")
    for m in range(4):
        am = nc.gpsimd.affine_select(
            out=smask[:, m * 512:(m + 1) * 512], in_=smask[:, m * 512:(m + 1) * 512],
            compare_op=ALU.is_ge, fill=0.0, base=-128 * m,
            pattern=[[1, 512]], channel_multiplier=-1)
        c.inc(am, "g")
    for x in range(2):
        c.inc(nc.sync.dma_start(out=sk[x][:], in_=k_src[x * 128:(x + 1) * 128, :]),
              "dh", 16)
        c.inc(nc.sync.dma_start(out=sq[x][:], in_=q_src[x * 128:(x + 1) * 128, :]),
              "dh", 16)
        for hh, sv in ((0, sva[x]), (1, svb[x])):
            ld = nc.sync.dma_start(
                out=sv.ap().rearrange("p (n c) -> p n c", c=65)[:, :, 0:64],
                in_=v_src.rearrange("(n p) c -> p n c", p=128)[
                    :, :, x * 128 + hh * 64:x * 128 + hh * 64 + 64])
            c.inc(ld, "dh", 16)
            c.inc(nc.gpsimd.memset(
                sv.ap().rearrange("p (n c) -> p n c", c=65)[:, :, 64:65], 1.0), "g")
    c.wait(nc.sync, "dh")
    c.wait(nc.gpsimd, "g")
    c.barrier()

    scs = ["scA", "scB"]; exs = ["exA", "exB"]; avs = ["avA", "avB"]
    mks = ["mkA", "mkB"]; rcs = ["rcA", "rcB"]; bcs = ["bcA", "bcB"]
    zes = ["zeA", "zeB"]
    cps = ["cpA", "cpB"]
    for n_ in scs + exs + avs + mks + rcs + bcs + zes + cps + ["zst0", "zst1"]:
        c.sem(n_)
    ex0 = [c.counts[exs[0]], c.counts[exs[1]]]
    mk0 = [c.counts[mks[0]], c.counts[mks[1]]]
    rc_done = [{}, {}]

    def e_scores(x, i, jq, kt):
        if i >= 1:
            c.wait(nc.tensor, exs[x], c.counts[exs[x]])
        if kt == 0:
            c.wait(nc.tensor, cps[x], c.counts[cps[x]])
        qsl = sq[x][:, jq * 512:(jq + 1) * 512]
        ksl = sk[x][:, kt * 128:(kt + 1) * 128]
        nc.tensor.matmul(ps[x][:, 0:512], ksl[0:64, :], qsl[0:64, :],
                         start=True, stop=True)
        mmb = nc.tensor.matmul(ps[x][:, 512:1024], ksl[64:128, :], qsl[64:128, :],
                               start=True, stop=True)
        c.inc(mmb, scs[x])

    def e_exp(x, i, jq, kt):
        par = i % 3
        c.wait(nc.scalar, scs[x])
        e = nc.scalar.activation(sattn[x][:, par * 1024:(par + 1) * 1024],
                                 ps[x][:], AF.Exp, scale=scale)
        c.inc(e, exs[x])

    def e_mask(x, i, jq, kt):
        par = i % 3
        a0 = par * 1024
        m = kt - 4 * jq
        eng = nc.vector if x == 0 else nc.gpsimd
        c.wait(eng, exs[x])
        msl = smask[:, m * 512:(m + 1) * 512]
        eng.tensor_mul(out=sattn[x][:, a0:a0 + 512],
                       in0=sattn[x][:, a0:a0 + 512], in1=msl)
        mb = eng.tensor_mul(out=sattn[x][:, a0 + 512:a0 + 1024],
                            in0=sattn[x][:, a0 + 512:a0 + 1024], in1=msl)
        c.inc(mb, mks[x])

    def e_av(x, i, jq, kt, dcount):
        nk = 4 * (jq + 1)
        if kt == 0:
            c.wait(nc.tensor, zes[x], c.counts[zes[x]])
        if kt >= 4 * jq:
            c.wait(nc.tensor, mks[x], mk0[x] + dcount)
        else:
            c.wait(nc.tensor, exs[x], ex0[x] + i + 1)
        par = i % 3
        a0 = par * 1024
        st, sp = (kt == 0), (kt == nk - 1)
        nc.tensor.matmul(pz[x][0:65, 0:512], sva[x][:, kt * 65:(kt + 1) * 65],
                         sattn[x][:, a0:a0 + 512], start=st, stop=sp,
                         skip_group_check=True)
        mm = nc.tensor.matmul(pz[x][0:65, 512:1024], svb[x][:, kt * 65:(kt + 1) * 65],
                              sattn[x][:, a0 + 512:a0 + 1024], start=st, stop=sp,
                              skip_group_check=True)
        c.inc(mm, avs[x])

    def early_evac(x, jq):
        sl = (jq % 2) * 1024
        c.wait(nc.scalar, avs[x])
        c.wait(nc.scalar, rcs[x])
        cpa = nc.scalar.activation(srecf[x][0:1, :], pz[x][64:65, :],
                                   AF.Identity)
        c.inc(cpa, rcs[x])
        c.wait(nc.vector, avs[x])
        c.wait(nc.vector, "dh")
        zc = nc.vector.tensor_copy(out=szr[x][0:64, sl:sl + 1024],
                                   in_=pz[x][0:64, 0:1024])
        c.inc(zc, zes[x])
        c.wait(nc.vector, rcs[x])
        rc = nc.vector.reciprocal(out=srecf[x][0:1, :], in_=srecf[x][0:1, :])
        c.inc(rc, "v")
        c.wait(nc.vector, "v")
        rc2 = nc.vector.tensor_copy(out=srec[x][0:1, sl:sl + 1024],
                                    in_=srecf[x][0:1, :])
        c.inc(rc2, rcs[x])

    def late_evac(x, jq):
        sl = (jq % 2) * 1024
        c.wait(nc.tensor, rcs[x], rc_done[x][jq])
        c.wait(nc.tensor, exs[x], c.counts[exs[x]])
        c.wait(nc.tensor, cps[x], c.counts[cps[x]])
        nc.tensor.matmul(ps[x][0:64, 0:512], sones1[:], srec[x][0:1, sl:sl + 512],
                         start=True, stop=True)
        bc = nc.tensor.matmul(ps[x][0:64, 512:1024], sones1[:],
                              srec[x][0:1, sl + 512:sl + 1024],
                              start=True, stop=True)
        c.inc(bc, bcs[x])
        c.wait(nc.scalar, bcs[x])
        c.wait(nc.scalar, "zst%d" % x, c.counts["zst%d" % x])
        cpc = nc.scalar.activation(sbc[x][0:64, :], ps[x][0:64, :], AF.Identity)
        c.inc(cpc, cps[x])
        c.wait(nc.vector, cps[x])
        nc.vector.tensor_mul(out=szr[x][0:64, sl:sl + 512],
                             in0=szr[x][0:64, sl:sl + 512],
                             in1=sbc[x][0:64, 0:512])
        zm = nc.vector.tensor_mul(out=szr[x][0:64, sl + 512:sl + 1024],
                                  in0=szr[x][0:64, sl + 512:sl + 1024],
                                  in1=sbc[x][0:64, 512:1024])
        c.inc(zm, "zst%d" % x)
        c.wait(nc.sync, "zst%d" % x, c.counts["zst%d" % x])
        r0 = (jq // 4) * 256 + x * 128
        c0 = (jq % 4) * 512
        st1 = nc.sync.dma_start(out=z_dst[r0:r0 + 64, c0:c0 + 512],
                                in_=szr[x][0:64, sl:sl + 512])
        c.inc(st1, "dh", 16)
        st2 = nc.sync.dma_start(out=z_dst[r0 + 64:r0 + 128, c0:c0 + 512],
                                in_=szr[x][0:64, sl + 512:sl + 1024])
        c.inc(st2, "dh", 16)

    pend = [[], []]
    ndiag = [0, 0]
    stepx = [0, 0]

    def substep(x, jq, kt):
        i = stepx[x]
        e_scores(x, i, jq, kt)
        e_exp(x, i, jq, kt)
        if kt >= 4 * jq:
            ndiag[x] += 1
            e_mask(x, i, jq, kt)
        pend[x].append((i, jq, kt, ndiag[x]))
        if len(pend[x]) > 2:
            e_av(x, *pend[x].pop(0))
        stepx[x] += 1

    def flush(x, jq):
        while pend[x]:
            e_av(x, *pend[x].pop(0))
        early_evac(x, jq)
        rc_done[x][jq] = c.counts[rcs[x]]

    def issue_zchunk(r):
        c.wait(nc.gpsimd, "dh", c.counts["dh"])
        cc = nc.gpsimd.collective_compute(
            "AllGather", mybir.AluOpType.bypass,
            replica_groups=[[0, 1, 2, 3], [4, 5, 6, 7]],
            ins=[z_dst[r * 256:(r + 1) * 256, :]],
            outs=[zg_out[r, :, :].squeeze()])
        c.inc(cc, "cc")

    LAG = 4
    for jq in range(16):
        nk = 4 * (jq + 1)
        for kt in range(nk + LAG):
            if kt < nk:
                if kt == 0 and jq >= 2:
                    late_evac(0, jq - 2)
                if kt == 2 and jq >= 1:
                    flush(1, jq - 1)
                substep(0, jq, kt)
            if kt == nk + 2:
                flush(0, jq)
            if kt >= LAG:
                if kt - LAG == 0 and jq >= 2:
                    late_evac(1, jq - 2)
                substep(1, jq, kt - LAG)
                if (zg_out is not None and kt == LAG and jq >= 5
                        and (jq - 5) % 4 == 0):
                    issue_zchunk((jq - 5) // 4)
    flush(1, 15)
    for jq in (14, 15):
        late_evac(0, jq)
        late_evac(1, jq)
    if zg_out is not None:
        issue_zchunk(3)
    c.wait(nc.sync, "dh")
    c.barrier()
    c.pop()


def emit_outproj_residual(c, z_own, w_ot, x_sb, tag, z_gathered=None):
    nc = c.nc
    c.push()
    sz = c.sb(f"oz_{tag}", [128, NCH * TO], BF16)
    sw = c.sb(f"ow_{tag}", [128, NCH * E], BF16)
    pp = c.ps(f"op_{tag}", [128, 2 * 512], F32)
    if z_gathered is None:
        ld = nc.sync.dma_start(out=sz.ap().rearrange("p (a t) -> p a t", a=NCH),
                               in_=z_own.rearrange("(a p) t -> p a t", p=128))
        c.inc(ld, "dh", 16)
    else:
        zg, qd, cc_need = z_gathered
        zgv = zg.rearrange("c (ch p) t -> (c ch) p t", p=128)
        c.wait(nc.sync, "cc", cc_need)
        for ch in range(NCH):
            ld = nc.sync.dma_start(
                out=sz[:, ch * TO:(ch + 1) * TO],
                in_=zgv[bass.ts(qd * NCH + ch, 1), :, :].squeeze())
            c.inc(ld, "dh", 16)
    ld = nc.gpsimd.dma_start(out=sw.ap().rearrange("p (a f) -> p a f", a=NCH),
                             in_=w_ot.rearrange("(a p) f -> p a f", p=128))
    c.inc(ld, "dg", 16)
    c.wait(nc.tensor, "dh")
    c.wait(nc.tensor, "dg")
    c.sem("oev")
    for ts in range(NTT):
        for fh in range(2):
            sl = (ts * 2 + fh) % 2
            if ts * 2 + fh >= 2:
                c.wait(nc.tensor, "oev", c.counts["oev"] - 1)
            for a in range(NCH):
                mm = nc.tensor.matmul(
                    pp[:, sl * 512:sl * 512 + 512],
                    sz[:, a * TO + ts * 128:a * TO + (ts + 1) * 128],
                    sw[:, a * E + fh * 512:a * E + fh * 512 + 512],
                    start=(a == 0), stop=(a == NCH - 1))
            c.inc(mm, "omm")
            c.wait(nc.vector, "omm")
            ad = nc.vector.tensor_add(
                out=x_sb[:, ts * E + fh * 512:ts * E + fh * 512 + 512],
                in0=x_sb[:, ts * E + fh * 512:ts * E + fh * 512 + 512],
                in1=pp[:, sl * 512:sl * 512 + 512])
            c.inc(ad, "oev")
    c.barrier()
    c.pop()


def emit_store_x(c, x_sb, x_dst):
    nc = c.nc
    st = nc.sync.dma_start(out=x_dst.rearrange("(n p) f -> p n f", p=128),
                           in_=x_sb.ap().rearrange("p (n f) -> p n f", f=E))
    c.inc(st, "dh", 16)
    c.wait(nc.sync, "dh")
    c.barrier()



def build_A():
    nc = bass.Bass("TRN2", target_bir_lowering=False)
    x_in = nc.dram_tensor("x_in", [TO, E], F32, kind="ExternalInput")
    xln = nc.dram_tensor("xln", [E, TO], BF16, kind="ExternalOutput")
    c = Ctx(nc)
    emit_ln_transpose(c, x_in.ap(), xln.ap(), "a")
    c.close()
    return nc


def build_B(rope, scale):
    nc = bass.Bass("TRN2", target_bir_lowering=False)
    xg = nc.dram_tensor("xg", [4 * E, TO], BF16, kind="ExternalInput")
    wq = nc.dram_tensor("wq", [E, 256], F32, kind="ExternalInput")
    wk = nc.dram_tensor("wk", [E, 256], F32, kind="ExternalInput")
    wv = nc.dram_tensor("wv", [E, 256], F32, kind="ExternalInput")
    zo = nc.dram_tensor("zo", [E, TO], BF16, kind="ExternalOutput")
    qd_ = nc.dram_tensor("q_i", [256, T], BF16)
    kd_ = nc.dram_tensor("k_i", [256, T], BF16)
    vd_ = nc.dram_tensor("v_i", [T, 256], BF16)
    rope_cs = None
    if rope:
        rc = nc.dram_tensor("rcos", [128, T], F32, kind="ExternalInput")
        rs = nc.dram_tensor("rsin", [128, T], F32, kind="ExternalInput")
        p32 = nc.dram_tensor("p32", [128, 128], F32, kind="ExternalInput")
        rope_cs = (rc.ap(), rs.ap(), p32.ap())
    c = Ctx(nc)
    emit_qkv(c, xg.ap(), wq.ap(), wk.ap(), wv.ap(), qd_.ap(), kd_.ap(), vd_.ap(),
             "b", rope_cs=rope_cs)
    emit_attention(c, qd_.ap(), kd_.ap(), vd_.ap(), zo.ap(), "b", scale)
    c.close()
    return nc


def build_Cpart():
    nc = bass.Bass("TRN2", target_bir_lowering=False)
    x_in = nc.dram_tensor("x_in", [TO, E], F32, kind="ExternalInput")
    z_in = nc.dram_tensor("z_in", [E, TO], BF16, kind="ExternalInput")
    w_ot = nc.dram_tensor("w_ot", [E, E], F32, kind="ExternalInput")
    x_out = nc.dram_tensor("x_out", [TO, E], F32, kind="ExternalOutput")
    c = Ctx(nc)
    x_sb = c.sb("xres", [128, NTT * E], F32)
    ld = nc.sync.dma_start(out=x_sb.ap().rearrange("p (n f) -> p n f", f=E),
                           in_=x_in.ap().rearrange("(n p) f -> p n f", p=128))
    c.inc(ld, "dh", 16)
    c.wait(nc.sync, "dh")
    c.barrier()
    emit_outproj_residual(c, z_in.ap(), w_ot.ap(), x_sb, "c")
    emit_store_x(c, x_sb, x_out.ap())
    c.close()
    return nc



def emit_bfly_matrices(c, bfw, mats, plans, tag):
    nc = c.nc
    c.push()
    Y0 = c.sb(f"y0_{tag}", [128, 4096], F32)
    Y1 = c.sb(f"y1_{tag}", [128, 4096], F32)
    tmp1 = c.sb(f"bt1_{tag}", [128, 2048], F32)
    tmp2 = c.sb(f"bt2_{tag}", [128, 2048], F32)
    wrow = [c.sb(f"bw{i}_{tag}", [1, 2 * 2048], BF16) for i in range(2)]
    wt = [c.sb(f"bwt{i}_{tag}", [128, 2 * 2048], F32) for i in range(2)]
    sone = c.sb(f"bon_{tag}", [1, 128], BF16)
    pwb = c.ps(f"bpw_{tag}", [128, 2 * 2048], F32)
    c.inc(nc.gpsimd.memset(sone[:], 1.0), "g")
    c.sem("v"); c.sem("s_pe"); c.sem("g"); c.sem("a")
    c.sem("bld0"); c.sem("bld1")

    state = {"pair": 0, "pwb_free": 0, "matcopy": 0}
    wrow_free = [0, 0]
    wt_ready = [0, 0]
    wt_users = [0, 0]

    def bcast_pair(base_k, pair):
        bi = state["pair"] % 2
        c.wait(nc.gpsimd, "s_pe", wrow_free[bi])
        ld = nc.gpsimd.dma_start(
            out=wrow[bi].ap().rearrange("o (w n) -> o w n", w=2),
            in_=bfw[base_k, 2 * pair:2 * pair + 2, :])
        nld = c.inc(ld, "bld%d" % bi, 16)
        c.wait(nc.tensor, "bld%d" % bi, nld)
        c.wait(nc.tensor, "a", state["pwb_free"])
        first = True
        for w2i in range(2):
            for col in range(4):
                a0 = w2i * 2048 + col * 512
                mmb = nc.tensor.matmul(pwb[:, a0:a0 + 512], sone[:],
                                       wrow[bi][0:1, a0:a0 + 512],
                                       start=True, stop=True)
            c.inc(mmb, "s_pe")
            c.wait(nc.scalar, "s_pe")
            if first:
                c.wait(nc.scalar, "v", wt_users[bi])
                first = False
            cpw = nc.scalar.activation(wt[bi][:, w2i * 2048:(w2i + 1) * 2048],
                                       pwb[:, w2i * 2048:(w2i + 1) * 2048],
                                       AF.Identity)
            c.inc(cpw, "a")
        wrow_free[bi] = c.counts["s_pe"]
        wt_ready[bi] = c.counts["a"]
        state["pwb_free"] = c.counts["a"]
        state["pair"] += 1
        return bi

    base = 0
    order = ["M1", "D1", "N1", "N2"]
    for name in order:
        st = plans[name]
        c.wait(nc.gpsimd, "v", c.counts["v"])
        c.inc(nc.gpsimd.memset(Y0[:], 1.0), "g")
        c.wait(nc.gpsimd, "g")
        ai = nc.gpsimd.affine_select(out=Y0[:], in_=Y0[:], compare_op=ALU.is_equal,
                                     fill=0.0, base=0, pattern=[[0, 32], [-1, 128]],
                                     channel_multiplier=1)
        c.inc(ai, "g")
        gp_init = c.counts["g"]
        cur, nxt = Y0, Y1
        for k, s2 in enumerate(st):
            nb = 128 // (2 * s2)

            def vw(t_, half):
                return t_.rearrange("p (blk kb r) -> p blk kb r", blk=32, kb=nb)[
                    :, :, :, half * s2:(half + 1) * s2]

            def cwv(bi_, i2):
                return wt[bi_].ap()[:, i2 * 2048:(i2 + 1) * 2048].rearrange(
                    "p (blk kb s) -> p blk kb s", blk=32, kb=nb)

            tmp1v = tmp1.ap().rearrange("p (blk kb s) -> p blk kb s", blk=32, kb=nb)
            tmp2v = tmp2.ap().rearrange("p (blk kb s) -> p blk kb s", blk=32, kb=nb)
            a_ap = vw(cur.ap(), 0)
            b_ap = vw(cur.ap(), 1)
            for pair in range(2):
                bi = bcast_pair(base + k, pair)
                c.wait(nc.vector, "a", wt_ready[bi])
                if k == 0:
                    c.wait(nc.vector, "g", gp_init)
                    c.wait(nc.vector, "a", state["matcopy"])
                c.wait(nc.vector, "v", c.counts["v"])
                m1_ = nc.vector.tensor_mul(out=tmp1v, in0=a_ap, in1=cwv(bi, 0))
                c.inc(m1_, "v")
                m2_ = nc.vector.tensor_mul(out=tmp2v, in0=b_ap, in1=cwv(bi, 1))
                c.inc(m2_, "v")
                c.wait(nc.vector, "v", c.counts["v"])
                ad = nc.vector.tensor_add(out=vw(nxt.ap(), pair), in0=tmp1v,
                                          in1=tmp2v)
                c.inc(ad, "v")
                wt_users[bi] = c.counts["v"]
            cur, nxt = nxt, cur
        base += len(st)
        c.wait(nc.scalar, "v", c.counts["v"])
        cc = nc.scalar.activation(mats[name][:], cur[:], AF.Identity)
        state["matcopy"] = c.inc(cc, "a")
    c.wait(nc.vector, "a", state["matcopy"])
    c.barrier()
    c.pop()


def emit_fold_w1(c, wexpT, bexp, M1_sb, w1e_dram, b1e_dram, tag):
    nc = c.nc
    c.push()
    swx = c.sb(f"fw_{tag}", [128, 32 * 128], BF16)
    sbe = c.sb(f"fb_{tag}", [128, 32], BF16)
    acc = c.sb(f"fa_{tag}", [128, 4096], BF16)
    brow = c.sb(f"fbr_{tag}", [1, 4096], F32)
    pf = c.ps(f"fp_{tag}", [128, 2 * 512], F32)
    c.sem("v"); c.sem("s_pe"); c.sem("dg"); c.sem("dh")

    ld = nc.gpsimd.dma_start(out=sbe.ap(), in_=bexp)
    c.inc(ld, "dgb", 16)
    c.wait(nc.tensor, "dgb")
    for co in range(32):
        c.wait(nc.tensor, "v", c.counts["v"])
        mm = nc.tensor.matmul(pf[0:1, 0:128], sbe[:, co:co + 1],
                              M1_sb[:, co * 128:(co + 1) * 128],
                              start=True, stop=True)
        c.inc(mm, "s_pe")
        c.wait(nc.vector, "s_pe")
        cp = nc.vector.tensor_copy(out=brow[0:1, co * 128:(co + 1) * 128],
                                   in_=pf[0:1, 0:128])
        c.inc(cp, "v")
    c.wait(nc.sync, "v")
    stb = nc.sync.dma_start(out=b1e_dram, in_=brow.ap())
    c.inc(stb, "dh", 16)
    c.wait(nc.tensor, "v")

    mm_after_ci = 0
    pf_free = [c.counts["v"], c.counts["v"]]
    for ci in range(NCH):
        if ci >= 1:
            c.wait(nc.gpsimd, "s_pe", mm_after_ci)
        ldw = nc.gpsimd.dma_start(
            out=swx.ap().rearrange("p (co q) -> p co q", co=32),
            in_=wexpT[:, ci * 128:(ci + 1) * 128].rearrange("(co p) q -> p co q", p=128))
        nld = c.inc(ldw, "dg", 16)
        c.wait(nc.tensor, "dg", nld)
        c.wait(nc.vector, "dh")
        for co in range(32):
            sl = co % 2
            c.wait(nc.tensor, "v", pf_free[sl])
            mm = nc.tensor.matmul(
                pf[:, sl * 512:sl * 512 + 128],
                swx[:, co * 128:(co + 1) * 128],
                M1_sb[:, co * 128:(co + 1) * 128],
                start=True, stop=True)
            c.inc(mm, "s_pe")
            c.wait(nc.vector, "s_pe")
            cp = nc.vector.tensor_copy(
                out=acc.ap().rearrange("p (c2 hi lo) -> p hi c2 lo",
                                       c2=32, hi=4)[:, :, :, co:co + 1],
                in_=pf[:, sl * 512:sl * 512 + 128].rearrange(
                    "p (hi c2 one) -> p hi c2 one", hi=4, c2=32))
            pf_free[sl] = c.inc(cp, "v")
        mm_after_ci = c.counts["s_pe"]
        c.wait(nc.sync, "v")
        stw = nc.sync.dma_start(
            out=w1e_dram[:, ci * 128:(ci + 1) * 128, :].rearrange("c2 p q -> p c2 q"),
            in_=acc.ap().rearrange("p (c2 q) -> p c2 q", c2=32))
        c.inc(stw, "dh", 16)
    c.wait(nc.sync, "dh")
    c.barrier()
    c.pop()


def emit_fold_w2(c, wconP, N2_sb, w2e_dram, tag):
    nc = c.nc
    c.push()
    swc = c.sb(f"gw_{tag}", [128, 1024], BF16)
    sev = c.sb(f"ge_{tag}", [128, 1024], BF16)
    pg = c.ps(f"gp_{tag}", [128, 2 * 512], F32)
    c.sem("v"); c.sem("s_pe"); c.sem("dg"); c.sem("dh")
    mm_after = 0
    for c2 in range(32):
        if c2 >= 1:
            c.wait(nc.gpsimd, "s_pe", mm_after)
        ldw = nc.gpsimd.dma_start(out=swc.ap(),
                                  in_=wconP[c2 * 128:(c2 + 1) * 128, :])
        nld = c.inc(ldw, "dg", 16)
        c.wait(nc.tensor, "dg", nld)
        c.wait(nc.tensor, "dh")
        for dh_ in range(2):
            mm = nc.tensor.matmul(pg[:, dh_ * 512:(dh_ + 1) * 512],
                                  N2_sb[:, c2 * 128:(c2 + 1) * 128],
                                  swc[:, dh_ * 512:(dh_ + 1) * 512],
                                  start=True, stop=True)
            c.inc(mm, "s_pe")
        mm_after = c.counts["s_pe"]
        c.wait(nc.vector, "s_pe")
        cp = nc.vector.tensor_copy(out=sev[:], in_=pg[:])
        c.inc(cp, "v")
        c.wait(nc.sync, "v")
        for hi in range(4):
            r0 = c2 + 32 * hi
            dst = w2e_dram.rearrange("(co r) d -> r co d", r=128)[r0:r0 + 1, :, :].squeeze()
            st = nc.sync.dma_start(out=dst, in_=sev[hi * 32:(hi + 1) * 32, :])
            c.inc(st, "dh", 16)
    c.wait(nc.sync, "dh")
    c.barrier()
    c.pop()


def emit_ffn_l0(c, xln_fm, w1e_dram, b1e_dram, up_bP, w2e_dram, brow2,
                D1_sb, N1_sb, relay_dram, x_sb, tag):
    nc = c.nc
    c.push()
    sxln = c.sb(f"lx_{tag}", [128, NCH * TO], BF16)
    sw1 = c.sb(f"lw1_{tag}", [128, 2 * NCH * 128], BF16)
    sh1 = c.sb(f"lh1_{tag}", [128, 2 * 512], BF16)
    sh2 = c.sb(f"lh2_{tag}", [128, 2 * 512], BF16)
    sh2o = c.sb(f"lho_{tag}", [128, 2 * 512], BF16)
    sh3 = c.sb(f"lh3_{tag}", [128, 32 * 512], BF16)
    sw2 = c.sb(f"lw2_{tag}", [128, 2 * 512], BF16)
    sb1e = c.sb(f"lb1_{tag}", [128, 32], F32)
    subp = c.sb(f"lbu_{tag}", [128, 32], F32)
    sbr2 = c.sb(f"lbr_{tag}", [2, 1024], BF16)
    sone2 = c.sb(f"lo2_{tag}", [2, 128], BF16)
    psA = c.ps(f"lpa_{tag}", [128, 2 * 512], F32)
    psB = c.ps(f"lpb_{tag}", [128, 2 * 512], F32)
    psY = c.ps(f"lpy_{tag}", [128, 4 * 512], F32)
    c.sem("v"); c.sem("s_pe"); c.sem("s"); c.sem("dg"); c.sem("dh"); c.sem("dgw")

    ld = nc.sync.dma_start(out=sxln.ap().rearrange("p (ch t) -> p ch t", ch=NCH),
                           in_=xln_fm.rearrange("(ch p) t -> p ch t", p=128))
    c.inc(ld, "dh", 16)
    for hi in range(4):
        ldb = nc.gpsimd.dma_start(
            out=sb1e[hi * 32:(hi + 1) * 32, :],
            in_=b1e_dram.rearrange("o (lo r) -> lo (o r)", lo=32)[
                :, 32 * hi:32 * hi + 32])
        c.inc(ldb, "dg", 16)
    ld = nc.gpsimd.dma_start(out=subp.ap(), in_=up_bP)
    c.inc(ld, "dg", 16)
    ld = nc.gpsimd.dma_start(out=sbr2.ap(), in_=brow2)
    c.inc(ld, "dg", 16)
    c.inc(nc.gpsimd.memset(sone2[:], 1.0), "g")
    c.wait(nc.sync, "dh")
    c.wait(nc.gpsimd, "dg")
    c.wait(nc.gpsimd, "g")
    c.barrier()

    w1used = [0, 0]
    w1ld = [0, 0]
    w2ld = [0, 0]
    hold = [0, 0]
    hgld = [0, 0]
    s_after_ev1 = {}
    s_after_ev2 = [0, 0]
    mmd_cnt = [0, 0]
    st_cnt = [0, 0]
    housed = [0, 0]
    ev3_cnt = [0, 0]
    w2used = [0, 0]
    qeng = [nc.gpsimd, nc.sync]
    for tt in range(4):
        t0 = tt * 512
        def ld_w1e(c2_):
            rot_ = c2_ % 2
            eng = qeng[rot_]
            wbuf_ = sw1[:, rot_ * NCH * 128:(rot_ + 1) * NCH * 128]
            c.wait(eng, "s_pe", w1used[rot_])
            ldw = eng.dma_start(
                out=wbuf_.rearrange("p (ci q) -> p ci q", ci=NCH),
                in_=w1e_dram[c2_, :, :].rearrange("(ci p) q -> p ci q", p=128))
            w1ld[rot_] = c.inc(ldw, "dl0w%d" % rot_, 16)

        def d1_tail(c2_):
            rot_ = c2_ % 2
            c.wait(nc.tensor, "s", s_after_ev1[c2_])
            c.wait(nc.tensor, "s", s_after_ev2[rot_])
            mmd = nc.tensor.matmul(psB[:, rot_ * 512:rot_ * 512 + 512],
                                   D1_sb[:, c2_ * 128:(c2_ + 1) * 128],
                                   sh1[:, rot_ * 512:rot_ * 512 + 512],
                                   start=True, stop=True)
            mmd_cnt[rot_] = c.inc(mmd, "s_pe")
            c.wait(nc.scalar, "s_pe")
            c.wait(nc.scalar, "str%d" % rot_, st_cnt[rot_])
            ev2 = nc.scalar.activation(sh2[:, rot_ * 512:rot_ * 512 + 512],
                                       psB[:, rot_ * 512:rot_ * 512 + 512],
                                       AF.Gelu, bias=subp[:, c2_:c2_ + 1])
            s_after_ev2[rot_] = c.inc(ev2, "s")
            eng = qeng[rot_]
            c.wait(eng, "s", s_after_ev2[rot_])
            str_ = eng.dma_start(out=relay_dram[c2_, :, :],
                                 in_=sh2[:, rot_ * 512:rot_ * 512 + 512])
            st_cnt[rot_] = c.inc(str_, "str%d" % rot_, 16)

        ld_w1e(0)
        pendP = []
        for c2 in range(32):
            rot = c2 % 2
            wbuf = sw1[:, rot * NCH * 128:(rot + 1) * NCH * 128]
            if c2 + 1 < 32:
                ld_w1e(c2 + 1)
            c.wait(nc.tensor, "dl0w%d" % rot, w1ld[rot])
            c.wait(nc.tensor, "s", s_after_ev1.get(c2 - 2, 0))
            for ci in range(NCH):
                mm = nc.tensor.matmul(psA[:, rot * 512:rot * 512 + 512],
                                      wbuf[:, ci * 128:(ci + 1) * 128],
                                      sxln[:, ci * TO + t0:ci * TO + t0 + 512],
                                      start=(ci == 0), stop=(ci == NCH - 1))
            w1used[rot] = c.inc(mm, "s_pe")
            c.wait(nc.scalar, "s_pe")
            c.wait(nc.scalar, "s_pe", mmd_cnt[rot])
            ev1 = nc.scalar.activation(sh1[:, rot * 512:rot * 512 + 512],
                                       psA[:, rot * 512:rot * 512 + 512],
                                       AF.Identity, bias=sb1e[:, c2:c2 + 1])
            s_after_ev1[c2] = c.inc(ev1, "s")
            pendP.append(c2)
            if len(pendP) > 1:
                d1_tail(pendP.pop(0))
        d1_tail(pendP.pop(0))
        c.wait(nc.sync, "str0")
        c.wait(nc.sync, "str1")
        c.barrier()
        def ld_h2o(co_):
            rot_ = co_ % 2
            c.wait(nc.sync, "s_pe", housed[rot_])
            c.wait(nc.gpsimd, "s_pe", housed[rot_])
            for hi in range(4):
                eng = qeng[hi % 2]
                sname = ("dhg%d" if hi % 2 == 0 else "dho%d") % rot_
                ldo = eng.dma_start(
                    out=sh2o[hi * 32:(hi + 1) * 32, rot_ * 512:rot_ * 512 + 512],
                    in_=relay_dram[:, co_ + 32 * hi, :])
                n_ = c.inc(ldo, sname, 16)
                if hi % 2 == 0:
                    hgld[rot_] = n_
                else:
                    hold[rot_] = n_
        ld_h2o(0)
        for co in range(32):
            rot = co % 2
            if co + 1 < 32:
                ld_h2o(co + 1)
            c.wait(nc.tensor, "dho%d" % rot, hold[rot])
            c.wait(nc.tensor, "dhg%d" % rot, hgld[rot])
            c.wait(nc.tensor, "v", ev3_cnt[rot])
            mmn = nc.tensor.matmul(psA[:, rot * 512:rot * 512 + 512],
                                   N1_sb[:, co * 128:(co + 1) * 128],
                                   sh2o[:, rot * 512:rot * 512 + 512],
                                   start=True, stop=True)
            housed[rot] = c.inc(mmn, "s_pe")
            c.wait(nc.vector, "s_pe")
            ev3 = nc.vector.tensor_copy(out=sh3[:, co * 512:(co + 1) * 512],
                                        in_=psA[:, rot * 512:rot * 512 + 512])
            ev3_cnt[rot] = c.inc(ev3, "v")
        c.barrier()
        for fh in range(2):
            yfree = c.counts["v"]

            def ld_w2e(co_, fh_):
                rot_ = co_ % 2
                eng = qeng[rot_]
                c.wait(eng, "s_pe", w2used[rot_])
                ldw = eng.dma_start(
                    out=sw2[:, rot_ * 512:(rot_ + 1) * 512],
                    in_=w2e_dram[co_ * 128:(co_ + 1) * 128,
                                 fh_ * 512:fh_ * 512 + 512])
                w2ld[rot_] = c.inc(ldw, "dl0w%d" % rot_, 16)
            if fh == 0:
                ld_w2e(0, 0)
            for co in range(32):
                rot = co % 2
                wb = sw2[:, rot * 512:(rot + 1) * 512]
                if co + 1 < 32:
                    ld_w2e(co + 1, fh)
                elif fh == 0:
                    ld_w2e(0, 1)
                c.wait(nc.tensor, "dl0w%d" % rot, w2ld[rot])
                if co == 0:
                    c.wait(nc.tensor, "v", yfree)
                for ts in range(4):
                    mm = nc.tensor.matmul(
                        psY[:, ts * 512:(ts + 1) * 512],
                        sh3[:, co * 512 + ts * 128:co * 512 + (ts + 1) * 128],
                        wb, start=(co == 0), stop=False)
                w2used[rot] = c.inc(mm, "s_pe")
            for ts in range(4):
                mm = nc.tensor.matmul(psY[:, ts * 512:(ts + 1) * 512],
                                      sone2[:], sbr2[:, fh * 512:fh * 512 + 512],
                                      start=False, stop=True)
            c.inc(mm, "s_pe")
            c.wait(nc.vector, "s_pe")
            for ts in range(4):
                gtt = (tt * 4 + ts)
                ad = nc.vector.tensor_add(
                    out=x_sb[:, gtt * E + fh * 512:gtt * E + fh * 512 + 512],
                    in0=x_sb[:, gtt * E + fh * 512:gtt * E + fh * 512 + 512],
                    in1=psY[:, ts * 512:(ts + 1) * 512])
                c.inc(ad, "v")
        c.barrier()
    c.barrier()
    c.pop()


def emit_quantize_dist(c, w1t, w2t, part_d, partg_d, wq1c_d, wq1g_d, wq2c_d,
                       wq2g_d, qd, tag):
    nc = c.nc
    c.push()
    swf = c.sb(f"qw_{tag}", [128, 2 * 4096], F32)
    sw2f = c.sb(f"q2_{tag}", [128, 8 * 1024], F32)
    sabs = c.sb(f"qa_{tag}", [128, 2], F32)
    srow = c.sb(f"qr_{tag}", [1, 128], F32)
    sg16 = c.sb(f"qg_{tag}", [1, 512], F32)
    sth = c.sb(f"qt_{tag}", [128, 4], F32)
    sqt = c.sb(f"qq_{tag}", [128, 8192], BF16)
    stmp = c.sb(f"qtm_{tag}", [128, 8192], F32)
    sone = c.sb(f"qo_{tag}", [1, 128], F32)
    sonec = c.sb(f"qoc_{tag}", [128, 1], F32)
    pq_ = c.ps(f"qp_{tag}", [1, 512], F32)
    pbc = c.ps(f"qb_{tag}", [128, 512], F32)
    c.sem("v"); c.sem("s_pe"); c.sem("dq"); c.sem("dh")
    c.inc(nc.gpsimd.memset(sone[:], 1.0), "g")
    c.inc(nc.gpsimd.memset(sonec[:], 1.0), "g")
    w1v = w1t.rearrange("(ci p) q -> ci p q", p=128)
    w2v = w2t.rearrange("(co p) q -> co p q", p=128)
    for l in range(2):
        ldl = nc.gpsimd.dma_start(out=swf[:, l * 4096:(l + 1) * 4096],
                                  in_=w1v[bass.ts(qd + 4 * l, 1), :, :].squeeze())
        c.inc(ldl, "dq", 16)
    ld2 = nc.gpsimd.dma_start(
        out=sw2f.ap().rearrange("p (j q) -> p j q", j=8),
        in_=w2v[bass.ts(qd, 8), :, :].rearrange("j p q -> p j q"))
    c.inc(ld2, "dq", 16)
    c.wait(nc.vector, "dq")
    rs = nc.vector.reduce_sum(out=sabs[:, 0:1], in_=swf[:],
                              axis=mybir.AxisListType.X,
                              apply_absolute_value=True)
    c.inc(rs, "v")
    rs2 = nc.vector.reduce_sum(out=sabs[:, 1:2], in_=sw2f[:],
                               axis=mybir.AxisListType.X,
                               apply_absolute_value=True)
    c.inc(rs2, "v")
    c.wait(nc.tensor, "v")
    c.wait(nc.tensor, "g")
    mm1 = nc.tensor.matmul(pq_[0:1, 0:2], sonec[:], sabs[:, 0:2],
                           start=True, stop=True)
    c.inc(mm1, "s_pe")
    c.wait(nc.vector, "s_pe")
    cpp = nc.vector.tensor_copy(out=srow[0:1, 0:2], in_=pq_[0:1, 0:2])
    c.inc(cpp, "v")
    c.wait(nc.sync, "v")
    stp = nc.sync.dma_start(out=part_d, in_=srow[0:1, 0:128])
    npd = c.inc(stp, "dh", 16)
    c.wait(nc.gpsimd, "dh", npd)
    ccp = nc.gpsimd.collective_compute(
        "AllGather", mybir.AluOpType.bypass,
        replica_groups=[[0, 1, 2, 3], [4, 5, 6, 7]],
        ins=[part_d],
        outs=[partg_d.rearrange("o (g w) -> (o g) w", w=128)])
    ccp_n = c.inc(ccp, "cc")
    c.wait(nc.sync, "cc", ccp_n)
    ldp = nc.sync.dma_start(out=sg16[:], in_=partg_d)
    npg = c.inc(ldp, "dh", 16)
    c.wait(nc.vector, "dh", npg)
    tot = nc.vector.reduce_sum(
        out=srow.ap().rearrange("o (n one) -> o n one", one=1)[:, 0:2, :],
        in_=sg16.ap().rearrange("o (g w) -> o w g", w=128)[:, 0:2, :],
        axis=mybir.AxisListType.X)
    c.inc(tot, "v")
    c.wait(nc.vector, "v")
    ts1 = nc.vector.tensor_scalar(out=srow[0:1, 2:4], in0=srow[0:1, 0:2],
                                  scalar1=1.0 / (DH * E), scalar2=None,
                                  op0=ALU.mult, op1=ALU.bypass)
    c.inc(ts1, "v")
    c.wait(nc.vector, "v")
    ts2 = nc.vector.tensor_scalar(out=srow[0:1, 4:6], in0=srow[0:1, 2:4],
                                  scalar1=EPS, scalar2=0.5,
                                  op0=ALU.add, op1=ALU.mult)
    c.inc(ts2, "v")
    c.wait(nc.vector, "v")
    ts3 = nc.vector.tensor_scalar(out=srow[0:1, 6:8], in0=srow[0:1, 4:6],
                                  scalar1=-1.0, scalar2=None,
                                  op0=ALU.mult, op1=ALU.bypass)
    c.inc(ts3, "v")
    c.wait(nc.tensor, "v")
    mmb = nc.tensor.matmul(pbc[:, 0:4], sone[:], srow[0:1, 4:8],
                           start=True, stop=True)
    c.inc(mmb, "s_pe")
    c.wait(nc.vector, "s_pe")
    cpb = nc.vector.tensor_copy(out=sth[:], in_=pbc[:, 0:4])
    c.inc(cpb, "v")
    c.wait(nc.vector, "v")
    for (buf, thc, st_out, st_in) in (
            (swf, 0,
             wq1c_d.rearrange("lf p q -> p lf q"),
             sqt.ap().rearrange("p (lf q) -> p lf q", lf=64)),
            (sw2f, 1,
             wq2c_d.rearrange("j p q -> p j q"),
             sqt.ap().rearrange("p (j q) -> p j q", j=8))):
        c.wait(nc.vector, "dh")
        tn = nc.vector.tensor_single_scalar(out=stmp[:], in_=buf[:],
                                            scalar=sth[:, 2 + thc:3 + thc],
                                            op=ALU.is_lt)
        c.inc(tn, "v")
        c.wait(nc.vector, "v")
        qq = nc.vector.scalar_tensor_tensor(out=sqt[:], in0=buf[:],
                                            scalar=sth[:, thc:thc + 1],
                                            in1=stmp[:],
                                            op0=ALU.is_gt, op1=ALU.subtract)
        c.inc(qq, "v")
        c.wait(nc.sync, "v")
        st_ = nc.sync.dma_start(out=st_out, in_=st_in)
        c.inc(st_, "dh", 16)
    c.wait(nc.gpsimd, "dh", c.counts["dh"])
    cc1_ = nc.gpsimd.collective_compute(
        "AllGather", mybir.AluOpType.bypass,
        replica_groups=[[0, 1, 2, 3], [4, 5, 6, 7]],
        ins=[wq1c_d], outs=[wq1g_d])
    c.inc(cc1_, "cc")
    cc2_ = nc.gpsimd.collective_compute(
        "AllGather", mybir.AluOpType.bypass,
        replica_groups=[[0, 1, 2, 3], [4, 5, 6, 7]],
        ins=[wq2c_d], outs=[wq2g_d.rearrange("(co p) q -> co p q", p=128)])
    c.inc(cc2_, "cc")
    c.barrier()
    c.pop()


def emit_ffn_l1(c, xln_fm, b1cm, b2row, wq1g, wq2g, x_sb, tag, cc_need=0):
    nc = c.nc
    c.sem("v"); c.sem("s_pe"); c.sem("s"); c.sem("dg"); c.sem("dh"); c.sem("dgw")
    wq1fo = wq1g.rearrange("(rl fo) p q -> fo rl p q", fo=32)

    c.push()
    sxln = c.sb(f"qx_{tag}", [128, NCH * TO], BF16)
    sw1 = c.sb(f"qw1_{tag}", [128, 2 * NCH * 128], BF16)
    sh = c.sb(f"qh_{tag}", [128, 32 * 512], BF16)
    sw2 = c.sb(f"qw2_{tag}", [128, 2 * 512], BF16)
    sb1 = c.sb(f"qb1_{tag}", [128, 32], F32)
    sbr2 = c.sb(f"qbr_{tag}", [2, 1024], BF16)
    sone2 = c.sb(f"qo2_{tag}", [2, 128], BF16)
    psA = c.ps(f"qpa_{tag}", [128, 2 * 512], F32)
    psY = c.ps(f"qpy_{tag}", [128, 4 * 512], F32)
    ld = nc.sync.dma_start(out=sxln.ap().rearrange("p (ch t) -> p ch t", ch=NCH),
                           in_=xln_fm.rearrange("(ch p) t -> p ch t", p=128))
    c.inc(ld, "dh", 16)
    ldb = nc.gpsimd.dma_start(out=sb1.ap(), in_=b1cm)
    c.inc(ldb, "dg", 16)
    c.inc(nc.gpsimd.memset(sbr2[:], 0.0), "g")
    c.wait(nc.gpsimd, "g")
    ldb = nc.gpsimd.dma_start(out=sbr2[0:1, :], in_=b2row)
    c.inc(ldb, "dg", 16)
    c.inc(nc.gpsimd.memset(sone2[:], 1.0), "g")
    c.wait(nc.gpsimd, "g")
    c.wait(nc.gpsimd, "dg")
    c.wait(nc.sync, "dh")
    c.wait(nc.gpsimd, "cc", cc_need)
    c.barrier()
    z1used = [0, 0]
    z2used = [0, 0]
    for tt in range(4):
        t0 = tt * 512
        for fo in range(32):
            rot = fo % 2
            wbuf = sw1[:, rot * NCH * 128:(rot + 1) * NCH * 128]
            c.wait(nc.gpsimd, "s_pe", z1used[rot])
            ldw = nc.gpsimd.dma_start(
                out=wbuf.rearrange("p (ci q) -> p ci q", ci=NCH),
                in_=wq1fo[fo, :, :, :].squeeze().rearrange("c p q -> p c q"))
            nldw = c.inc(ldw, "dgw%d" % rot, 16)
            c.wait(nc.tensor, "dgw%d" % rot, nldw)
            if c.counts["s"] >= 2:
                c.wait(nc.tensor, "s", c.counts["s"] - 1)
            for b in range(NCH):
                ci = (b % 2) * 4 + b // 2
                mm = nc.tensor.matmul(psA[:, rot * 512:rot * 512 + 512],
                                      wbuf[:, b * 128:(b + 1) * 128],
                                      sxln[:, ci * TO + t0:ci * TO + t0 + 512],
                                      start=(b == 0), stop=(b == NCH - 1))
            z1used[rot] = c.inc(mm, "s_pe")
            c.wait(nc.scalar, "s_pe")
            ev = nc.scalar.activation(sh[:, fo * 512:(fo + 1) * 512],
                                      psA[:, rot * 512:rot * 512 + 512],
                                      AF.Relu, bias=sb1[:, fo:fo + 1])
            c.inc(ev, "s")
        c.wait(nc.tensor, "s")
        c.barrier()
        for fh in range(2):
            yfree = c.counts["v"]
            for co in range(32):
                rot = co % 2
                wb = sw2[:, rot * 512:(rot + 1) * 512]
                c.wait(nc.gpsimd, "s_pe", z2used[rot])
                ldw = nc.gpsimd.dma_start(
                    out=wb, in_=wq2g[co * 128:(co + 1) * 128,
                                     fh * 512:fh * 512 + 512])
                nldw = c.inc(ldw, "dgw%d" % rot, 16)
                c.wait(nc.tensor, "dgw%d" % rot, nldw)
                if co == 0:
                    c.wait(nc.tensor, "v", yfree)
                for ts in range(4):
                    mm = nc.tensor.matmul(
                        psY[:, ts * 512:(ts + 1) * 512],
                        sh[:, co * 512 + ts * 128:co * 512 + (ts + 1) * 128],
                        wb, start=(co == 0), stop=False)
                z2used[rot] = c.inc(mm, "s_pe")
            for ts in range(4):
                mm = nc.tensor.matmul(psY[:, ts * 512:(ts + 1) * 512],
                                      sone2[:], sbr2[:, fh * 512:fh * 512 + 512],
                                      start=False, stop=True)
            c.inc(mm, "s_pe")
            c.wait(nc.vector, "s_pe")
            for ts in range(4):
                gtt = (tt * 4 + ts)
                ad = nc.vector.tensor_add(
                    out=x_sb[:, gtt * E + fh * 512:gtt * E + fh * 512 + 512],
                    in0=x_sb[:, gtt * E + fh * 512:gtt * E + fh * 512 + 512],
                    in1=psY[:, ts * 512:(ts + 1) * 512])
                c.inc(ad, "v")
        c.barrier()
    c.barrier()
    c.pop()



def p_of_f(f):
    return (f >> 7) | (((f >> 5) & 3) << 5)


def f_of_p(c2, p2):
    return c2 | ((p2 & 31) << 7) | (((p2 >> 5) & 3) << 5)


def _wvec4(stage_w, s, f_a):
    s2g = 2 ** s
    wr = stage_w.reshape(4096 // (2 * s2g), s2g, 2, 2)
    i = f_a >> (s + 1)
    j = f_a & (s2g - 1)
    w = wr[i, j]
    return np.stack([w[:, 0, 0], w[:, 0, 1], w[:, 1, 0], w[:, 1, 1]])


def _fa_orig(s2):
    co = np.arange(32)
    nb = 128 // (2 * s2)
    kb = np.arange(nb)
    ks = np.arange(s2)
    f = (co[:, None, None] * 128 + kb[None, :, None] * 2 * s2 + ks[None, None, :])
    return f.reshape(-1)


def _fa_P(s2p):
    c2 = np.arange(32)
    nb = 128 // (2 * s2p)
    kb = np.arange(nb)
    ks = np.arange(s2p)
    p2a = kb[:, None] * 2 * s2p + ks[None, :]
    f = f_of_p(c2[:, None, None], p2a[None])
    return f.reshape(-1)


def butterfly_builder_plan(up_w, dn_w):
    rows = []
    plans = {}
    st = []
    for s in range(7):
        s2 = 2 ** s
        rows.append(_wvec4(up_w[s], s, _fa_orig(s2)))
        st.append(s2)
    plans["M1"] = st
    st = []
    for s in range(7, 12):
        s2p = 2 ** (s - 7)
        rows.append(_wvec4(up_w[s], s, _fa_P(s2p)))
        st.append(s2p)
    plans["D1"] = st
    st = []
    for s in range(7):
        s2 = 2 ** s
        rows.append(_wvec4(dn_w[s], s, _fa_orig(s2)))
        st.append(s2)
    plans["N1"] = st
    st = []
    for s in range(7, 12):
        s2p = 2 ** (s - 7)
        rows.append(_wvec4(dn_w[s], s, _fa_P(s2p)))
        st.append(s2p)
    plans["N2"] = st
    bfw = np.stack(rows).astype(np.float32)
    return bfw, plans



def host_slices(inputs):
    per_core = []
    x = inputs["L0_attn_x"]
    W1 = inputs["L0_attn_W1"]
    Q1, K1, V1 = inputs["L1_attn_Q"], inputs["L1_attn_K"], inputs["L1_attn_V"]
    for cid in range(NCORE):
        b, qd = cid // 4, cid % 4
        hs = list(range(4 * qd, 4 * qd + 4))
        d = {}
        d["x0"] = np.ascontiguousarray(x[b, qd * TO:(qd + 1) * TO, :])
        d["wq0"] = np.ascontiguousarray(
            np.concatenate([W1[h * D:(h + 1) * D, :].T for h in hs], axis=1))
        d["wk0"] = np.ascontiguousarray(
            np.concatenate([W1[E + h * D:E + (h + 1) * D, :].T for h in hs], axis=1))
        d["wv0"] = np.ascontiguousarray(
            np.concatenate([W1[2 * E + h * D:2 * E + (h + 1) * D, :].T for h in hs], axis=1))
        d["wq1"] = np.ascontiguousarray(Q1[hs].transpose(2, 0, 1).reshape(E, 4 * D))
        d["wk1"] = np.ascontiguousarray(K1[hs].transpose(2, 0, 1).reshape(E, 4 * D))
        d["wv1"] = np.ascontiguousarray(V1[hs].transpose(2, 0, 1).reshape(E, 4 * D))
        per_core.append(d)
    shared = {}
    shared["w2t"] = np.ascontiguousarray(inputs["L0_attn_W2"].T)
    shared["ot"] = np.ascontiguousarray(inputs["L1_attn_O"].T)
    shared["wexpT"] = np.ascontiguousarray(inputs["L0_ffn_W_exp"].T)
    shared["bexp_cm"] = np.ascontiguousarray(
        inputs["L0_ffn_b_exp"].reshape(32, 128).T)
    bfw, _plans = butterfly_builder_plan(inputs["L0_ffn_up_weights"],
                                         inputs["L0_ffn_down_weights"])
    shared["bfw"] = bfw
    fidx = np.array([f_of_p(i // 128, i % 128) for i in range(4096)])
    ub = inputs["L0_ffn_up_bias"]
    up_bP = ub[fidx].reshape(32, 128).T
    shared["up_bP"] = np.ascontiguousarray(up_bP.astype(np.float32))
    shared["wconP"] = np.ascontiguousarray(inputs["L0_ffn_W_con"][fidx])
    shared["brow2"] = np.ascontiguousarray(
        np.stack([inputs["L0_ffn_b_con"], inputs["L0_ffn_down_bias"]]))
    shared["w1t"] = np.ascontiguousarray(inputs["L1_ffn_W1"].T)
    shared["b1cm"] = np.ascontiguousarray(inputs["L1_ffn_b1"].reshape(32, 128).T)
    shared["w2t_ffn"] = np.ascontiguousarray(inputs["L1_ffn_W2"].T)
    shared["b2row"] = np.ascontiguousarray(inputs["L1_ffn_b2"][None, :])
    p = np.arange(128)
    inv = (1.0 / 10000.0 ** (np.arange(0, D, 2, dtype=np.float32) / D))
    ang = np.arange(T, dtype=np.float32)[None, :] * inv[p % 32][:, None]
    cos = np.cos(ang).astype(np.float32)
    sin = np.sin(ang).astype(np.float32)
    sgn = np.where(((p ^ 32) % 64) < 32, -1.0, 1.0).astype(np.float32)[:, None]
    shared["rope_cos"] = cos
    shared["rope_sin"] = (sin * sgn)
    p32 = np.zeros((128, 128), dtype=np.float32)
    p32[p, p ^ 32] = 1.0
    shared["p32"] = p32
    return per_core, shared


def build_C0():
    nc = bass.Bass("TRN2", target_bir_lowering=False)
    x_in = nc.dram_tensor("x_in", [TO, E], F32, kind="ExternalInput")
    z_in = nc.dram_tensor("z_in", [E, TO], BF16, kind="ExternalInput")
    w_ot = nc.dram_tensor("w_ot", [E, E], F32, kind="ExternalInput")
    wexpT = nc.dram_tensor("wexpT", [DH, E], F32, kind="ExternalInput")
    bexp_cm = nc.dram_tensor("bexp_cm", [128, 32], F32, kind="ExternalInput")
    bfw = nc.dram_tensor("bfw", [24, 4, 2048], F32, kind="ExternalInput")
    up_bP = nc.dram_tensor("up_bP", [128, 32], F32, kind="ExternalInput")
    wconP = nc.dram_tensor("wconP", [DH, E], F32, kind="ExternalInput")
    brow2 = nc.dram_tensor("brow2", [2, E], F32, kind="ExternalInput")
    x_out = nc.dram_tensor("x_out", [TO, E], F32, kind="ExternalOutput")
    xln_out = nc.dram_tensor("xln_out", [E, TO], BF16, kind="ExternalOutput")
    xln2 = nc.dram_tensor("xln2_i", [E, TO], BF16)
    w1e = nc.dram_tensor("w1e_i", [32, E, 128], BF16)
    b1e = nc.dram_tensor("b1e_i", [1, DH], F32)
    w2e = nc.dram_tensor("w2e_i", [DH, E], BF16)
    relay = nc.dram_tensor("relay_i", [32, 128, 512], BF16)
    c = Ctx(nc)
    x_sb = c.sb("xres", [128, NTT * E], F32)
    mats = {nm: c.sb(f"mat_{nm}", [128, 32 * 128], BF16)
            for nm in ("M1", "D1", "N1", "N2")}
    _, plans = butterfly_builder_plan(np.zeros((12, 2048, 2, 2), np.float32),
                                     np.zeros((12, 2048, 2, 2), np.float32))
    ld = nc.sync.dma_start(out=x_sb.ap().rearrange("p (n f) -> p n f", f=E),
                           in_=x_in.ap().rearrange("(n p) f -> p n f", p=128))
    c.inc(ld, "dh", 16)
    c.wait(nc.sync, "dh")
    c.barrier()
    emit_outproj_residual(c, z_in.ap(), w_ot.ap(), x_sb, "c0")
    emit_ln_transpose(c, None, xln2.ap(), "c0l", x_sb=x_sb)
    emit_bfly_matrices(c, bfw.ap(), mats, plans, "c0")
    emit_fold_w1(c, wexpT.ap(), bexp_cm.ap(), mats["M1"], w1e.ap(), b1e.ap(), "c0")
    emit_fold_w2(c, wconP.ap(), mats["N2"], w2e.ap(), "c0")
    emit_ffn_l0(c, xln2.ap(), w1e.ap(), b1e.ap(), up_bP.ap(), w2e.ap(),
                brow2.ap(), mats["D1"], mats["N1"], relay.ap(), x_sb, "c0")
    emit_ln_transpose(c, None, xln_out.ap(), "c0m", x_sb=x_sb)
    emit_store_x(c, x_sb, x_out.ap())
    c.close()
    return nc


def build_C1():
    nc = bass.Bass("TRN2", target_bir_lowering=False)
    x_in = nc.dram_tensor("x_in", [TO, E], F32, kind="ExternalInput")
    z_in = nc.dram_tensor("z_in", [E, TO], BF16, kind="ExternalInput")
    w_ot = nc.dram_tensor("w_ot", [E, E], F32, kind="ExternalInput")
    w1t = nc.dram_tensor("w1t", [E, DH], F32, kind="ExternalInput")
    b1cm = nc.dram_tensor("b1cm", [128, 32], F32, kind="ExternalInput")
    w2t = nc.dram_tensor("w2t", [DH, E], F32, kind="ExternalInput")
    b2row = nc.dram_tensor("b2row", [1, E], F32, kind="ExternalInput")
    x_out = nc.dram_tensor("x_out", [TO, E], F32, kind="ExternalOutput")
    xln2 = nc.dram_tensor("xln2_i", [E, TO], BF16)
    wq1 = nc.dram_tensor("wq1_i", [32, E, 128], BF16)
    wq2 = nc.dram_tensor("wq2_i", [DH, E], BF16)
    c = Ctx(nc)
    x_sb = c.sb("xres", [128, NTT * E], F32)
    ld = nc.sync.dma_start(out=x_sb.ap().rearrange("p (n f) -> p n f", f=E),
                           in_=x_in.ap().rearrange("(n p) f -> p n f", p=128))
    c.inc(ld, "dh", 16)
    c.wait(nc.sync, "dh")
    c.barrier()
    emit_outproj_residual(c, z_in.ap(), w_ot.ap(), x_sb, "c1")
    emit_ln_transpose(c, None, xln2.ap(), "c1l", x_sb=x_sb)
    emit_ffn_l1(c, xln2.ap(), w1t.ap(), b1cm.ap(), w2t.ap(), b2row.ap(),
                wq1.ap(), wq2.ap(), x_sb, "c1")
    emit_store_x(c, x_sb, x_out.ap())
    c.close()
    return nc



def _gather_xln(outs, key):
    xgs = []
    for b in range(2):
        xg = np.concatenate([outs[4 * b + r][key] for r in range(4)], axis=0)
        xgs.append(np.ascontiguousarray(xg))
    return [xgs[cid // 4] for cid in range(NCORE)]


def _a2a_z(outs, key):
    res = []
    for cid in range(NCORE):
        b, qd = cid // 4, cid % 4
        z = np.concatenate(
            [outs[4 * b + r][key][qd * 256:(qd + 1) * 256, :] for r in range(4)],
            axis=0)
        res.append(np.ascontiguousarray(z))
    return res


def kernel_unfused(**inputs):
    inputs = {k: np.ascontiguousarray(v) for k, v in inputs.items()}
    per_core, shared = host_slices(inputs)
    cores = list(range(NCORE))

    ncA = build_A()
    rA = run_bass_kernel_spmd(ncA, [{"x_in": pc["x0"]} for pc in per_core],
                              core_ids=cores)
    xg = _gather_xln(rA.results, "xln")

    ncB0 = build_B(rope=True, scale=0.125)
    inB0 = [{"xg": bfc(xg[i]), "wq": per_core[i]["wq0"], "wk": per_core[i]["wk0"],
             "wv": per_core[i]["wv0"], "rcos": shared["rope_cos"],
             "rsin": shared["rope_sin"], "p32": shared["p32"]} for i in cores]
    rB0 = run_bass_kernel_spmd(ncB0, inB0, core_ids=cores)
    z0 = _a2a_z(rB0.results, "zo")

    ncC0 = build_C0()
    inC0 = [{"x_in": per_core[i]["x0"], "z_in": z0[i], "w_ot": shared["w2t"],
             "wexpT": shared["wexpT"], "bexp_cm": shared["bexp_cm"],
             "bfw": shared["bfw"], "up_bP": shared["up_bP"],
             "wconP": shared["wconP"], "brow2": shared["brow2"]} for i in cores]
    rC0 = run_bass_kernel_spmd(ncC0, inC0, core_ids=cores)
    xg1 = _gather_xln(rC0.results, "xln_out")

    ncB1 = build_B(rope=False, scale=1.0)
    inB1 = [{"xg": bfc(xg1[i]), "wq": per_core[i]["wq1"], "wk": per_core[i]["wk1"],
             "wv": per_core[i]["wv1"]} for i in cores]
    rB1 = run_bass_kernel_spmd(ncB1, inB1, core_ids=cores)
    z1 = _a2a_z(rB1.results, "zo")

    ncC1 = build_C1()
    inC1 = [{"x_in": rC0.results[i]["x_out"], "z_in": z1[i], "w_ot": shared["ot"],
             "w1t": shared["w1t"], "b1cm": shared["b1cm"],
             "w2t": shared["w2t_ffn"], "b2row": shared["b2row"]} for i in cores]
    rC1 = run_bass_kernel_spmd(ncC1, inC1, core_ids=cores)

    out = np.empty((B, T, E), dtype=np.float32)
    for cid in cores:
        b, qd = cid // 4, cid % 4
        out[b, qd * TO:(qd + 1) * TO, :] = rC1.results[cid]["x_out"]
    return out



def emit_collective(c, kind, in_ap, out_ap, tag):
    nc = c.nc
    cc = nc.gpsimd.collective_compute(
        kind, mybir.AluOpType.bypass,
        replica_groups=[[0, 1, 2, 3], [4, 5, 6, 7]],
        ins=[in_ap], outs=[out_ap])
    c.inc(cc, "cc")
    c.wait(nc.gpsimd, "cc")
    c.barrier()


def build_fused():
    nc = bass.Bass("TRN2", target_bir_lowering=False)
    t_in = {}
    def inp(name, shape, dtype=F32):
        t_in[name] = nc.dram_tensor(name, shape, dtype, kind="ExternalInput")
        return t_in[name].ap()
    x0 = inp("x0", [TO, E])
    wq0 = inp("wq0", [E, 256]); wk0 = inp("wk0", [E, 256]); wv0 = inp("wv0", [E, 256])
    rcos = inp("rcos", [128, T]); rsin = inp("rsin", [128, T]); p32 = inp("p32", [128, 128])
    w2t = inp("w2t", [E, E])
    wexpT = inp("wexpT", [DH, E]); bexp_cm = inp("bexp_cm", [128, 32])
    bfw = inp("bfw", [24, 4, 2048]); up_bP = inp("up_bP", [128, 32])
    wconP = inp("wconP", [DH, E]); brow2 = inp("brow2", [2, E])
    wq1 = inp("wq1", [E, 256]); wk1 = inp("wk1", [E, 256]); wv1 = inp("wv1", [E, 256])
    ot = inp("ot", [E, E])
    w1t = inp("w1t", [E, DH]); b1cm = inp("b1cm", [128, 32])
    w2t_ffn = inp("w2t_ffn", [DH, E]); b2row = inp("b2row", [1, E])
    x_out = nc.dram_tensor("x_out", [TO, E], F32, kind="ExternalOutput")

    def internal(name, shape, dtype, **kw):
        return nc.dram_tensor(name, shape, dtype, **kw).ap()
    xln_a = internal("xln_a", [4, E, 512], BF16)
    xg_a = internal("xg_a", [4, 4 * E, 512], BF16)
    q0d = internal("q0d", [256, T], BF16)
    k0d = internal("k0d", [256, T], BF16)
    v0d = internal("v0d", [T, 256], BF16)
    zsh0 = internal("zsh0", [E, TO], BF16)
    zg0 = internal("zg0", [4, 1024, TO], BF16)
    xln2 = internal("xln2", [E, TO], BF16)
    w1e = internal("w1e", [32, E, 128], BF16)
    b1e = internal("b1e", [1, DH], F32)
    w2e = internal("w2e", [DH, E], BF16)
    relay = internal("relay", [32, 128, 512], BF16)
    xln_b = internal("xln_b", [4, E, 512], BF16)
    xg_b = internal("xg_b", [4, 4 * E, 512], BF16)
    q1d = internal("q1d", [256, T], BF16)
    k1d = internal("k1d", [256, T], BF16)
    v1d = internal("v1d", [T, 256], BF16)
    zsh1 = internal("zsh1", [E, TO], BF16)
    zg1 = internal("zg1", [4, 1024, TO], BF16)
    xln3 = internal("xln3", [E, TO], BF16)
    part_i = internal("part_i", [1, 128], F32)
    partg_i = internal("partg_i", [1, 512], F32)
    wq1c = internal("wq1c", [64, 128, 128], BF16)
    wq1g = internal("wq1g", [256, 128, 128], BF16)
    wq2c = internal("wq2c", [8, 128, 1024], BF16)
    wq2g = internal("wq2g", [DH, E], BF16)

    c = Ctx(nc)
    c.sem("cc")
    qd_sv = nc.partition_id() % 4
    x_sb = c.sb("xres", [128, NTT * E], F32)
    ld = nc.sync.dma_start(out=x_sb.ap().rearrange("p (n f) -> p n f", f=E),
                           in_=x0.rearrange("(n p) f -> p n f", p=128))
    c.inc(ld, "dh", 16)
    c.wait(nc.sync, "dh")
    c.barrier()
    cc0 = c.counts.get("cc", 0)
    emit_ln_transpose(c, None, xln_a, "fa", x_sb=x_sb, gather=xg_a)
    emit_qkv(c, xg_a, wq0, wk0, wv0, q0d, k0d, v0d, "f0",
             rope_cs=(rcos, rsin, p32), cc_base=cc0)
    emit_attention(c, q0d, k0d, v0d, zsh0, "f0", 0.125, zg_out=zg0)
    emit_outproj_residual(c, None, w2t, x_sb, "f0",
                          z_gathered=(zg0, qd_sv, c.counts["cc"]))
    emit_ln_transpose(c, None, xln2, "f0l", x_sb=x_sb)
    c.push()
    mats = {nm: c.sb(f"mat_{nm}", [128, 32 * 128], BF16)
            for nm in ("M1", "D1", "N1", "N2")}
    _, plans = butterfly_builder_plan(np.zeros((12, 2048, 2, 2), np.float32),
                                     np.zeros((12, 2048, 2, 2), np.float32))
    emit_bfly_matrices(c, bfw, mats, plans, "f0")
    emit_fold_w1(c, wexpT, bexp_cm, mats["M1"], w1e, b1e, "f0")
    emit_fold_w2(c, wconP, mats["N2"], w2e, "f0")
    emit_ffn_l0(c, xln2, w1e, b1e, up_bP, w2e, brow2,
                mats["D1"], mats["N1"], relay, x_sb, "f0")
    c.pop()
    emit_quantize_dist(c, w1t, w2t_ffn, part_i, partg_i, wq1c, wq1g,
                       wq2c, wq2g, qd_sv, "fq")
    ccq = c.counts["cc"]
    cc1 = c.counts["cc"]
    emit_ln_transpose(c, None, xln_b, "fb", x_sb=x_sb, gather=xg_b)
    emit_qkv(c, xg_b, wq1, wk1, wv1, q1d, k1d, v1d, "f1", rope_cs=None,
             cc_base=cc1)
    emit_attention(c, q1d, k1d, v1d, zsh1, "f1", 1.0, zg_out=zg1)
    emit_outproj_residual(c, None, ot, x_sb, "f1",
                          z_gathered=(zg1, qd_sv, c.counts["cc"]))
    emit_ln_transpose(c, None, xln3, "f1l", x_sb=x_sb)
    emit_ffn_l1(c, xln3, b1cm, b2row, wq1g, wq2g, x_sb, "f1",
                cc_need=ccq)
    emit_store_x(c, x_sb, x_out.ap())
    c.close()
    return nc


def kernel(**inputs):
    inputs = {k: np.ascontiguousarray(v) for k, v in inputs.items()}
    per_core, shared = host_slices(inputs)
    cores = list(range(NCORE))
    nc = build_fused()
    in_maps = []
    for i in cores:
        pc = per_core[i]
        in_maps.append({
            "x0": pc["x0"], "wq0": pc["wq0"], "wk0": pc["wk0"], "wv0": pc["wv0"],
            "rcos": shared["rope_cos"], "rsin": shared["rope_sin"],
            "p32": shared["p32"], "w2t": shared["w2t"],
            "wexpT": shared["wexpT"], "bexp_cm": shared["bexp_cm"],
            "bfw": shared["bfw"], "up_bP": shared["up_bP"],
            "wconP": shared["wconP"], "brow2": shared["brow2"],
            "wq1": pc["wq1"], "wk1": pc["wk1"], "wv1": pc["wv1"],
            "ot": shared["ot"], "w1t": shared["w1t"], "b1cm": shared["b1cm"],
            "w2t_ffn": shared["w2t_ffn"], "b2row": shared["b2row"],
        })
    res = run_bass_kernel_spmd(nc, in_maps, core_ids=cores)
    out = np.empty((B, T, E), dtype=np.float32)
    for cid in cores:
        b, qd = cid // 4, cid % 4
        out[b, qd * TO:(qd + 1) * TO, :] = res.results[cid]["x_out"]
    return out
